# version 02
# baseline (speedup 1.0000x reference)
import numpy as np
import ml_dtypes
from contextlib import ExitStack

import concourse.bass as bass
import concourse.mybir as mybir
from concourse.bass_utils import run_bass_kernel_spmd

F32 = mybir.dt.float32
BF16 = mybir.dt.bfloat16
U8 = mybir.dt.uint8
U16 = mybir.dt.uint16
AF = mybir.ActivationFunctionType
ALU = mybir.AluOpType
AX = mybir.AxisListType
BF16NP = ml_dtypes.bfloat16

T = 2048
D = 2048
DC = 16
NTT = 4
EPS = 1e-5
TWO126 = float(2.0 ** 126)
RG4 = [[0, 1, 2, 3], [4, 5, 6, 7]]
RG8 = [[0, 1, 2, 3, 4, 5, 6, 7]]
HID1 = 5632
HC1 = HID1 // 128
MLP0 = 16384


class Prog:
    def __init__(self):
        self.nc = bass.Bass("TRN2", target_bir_lowering=False)
        self.ctx = ExitStack()
        self.step = 0
        self.ld = 0
        self.st = 0
        self.cc = 0

    def start(self):
        c = self.ctx.enter_context
        self.fl = c(self.nc.semaphore("fl"))
        self.ldh = c(self.nc.semaphore("ldh"))
        self.sth = c(self.nc.semaphore("sth"))
        self.cch = c(self.nc.semaphore("cch"))
        self.block = c(self.nc.Block())
        self.pA = c(self.nc.psum_tensor("pA", [128, 512], F32))
        self.pB = c(self.nc.psum_tensor("pB", [128, 512], F32))
        self.pC = c(self.nc.psum_tensor("pC", [128, 512], F32))
        self.pD = c(self.nc.psum_tensor("pD", [128, 1024], F32))
        self.pE = c(self.nc.psum_tensor("pE", [128, 512], F32))
        self.pF = c(self.nc.psum_tensor("pF", [128, 512], F32))
        self.pT = c(self.nc.psum_tensor("pT", [128, 1024], BF16))
        return self

    def sb(self, name, shape, dtype):
        return self.ctx.enter_context(self.nc.sbuf_tensor(name, shape, dtype))

    def do(self, section, emit):
        dep = self.step

        def wrapped(eng):
            eng.wait_ge(self.fl, dep)
            last = emit(eng)
            last.then_inc(self.fl, 1)
        section(wrapped)
        self.step += 1
        return self.step

    def load(self, emit, n=1):
        self.block.sync(emit)
        return self.ld

    def dma(self, q, out, in_, accum=False):
        kw = {"accum_op": ALU.add} if accum else {}
        q.dma_start(out=out, in_=in_, **kw).then_inc(self.ldh, 16)
        self.ld += 16
        return self.ld

    def dma_st(self, q, out, in_, accum=False):
        kw = {"accum_op": ALU.add} if accum else {}
        q.dma_start(out=out, in_=in_, **kw).then_inc(self.sth, 16)
        self.st += 16
        return self.st

    def close(self):
        self.ctx.close()


_ESZ = {F32: 4, BF16: 2, U8: 1, U16: 2}


class SAlloc:

    def __init__(self, p, pre, fuse):
        self.p = p
        self.pre = pre
        self.fuse = fuse
        self.off = 0

    def t(self, name, shape, dtype):
        if self.fuse is None:
            return self.p.sb(self.pre + name, shape, dtype)
        nb = shape[1] * _ESZ[dtype]
        nb_pad = (nb + 31) // 32 * 32
        o = self.off
        self.off += nb_pad
        R = self.fuse["REST"]
        assert self.off <= R.shape[1], f"REST overflow at {name}: {self.off}"
        v = R[:, o:o + nb].bitcast(dtype)
        if shape[0] != 128:
            v = v[0:shape[0], :]
        return v



def bview(buf, byte_off, dtype, cols):
    esz = _ESZ[dtype]
    bsz = _ESZ[buf.dtype]
    assert byte_off % bsz == 0 and (cols * esz) % bsz == 0
    return buf[:, byte_off // bsz:(byte_off + cols * esz) // bsz].bitcast(dtype)



def emit_stats(p, src_bf, C, srow, rstd, sq):
    block = p.block
    onesd = C["ones_col_div"]

    for tt in range(NTT):
        def f(t, tt=tt):
            for j in range(DC):
                mm = t.matmul(p.pE[0:1, 0:512], onesd[:, 0:1],
                              src_bf[:, j * T + tt * 512:j * T + (tt + 1) * 512],
                              start=(j == 0), stop=(j == DC - 1))
            return mm
        p.do(block.tensor, f)

        def f(v, tt=tt):
            return v.tensor_copy(srow[0:1, tt * 512:(tt + 1) * 512], p.pE[0:1, 0:512])
        p.do(block.vector, f)

    for tt in range(NTT):
        for j in range(DC):
            def f(sc, tt=tt, j=j):
                return sc.activation(sq[:, (j % 2) * 1024:(j % 2) * 1024 + 1024].bitcast(BF16)[:, 0:512],
                                     src_bf[:, j * T + tt * 512:j * T + (tt + 1) * 512],
                                     AF.Square)
            p.do(block.scalar, f)

            def f(t, tt=tt, j=j):
                return t.matmul(p.pF[0:1, 0:512], onesd[:, 0:1],
                                sq[:, (j % 2) * 1024:(j % 2) * 1024 + 1024].bitcast(BF16)[:, 0:512],
                                start=(j == 0), stop=(j == DC - 1))
            p.do(block.tensor, f)

        def f(v, tt=tt):
            return v.tensor_copy(srow[0:1, T + tt * 512:T + (tt + 1) * 512], p.pF[0:1, 0:512])
        p.do(block.vector, f)

    def f(v):
        v.tensor_tensor(rstd[0:1, 0:T], srow[0:1, 0:T], srow[0:1, 0:T], ALU.mult)
        return v.tensor_tensor(srow[0:1, T:2 * T], srow[0:1, T:2 * T], rstd[0:1, 0:T], ALU.subtract)
    p.do(block.vector, f)

    def f(sc):
        return sc.activation(rstd[0:1, 0:T], srow[0:1, T:2 * T], AF.Sqrt, bias=np.float32(EPS), scale=np.float32(1.0))
    p.do(block.scalar, f)

    def f(v):
        return v.reciprocal(rstd[0:1, 0:T], rstd[0:1, 0:T])
    p.do(block.vector, f)


def emit_bcast_stats(p, C, srow, rstd, mbc, rbc):
    block = p.block
    ones_row = C["ones_row_f32"]
    for tt in range(NTT):
        def f(t, tt=tt):
            t.matmul(p.pD[:, 0:512], ones_row[0:1, :],
                     srow[0:1, tt * 512:(tt + 1) * 512], start=True, stop=True)
            return t.matmul(p.pD[:, 512:1024], ones_row[0:1, :],
                            rstd[0:1, tt * 512:(tt + 1) * 512], start=True, stop=True)
        p.do(block.tensor, f)

        def f(v, tt=tt):
            v.tensor_copy(mbc[:, tt * 512:(tt + 1) * 512], p.pD[:, 0:512])
            return v.tensor_copy(rbc[:, tt * 512:(tt + 1) * 512], p.pD[:, 512:1024])
        p.do(block.vector, f)


def emit_ln(p, x_dram, xbf, C, scratch):
    block = p.block
    xf, sq = scratch["xf"], scratch["sq"]
    srow, rstd = scratch["srow"], scratch["rstd"]
    mbc, rbc, tmp = scratch["mbc"], scratch["rbc"], scratch["tmp"]

    ld0 = p.ld
    copy_step0 = p.step

    def s_load(sync):
        sync.wait_ge(p.fl, copy_step0)
        for j in range(DC):
            if j >= 2:
                sync.wait_ge(p.fl, copy_step0 + (j - 2) + 1)
            p.dma(sync, xf[:, (j % 2) * T:(j % 2 + 1) * T], x_dram[j * 128:(j + 1) * 128, :])
    p.block.sync(s_load)

    for j in range(DC):
        def f(v, j=j):
            v.wait_ge(p.ldh, ld0 + 16 * (j + 1))
            return v.tensor_copy(xbf[:, j * T:(j + 1) * T], xf[:, (j % 2) * T:(j % 2 + 1) * T])
        p.do(block.vector, f)

    emit_stats(p, xbf, C, srow, rstd, sq)
    emit_bcast_stats(p, C, srow, rstd, mbc, rbc)

    for j in range(DC):
        def f(v, j=j):
            for tt in range(NTT):
                sl = slice(j * T + tt * 512, j * T + (tt + 1) * 512)
                st = slice(tt * 512, (tt + 1) * 512)
                v.tensor_tensor(tmp[:, 0:512], xbf[:, sl], mbc[:, st], ALU.subtract)
                last = v.tensor_tensor(xbf[:, sl], tmp[:, 0:512], rbc[:, st], ALU.mult)
            return last
        p.do(block.vector, f)


def emit_postln_residual(p, mix_dram, x_dram, out_dram, C, scratch, mbf):
    block = p.block
    xf, sq = scratch["xf"], scratch["sq"]
    srow, rstd = scratch["srow"], scratch["rstd"]
    mbc, rbc, tmp = scratch["mbc"], scratch["rbc"], scratch["tmp"]
    mf2 = scratch["mf2"]

    ld0 = p.ld
    copy_step0 = p.step

    def s_load(sync):
        sync.wait_ge(p.fl, copy_step0)
        for j in range(DC):
            if j >= 2:
                sync.wait_ge(p.fl, copy_step0 + (j - 2) + 1)
            p.dma(sync, xf[:, (j % 2) * T:(j % 2 + 1) * T], mix_dram[j * 128:(j + 1) * 128, :])
    block.sync(s_load)

    for j in range(DC):
        def f(v, j=j):
            v.wait_ge(p.ldh, ld0 + 16 * (j + 1))
            return v.tensor_copy(mbf[:, j * T:(j + 1) * T], xf[:, (j % 2) * T:(j % 2 + 1) * T])
        p.do(block.vector, f)

    emit_stats(p, mbf, C, srow, rstd, sq)
    emit_bcast_stats(p, C, srow, rstd, mbc, rbc)

    ld1 = p.ld
    norm_step0 = p.step

    for j in range(DC):
        def f(v, j=j):
            v.wait_ge(p.ldh, ld1 + 32 * (j + 1))
            for tt in range(NTT):
                slm = slice((j % 2) * T + tt * 512, (j % 2) * T + (tt + 1) * 512)
                st = slice(tt * 512, (tt + 1) * 512)
                v.tensor_tensor(tmp[:, 0:512], mf2[:, slm], mbc[:, st], ALU.subtract)
                v.tensor_tensor(tmp[:, 0:512], tmp[:, 0:512], rbc[:, st], ALU.mult)
                last = v.tensor_tensor(mf2[:, slm], xf[:, slm], tmp[:, 0:512], ALU.add)
            return last
        p.do(block.vector, f)

    def s_stream(sync):
        st_after = {}
        for j in range(DC):
            if j >= 2:
                sync.wait_ge(p.fl, norm_step0 + (j - 2) + 1)
                p.dma_st(sync, out_dram[(j - 2) * 128:(j - 1) * 128, :],
                         mf2[:, (j % 2) * T:(j % 2 + 1) * T])
                st_after[j - 2] = p.st
            if j >= 4:
                sync.wait_ge(p.sth, st_after[j - 4])
            p.dma(sync, xf[:, (j % 2) * T:(j % 2 + 1) * T], x_dram[j * 128:(j + 1) * 128, :])
            p.dma(sync, mf2[:, (j % 2) * T:(j % 2 + 1) * T], mix_dram[j * 128:(j + 1) * 128, :])
        for j in (DC - 2, DC - 1):
            sync.wait_ge(p.fl, norm_step0 + j + 1)
            p.dma_st(sync, out_dram[j * 128:(j + 1) * 128, :],
                     mf2[:, (j % 2) * T:(j % 2 + 1) * T])
    block.sync(s_stream)


def emit_store_wait(p):
    def f(g):
        g.wait_ge(p.sth, p.st)
        return g.engine_nop()
    p.do(p.block.gpsimd, f)



def build_l1ffn(fuse=None):
    if fuse is None:
        p = Prog().start()
        own = True
    else:
        p, own = fuse["p"], False
    nc = p.nc
    pre = "" if own else fuse["pre"]
    if own:
        x_in = nc.dram_tensor("x_in", [D, T], F32, kind="ExternalInput").ap()
    else:
        x_in = fuse["x_in"]
    wg = nc.dram_tensor("wg", [D, HID1], U8, kind="ExternalInput").ap()
    wu = nc.dram_tensor("wu", [D, HID1], U8, kind="ExternalInput").ap()
    wd = nc.dram_tensor("wd", [HID1, D], U8, kind="ExternalInput").ap()
    sg = nc.dram_tensor("sg", [1, HID1], F32, kind="ExternalInput").ap()
    su = nc.dram_tensor("su", [1, HID1], F32, kind="ExternalInput").ap()
    sd = nc.dram_tensor("sd", [1, D], F32, kind="ExternalInput").ap()
    if own:
        cdiv = nc.dram_tensor("cdiv", [128, 1], BF16, kind="ExternalInput").ap()
        crow = nc.dram_tensor("crow", [1, 128], F32, kind="ExternalInput").ap()
    cm128 = nc.dram_tensor("cm128", [128, 1], F32, kind="ExternalInput").ap()
    if own:
        x_out = nc.dram_tensor("x_out", [D, T], F32, kind="ExternalOutput").ap()
    else:
        x_out = fuse["x_out"]

    i_dram = nc.dram_tensor(pre + "i_dram", [HID1, T], BF16)
    wd_dram = nc.dram_tensor(pre + "wd_dram", [HID1, D], BF16)
    mix_dram = nc.dram_tensor(pre + "mix_dram", [D, T], F32)

    S = SAlloc(p, pre, fuse)
    if fuse is None:
        xbf = p.sb("xbf", [128, DC * T], BF16)
        aW = p.sb("aW", [128, 49152], U8)
    else:
        xbf = bview(fuse["P1"], 0, BF16, DC * T)
        aW = bview(fuse["P2"], 0, U8, 49152)
    aM = S.t("aM", [128, 24576], U8)
    xf = S.t("xf", [128, 2 * T], F32)
    mf2 = bview(aW, 0, F32, 2 * T)
    mbc = bview(aM, 0, F32, T)
    rbc = bview(aM, 8192, F32, T)
    sq = S.t("sq", [128, 2048], U8)
    srow = aW[0:1, 16384:16384 + 8 * T].bitcast(F32)
    rstd = aW[0:1, 32768:32768 + 4 * T].bitcast(F32)
    tmp = S.t("tmp", [128, 512], F32)
    sbS = S.t("sbS", [128, 2 * HID1 + D], BF16)
    scratch = dict(xf=xf, sq=sq, srow=srow, rstd=rstd, mbc=mbc, rbc=rbc,
                   tmp=tmp, mf2=mf2)

    C = {}
    block = p.block
    if fuse is None:
        C["ones_col_div"] = p.sb("c_div", [128, 1], BF16)
        C["ones_row_f32"] = p.sb("c_row", [1, 128], F32)
    else:
        C["ones_col_div"] = fuse["C"]["ones_col_div"]
        C["ones_row_f32"] = fuse["C"]["ones_row_f32"]
    C["m128_col"] = S.t("c_m128", [128, 1], F32)
    C["sg_row"] = aW[0:1, 0:4 * HID1].bitcast(F32)
    C["su_row"] = aW[0:1, 24576:24576 + 4 * HID1].bitcast(F32)
    C["sd_row"] = aM[0:1, 16384:16384 + 4 * D].bitcast(F32)

    def s_consts(sync):
        if fuse is None:
            p.dma(sync, C["ones_col_div"][:], cdiv)
            p.dma(sync, C["ones_row_f32"][:], crow)
        p.dma(sync, C["m128_col"][:], cm128)
        p.dma(sync, C["sg_row"], sg)
        p.dma(sync, C["su_row"], su)
        p.dma(sync, C["sd_row"], sd)
    block.sync(s_consts)
    ld_consts = p.ld

    def f(v):
        v.wait_ge(p.ldh, ld_consts)
        return v.engine_nop()
    p.do(block.vector, f)

    def sbS_view(which):
        if which == "g":
            return sbS[:, 0:HID1]
        if which == "u":
            return sbS[:, HID1:2 * HID1]
        return sbS[:, 2 * HID1:2 * HID1 + D]

    for which, row, n in (("g", "sg_row", HID1), ("u", "su_row", HID1), ("d", "sd_row", D)):
        for i in range(n // 512):
            def f(t, which=which, row=row, i=i):
                return t.matmul(p.pD[:, 0:512], C["ones_row_f32"][0:1, :],
                                C[row][0:1, i * 512:(i + 1) * 512], start=True, stop=True)
            p.do(block.tensor, f)

            def f(v, which=which, i=i):
                return v.tensor_copy(sbS_view(which)[:, i * 512:(i + 1) * 512], p.pD[:, 0:512])
            p.do(block.vector, f)

    emit_ln(p, x_in, xbf, C, scratch)

    cod_g = bview(aW, 0, U8, 16 * 512)
    cod_u = bview(aW, 8192, U8, 16 * 512)
    wg_bf = bview(aW, 16384, BF16, 16 * 512)
    wu_bf = bview(aW, 32768, BF16, 16 * 512)
    cod_d = bview(aM, 0, U8, 4 * 2048)
    wd_bf = bview(aM, 8192, BF16, 4 * 2048)
    gsil = S.t("gsil", [128, 512], BF16)
    itile = S.t("itile", [128, 1024], BF16)
    it_st = []
    wd_st = []

    NHS = HID1 // 512
    for hs in range(NHS):
        ld0 = p.ld
        dep = p.step

        def s_w(sync, hs=hs, dep=dep):
            sync.wait_ge(p.fl, dep)
            p.dma(sync, cod_g.rearrange("p (c h) -> p c h", h=512),
                  wg[:, hs * 512:(hs + 1) * 512].rearrange("(c p) h -> p c h", p=128))
            p.dma(sync, cod_u.rearrange("p (c h) -> p c h", h=512),
                  wu[:, hs * 512:(hs + 1) * 512].rearrange("(c p) h -> p c h", p=128))
            p.dma(sync, cod_d.rearrange("p (c o) -> p c o", o=2048),
                  wd[hs * 512:(hs + 1) * 512, :].rearrange("(c p) o -> p c o", p=128))
        block.sync(s_w)

        def f(sc, hs=hs, ld0=ld0):
            sc.wait_ge(p.ldh, ld0 + 32)
            sc.activation(wg_bf[:], cod_g[:], AF.Copy, bias=-128.0, scale=1.0)
            return sc.activation(wu_bf[:], cod_u[:], AF.Copy, bias=-128.0, scale=1.0)
        p.do(block.scalar, f)

        def f(v, hs=hs):
            for c in range(DC):
                v.tensor_tensor(wg_bf[:, c * 512:(c + 1) * 512], wg_bf[:, c * 512:(c + 1) * 512],
                                sbS_view("g")[:, hs * 512:(hs + 1) * 512], ALU.mult)
                last = v.tensor_tensor(wu_bf[:, c * 512:(c + 1) * 512], wu_bf[:, c * 512:(c + 1) * 512],
                                       sbS_view("u")[:, hs * 512:(hs + 1) * 512], ALU.mult)
            return last
        p.do(block.vector, f)

        def f(sc, hs=hs, ld0=ld0):
            sc.wait_ge(p.ldh, ld0 + 48)
            if wd_st:
                sc.wait_ge(p.sth, wd_st[-1])
            return sc.activation(wd_bf[:], cod_d[:], AF.Copy, bias=-128.0, scale=1.0)
        p.do(block.scalar, f)

        def f(v, hs=hs):
            for c in range(4):
                last = v.tensor_tensor(wd_bf[:, c * 2048:(c + 1) * 2048],
                                       wd_bf[:, c * 2048:(c + 1) * 2048],
                                       sbS_view("d")[:, 0:2048], ALU.mult)
            return last
        p.do(block.vector, f)

        def s_wd(sync, hs=hs):
            sync.wait_ge(p.fl, p.step)
            p.dma_st(sync, wd_dram[hs * 512:(hs + 1) * 512, :].rearrange("(c p) o -> p c o", p=128),
                     wd_bf.rearrange("p (c o) -> p c o", o=2048))
        block.sync(s_wd)
        wd_st.append(p.st)

        for hc in range(4):
            for tt in range(NTT):
                def f(t, hs=hs, hc=hc, tt=tt):
                    for j in range(DC):
                        t.matmul(p.pA[:, 0:512],
                                 wg_bf[:, j * 512 + hc * 128:j * 512 + hc * 128 + 128],
                                 xbf[:, j * T + tt * 512:j * T + (tt + 1) * 512],
                                 start=(j == 0), stop=(j == DC - 1))
                    for j in range(DC):
                        mm = t.matmul(p.pB[:, 0:512],
                                      wu_bf[:, j * 512 + hc * 128:j * 512 + hc * 128 + 128],
                                      xbf[:, j * T + tt * 512:j * T + (tt + 1) * 512],
                                      start=(j == 0), stop=(j == DC - 1))
                    return mm
                p.do(block.tensor, f)

                def f(sc):
                    return sc.activation(gsil[:], p.pA[:, 0:512], AF.Silu)
                p.do(block.scalar, f)

                k = len(it_st)
                slot = slice((k % 2) * 512, (k % 2) * 512 + 512)

                def f(v, k=k, slot=slot):
                    if k >= 2:
                        v.wait_ge(p.sth, it_st[k - 2])
                    return v.tensor_tensor(itile[:, slot], gsil[:], p.pB[:, 0:512], ALU.mult)
                p.do(block.vector, f)

                def s_i(sync, hs=hs, hc=hc, tt=tt, slot=slot):
                    sync.wait_ge(p.fl, p.step)
                    p.dma_st(sync, i_dram[hs * 512 + hc * 128:hs * 512 + (hc + 1) * 128,
                                          tt * 512:(tt + 1) * 512], itile[:, slot])
                block.sync(s_i)
                it_st.append(p.st)

    icol = bview(aW, 0, BF16, HC1 * 512)
    wds = bview(aM, 0, BF16, HC1 * 128)
    mtile = S.t("mtile", [128, 1024], F32)
    mt_st = []
    st_wd_done = p.st
    for tt in range(NTT):
        ld0 = p.ld
        dep = p.step

        def s_i2(sync, tt=tt, dep=dep):
            sync.wait_ge(p.fl, dep)
            sync.wait_ge(p.sth, st_wd_done)
            p.dma(sync, icol.rearrange("p (c t) -> p c t", t=512),
                  i_dram[:, tt * 512:(tt + 1) * 512].rearrange("(c p) t -> p c t", p=128))
        block.sync(s_i2)

        for oc in range(DC):
            ld1 = p.ld
            dep = p.step

            def s_wd2(sync, tt=tt, oc=oc, dep=dep):
                sync.wait_ge(p.fl, dep)
                p.dma(sync, wds.rearrange("p (c o) -> p c o", o=128),
                      wd_dram[:, oc * 128:(oc + 1) * 128].rearrange("(c p) o -> p c o", p=128))
            block.sync(s_wd2)

            def f(t, tt=tt, oc=oc, ld0=ld0, ld1=ld1):
                t.wait_ge(p.ldh, ld0 + 16)
                t.wait_ge(p.ldh, ld1 + 16)
                for c in range(HC1):
                    mm = t.matmul(p.pC[:, 0:512], wds[:, c * 128:(c + 1) * 128],
                                  icol[:, c * 512:(c + 1) * 512],
                                  start=(c == 0), stop=(c == HC1 - 1))
                return mm
            p.do(block.tensor, f)

            k = len(mt_st)
            slot = slice((k % 2) * 512, (k % 2) * 512 + 512)

            def f(v, k=k, slot=slot):
                if k >= 2:
                    v.wait_ge(p.sth, mt_st[k - 2])
                return v.tensor_copy(mtile[:, slot], p.pC[:, 0:512])
            p.do(block.vector, f)

            def s_m(sync, tt=tt, oc=oc, slot=slot):
                sync.wait_ge(p.fl, p.step)
                p.dma_st(sync, mix_dram[oc * 128:(oc + 1) * 128, tt * 512:(tt + 1) * 512],
                         mtile[:, slot])
            block.sync(s_m)
            mt_st.append(p.st)

    def f(g):
        g.wait_ge(p.sth, p.st)
        return g.engine_nop()
    p.do(block.gpsimd, f)
    emit_postln_residual(p, mix_dram, x_in, x_out, C, scratch, xbf)
    emit_store_wait(p)
    return p



NC0 = MLP0 // 128
NSL0 = MLP0 // 512


def emit_decode_tile(p, v, wide, cod, bits, outW, B, n):
    v.tensor_copy(wide[:, 0:n], cod)
    v.tensor_scalar(bits[:, 0:n], wide[:, 0:n], 7, 6, ALU.bitwise_and, ALU.logical_shift_left)
    v.tensor_scalar(wide[:, n:2 * n], wide[:, 0:n], 8, 12, ALU.bitwise_and, ALU.logical_shift_left)
    v.tensor_tensor(bits[:, 0:n], bits[:, 0:n], wide[:, n:2 * n], ALU.bitwise_or)
    v.tensor_tensor(outW(0), bits[:, 0:n].bitcast(BF16), B, ALU.mult)
    v.tensor_scalar(bits[:, 0:n], wide[:, 0:n], 0x70, 2, ALU.bitwise_and, ALU.logical_shift_left)
    v.tensor_scalar(wide[:, n:2 * n], wide[:, 0:n], 0x80, 8, ALU.bitwise_and, ALU.logical_shift_left)
    v.tensor_tensor(bits[:, 0:n], bits[:, 0:n], wide[:, n:2 * n], ALU.bitwise_or)
    return v.tensor_tensor(outW(1), bits[:, 0:n].bitcast(BF16), B, ALU.mult)


def build_l0ffn(debug=False, fuse=None):
    if fuse is None:
        p = Prog().start()
        own = True
    else:
        p, own = fuse["p"], False
    nc = p.nc
    pre = "" if own else fuse["pre"]
    if own:
        x_in = nc.dram_tensor("x_in", [D, T], F32, kind="ExternalInput").ap()
    else:
        x_in = fuse["x_in"]
    cg = nc.dram_tensor("cg", [1024, MLP0], U8, kind="ExternalInput").ap()
    cu = nc.dram_tensor("cu", [1024, MLP0], U8, kind="ExternalInput").ap()
    cd = nc.dram_tensor("cd", [MLP0 // 2, D], U8, kind="ExternalInput").ap()
    scg = nc.dram_tensor("scg", [64, MLP0], F32, kind="ExternalInput").ap()
    scu = nc.dram_tensor("scu", [64, MLP0], F32, kind="ExternalInput").ap()
    scd = nc.dram_tensor("scd", [512, D], F32, kind="ExternalInput").ap()
    if own:
        cdiv = nc.dram_tensor("cdiv", [128, 1], BF16, kind="ExternalInput").ap()
        crow = nc.dram_tensor("crow", [1, 128], F32, kind="ExternalInput").ap()
    ceall = nc.dram_tensor("ceall", [64, 1024], BF16, kind="ExternalInput").ap()
    ce8 = nc.dram_tensor("ce8", [8, 128], BF16, kind="ExternalInput").ap()
    cperm = nc.dram_tensor("cperm", [128, 512], BF16, kind="ExternalInput").ap()
    if own:
        x_out = nc.dram_tensor("x_out", [D, T], F32, kind="ExternalOutput").ap()
    else:
        x_out = fuse["x_out"]
    if debug:
        dbg_mix = nc.dram_tensor("dbg_mix", [D, T], F32, kind="ExternalOutput").ap()
        dbg_kv0 = nc.dram_tensor("dbg_kv0", [2048, 132], F32, kind="ExternalOutput").ap()
        dbg_qs0 = nc.dram_tensor("dbg_qs0", [128, 16], F32, kind="ExternalOutput").ap()
        dbg_c = nc.dram_tensor("dbg_c", [128, 1024], F32, kind="ExternalOutput").ap()
        dbg_c2 = nc.dram_tensor("dbg_c2", [128, 132], F32, kind="ExternalOutput").ap()
        dbg_xp = nc.dram_tensor("dbg_xp", [D, T], BF16, kind="ExternalOutput").ap()
        dbg_wg = nc.dram_tensor("dbg_wg", [D, 512], BF16, kind="ExternalOutput").ap()
        dbg_i = nc.dram_tensor("dbg_i", [512, T], BF16, kind="ExternalOutput").ap()
        dbg_wd = nc.dram_tensor("dbg_wd", [512, D], BF16, kind="ExternalOutput").ap()

    i_dram = nc.dram_tensor(pre + "i_dram", [MLP0, T], BF16)
    wd_dram = nc.dram_tensor(pre + "wd_dram", [MLP0, D], BF16)
    mix_dram = nc.dram_tensor(pre + "mix_dram", [D, T], F32)

    S = SAlloc(p, pre, fuse)
    if fuse is None:
        xbf = p.sb("xbf", [128, DC * T], BF16)
        aW = p.sb("aW", [128, 49152], U8)
    else:
        xbf = bview(fuse["P1"], 0, BF16, DC * T)
        aW = bview(fuse["P2"], 0, U8, 49152)
    xpt = S.t("xpt", [128, 2 * T], BF16)
    xP = xbf
    aM = S.t("aM", [128, 24576], U8)
    xf = S.t("xf", [128, 2 * T], F32)
    sq = S.t("sq", [128, 2048], U8)
    tmp = S.t("tmp", [128, 512], F32)
    mf2 = bview(aW, 0, F32, 2 * T)
    mbc = bview(aM, 0, F32, T)
    rbc = bview(aM, 8192, F32, T)
    srow = aW[0:1, 16384:16384 + 8 * T].bitcast(F32)
    rstd = aW[0:1, 32768:32768 + 4 * T].bitcast(F32)
    scratch = dict(xf=xf, sq=sq, srow=srow, rstd=rstd, mbc=mbc, rbc=rbc,
                   tmp=tmp, mf2=mf2)

    C = {}
    if fuse is None:
        C["ones_col_div"] = p.sb("c_div", [128, 1], BF16)
        C["ones_row_f32"] = p.sb("c_row", [1, 128], F32)
    else:
        C["ones_col_div"] = fuse["C"]["ones_col_div"]
        C["ones_row_f32"] = fuse["C"]["ones_row_f32"]
    C["eall"] = S.t("c_eall", [64, 1024], BF16)
    C["e8"] = S.t("c_e8", [8, 128], BF16)
    C["perm"] = S.t("c_perm", [128, 512], BF16)
    block = p.block

    def s_consts(sync):
        if fuse is None:
            p.dma(sync, C["ones_col_div"][:], cdiv)
            p.dma(sync, C["ones_row_f32"][:], crow)
        p.dma(sync, C["eall"][:], ceall)
        p.dma(sync, C["e8"][:], ce8)
        p.dma(sync, C["perm"][:], cperm)
    block.sync(s_consts)
    ld_consts = p.ld

    def f(v):
        v.wait_ge(p.ldh, ld_consts)
        return v.engine_nop()
    p.do(block.vector, f)

    emit_ln(p, x_in, xbf, C, scratch)

    for jc in range(8):
        for s in range(2):
            for tt in range(NTT):
                def f(t, jc=jc, s=s, tt=tt):
                    t.matmul(p.pA[:, 0:512], C["perm"][:, (0 + s * 2) * 128:(1 + s * 2) * 128],
                             xbf[:, (2 * jc) * T + tt * 512:(2 * jc) * T + (tt + 1) * 512],
                             start=True, stop=False)
                    return t.matmul(p.pA[:, 0:512], C["perm"][:, (1 + s * 2) * 128:(2 + s * 2) * 128],
                                    xbf[:, (2 * jc + 1) * T + tt * 512:(2 * jc + 1) * T + (tt + 1) * 512],
                                    start=False, stop=True)
                p.do(block.tensor, f)

                def f(v, s=s, tt=tt):
                    return v.tensor_copy(xpt[:, s * T + tt * 512:s * T + (tt + 1) * 512],
                                         p.pA[:, 0:512])
                p.do(block.vector, f)

        def f(v, jc=jc):
            v.tensor_copy(xbf[:, (2 * jc) * T:(2 * jc + 1) * T], xpt[:, 0:T])
            return v.tensor_copy(xbf[:, (2 * jc + 1) * T:(2 * jc + 2) * T], xpt[:, T:2 * T])
        p.do(block.vector, f)

    codg = bview(aW, 0, U8, 8 * 512)
    codu = bview(aW, 4096, U8, 8 * 512)
    wgd = bview(aW, 8192, BF16, 16 * 512)
    wud = bview(aW, 24576, BF16, 16 * 512)
    wide = bview(aW, 40960, U16, 1024)
    bits = bview(aW, 43008, U16, 1024)
    bsl = bview(aW, 45056, BF16, 1024)
    sgt = aM[0:64, 0:2048].bitcast(F32)
    sut = aM[0:64, 2048:4096].bitcast(F32)
    sgb = aM[0:64, 4096:5120].bitcast(BF16)
    sub = aM[0:64, 5120:6144].bitcast(BF16)
    itile = S.t("itile", [128, 1024], BF16)
    gsil = S.t("gsil", [128, 512], BF16)
    it_st = []

    for hs in range(NSL0):
        ld0 = p.ld
        dep = p.step

        def s_w(sync, hs=hs, dep=dep):
            sync.wait_ge(p.fl, dep)
            p.dma(sync, codg.rearrange("p (c n) -> p c n", n=512),
                  cg[:, hs * 512:(hs + 1) * 512].rearrange("(c p) n -> p c n", p=128))
            p.dma(sync, codu.rearrange("p (c n) -> p c n", n=512),
                  cu[:, hs * 512:(hs + 1) * 512].rearrange("(c p) n -> p c n", p=128))
            p.dma(sync, sgt, scg[:, hs * 512:(hs + 1) * 512])
            p.dma(sync, sut, scu[:, hs * 512:(hs + 1) * 512])
        block.sync(s_w)

        def f(v, ld0=ld0):
            v.wait_ge(p.ldh, ld0 + 64)
            v.tensor_copy(sgb[:, 0:512], sgt)
            return v.tensor_copy(sub[:, 0:512], sut)
        p.do(block.vector, f)

        for jc in range(8):
            def f(t, jc=jc):
                t.matmul(p.pD[:, 0:512], C["eall"][:, jc * 128:(jc + 1) * 128],
                         sgb[:, 0:512], start=True, stop=True)
                return t.matmul(p.pD[:, 512:1024], C["eall"][:, jc * 128:(jc + 1) * 128],
                                sub[:, 0:512], start=True, stop=True)
            p.do(block.tensor, f)

            def f(v):
                return v.tensor_copy(bsl[:, 0:1024], p.pD[:, 0:1024])
            p.do(block.vector, f)

            def f(v, jc=jc):
                emit_decode_tile(p, v, wide, codg[:, jc * 512:(jc + 1) * 512], bits,
                                 lambda s, jc=jc: wgd[:, (2 * jc + s) * 512:(2 * jc + s + 1) * 512],
                                 bsl[:, 0:512], 512)
                return emit_decode_tile(p, v, wide, codu[:, jc * 512:(jc + 1) * 512], bits,
                                        lambda s, jc=jc: wud[:, (2 * jc + s) * 512:(2 * jc + s + 1) * 512],
                                        bsl[:, 512:1024], 512)
            p.do(block.vector, f)

        if debug and hs == 0:
            def s_dw(sync):
                sync.wait_ge(p.fl, p.step)
                for pc2 in range(16):
                    p.dma_st(sync, dbg_wg[pc2 * 128:(pc2 + 1) * 128, :],
                             wgd[:, pc2 * 512:(pc2 + 1) * 512])
            block.sync(s_dw)

        for w in range(2):
            for s in range(2):
                for tt in range(NTT):
                    def f(t, w=w, s=s, tt=tt):
                        for pc in range(16):
                            t.matmul(p.pA[:, 0:512],
                                     wgd[:, pc * 512 + w * 256 + s: pc * 512 + (w + 1) * 256: 2],
                                     xP[:, pc * T + tt * 512:pc * T + (tt + 1) * 512],
                                     start=(pc == 0), stop=(pc == 15))
                        for pc in range(16):
                            mm = t.matmul(p.pB[:, 0:512],
                                          wud[:, pc * 512 + w * 256 + s: pc * 512 + (w + 1) * 256: 2],
                                          xP[:, pc * T + tt * 512:pc * T + (tt + 1) * 512],
                                          start=(pc == 0), stop=(pc == 15))
                        return mm
                    p.do(block.tensor, f)

                    def f(sc):
                        return sc.activation(gsil[:], p.pA[:, 0:512], AF.Gelu_apprx_tanh)
                    p.do(block.scalar, f)

                    k = len(it_st)
                    slot = slice((k % 2) * 512, (k % 2) * 512 + 512)

                    def f(v, k=k, slot=slot):
                        if k >= 2:
                            v.wait_ge(p.sth, it_st[k - 2])
                        return v.tensor_tensor(itile[:, slot], gsil[:], p.pB[:, 0:512], ALU.mult)
                    p.do(block.vector, f)

                    rbase = 128 * (4 * hs + 2 * w + s)

                    def s_i(sync, rbase=rbase, tt=tt, slot=slot):
                        sync.wait_ge(p.fl, p.step)
                        p.dma_st(sync, i_dram[rbase:rbase + 128, tt * 512:(tt + 1) * 512],
                                 itile[:, slot])
                    block.sync(s_i)
                    it_st.append(p.st)

    codd = bview(aW, 0, U8, 4 * 2048)
    widd = bview(aW, 16384, U16, 2048)
    bitd = bview(aW, 20480, U16, 2048)
    wdd = bview(aW, 24576, BF16, 2048)
    sdt = aM[0:8, 0:8192].bitcast(F32)
    sdb = aM[0:8, 8192:12288].bitcast(BF16)
    bdt = bview(aM, 16384, BF16, 2048)
    wd_st2 = []
    for ds in range(16):
        ld0 = p.ld
        dep = p.step

        def s_w(sync, ds=ds, dep=dep):
            sync.wait_ge(p.fl, dep)
            p.dma(sync, codd.rearrange("p (c o) -> p c o", o=2048),
                  cd[ds * 512:(ds + 1) * 512, :].rearrange("(c p) o -> p c o", p=128))
        block.sync(s_w)

        for jq in range(4):
            jcp = ds * 4 + jq
            def s_s(sync, jcp=jcp):
                sync.wait_ge(p.fl, p.step)
                p.dma(sync, sdt, scd[jcp * 8:(jcp + 1) * 8, :])
            block.sync(s_s)
            ld1 = p.ld

            def f(v, ld1=ld1):
                v.wait_ge(p.ldh, ld1)
                return v.tensor_copy(sdb[:, 0:2048], sdt)
            p.do(block.vector, f)

            def f(t):
                for q in range(4):
                    mm = t.matmul(p.pD[:, 0:512] if q == 0 else [p.pD[:, 512:1024], p.pE[:, 0:512], p.pF[:, 0:512]][q - 1],
                                  C["e8"][:, 0:128], sdb[:, q * 512:(q + 1) * 512],
                                  start=True, stop=True)
                return mm
            p.do(block.tensor, f)

            def f(v):
                v.tensor_copy(bdt[:, 0:512], p.pD[:, 0:512])
                v.tensor_copy(bdt[:, 512:1024], p.pD[:, 512:1024])
                v.tensor_copy(bdt[:, 1024:1536], p.pE[:, 0:512])
                return v.tensor_copy(bdt[:, 1536:2048], p.pF[:, 0:512])
            p.do(block.vector, f)

            for s in range(2):
                k2 = len(wd_st2)

                def f(v, jq=jq, s=s, k2=k2, ld0=ld0):
                    v.wait_ge(p.ldh, ld0 + 16)
                    if k2 >= 1:
                        v.wait_ge(p.sth, wd_st2[k2 - 1])
                    cod = codd[:, jq * 2048:(jq + 1) * 2048]
                    v.tensor_copy(widd[:, 0:2048], cod)
                    if s == 0:
                        v.tensor_scalar(bitd[:, 0:2048], widd[:, 0:2048], 7, 6, ALU.bitwise_and, ALU.logical_shift_left)
                        v.tensor_scalar(widd[:, 0:2048], widd[:, 0:2048], 8, 12, ALU.bitwise_and, ALU.logical_shift_left)
                    else:
                        v.tensor_scalar(bitd[:, 0:2048], widd[:, 0:2048], 0x70, 2, ALU.bitwise_and, ALU.logical_shift_left)
                        v.tensor_scalar(widd[:, 0:2048], widd[:, 0:2048], 0x80, 8, ALU.bitwise_and, ALU.logical_shift_left)
                    v.tensor_tensor(bitd[:, 0:2048], bitd[:, 0:2048], widd[:, 0:2048], ALU.bitwise_or)
                    return v.tensor_tensor(wdd[:, 0:2048], bitd[:, 0:2048].bitcast(BF16), bdt[:, 0:2048], ALU.mult)
                p.do(block.vector, f)

                rbase = 128 * (2 * jcp + s)

                def s_wd(sync, rbase=rbase):
                    sync.wait_ge(p.fl, p.step)
                    p.dma_st(sync, wd_dram[rbase:rbase + 128, :], wdd[:, 0:2048])
                block.sync(s_wd)
                wd_st2.append(p.st)

    if debug:
        def s_dbg(sync):
            sync.wait_ge(p.fl, p.step)
            for j in range(DC):
                p.dma_st(sync, dbg_xp[j * 128:(j + 1) * 128, :], xbf[:, j * T:(j + 1) * T])
        block.sync(s_dbg)

        def s_dbg2(sync):
            p.dma_st(sync, dbg_i[:, :], i_dram[0:512, :])
            p.dma_st(sync, dbg_wd[:, :], wd_dram[0:512, :])
        block.sync(s_dbg2)

    ibuf = xbf
    wcol = bview(aW, 0, BF16, 16 * 128)
    mtile = S.t("mtile", [128, 1024], F32)
    mt_st = []
    acc_prev = {}
    st_before = p.st
    for g in range(8):
        ld0 = p.ld
        dep = p.step

        def s_i2(sync, g=g, dep=dep, st_before=st_before):
            sync.wait_ge(p.fl, dep)
            sync.wait_ge(p.sth, st_before)
            p.dma(sync, ibuf[:].rearrange("p (c t) -> p c t", t=T),
                  i_dram[g * 2048:(g + 1) * 2048, :].rearrange("(c p) t -> p c t", p=128))
        block.sync(s_i2)

        for oc in range(DC):
            ld1 = p.ld
            dep = p.step

            def s_wc(sync, g=g, oc=oc, dep=dep):
                sync.wait_ge(p.fl, dep)
                p.dma(sync, wcol.rearrange("p (c o) -> p c o", o=128),
                      wd_dram[g * 2048:(g + 1) * 2048, oc * 128:(oc + 1) * 128]
                      .rearrange("(c p) o -> p c o", p=128))
            block.sync(s_wc)

            for tt in range(NTT):
                def f(t, g=g, oc=oc, tt=tt, ld0=ld0, ld1=ld1):
                    t.wait_ge(p.ldh, ld0 + 16)
                    t.wait_ge(p.ldh, ld1 + 16)
                    for c in range(16):
                        mm = t.matmul(p.pC[:, 0:512], wcol[:, c * 128:(c + 1) * 128],
                                      ibuf[:, c * T + tt * 512:c * T + (tt + 1) * 512],
                                      start=(c == 0), stop=(c == 15))
                    return mm
                p.do(block.tensor, f)

                k = len(mt_st)
                slot = slice((k % 2) * 512, (k % 2) * 512 + 512)

                def f(v, k=k, slot=slot):
                    if k >= 2:
                        v.wait_ge(p.sth, mt_st[k - 2])
                    return v.tensor_copy(mtile[:, slot], p.pC[:, 0:512])
                p.do(block.vector, f)

                def s_m(gq, g=g, oc=oc, tt=tt, slot=slot):
                    gq.wait_ge(p.fl, p.step)
                    if g > 0:
                        gq.wait_ge(p.sth, acc_prev[(oc, tt)])
                    p.dma_st(gq, mix_dram[oc * 128:(oc + 1) * 128, tt * 512:(tt + 1) * 512],
                             mtile[:, slot], accum=(g > 0))
                block.gpsimd(s_m)
                mt_st.append(p.st)
                acc_prev[(oc, tt)] = p.st

    def f(g):
        g.wait_ge(p.sth, p.st)
        return g.engine_nop()
    p.do(block.gpsimd, f)
    if debug:
        def s_dm2(sync):
            p.dma_st(sync, dbg_mix[:, :], mix_dram[:, :])
        block.sync(s_dm2)
    emit_postln_residual(p, mix_dram, x_in, x_out, C, scratch, xP)
    emit_store_wait(p)
    return p


def prep_l0ffn_inputs(full):
    def codesT(w):
        return np.ascontiguousarray(_u8codes(np.asarray(w)).T)
    perm = np.zeros((128, 512), BF16NP)
    for s in range(2):
        for i in range(64):
            perm[2 * i + s, s * 256 + i] = 1
            perm[2 * i + s, s * 256 + 128 + 64 + i] = 1
    eall = np.zeros((64, 1024), BF16NP)
    for jc in range(8):
        for i in range(128):
            eall[(128 * jc + i) // 16, jc * 128 + i] = BF16NP(TWO126)
    e8 = np.zeros((8, 128), BF16NP)
    for i in range(128):
        e8[i // 16, i] = BF16NP(TWO126)
    shared = {
        "cg": codesT(full["L0_ffn_gate_W_packed"]),
        "cu": codesT(full["L0_ffn_up_W_packed"]),
        "cd": codesT(full["L0_ffn_down_W_packed"]),
        "scg": np.ascontiguousarray(np.asarray(full["L0_ffn_gate_scales"], np.float32).T),
        "scu": np.ascontiguousarray(np.asarray(full["L0_ffn_up_scales"], np.float32).T),
        "scd": np.ascontiguousarray(np.asarray(full["L0_ffn_down_scales"], np.float32).T),
        "cdiv": np.full((128, 1), 1.0 / 2048.0, BF16NP),
        "crow": np.ones((1, 128), np.float32),
        "ceall": eall, "ce8": e8, "cperm": perm,
    }
    return shared


def run_l0ffn(x_flat, full, debug=False):
    shared = prep_l0ffn_inputs(full)
    shards = shard_fm(x_flat)
    p = build_l0ffn(debug)
    in_maps = [dict(shared, x_in=s) for s in shards]
    res = run_bass_kernel_spmd(p.nc, in_maps, core_ids=list(range(8)))
    p.close()
    if debug:
        return res
    return unshard_fm([res.results[c]["x_out"] for c in range(8)])



def build_l0attn(debug=False, fuse=None):
    if fuse is None:
        p = Prog().start()
        own = True
    else:
        p, own = fuse["p"], False
    nc = p.nc
    pre = "" if own else fuse["pre"]
    if own:
        x_in = nc.dram_tensor("x_in", [D, T], F32, kind="ExternalInput").ap()
    else:
        x_in = fuse["x_in"]
    wqkvT = nc.dram_tensor("wqkvT", [D, 3 * D], F32, kind="ExternalInput").ap()
    woutT = nc.dram_tensor("woutT", [D, D], F32, kind="ExternalInput").ap()
    cbias = nc.dram_tensor("cbias", [128, 32], F32, kind="ExternalInput").ap()
    cbrow = nc.dram_tensor("cbrow", [1, D], F32, kind="ExternalInput").ap()
    cbout = nc.dram_tensor("cbout", [128, 16], F32, kind="ExternalInput").ap()
    if own:
        cdiv = nc.dram_tensor("cdiv", [128, 1], BF16, kind="ExternalInput").ap()
        crow = nc.dram_tensor("crow", [1, 128], F32, kind="ExternalInput").ap()
        cid = nc.dram_tensor("cid", [128, 128], BF16, kind="ExternalInput").ap()
    if own:
        x_out = nc.dram_tensor("x_out", [D, T], F32, kind="ExternalOutput").ap()
    else:
        x_out = fuse["x_out"]
    if debug:
        dbg_q = nc.dram_tensor("dbg_q", [D, T], BF16, kind="ExternalOutput").ap()
        dbg_s = nc.dram_tensor("dbg_s", [T, 1024], F32, kind="ExternalOutput").ap()
        dbg_o = nc.dram_tensor("dbg_o", [T, D], F32, kind="ExternalOutput").ap()
        dbg_mx = nc.dram_tensor("dbg_mx", [128, 48], F32, kind="ExternalOutput").ap()
        dbg_mix = nc.dram_tensor("dbg_mix", [D, T], F32, kind="ExternalOutput").ap()
        dbg_kv0 = nc.dram_tensor("dbg_kv0", [2048, 132], F32, kind="ExternalOutput").ap()
        dbg_qs0 = nc.dram_tensor("dbg_qs0", [128, 16], F32, kind="ExternalOutput").ap()
        dbg_c = nc.dram_tensor("dbg_c", [128, 1024], F32, kind="ExternalOutput").ap()
        dbg_c2 = nc.dram_tensor("dbg_c2", [128, 132], F32, kind="ExternalOutput").ap()
        dbg_kg = nc.dram_tensor("dbg_kg", [512, T], BF16, kind="ExternalOutput").ap()

    kdram = nc.dram_tensor(pre + "kdram", [D, T], BF16)
    vdram = nc.dram_tensor(pre + "vdram", [T, D], BF16)
    kg = nc.dram_tensor(pre + "kg", [4 * D, T], BF16)
    vg = nc.dram_tensor(pre + "vg", [4 * T, D], BF16)
    sdram = nc.dram_tensor(pre + "sdram", [T, 4 * T], F32)
    odram = nc.dram_tensor(pre + "odram", [T, D], F32)
    mix_dram = nc.dram_tensor(pre + "mix_dram", [D, T], F32)

    S = SAlloc(p, pre, fuse)
    if fuse is None:
        xbf = p.sb("xbf", [128, DC * T], BF16)
        qfm = p.sb("qfm", [128, DC * T], BF16)
    else:
        xbf = bview(fuse["P1"], 0, BF16, DC * T)
        qfm = bview(fuse["P2"], 0, BF16, DC * T)
    aW = S.t("aW", [128, 24576], U8)
    aM = S.t("aM", [128, 24576], U8)
    xf = S.t("xf", [128, 2 * T], F32)
    sq = S.t("sq", [128, 2048], U8)
    tmp = S.t("tmp", [128, 512], F32)
    mx = S.t("mx", [128, 16], F32)
    lsum = S.t("lsum", [128, 16], F32)
    rs1 = S.t("rs1", [128, 1], F32)
    bias_sb = S.t("bias_sb", [128, 32], F32)
    bout_sb = S.t("bout_sb", [128, 16], F32)

    mbc = bview(aM, 0, F32, T)
    rbc = bview(aM, 8192, F32, T)
    vbias = bview(aM, 16384, F32, 2048)
    srow = qfm[0:1, 0:8192].bitcast(F32)
    rstd = qfm[0:1, 8192:12288].bitcast(F32)
    mf2 = bview(qfm, 32768, F32, 2 * T)
    scratch = dict(xf=xf, sq=sq, srow=srow, rstd=rstd, mbc=mbc, rbc=rbc,
                   tmp=tmp, mf2=mf2)

    C = {}
    if fuse is None:
        C["ones_col_div"] = p.sb("c_div", [128, 1], BF16)
        C["ones_row_f32"] = p.sb("c_row", [1, 128], F32)
        C["id128"] = p.sb("c_id", [128, 128], BF16)
    else:
        C["ones_col_div"] = fuse["C"]["ones_col_div"]
        C["ones_row_f32"] = fuse["C"]["ones_row_f32"]
        C["id128"] = fuse["C"]["id128"]
    block = p.block

    def s_consts(sync):
        if fuse is None:
            p.dma(sync, C["ones_col_div"][:], cdiv)
            p.dma(sync, C["ones_row_f32"][:], crow)
            p.dma(sync, C["id128"][:], cid)
        p.dma(sync, bias_sb[:], cbias)
        p.dma(sync, bout_sb[:], cbout)
    block.sync(s_consts)
    ld_consts = p.ld

    def f(v):
        v.wait_ge(p.ldh, ld_consts)
        return v.engine_nop()
    p.do(block.vector, f)

    emit_ln(p, x_in, xbf, C, scratch)

    vbrow = aW[0:1, 0:8192].bitcast(F32)

    def s_vb(sync):
        p.dma(sync, vbrow, cbrow)
    block.sync(s_vb)
    ld_vb = p.ld

    def f(t):
        t.wait_ge(p.ldh, ld_vb)
        for et in range(4):
            mm = t.matmul(p.pD[:, 0:512], C["ones_row_f32"][0:1, :],
                          vbrow[0:1, et * 512:(et + 1) * 512], start=True, stop=True)
        return mm
    for et in range(4):
        def f(t, et=et):
            if et == 0:
                t.wait_ge(p.ldh, ld_vb)
            return t.matmul(p.pD[:, 0:512], C["ones_row_f32"][0:1, :],
                            vbrow[0:1, et * 512:(et + 1) * 512], start=True, stop=True)
        p.do(block.tensor, f)

        def f(v, et=et):
            return v.tensor_copy(vbias[:, et * 512:(et + 1) * 512], p.pD[:, 0:512])
        p.do(block.vector, f)

    wsl = bview(aW, 8192, BF16, 16 * 512)
    ktile = S.t("ktile", [128, 1024], BF16)
    vtile = S.t("vtile", [128, 1024], BF16)
    k_st = []
    v_st = []

    for ec in range(32):
        dep = p.step

        def s_w(gq, ec=ec, dep=dep):
            gq.wait_ge(p.fl, dep)
            p.dma(gq, wsl[:, 0:16 * 128].rearrange("p (c o) -> p c o", o=128),
                  wqkvT[:, ec * 128:(ec + 1) * 128].rearrange("(c p) o -> p c o", p=128))
        block.gpsimd(s_w)
        ld0 = p.ld

        for tt in range(NTT):
            def f(t, ec=ec, tt=tt, ld0=ld0):
                t.wait_ge(p.ldh, ld0)
                for j in range(DC):
                    mm = t.matmul(p.pA[:, 0:512], wsl[:, j * 128:(j + 1) * 128],
                                  xbf[:, j * T + tt * 512:j * T + (tt + 1) * 512],
                                  start=(j == 0), stop=(j == DC - 1))
                return mm
            p.do(block.tensor, f)

            if ec < 16:
                def f(v, ec=ec, tt=tt):
                    return v.tensor_scalar(qfm[:, ec * T + tt * 512:ec * T + (tt + 1) * 512],
                                           p.pA[:, 0:512], bias_sb[:, ec:ec + 1], None, ALU.add)
                p.do(block.vector, f)
            else:
                kk = len(k_st)
                slot = slice((kk % 2) * 512, (kk % 2) * 512 + 512)

                def f(v, ec=ec, tt=tt, kk=kk, slot=slot):
                    if kk >= 2:
                        v.wait_ge(p.sth, k_st[kk - 2])
                    return v.tensor_scalar(ktile[:, slot], p.pA[:, 0:512],
                                           bias_sb[:, ec:ec + 1], None, ALU.add)
                p.do(block.vector, f)

                def s_k(sync, ec=ec, tt=tt, slot=slot):
                    sync.wait_ge(p.fl, p.step)
                    p.dma_st(sync, kdram[(ec - 16) * 128:(ec - 15) * 128, tt * 512:(tt + 1) * 512],
                             ktile[:, slot])
                block.sync(s_k)
                k_st.append(p.st)

    vsl = bview(aW, 8192, BF16, 16 * 512)
    for et in range(4):
        dep = p.step

        def s_w(gq, et=et, dep=dep):
            gq.wait_ge(p.fl, dep)
            p.dma(gq, vsl[:].rearrange("p (c o) -> p c o", o=512),
                  wqkvT[:, 2 * D + et * 512:2 * D + (et + 1) * 512].rearrange("(c p) o -> p c o", p=128))
        block.gpsimd(s_w)
        ld0 = p.ld

        for tc in range(DC):
            def f(t, et=et, tc=tc, ld0=ld0):
                t.wait_ge(p.ldh, ld0)
                for j in range(DC):
                    mm = t.matmul(p.pA[:, 0:512], xbf[:, j * T + tc * 128:j * T + tc * 128 + 128],
                                  vsl[:, j * 512:(j + 1) * 512],
                                  start=(j == 0), stop=(j == DC - 1))
                return mm
            p.do(block.tensor, f)

            kk = len(v_st)
            slot = slice((kk % 2) * 512, (kk % 2) * 512 + 512)

            def f(v, et=et, kk=kk, slot=slot):
                if kk >= 2:
                    v.wait_ge(p.sth, v_st[kk - 2])
                return v.tensor_tensor(vtile[:, slot], p.pA[:, 0:512],
                                       vbias[:, et * 512:(et + 1) * 512], ALU.add)
            p.do(block.vector, f)

            def s_v(sync, et=et, tc=tc, slot=slot):
                sync.wait_ge(p.fl, p.step)
                p.dma_st(sync, vdram[tc * 128:(tc + 1) * 128, et * 512:(et + 1) * 512],
                         vtile[:, slot])
            block.sync(s_v)
            v_st.append(p.st)

    st_kv = p.st

    def s_cc(gq):
        gq.wait_ge(p.fl, p.step)
        gq.wait_ge(p.sth, st_kv)
        gq.collective_compute("AllGather", ALU.bypass, replica_groups=RG4,
                              ins=[kdram.ap().opt()], outs=[kg.ap().opt()]).then_inc(p.cch)
        gq.collective_compute("AllGather", ALU.bypass, replica_groups=RG4,
                              ins=[vdram.ap().opt()], outs=[vg.ap().opt()]).then_inc(p.cch)
    block.gpsimd(s_cc)
    p.cc += 2
    cc_after = p.cc

    kslab = bview(xbf, 0, BF16, 16 * 512)
    kslab2 = bview(xbf, 16384, BF16, 16 * 512)
    stile = bview(xbf, 32768, F32, 1024)
    s_st = []

    def f(v):
        return v.memset(mx[:], -3.0e38)
    p.do(block.vector, f)

    for kt in range(16):
        dep = p.step
        ksl = [kslab, kslab2][kt % 2]

        def s_ks(sync, kt=kt, dep=dep, ksl=ksl):
            sync.wait_ge(p.fl, max(0, dep - 2))
            if kt == 0:
                sync.wait_ge(p.cch, cc_after)
            r, tl = kt // 4, kt % 4
            p.dma(sync, ksl[:].rearrange("p (c t) -> p c t", t=512),
                  kg[r * D:(r + 1) * D, tl * 512:(tl + 1) * 512].rearrange("(c p) t -> p c t", p=128))
        block.sync(s_ks)
        ld0 = p.ld

        for qc in range(16):
            def f(t, kt=kt, qc=qc, ld0=ld0, ksl=ksl):
                t.wait_ge(p.ldh, ld0)
                for ec in range(DC):
                    mm = t.matmul(p.pA[:, 0:512], qfm[:, ec * T + qc * 128:ec * T + qc * 128 + 128],
                                  ksl[:, ec * 512:(ec + 1) * 512],
                                  start=(ec == 0), stop=(ec == DC - 1))
                return mm
            p.do(block.tensor, f)

            kk = len(s_st)
            slot = slice((kk % 2) * 512, (kk % 2) * 512 + 512)

            def f(v, kk=kk, slot=slot, qc=qc):
                if kk >= 2:
                    v.wait_ge(p.sth, s_st[kk - 2])
                v.tensor_copy(stile[:, slot], p.pA[:, 0:512])
                return v.tensor_reduce(rs1[:], p.pA[:, 0:512], AX.X, ALU.max)
            p.do(block.vector, f)

            def f(v, qc=qc):
                return v.tensor_tensor(mx[:, qc:qc + 1], mx[:, qc:qc + 1], rs1[:], ALU.max)
            p.do(block.vector, f)

            def s_s(sync, kt=kt, qc=qc, slot=slot):
                sync.wait_ge(p.fl, p.step)
                p.dma_st(sync, sdram[qc * 128:(qc + 1) * 128, kt * 512:(kt + 1) * 512],
                         stile[:, slot])
            block.sync(s_s)
            s_st.append(p.st)

    SCALE0 = float(1.0 / np.sqrt(2048.0))
    bmax = S.t("bmax", [128, 16], F32)
    st_ph3 = p.st

    def f(v):
        v.wait_ge(p.sth, st_ph3)
        v.memset(lsum[:], 0.0)
        return v.tensor_scalar(bmax[:], mx[:], float(-SCALE0), None, ALU.mult)
    p.do(block.vector, f)

    vslab = bview(xbf, 0, BF16, 16 * T)
    sf = bview(aW, 0, F32, 1024)
    ptb = bview(aW, 8192, BF16, 16 * 128)
    ptile = bview(aW, 12288, BF16, 512)
    otile = bview(aW, 16384, F32, 1024)
    o_acc_prev = {}
    o_st = []
    sload = []

    for grp in range(4):
        dep = p.step

        def s_vs(sync, grp=grp, dep=dep):
            sync.wait_ge(p.fl, dep)
            p.dma(sync, vslab[:].rearrange("p (c e) -> p c e", e=D),
                  vg[grp * T:(grp + 1) * T, :].rearrange("(c p) e -> p c e", p=128))
        block.sync(s_vs)
        ld_v = p.ld

        for qc in range(16):
            for k4 in range(4):
                kt = grp * 4 + k4
                dep = p.step

                def s_sf(sync, kt=kt, qc=qc, dep=dep):
                    sync.wait_ge(p.fl, max(0, dep - 1))
                    p.dma(sync, sf[:, (kt % 2) * 512:(kt % 2) * 512 + 512],
                          sdram[qc * 128:(qc + 1) * 128, kt * 512:(kt + 1) * 512])
                block.sync(s_sf)
                ld0 = p.ld

                def f(sc, kt=kt, qc=qc, ld0=ld0):
                    sc.wait_ge(p.ldh, ld0)
                    return sc.activation(ptile[:, 0:512], sf[:, (kt % 2) * 512:(kt % 2) * 512 + 512],
                                         AF.Exp, bias=bmax[:, qc:qc + 1], scale=np.float32(SCALE0),
                                         accum_out=rs1[:])
                p.do(block.scalar, f)

                def f(v, qc=qc):
                    return v.tensor_tensor(lsum[:, qc:qc + 1], lsum[:, qc:qc + 1], rs1[:], ALU.add)
                p.do(block.vector, f)

                for i in range(4):
                    def f(t, i=i):
                        return t.transpose(p.pT[:, i * 128:(i + 1) * 128],
                                           ptile[:, i * 128:(i + 1) * 128], C["id128"][:])
                    p.do(block.tensor, f)

                    def f(v, k4=k4, i=i):
                        return v.tensor_copy(ptb[:, (k4 * 4 + i) * 128:(k4 * 4 + i + 1) * 128],
                                             p.pT[:, i * 128:(i + 1) * 128])
                    p.do(block.vector, f)

            for et in range(4):
                def f(t, et=et, ld_v=ld_v):
                    t.wait_ge(p.ldh, ld_v)
                    for c in range(16):
                        mm = t.matmul(p.pB[:, 0:512], ptb[:, c * 128:(c + 1) * 128],
                                      vslab[:, c * D + et * 512:c * D + (et + 1) * 512],
                                      start=(c == 0), stop=(c == 15))
                    return mm
                p.do(block.tensor, f)

                kk = len(o_st)
                slot = slice((kk % 2) * 512, (kk % 2) * 512 + 512)

                def f(v, kk=kk, slot=slot):
                    if kk >= 2:
                        v.wait_ge(p.sth, o_st[kk - 2])
                    return v.tensor_copy(otile[:, slot], p.pB[:, 0:512])
                p.do(block.vector, f)

                def s_o(gq, grp=grp, qc=qc, et=et, slot=slot):
                    gq.wait_ge(p.fl, p.step)
                    if grp > 0:
                        gq.wait_ge(p.sth, o_acc_prev[(qc, et)])
                    p.dma_st(gq, odram[qc * 128:(qc + 1) * 128, et * 512:(et + 1) * 512],
                             otile[:, slot], accum=(grp > 0))
                block.gpsimd(s_o)
                o_st.append(p.st)
                o_acc_prev[(qc, et)] = p.st

    if debug:
        def s_d1(sync):
            sync.wait_ge(p.fl, p.step)
            sync.wait_ge(p.sth, p.st)
            for ec2 in range(DC):
                p.dma_st(sync, dbg_q[ec2 * 128:(ec2 + 1) * 128, :], qfm[:, ec2 * T:(ec2 + 1) * T])
            p.dma_st(sync, dbg_s[:, :], sdram[:, 0:1024])
            p.dma_st(sync, dbg_o[:, :], odram[:, :])
            p.dma_st(sync, dbg_mx[:, 0:16], mx[:])
            p.dma_st(sync, dbg_mx[:, 16:32], lsum[:])
            p.dma_st(sync, dbg_mx[:, 32:48], bmax[:])
            p.dma_st(sync, dbg_kg[:, :], kg[0:512, :])
        block.sync(s_d1)

    linv = S.t("linv", [128, 16], F32)

    def f(v):
        return v.reciprocal(linv[:], lsum[:])
    p.do(block.vector, f)

    ctxT = bview(xbf, 0, BF16, DC * T)
    obuf = bview(aW, 0, F32, 2 * T)
    otm = bview(aW, 16384, BF16, T)
    st_o_done = p.st
    for qc in range(16):
        dep = p.step

        def s_ob(sync, qc=qc, dep=dep):
            sync.wait_ge(p.fl, max(0, dep - 1))
            if qc == 0:
                sync.wait_ge(p.sth, st_o_done)
            p.dma(sync, obuf[:, (qc % 2) * T:(qc % 2 + 1) * T], odram[qc * 128:(qc + 1) * 128, :])
        block.sync(s_ob)
        ld0 = p.ld

        def f(v, qc=qc, ld0=ld0):
            v.wait_ge(p.ldh, ld0)
            return v.tensor_scalar(otm[:, 0:T], obuf[:, (qc % 2) * T:(qc % 2 + 1) * T],
                                   linv[:, qc:qc + 1], None, ALU.mult)
        p.do(block.vector, f)

        for ec in range(16):
            def f(t, ec=ec):
                return t.transpose(p.pT[:, (ec % 8) * 128:(ec % 8 + 1) * 128],
                                   otm[:, ec * 128:(ec + 1) * 128], C["id128"][:])
            p.do(block.tensor, f)

            def f(v, qc=qc, ec=ec):
                return v.tensor_copy(ctxT[:, ec * T + qc * 128:ec * T + qc * 128 + 128],
                                     p.pT[:, (ec % 8) * 128:(ec % 8 + 1) * 128])
            p.do(block.vector, f)

    wosl = bview(aW, 0, BF16, 16 * 128)
    mtile = S.t("mtile", [128, 1024], F32)
    mt_st = []
    for oc in range(16):
        dep = p.step

        def s_w(gq, oc=oc, dep=dep):
            gq.wait_ge(p.fl, dep)
            p.dma(gq, wosl[:].rearrange("p (c o) -> p c o", o=128),
                  woutT[:, oc * 128:(oc + 1) * 128].rearrange("(c p) o -> p c o", p=128))
        block.gpsimd(s_w)
        ld0 = p.ld

        for tt in range(NTT):
            def f(t, oc=oc, tt=tt, ld0=ld0):
                t.wait_ge(p.ldh, ld0)
                for ec in range(DC):
                    mm = t.matmul(p.pC[:, 0:512], wosl[:, ec * 128:(ec + 1) * 128],
                                  ctxT[:, ec * T + tt * 512:ec * T + (tt + 1) * 512],
                                  start=(ec == 0), stop=(ec == DC - 1))
                return mm
            p.do(block.tensor, f)

            kk = len(mt_st)
            slot = slice((kk % 2) * 512, (kk % 2) * 512 + 512)

            def f(v, oc=oc, kk=kk, slot=slot):
                if kk >= 2:
                    v.wait_ge(p.sth, mt_st[kk - 2])
                return v.tensor_scalar(mtile[:, slot], p.pC[:, 0:512],
                                       bout_sb[:, oc:oc + 1], None, ALU.add)
            p.do(block.vector, f)

            def s_m(sync, oc=oc, tt=tt, slot=slot):
                sync.wait_ge(p.fl, p.step)
                p.dma_st(sync, mix_dram[oc * 128:(oc + 1) * 128, tt * 512:(tt + 1) * 512],
                         mtile[:, slot])
            block.sync(s_m)
            mt_st.append(p.st)

    def f(g):
        g.wait_ge(p.sth, p.st)
        return g.engine_nop()
    p.do(block.gpsimd, f)
    if debug:
        def s_d2(sync):
            p.dma_st(sync, dbg_mix[:, :], mix_dram[:, :])
        block.sync(s_d2)
    emit_postln_residual(p, mix_dram, x_in, x_out, C, scratch, xbf)
    emit_store_wait(p)
    return p


def prep_l0attn_inputs(full):
    wqkv = np.asarray(full["L0_attn_w_qkv"], np.float32)
    wout = np.asarray(full["L0_attn_w_out"], np.float32)
    bqkv = np.asarray(full["L0_attn_b_qkv"], np.float32)
    bout = np.asarray(full["L0_attn_b_out"], np.float32)
    shared = {
        "wqkvT": np.ascontiguousarray(wqkv.T),
        "woutT": np.ascontiguousarray(wout.T),
        "cbias": np.ascontiguousarray(bqkv[0:4096].reshape(32, 128).T),
        "cbrow": bqkv[4096:6144].reshape(1, 2048),
        "cbout": np.ascontiguousarray(bout.reshape(16, 128).T),
        "cdiv": np.full((128, 1), 1.0 / 2048.0, BF16NP),
        "crow": np.ones((1, 128), np.float32),
        "cid": np.eye(128, dtype=BF16NP),
    }
    return shared


def run_l0attn(x_flat, full, debug=False):
    shared = prep_l0attn_inputs(full)
    shards = shard_fm(x_flat)
    p = build_l0attn(debug)
    in_maps = [dict(shared, x_in=s) for s in shards]
    res = run_bass_kernel_spmd(p.nc, in_maps, core_ids=list(range(8)))
    p.close()
    if debug:
        return res
    return unshard_fm([res.results[c]["x_out"] for c in range(8)])



def build_l1attn(debug=False, fuse=None):
    if fuse is None:
        p = Prog().start()
        own = True
    else:
        p, own = fuse["p"], False
    nc = p.nc
    pre = "" if own else fuse["pre"]
    if own:
        x_in = nc.dram_tensor("x_in", [D, T], F32, kind="ExternalInput").ap()
    else:
        x_in = fuse["x_in"]
    wqT = nc.dram_tensor("wqT", [D, D], F32, kind="ExternalInput").ap()
    wkT = nc.dram_tensor("wkT", [D, D], F32, kind="ExternalInput").ap()
    wvT = nc.dram_tensor("wvT", [D, D], F32, kind="ExternalInput").ap()
    woT = nc.dram_tensor("woT", [D, D], F32, kind="ExternalInput").ap()
    memaug = nc.dram_tensor("memaug", [64 * 128, 132], F32, kind="ExternalInput").ap()
    mnormT = nc.dram_tensor("mnormT", [128, 64], F32, kind="ExternalInput").ap()
    gatein = nc.dram_tensor("gatein", [1, 16], F32, kind="ExternalInput").ap()
    maskpre = nc.dram_tensor("maskpre", [128, 4], F32, kind="ExternalInput").ap()
    ctril = nc.dram_tensor("ctril", [128, 128], BF16, kind="ExternalInput").ap()
    if own:
        cdiv = nc.dram_tensor("cdiv", [128, 1], BF16, kind="ExternalInput").ap()
        crow = nc.dram_tensor("crow", [1, 128], F32, kind="ExternalInput").ap()
        cid = nc.dram_tensor("cid", [128, 128], BF16, kind="ExternalInput").ap()
    conebf = nc.dram_tensor("conebf", [128, 1], BF16, kind="ExternalInput").ap()
    crelrow = nc.dram_tensor("crelrow", [128, 1], F32, kind="ExternalInput").ap()
    if own:
        x_out = nc.dram_tensor("x_out", [D, T], F32, kind="ExternalOutput").ap()
    else:
        x_out = fuse["x_out"]
    if debug:
        dbg_sk = nc.dram_tensor("dbg_sk", [T, D], BF16, kind="ExternalOutput").ap()
        dbg_v = nc.dram_tensor("dbg_v", [T, D], BF16, kind="ExternalOutput").ap()
        dbg_kvg = nc.dram_tensor("dbg_kvg", [4 * 2048, 132], F32, kind="ExternalOutput").ap()
        dbg_qs = nc.dram_tensor("dbg_qs", [128, 16], F32, kind="ExternalOutput").ap()
        dbg_w = nc.dram_tensor("dbg_w", [128, 64 + 16 + 132], F32, kind="ExternalOutput").ap()
        dbg_mix = nc.dram_tensor("dbg_mix", [D, T], F32, kind="ExternalOutput").ap()
        dbg_kv0 = nc.dram_tensor("dbg_kv0", [2048, 132], F32, kind="ExternalOutput").ap()
        dbg_qs0 = nc.dram_tensor("dbg_qs0", [128, 16], F32, kind="ExternalOutput").ap()
        dbg_c = nc.dram_tensor("dbg_c", [128, 1024], F32, kind="ExternalOutput").ap()
        dbg_c2 = nc.dram_tensor("dbg_c2", [128, 132], F32, kind="ExternalOutput").ap()

    sk_dram = nc.dram_tensor(pre + "sk_dram", [T, D], BF16)
    v_dram = nc.dram_tensor(pre + "v_dram", [T, D], BF16)
    kv_dram = nc.dram_tensor(pre + "kv_dram", [16 * 128, 132], F32)
    kvg_dram = nc.dram_tensor(pre + "kvg_dram", [4 * 16 * 128, 132], F32)
    qs_dram = nc.dram_tensor(pre + "qs_dram", [128, 16], F32)
    qsg_dram = nc.dram_tensor(pre + "qsg_dram", [128, 16], F32)
    w_dram = nc.dram_tensor(pre + "w_dram", [16, 4], F32)
    mix_dram = nc.dram_tensor(pre + "mix_dram", [D, T], F32)

    S = SAlloc(p, pre, fuse)
    if fuse is None:
        xbf = p.sb("xbf", [128, DC * T], BF16)
        aG = p.sb("aG", [128, 65536], U8)
    else:
        xbf = bview(fuse["P1"], 0, BF16, DC * T)
        aG = bview(fuse["P2"], 0, U8, 65536)
    aM = S.t("aM", [128, 16384], U8)
    xf = S.t("xf", [128, 2 * T], F32)
    sq = S.t("sq", [128, 2048], U8)
    tmp = S.t("tmp", [128, 512], F32)
    memb = S.t("memb", [128, 64 * 132], BF16)
    spv = S.t("spv", [128, 16 * 132], F32)
    stbf = S.t("stbf", [128, 132], BF16)
    mtb = S.t("mtb", [128, 132], BF16)
    sqt = S.t("sqt", [128, 128], BF16)
    wsl = S.t("wsl", [128, 16 * 128], BF16)
    qsum = S.t("qsum", [128, 16], F32)
    xsb = S.t("xsb", [128, 16], BF16)
    wcol = S.t("wcol", [128, 64], F32)
    gcol = S.t("gcol", [128, 16], F32)
    relc = S.t("relc", [16, 8], F32)
    rs1 = S.t("rs1", [128, 1], F32)
    accmem = S.t("accmem", [128, 128], F32)
    rs2 = S.t("rs2", [128, 1], F32)
    oloc = S.t("oloc", [128, 128], F32)
    t128 = S.t("t128", [128, 128], F32)
    sktm = S.t("sktm", [128, 128], BF16)
    stm = S.t("stm", [128, 128], BF16)
    otm = S.t("otm", [128, 128], BF16)

    kq_buf = bview(aG, 0, BF16, 16 * 512)
    skt_buf = bview(aG, 16384, BF16, 4 * 2048)
    vg_buf = bview(aG, 32768, BF16, 4 * 2048)
    ofm_buf = bview(aG, 49152, BF16, 16 * 512)
    mf2 = bview(aG, 0, F32, 2 * T)
    srow = aG[0:1, 16384:32768].bitcast(F32)
    rstd = aG[0:1, 32768:40960].bitcast(F32)
    mbc = bview(aM, 0, F32, T)
    rbc = bview(aM, 8192, F32, T)
    scratch = dict(xf=xf, sq=sq, srow=srow, rstd=rstd, mbc=mbc, rbc=rbc,
                   tmp=tmp, mf2=mf2)

    C = {}
    if fuse is None:
        C["ones_col_div"] = p.sb("c_div", [128, 1], BF16)
        C["ones_row_f32"] = p.sb("c_row", [1, 128], F32)
        C["id128"] = p.sb("c_id", [128, 128], BF16)
    else:
        C["ones_col_div"] = fuse["C"]["ones_col_div"]
        C["ones_row_f32"] = fuse["C"]["ones_row_f32"]
        C["id128"] = fuse["C"]["id128"]
    C["onebf"] = S.t("c_onebf", [128, 1], BF16)
    C["relrow"] = S.t("c_relrow", [128, 1], F32)
    C["tril"] = S.t("c_tril", [128, 128], BF16)
    C["maskpre"] = S.t("c_maskpre", [128, 4], F32)
    mnorm_sb = S.t("mnorm_sb", [128, 64], F32)
    grow = S.t("grow", [1, 16], F32)
    block = p.block

    def s_consts(sync):
        if fuse is None:
            p.dma(sync, C["ones_col_div"][:], cdiv)
            p.dma(sync, C["ones_row_f32"][:], crow)
            p.dma(sync, C["id128"][:], cid)
        p.dma(sync, C["onebf"][:], conebf)
        p.dma(sync, C["relrow"][:], crelrow)
        p.dma(sync, C["tril"][:], ctril)
        p.dma(sync, C["maskpre"][:], maskpre)
        p.dma(sync, mnorm_sb[:], mnormT)
        p.dma(sync, grow[:], gatein)
    block.sync(s_consts)

    def s_memb(gq):
        p.dma(gq, memb[:].rearrange("p (c e) -> p c e", e=132),
              memaug.rearrange("(c p) e -> p c e", p=128))
    block.gpsimd(s_memb)
    ld_consts = p.ld

    def f(v):
        v.wait_ge(p.ldh, ld_consts)
        return v.engine_nop()
    p.do(block.vector, f)

    emit_ln(p, x_in, xbf, C, scratch)

    def f(v):
        for j in range(DC):
            last = v.tensor_reduce(qsum[:, j:j + 1], xbf[:, j * T:(j + 1) * T], AX.X, ALU.add)
        return last
    p.do(block.vector, f)

    def f(v):
        return v.tensor_copy(xsb[:], qsum[:])
    p.do(block.vector, f)

    for ec in range(DC):
        dep = p.step

        def s_w(gq, ec=ec, dep=dep):
            gq.wait_ge(p.fl, dep)
            p.dma(gq, wsl[:].rearrange("p (c o) -> p c o", o=128),
                  wqT[:, ec * 128:(ec + 1) * 128].rearrange("(c p) o -> p c o", p=128))
        block.gpsimd(s_w)
        ld0 = p.ld

        def f(t, ec=ec, ld0=ld0):
            t.wait_ge(p.ldh, ld0)
            for j in range(DC):
                mm = t.matmul(p.pA[:, 0:1], wsl[:, j * 128:(j + 1) * 128], xsb[:, j:j + 1],
                              start=(j == 0), stop=(j == DC - 1))
            return mm
        p.do(block.tensor, f)

        def f(v, ec=ec):
            return v.tensor_copy(qsum[:, ec:ec + 1], p.pA[:, 0:1])
        p.do(block.vector, f)

    def s_q(sync):
        sync.wait_ge(p.fl, p.step)
        p.dma_st(sync, qs_dram.ap(), qsum[:])
    block.sync(s_q)
    st_q = p.st

    def s_cc1(gq):
        gq.wait_ge(p.sth, st_q)
        gq.collective_compute("AllReduce", ALU.add, replica_groups=RG4,
                              ins=[qs_dram.ap().opt()], outs=[qsg_dram.ap().opt()]).then_inc(p.cch)
    block.gpsimd(s_cc1)
    p.cc += 1

    def s_qg(sync):
        sync.wait_ge(p.cch, p.cc)
        p.dma(sync, qsum[:], qsg_dram.ap())
    block.sync(s_qg)
    ld_qg = p.ld

    for m in range(4):
        def f(v, m=m, ld_qg=ld_qg):
            v.wait_ge(p.ldh, ld_qg)
            return v.tensor_tensor(t128[:, 0:16], qsum[:], mnorm_sb[:, m * 16:(m + 1) * 16], ALU.mult)
        p.do(block.vector, f)

        def f(t, m=m):
            t.matmul(p.pA[0:1, 0:16], C["relrow"][:, 0:1], t128[:, 0:16], start=True, stop=True)
            return t.transpose(p.pB[0:16, 0:1], p.pA[0:1, 0:16], C["ones_row_f32"][0:1, 0:1])
        break
    relrow_sb = S.t("relrow_sb", [1, 16], F32)
    for m in range(4):
        def f(v, m=m, ld_qg=ld_qg):
            v.wait_ge(p.ldh, ld_qg)
            return v.tensor_tensor(t128[:, 0:16], qsum[:], mnorm_sb[:, m * 16:(m + 1) * 16], ALU.mult)
        p.do(block.vector, f)

        def f(t, m=m):
            return t.matmul(p.pA[0:1, 0:16], C["relrow"][:, 0:1], t128[:, 0:16], start=True, stop=True)
        p.do(block.tensor, f)

        def f(v, m=m):
            return v.tensor_copy(relrow_sb[0:1, :], p.pA[0:1, 0:16])
        p.do(block.vector, f)

        def f(t, m=m):
            return t.transpose(p.pB[0:16, 0:1], relrow_sb[0:1, 0:16], C["ones_row_f32"][0:1, 0:1])
        p.do(block.tensor, f)

        def f(v, m=m):
            return v.tensor_copy(relc[0:16, m:m + 1], p.pB[0:16, 0:1])
        p.do(block.vector, f)

    def f(v):
        v.tensor_reduce(relc[0:16, 4:5], relc[0:16, 0:4], AX.X, ALU.max, negate=True)
        return v.engine_nop()
    p.do(block.vector, f)

    def f(sc):
        return sc.activation(relc[0:16, 0:4], relc[0:16, 0:4], AF.Exp, bias=relc[0:16, 4:5],
                             scale=np.float32(1.0))
    p.do(block.scalar, f)

    def f(v):
        return v.tensor_reduce(relc[0:16, 5:6], relc[0:16, 0:4], AX.X, ALU.add)
    p.do(block.vector, f)

    def f(v):
        return v.reciprocal(relc[0:16, 6:7], relc[0:16, 5:6])
    p.do(block.vector, f)

    def f(v):
        return v.tensor_scalar(relc[0:16, 0:4], relc[0:16, 0:4], relc[0:16, 6:7], None, ALU.mult)
    p.do(block.vector, f)

    def s_wd(sync):
        sync.wait_ge(p.fl, p.step)
        p.dma_st(sync, w_dram.ap(), relc[0:16, 0:4])
    block.sync(s_wd)
    st_w = p.st

    def s_wl(sync):
        sync.wait_ge(p.sth, st_w)
        p.dma(sync, t128[0:1, 0:64].bitcast(F32), w_dram.ap().rearrange("(o h) m -> o (h m)", o=1))
    block.sync(s_wl)
    ld_w = p.ld

    def f(t, ld_w=ld_w):
        t.wait_ge(p.ldh, ld_w)
        return t.matmul(p.pA[:, 0:64], C["ones_row_f32"][0:1, :], t128[0:1, 0:64],
                        start=True, stop=True)
    p.do(block.tensor, f)

    def f(v):
        return v.tensor_copy(wcol[:], p.pA[:, 0:64])
    p.do(block.vector, f)

    def f(sc):
        return sc.activation(grow[0:1, :], grow[0:1, :], AF.Sigmoid)
    p.do(block.scalar, f)

    def f(t):
        return t.matmul(p.pA[:, 0:16], C["ones_row_f32"][0:1, :], grow[0:1, 0:16],
                        start=True, stop=True)
    p.do(block.tensor, f)

    def f(v):
        return v.tensor_copy(gcol[:], p.pA[:, 0:16])
    p.do(block.vector, f)

    def f(v):
        return v.memset(t128[:], 0.0)
    p.do(block.vector, f)

    totkv = spv

    def f(v):
        return v.memset(totkv[:], 0.0)
    p.do(block.vector, f)

    sct = S.t("sct", [128, 512], F32)
    sp_st = []

    def emit_sigma(v, buf, ec):
        sl = slice(ec * 512, (ec + 1) * 512)
        v.tensor_scalar(sct[:, 0:512], buf[:, sl], 0.0, None, ALU.min)
        return sl

    for g in range(4):
        for ec in range(DC):
            dep = p.step

            def s_w(gq, g=g, ec=ec, dep=dep):
                gq.wait_ge(p.fl, dep)
                p.dma(gq, wsl[:].rearrange("p (c o) -> p c o", o=128),
                      wkT[:, ec * 128:(ec + 1) * 128].rearrange("(c p) o -> p c o", p=128))
            block.gpsimd(s_w)
            ld0 = p.ld

            def f(t, g=g, ec=ec, ld0=ld0):
                t.wait_ge(p.ldh, ld0)
                for j in range(DC):
                    mm = t.matmul(p.pA[:, 0:512], wsl[:, j * 128:(j + 1) * 128],
                                  xbf[:, j * T + g * 512:j * T + (g + 1) * 512],
                                  start=(j == 0), stop=(j == DC - 1))
                return mm
            p.do(block.tensor, f)

            def f(v, ec=ec):
                return v.tensor_scalar(sct[:, 0:512], p.pA[:, 0:512], 0.0, None, ALU.min)
            p.do(block.vector, f)

            def f(sc, ec=ec):
                return sc.activation(sct[:, 0:512], sct[:, 0:512], AF.Exp)
            p.do(block.scalar, f)

            def f(v, ec=ec):
                v.tensor_scalar(kq_buf[:, ec * 512:(ec + 1) * 512], p.pA[:, 0:512], 0.0, None, ALU.max)
                return v.tensor_tensor(kq_buf[:, ec * 512:(ec + 1) * 512],
                                       kq_buf[:, ec * 512:(ec + 1) * 512], sct[:, 0:512], ALU.add)
            p.do(block.vector, f)

        for et in range(4):
            dep = p.step

            def s_w(gq, g=g, et=et, dep=dep):
                gq.wait_ge(p.fl, dep)
                p.dma(gq, bview(aG, 49152, BF16, 16 * 512)[:].rearrange("p (c o) -> p c o", o=512),
                      wvT[:, et * 512:(et + 1) * 512].rearrange("(c p) o -> p c o", p=128))
            block.gpsimd(s_w)
            ld0 = p.ld
            vsl = bview(aG, 49152, BF16, 16 * 512)

            for tc in range(4):
                def f(t, g=g, et=et, tc=tc, ld0=ld0, vsl=vsl):
                    t.wait_ge(p.ldh, ld0)
                    for j in range(DC):
                        mm = t.matmul(p.pB[:, 0:512],
                                      xbf[:, j * T + g * 512 + tc * 128:j * T + g * 512 + tc * 128 + 128],
                                      vsl[:, j * 512:(j + 1) * 512],
                                      start=(j == 0), stop=(j == DC - 1))
                    return mm
                p.do(block.tensor, f)

                def f(v, tc=tc, et=et):
                    if tc == 0 and et == 0 and sp_st:
                        v.wait_ge(p.sth, sp_st[-1])
                    return v.tensor_copy(vg_buf[:, tc * 2048 + et * 512:tc * 2048 + (et + 1) * 512],
                                         p.pB[:, 0:512])
                p.do(block.vector, f)

        for tc in range(4):
            for h in range(DC):
                def f(t, tc=tc, h=h):
                    return t.transpose(p.pT[:, (h % 8) * 128:(h % 8 + 1) * 128],
                                       kq_buf[:, h * 512 + tc * 128:h * 512 + tc * 128 + 128],
                                       C["id128"][:])
                p.do(block.tensor, f)

                def f(v, tc=tc, h=h):
                    if tc == 0 and h == 0 and sp_st:
                        v.wait_ge(p.sth, sp_st[-1])
                    return v.tensor_copy(skt_buf[:, tc * 2048 + h * 128:tc * 2048 + (h + 1) * 128],
                                         p.pT[:, (h % 8) * 128:(h % 8 + 1) * 128])
                p.do(block.vector, f)

        def s_sp(sync, g=g):
            sync.wait_ge(p.fl, p.step)
            for tc in range(4):
                p.dma_st(sync, sk_dram[(g * 4 + tc) * 128:(g * 4 + tc + 1) * 128, :],
                         skt_buf[:, tc * 2048:(tc + 1) * 2048])
                p.dma_st(sync, v_dram[(g * 4 + tc) * 128:(g * 4 + tc + 1) * 128, :],
                         vg_buf[:, tc * 2048:(tc + 1) * 2048])
        block.sync(s_sp)
        sp_st.append(p.st)

        for h in range(DC):
            def f(t, g=g, h=h):
                for tc in range(4):
                    t.matmul(p.pD[:, 0:128], skt_buf[:, tc * 2048 + h * 128:tc * 2048 + (h + 1) * 128],
                             vg_buf[:, tc * 2048 + h * 128:tc * 2048 + (h + 1) * 128],
                             start=(tc == 0), stop=False)
                    mm = t.matmul(p.pD[:, 128:129], skt_buf[:, tc * 2048 + h * 128:tc * 2048 + (h + 1) * 128],
                                  C["onebf"][:, 0:1], start=False, stop=(tc == 3))
                return mm
            p.do(block.tensor, f)

            def f(v, h=h):
                return v.tensor_tensor(totkv[:, h * 132:h * 132 + 129], totkv[:, h * 132:h * 132 + 129],
                                       p.pD[:, 0:129], ALU.add)
            p.do(block.vector, f)

    def s_kv(sync):
        sync.wait_ge(p.fl, p.step)
        p.dma_st(sync, kv_dram.ap().rearrange("(h p) c -> p h c", p=128),
                 totkv[:].rearrange("p (h c) -> p h c", c=132))
    block.sync(s_kv)
    st_kv = p.st

    def s_cc2(gq):
        gq.wait_ge(p.sth, st_kv)
        gq.collective_compute("AllGather", ALU.bypass, replica_groups=RG4,
                              ins=[kv_dram.ap().opt()], outs=[kvg_dram.ap().opt()]).then_inc(p.cch)
    block.gpsimd(s_cc2)
    p.cc += 1

    def f(v):
        v.wait_ge(p.sth, st_kv)
        v.wait_ge(p.cch, p.cc)
        return v.memset(spv[:], 0.0)
    p.do(block.vector, f)

    rbuf = bview(aG, 49152, F32, 16 * 132)
    for r in range(4):
        dep = p.step

        def s_r(sync, r=r, dep=dep):
            sync.wait_ge(p.fl, dep)
            if r == 0:
                sync.wait_ge(p.cch, p.cc)
            p.dma(sync, rbuf[:].rearrange("p (h c) -> p h c", c=132),
                  kvg_dram[r * 2048:(r + 1) * 2048, :].rearrange("(h p) c -> p h c", p=128))
        block.sync(s_r)
        ld0 = p.ld

        def f(v, r=r, ld0=ld0):
            v.wait_ge(p.ldh, ld0)
            return v.tensor_scalar(rbuf[:], rbuf[:], C["maskpre"][:, r:r + 1], None, ALU.mult)
        p.do(block.vector, f)

        def f(v, r=r):
            return v.tensor_tensor(spv[:], spv[:], rbuf[:], ALU.add)
        p.do(block.vector, f)

    if debug:
        def s_dbg(sync):
            sync.wait_ge(p.fl, p.step)
            sync.wait_ge(p.sth, p.st)
            p.dma_st(sync, dbg_sk[:, :], sk_dram.ap())
            p.dma_st(sync, dbg_v[:, :], v_dram.ap())
            p.dma_st(sync, dbg_kvg[:, :], kvg_dram.ap())
            p.dma_st(sync, dbg_qs[:, :], qsum[:])
            p.dma_st(sync, dbg_w[:, 0:64], wcol[:])
            p.dma_st(sync, dbg_w[:, 64:80], gcol[:])
            p.dma_st(sync, dbg_w[:, 80:212], spv[:, 0:132])
            p.dma_st(sync, dbg_kv0[:, :], kv_dram.ap())
            p.dma_st(sync, dbg_qs0[:, :], qs_dram.ap())
        block.sync(s_dbg)

    EPS1 = 1e-6
    for g in range(4):
        for ec in range(DC):
            dep = p.step

            def s_w(gq, g=g, ec=ec, dep=dep):
                gq.wait_ge(p.fl, dep)
                p.dma(gq, wsl[:].rearrange("p (c o) -> p c o", o=128),
                      wqT[:, ec * 128:(ec + 1) * 128].rearrange("(c p) o -> p c o", p=128))
            block.gpsimd(s_w)
            ld0 = p.ld

            def f(t, g=g, ec=ec, ld0=ld0):
                t.wait_ge(p.ldh, ld0)
                for j in range(DC):
                    mm = t.matmul(p.pA[:, 0:512], wsl[:, j * 128:(j + 1) * 128],
                                  xbf[:, j * T + g * 512:j * T + (g + 1) * 512],
                                  start=(j == 0), stop=(j == DC - 1))
                return mm
            p.do(block.tensor, f)

            def f(v, ec=ec):
                return v.tensor_scalar(sct[:, 0:512], p.pA[:, 0:512], 0.0, None, ALU.min)
            p.do(block.vector, f)

            def f(sc, ec=ec):
                return sc.activation(sct[:, 0:512], sct[:, 0:512], AF.Exp)
            p.do(block.scalar, f)

            def f(v, ec=ec):
                v.tensor_scalar(kq_buf[:, ec * 512:(ec + 1) * 512], p.pA[:, 0:512], 0.0, None, ALU.max)
                return v.tensor_tensor(kq_buf[:, ec * 512:(ec + 1) * 512],
                                       kq_buf[:, ec * 512:(ec + 1) * 512], sct[:, 0:512], ALU.add)
            p.do(block.vector, f)

        dep = p.step

        def s_rl(sync, g=g, dep=dep):
            sync.wait_ge(p.fl, dep)
            if sp_st:
                sync.wait_ge(p.sth, sp_st[-1])
            p.dma(sync, skt_buf[:].rearrange("p (c e) -> p c e", e=2048),
                  sk_dram[g * 512:(g + 1) * 512, :].rearrange("(c p) e -> p c e", p=128))
            p.dma(sync, vg_buf[:].rearrange("p (c e) -> p c e", e=2048),
                  v_dram[g * 512:(g + 1) * 512, :].rearrange("(c p) e -> p c e", p=128))
        block.sync(s_rl)
        ld_rl = p.ld

        for tc in range(4):
            for h in range(DC):
                skt_sl = skt_buf[:, tc * 2048 + h * 128:tc * 2048 + (h + 1) * 128]
                vg_sl = vg_buf[:, tc * 2048 + h * 128:tc * 2048 + (h + 1) * 128]
                sq_sl = kq_buf[:, h * 512 + tc * 128:h * 512 + tc * 128 + 128]

                def f(v, h=h):
                    return v.tensor_copy(stbf[:, 0:132], spv[:, h * 132:(h + 1) * 132])
                p.do(block.vector, f)

                def f(t, tc=tc, h=h, ld_rl=ld_rl, skt_sl=skt_sl):
                    t.wait_ge(p.ldh, ld_rl)
                    return t.transpose(p.pT[:, 0:128], skt_sl, C["id128"][:])
                p.do(block.tensor, f)

                def f(v):
                    return v.tensor_copy(sktm[:], p.pT[:, 0:128])
                p.do(block.vector, f)

                def f(t, sq_sl=sq_sl):
                    return t.matmul(p.pE[:, 0:128], sktm[:], sq_sl, start=True, stop=True)
                p.do(block.tensor, f)

                def f(v):
                    return v.scalar_tensor_tensor(stm[:], p.pE[:, 0:128], 1.0, C["tril"][:],
                                                  ALU.mult, ALU.mult)
                p.do(block.vector, f)

                def f(t, sq_sl=sq_sl, vg_sl=vg_sl):
                    t.matmul(p.pD[:, 0:129], sq_sl, stbf[:, 0:129], start=True, stop=False)
                    t.matmul(p.pD[:, 0:128], stm[:], vg_sl, start=False, stop=False)
                    return t.matmul(p.pD[:, 128:129], stm[:], C["onebf"][:, 0:1],
                                    start=False, stop=True)
                p.do(block.tensor, f)

                if debug and g == 0 and tc == 0 and h == 0:
                    dbgc = bview(aM, 0, F32, 1024)

                    def f(v):
                        v.tensor_copy(dbgc[:, 0:129], p.pD[:, 0:129])
                        return v.tensor_copy(dbgc[:, 256:256 + 128], stm[:])
                    p.do(block.vector, f)

                def f(v):
                    return v.memset(accmem[:], 0.0)
                p.do(block.vector, f)

                def f(v, sq_sl=sq_sl):
                    return v.tensor_copy(sqt[:], sq_sl)
                p.do(block.vector, f)

                for m in range(4):
                    def f(t, m=m, h=h):
                        return t.matmul(p.pB[:, 0:132], sqt[:],
                                        memb[:, (m * 16 + h) * 132:(m * 16 + h + 1) * 132],
                                        start=True, stop=True)
                    p.do(block.tensor, f)

                    if debug and g == 0 and tc == 0 and h == 0 and m == 0:
                        def f(v):
                            return v.tensor_copy(bview(aM, 8192, F32, 132), p.pB[:, 0:132])
                        p.do(block.vector, f)

                    def f(v, m=m, h=h):
                        return v.tensor_scalar(rs1[:], p.pB[:, 128:129], float(EPS1), None, ALU.max)
                    p.do(block.vector, f)

                    def f(v):
                        return v.reciprocal(rs2[:], rs1[:])
                    p.do(block.vector, f)

                    def f(v, m=m, h=h):
                        return v.tensor_tensor(rs1[:], rs2[:], wcol[:, h * 4 + m:h * 4 + m + 1], ALU.mult)
                    p.do(block.vector, f)

                    def f(v, m=m, h=h):
                        return v.tensor_scalar(sct[:, 0:128], p.pB[:, 0:128], rs1[:], None, ALU.mult)
                    p.do(block.vector, f)

                    def f(v):
                        return v.tensor_tensor(accmem[:], accmem[:], sct[:, 0:128], ALU.add)
                    p.do(block.vector, f)

                def f(v, h=h):
                    return v.tensor_scalar(rs1[:], p.pD[:, 128:129], float(EPS1), None, ALU.max)
                p.do(block.vector, f)

                def f(v):
                    return v.reciprocal(rs2[:], rs1[:])
                p.do(block.vector, f)

                def f(v, h=h):
                    v.tensor_scalar(oloc[:], p.pD[:, 0:128], rs2[:], None, ALU.mult)
                    v.tensor_tensor(t128[:], accmem[:], oloc[:], ALU.subtract)
                    return v.scalar_tensor_tensor(otm[:], t128[:], gcol[:, h:h + 1], oloc[:],
                                                  ALU.mult, ALU.add)
                p.do(block.vector, f)

                if debug and g == 0 and tc == 0 and h == 0:
                    def f(v):
                        v.tensor_copy(dbgc[:, 384:384 + 128], accmem[:])
                        v.tensor_copy(dbgc[:, 512:512 + 128], oloc[:])
                        v.tensor_copy(dbgc[:, 640:640 + 128], otm[:])
                        return v.tensor_copy(dbgc[:, 768:768 + 132], stbf[:, 0:132])
                    p.do(block.vector, f)

                def f(t):
                    return t.transpose(p.pT[:, 128:256], otm[:], C["id128"][:])
                p.do(block.tensor, f)

                def f(v, tc=tc, h=h):
                    return v.tensor_copy(ofm_buf[:, h * 512 + tc * 128:h * 512 + tc * 128 + 128],
                                         p.pT[:, 128:256])
                p.do(block.vector, f)

                def f(t, skt_sl=skt_sl, vg_sl=vg_sl):
                    t.matmul(p.pF[:, 0:128], skt_sl, vg_sl, start=True, stop=True)
                    return t.matmul(p.pF[:, 128:129], skt_sl, C["onebf"][:, 0:1],
                                    start=True, stop=True)
                p.do(block.tensor, f)

                def f(v, h=h):
                    return v.tensor_tensor(spv[:, h * 132:h * 132 + 129],
                                           spv[:, h * 132:h * 132 + 129], p.pF[:, 0:129], ALU.add)
                p.do(block.vector, f)

        mt_st = []
        for oc in range(DC):
            dep = p.step

            def s_w(gq, oc=oc, dep=dep):
                gq.wait_ge(p.fl, dep)
                p.dma(gq, wsl[:].rearrange("p (c o) -> p c o", o=128),
                      woT[:, oc * 128:(oc + 1) * 128].rearrange("(c p) o -> p c o", p=128))
            block.gpsimd(s_w)
            ld0 = p.ld

            def f(t, oc=oc, ld0=ld0):
                t.wait_ge(p.ldh, ld0)
                for ec in range(DC):
                    mm = t.matmul(p.pC[:, 0:512], wsl[:, ec * 128:(ec + 1) * 128],
                                  ofm_buf[:, ec * 512:(ec + 1) * 512],
                                  start=(ec == 0), stop=(ec == DC - 1))
                return mm
            p.do(block.tensor, f)

            def f(v):
                return v.tensor_copy(tmp[:, 0:512], p.pC[:, 0:512])
            p.do(block.vector, f)

            def s_m(sync, g=g, oc=oc):
                sync.wait_ge(p.fl, p.step)
                p.dma_st(sync, mix_dram[oc * 128:(oc + 1) * 128, g * 512:(g + 1) * 512],
                         tmp[:, 0:512])
            block.sync(s_m)
            def f(v):
                v.wait_ge(p.sth, p.st)
                return v.engine_nop()
            p.do(block.vector, f)

    if debug:
        def s_dm(sync):
            sync.wait_ge(p.fl, p.step)
            p.dma_st(sync, dbg_mix[:, :], mix_dram.ap())
            p.dma_st(sync, dbg_c[:, :], bview(aM, 0, F32, 1024))
            p.dma_st(sync, dbg_c2[:, :], bview(aM, 8192, F32, 132))
        block.sync(s_dm)

    def f(g2):
        g2.wait_ge(p.sth, p.st)
        return g2.engine_nop()
    p.do(block.gpsimd, f)
    emit_postln_residual(p, mix_dram, x_in, x_out, C, scratch, xbf)
    emit_store_wait(p)
    return p


def prep_l1attn_inputs(full, core):
    mem = np.asarray(full["L1_attn_memories"], np.float32)
    mn = np.asarray(full["L1_attn_memory_norms"], np.float32)
    memaug = np.concatenate([mem, mn[..., None], np.zeros((4, 16, 128, 3), np.float32)], axis=3)
    memaug = memaug.reshape(64 * 128, 132)
    mnormT = np.ascontiguousarray(mn.transpose(2, 0, 1).reshape(128, 64))
    tril = np.triu(np.ones((128, 128), np.float32)).astype(BF16NP)
    maskpre = np.zeros((128, 4), np.float32)
    pos = core % 4
    maskpre[:, 0:pos] = 1.0
    shared = {
        "wqT": np.ascontiguousarray(np.asarray(full["L1_attn_w_q"], np.float32).T),
        "wkT": np.ascontiguousarray(np.asarray(full["L1_attn_w_k"], np.float32).T),
        "wvT": np.ascontiguousarray(np.asarray(full["L1_attn_w_v"], np.float32).T),
        "woT": np.ascontiguousarray(np.asarray(full["L1_attn_w_o"], np.float32).T),
        "memaug": memaug, "mnormT": mnormT,
        "gatein": np.asarray(full["L1_attn_gate"], np.float32).reshape(1, 16),
        "maskpre": maskpre,
        "ctril": tril,
        "cdiv": np.full((128, 1), 1.0 / 2048.0, BF16NP),
        "crow": np.ones((1, 128), np.float32),
        "cid": np.eye(128, dtype=BF16NP),
        "conebf": np.ones((128, 1), BF16NP),
        "crelrow": np.full((128, 1), (128.0 ** -0.5) / 8192.0, np.float32),
    }
    return shared


def run_l1attn(x_flat, full, debug=False):
    shards = shard_fm(x_flat)
    p = build_l1attn(debug)
    in_maps = [dict(prep_l1attn_inputs(full, c), x_in=s) for c, s in enumerate(shards)]
    res = run_bass_kernel_spmd(p.nc, in_maps, core_ids=list(range(8)))
    p.close()
    if debug:
        return res
    return unshard_fm([res.results[c]["x_out"] for c in range(8)])



def _u8codes(w_int32):
    return np.ascontiguousarray(w_int32.view(np.uint8).reshape(w_int32.shape[0], -1)[:, 0::4])


def prep_l1ffn_inputs(full):
    wg = np.ascontiguousarray(_u8codes(np.asarray(full["L1_ffn_wq_gate"])).T)
    wu = np.ascontiguousarray(_u8codes(np.asarray(full["L1_ffn_wq_up"])).T)
    wd = np.ascontiguousarray(_u8codes(np.asarray(full["L1_ffn_wq_down"])).T)
    shared = {
        "wg": wg, "wu": wu, "wd": wd,
        "sg": np.asarray(full["L1_ffn_s_gate"], np.float32).reshape(1, -1),
        "su": np.asarray(full["L1_ffn_s_up"], np.float32).reshape(1, -1),
        "sd": np.asarray(full["L1_ffn_s_down"], np.float32).reshape(1, -1),
        "cdiv": np.full((128, 1), 1.0 / 2048.0, BF16NP),
        "crow": np.ones((1, 128), np.float32),
        "cm128": np.full((128, 1), -128.0, np.float32),
    }
    return shared


def shard_fm(x_flat):
    return [np.ascontiguousarray(x_flat[c * T:(c + 1) * T].T) for c in range(8)]


def unshard_fm(shards):
    return np.concatenate([s.T for s in shards], axis=0)


def run_l1ffn(x_flat, full):
    shared = prep_l1ffn_inputs(full)
    shards = shard_fm(x_flat)
    p = build_l1ffn()
    in_maps = [dict(shared, x_in=s) for s in shards]
    res = run_bass_kernel_spmd(p.nc, in_maps, core_ids=list(range(8)))
    p.close()
    return unshard_fm([res.results[c]["x_out"] for c in range(8)])


def build_fused():
    p = Prog().start()
    nc = p.nc
    x0 = nc.dram_tensor("x_in", [D, T], F32, kind="ExternalInput").ap()
    xo = nc.dram_tensor("x_out", [D, T], F32, kind="ExternalOutput").ap()
    x1 = nc.dram_tensor("x1", [D, T], F32).ap()
    x2 = nc.dram_tensor("x2", [D, T], F32).ap()
    x3 = nc.dram_tensor("x3", [D, T], F32).ap()
    cdiv = nc.dram_tensor("cdiv", [128, 1], BF16, kind="ExternalInput").ap()
    crow = nc.dram_tensor("crow", [1, 128], F32, kind="ExternalInput").ap()
    cid = nc.dram_tensor("cid", [128, 128], BF16, kind="ExternalInput").ap()
    P1 = p.sb("P1", [128, 65536], U8)
    P2 = p.sb("P2", [128, 65536], U8)
    REST = p.sb("REST", [128, 80896], U8)
    C = {
        "ones_col_div": p.sb("g_div", [128, 1], BF16),
        "ones_row_f32": p.sb("g_row", [1, 128], F32),
        "id128": p.sb("g_id", [128, 128], BF16),
    }
    block = p.block

    def s_c(sync):
        p.dma(sync, C["ones_col_div"][:], cdiv)
        p.dma(sync, C["ones_row_f32"][:], crow)
        p.dma(sync, C["id128"][:], cid)
    block.sync(s_c)
    ldc = p.ld

    def f(v):
        v.wait_ge(p.ldh, ldc)
        return v.engine_nop()
    p.do(block.vector, f)

    def barrier():
        def f(g):
            g.wait_ge(p.sth, p.st)
            return g.engine_nop()
        p.do(block.gpsimd, f)

    base = dict(p=p, P1=P1, P2=P2, REST=REST, C=C)
    build_l0attn(fuse=dict(base, pre="a_", x_in=x0, x_out=x1))
    barrier()
    build_l0ffn(fuse=dict(base, pre="f_", x_in=x1, x_out=x2))
    barrier()
    build_l1attn(fuse=dict(base, pre="b_", x_in=x2, x_out=x3))
    barrier()
    build_l1ffn(fuse=dict(base, pre="g_", x_in=x3, x_out=xo))
    return p


def kernel(**inputs):
    x = np.asarray(inputs["L0_attn_hidden_states"], np.float32).reshape(16384, 2048)
    shards = shard_fm(x)
    shared = {}
    shared.update(prep_l0attn_inputs(inputs))
    shared.update(prep_l0ffn_inputs(inputs))
    shared.update(prep_l1ffn_inputs(inputs))
    p = build_fused()
    in_maps = []
    for c in range(8):
        m = dict(shared)
        m.update(prep_l1attn_inputs(inputs, c))
        m["x_in"] = shards[c]
        in_maps.append(m)
    res = run_bass_kernel_spmd(p.nc, in_maps, core_ids=list(range(8)))
    p.close()
    out = unshard_fm([res.results[c]["x_out"] for c in range(8)])
    return np.ascontiguousarray(out.reshape(2, 8192, 2048).astype(np.float32))


def kernel_multilaunch(**inputs):
    x = np.asarray(inputs["L0_attn_hidden_states"], np.float32).reshape(16384, 2048)
    x = run_l0attn(x, inputs)
    x = run_l0ffn(x, inputs)
    x = run_l1attn(x, inputs)
    x = run_l1ffn(x, inputs)
    return np.ascontiguousarray(x.reshape(2, 8192, 2048).astype(np.float32))


# version 03
# speedup vs baseline: 1.1236x; 1.1236x over previous
import numpy as np
import ml_dtypes
from contextlib import ExitStack

import concourse.bass as bass
import concourse.mybir as mybir
from concourse.bass_utils import run_bass_kernel_spmd

F32 = mybir.dt.float32
BF16 = mybir.dt.bfloat16
U8 = mybir.dt.uint8
U16 = mybir.dt.uint16
AF = mybir.ActivationFunctionType
ALU = mybir.AluOpType
AX = mybir.AxisListType
BF16NP = ml_dtypes.bfloat16

T = 2048
D = 2048
DC = 16
NTT = 4
EPS = 1e-5
TWO126 = float(2.0 ** 126)
RG4 = [[0, 1, 2, 3], [4, 5, 6, 7]]
RG8 = [[0, 1, 2, 3, 4, 5, 6, 7]]
HID1 = 5632
HC1 = HID1 // 128
MLP0 = 16384


class Prog:
    def __init__(self):
        self.nc = bass.Bass("TRN2", target_bir_lowering=False)
        self.ctx = ExitStack()
        self.step = 0
        self.ld = 0
        self.st = 0
        self.cc = 0

    def start(self):
        c = self.ctx.enter_context
        self.fl = c(self.nc.semaphore("fl"))
        self.ldh = c(self.nc.semaphore("ldh"))
        self.sth = c(self.nc.semaphore("sth"))
        self.cch = c(self.nc.semaphore("cch"))
        self.block = c(self.nc.Block())
        self.pA = c(self.nc.psum_tensor("pA", [128, 512], F32))
        self.pB = c(self.nc.psum_tensor("pB", [128, 512], F32))
        self.pC = c(self.nc.psum_tensor("pC", [128, 512], F32))
        self.pD = c(self.nc.psum_tensor("pD", [128, 1024], F32))
        self.pE = c(self.nc.psum_tensor("pE", [128, 512], F32))
        self.pF = c(self.nc.psum_tensor("pF", [128, 512], F32))
        self.pT = c(self.nc.psum_tensor("pT", [128, 1024], BF16))
        return self

    def sb(self, name, shape, dtype):
        return self.ctx.enter_context(self.nc.sbuf_tensor(name, shape, dtype))

    def do(self, section, emit):
        dep = self.step

        def wrapped(eng):
            eng.wait_ge(self.fl, dep)
            last = emit(eng)
            last.then_inc(self.fl, 1)
        section(wrapped)
        self.step += 1
        return self.step

    def load(self, emit, n=1):
        self.block.sync(emit)
        return self.ld

    def dma(self, q, out, in_, accum=False):
        kw = {"accum_op": ALU.add} if accum else {}
        q.dma_start(out=out, in_=in_, **kw).then_inc(self.ldh, 16)
        self.ld += 16
        return self.ld

    def dma_st(self, q, out, in_, accum=False):
        kw = {"accum_op": ALU.add} if accum else {}
        q.dma_start(out=out, in_=in_, **kw).then_inc(self.sth, 16)
        self.st += 16
        return self.st

    def close(self):
        self.ctx.close()


_ESZ = {F32: 4, BF16: 2, U8: 1, U16: 2}


class SAlloc:

    def __init__(self, p, pre, fuse):
        self.p = p
        self.pre = pre
        self.fuse = fuse
        self.off = 0

    def t(self, name, shape, dtype):
        if self.fuse is None:
            return self.p.sb(self.pre + name, shape, dtype)
        nb = shape[1] * _ESZ[dtype]
        nb_pad = (nb + 31) // 32 * 32
        o = self.off
        self.off += nb_pad
        R = self.fuse["REST"]
        assert self.off <= R.shape[1], f"REST overflow at {name}: {self.off}"
        v = R[:, o:o + nb].bitcast(dtype)
        if shape[0] != 128:
            v = v[0:shape[0], :]
        return v



def bview(buf, byte_off, dtype, cols):
    esz = _ESZ[dtype]
    bsz = _ESZ[buf.dtype]
    assert byte_off % bsz == 0 and (cols * esz) % bsz == 0
    return buf[:, byte_off // bsz:(byte_off + cols * esz) // bsz].bitcast(dtype)



def emit_stats(p, src_bf, C, srow, rstd, sq):
    block = p.block
    onesd = C["ones_col_div"]

    for tt in range(NTT):
        def f(t, tt=tt):
            for j in range(DC):
                mm = t.matmul(p.pE[0:1, 0:512], onesd[:, 0:1],
                              src_bf[:, j * T + tt * 512:j * T + (tt + 1) * 512],
                              start=(j == 0), stop=(j == DC - 1))
            return mm
        p.do(block.tensor, f)

        def f(v, tt=tt):
            return v.tensor_copy(srow[0:1, tt * 512:(tt + 1) * 512], p.pE[0:1, 0:512])
        p.do(block.vector, f)

    nsl = min(4, sq.shape[1] // 1024)
    sqv = sq[:, 0:nsl * 1024].bitcast(BF16)
    for tt in range(NTT):
        for jg in range(DC // nsl):
            def f(sc, tt=tt, jg=jg):
                for i in range(nsl):
                    j2 = jg * nsl + i
                    last = sc.activation(sqv[:, i * 512:(i + 1) * 512],
                                         src_bf[:, j2 * T + tt * 512:j2 * T + (tt + 1) * 512],
                                         AF.Square)
                return last
            p.do(block.scalar, f)

            def f(t, tt=tt, jg=jg):
                for i in range(nsl):
                    j2 = jg * nsl + i
                    last = t.matmul(p.pF[0:1, 0:512], onesd[:, 0:1],
                                    sqv[:, i * 512:(i + 1) * 512],
                                    start=(j2 == 0), stop=(j2 == DC - 1))
                return last
            p.do(block.tensor, f)

        def f(v, tt=tt):
            return v.tensor_copy(srow[0:1, T + tt * 512:T + (tt + 1) * 512], p.pF[0:1, 0:512])
        p.do(block.vector, f)

    def f(v):
        v.tensor_tensor(rstd[0:1, 0:T], srow[0:1, 0:T], srow[0:1, 0:T], ALU.mult)
        return v.tensor_tensor(srow[0:1, T:2 * T], srow[0:1, T:2 * T], rstd[0:1, 0:T], ALU.subtract)
    p.do(block.vector, f)

    def f(sc):
        return sc.activation(rstd[0:1, 0:T], srow[0:1, T:2 * T], AF.Sqrt, bias=np.float32(EPS), scale=np.float32(1.0))
    p.do(block.scalar, f)

    def f(v):
        return v.reciprocal(rstd[0:1, 0:T], rstd[0:1, 0:T])
    p.do(block.vector, f)


def emit_bcast_stats(p, C, srow, rstd, mbc, rbc):
    block = p.block
    ones_row = C["ones_row_f32"]
    for tt in range(NTT):
        def f(t, tt=tt):
            t.matmul(p.pD[:, 0:512], ones_row[0:1, :],
                     srow[0:1, tt * 512:(tt + 1) * 512], start=True, stop=True)
            return t.matmul(p.pD[:, 512:1024], ones_row[0:1, :],
                            rstd[0:1, tt * 512:(tt + 1) * 512], start=True, stop=True)
        p.do(block.tensor, f)

        def f(v, tt=tt):
            v.tensor_copy(mbc[:, tt * 512:(tt + 1) * 512], p.pD[:, 0:512])
            return v.tensor_copy(rbc[:, tt * 512:(tt + 1) * 512], p.pD[:, 512:1024])
        p.do(block.vector, f)


def emit_ln(p, x_dram, xbf, C, scratch):
    block = p.block
    xf, sq = scratch["xf"], scratch["sq"]
    srow, rstd = scratch["srow"], scratch["rstd"]
    mbc, rbc, tmp = scratch["mbc"], scratch["rbc"], scratch["tmp"]

    ld0 = p.ld
    copy_step0 = p.step

    def s_load(sync):
        sync.wait_ge(p.fl, copy_step0)
        for j in range(DC):
            if j >= 2:
                sync.wait_ge(p.fl, copy_step0 + (j - 2) + 1)
            p.dma(sync, xf[:, (j % 2) * T:(j % 2 + 1) * T], x_dram[j * 128:(j + 1) * 128, :])
    p.block.sync(s_load)

    for j in range(DC):
        def f(v, j=j):
            v.wait_ge(p.ldh, ld0 + 16 * (j + 1))
            return v.tensor_copy(xbf[:, j * T:(j + 1) * T], xf[:, (j % 2) * T:(j % 2 + 1) * T])
        p.do(block.vector, f)

    emit_stats(p, xbf, C, srow, rstd, sq)
    emit_bcast_stats(p, C, srow, rstd, mbc, rbc)

    for j in range(DC):
        def f(v, j=j):
            for tt in range(NTT):
                sl = slice(j * T + tt * 512, j * T + (tt + 1) * 512)
                st = slice(tt * 512, (tt + 1) * 512)
                v.tensor_tensor(tmp[:, 0:512], xbf[:, sl], mbc[:, st], ALU.subtract)
                last = v.tensor_tensor(xbf[:, sl], tmp[:, 0:512], rbc[:, st], ALU.mult)
            return last
        p.do(block.vector, f)


def emit_postln_residual(p, mix_dram, x_dram, out_dram, C, scratch, mbf):
    block = p.block
    xf, sq = scratch["xf"], scratch["sq"]
    srow, rstd = scratch["srow"], scratch["rstd"]
    mbc, rbc, tmp = scratch["mbc"], scratch["rbc"], scratch["tmp"]
    mf2 = scratch["mf2"]

    ld0 = p.ld
    copy_step0 = p.step

    def s_load(sync):
        sync.wait_ge(p.fl, copy_step0)
        for j in range(DC):
            if j >= 2:
                sync.wait_ge(p.fl, copy_step0 + (j - 2) + 1)
            p.dma(sync, xf[:, (j % 2) * T:(j % 2 + 1) * T], mix_dram[j * 128:(j + 1) * 128, :])
    block.sync(s_load)

    for j in range(DC):
        def f(v, j=j):
            v.wait_ge(p.ldh, ld0 + 16 * (j + 1))
            return v.tensor_copy(mbf[:, j * T:(j + 1) * T], xf[:, (j % 2) * T:(j % 2 + 1) * T])
        p.do(block.vector, f)

    emit_stats(p, mbf, C, srow, rstd, sq)
    emit_bcast_stats(p, C, srow, rstd, mbc, rbc)

    ld1 = p.ld
    norm_step0 = p.step

    for j in range(DC):
        def f(v, j=j):
            v.wait_ge(p.ldh, ld1 + 32 * (j + 1))
            for tt in range(NTT):
                slm = slice((j % 2) * T + tt * 512, (j % 2) * T + (tt + 1) * 512)
                st = slice(tt * 512, (tt + 1) * 512)
                v.tensor_tensor(tmp[:, 0:512], mf2[:, slm], mbc[:, st], ALU.subtract)
                v.tensor_tensor(tmp[:, 0:512], tmp[:, 0:512], rbc[:, st], ALU.mult)
                last = v.tensor_tensor(mf2[:, slm], xf[:, slm], tmp[:, 0:512], ALU.add)
            return last
        p.do(block.vector, f)

    def s_stream(sync):
        st_after = {}
        for j in range(DC):
            if j >= 2:
                sync.wait_ge(p.fl, norm_step0 + (j - 2) + 1)
                p.dma_st(sync, out_dram[(j - 2) * 128:(j - 1) * 128, :],
                         mf2[:, (j % 2) * T:(j % 2 + 1) * T])
                st_after[j - 2] = p.st
            if j >= 4:
                sync.wait_ge(p.sth, st_after[j - 4])
            p.dma(sync, xf[:, (j % 2) * T:(j % 2 + 1) * T], x_dram[j * 128:(j + 1) * 128, :])
            p.dma(sync, mf2[:, (j % 2) * T:(j % 2 + 1) * T], mix_dram[j * 128:(j + 1) * 128, :])
        for j in (DC - 2, DC - 1):
            sync.wait_ge(p.fl, norm_step0 + j + 1)
            p.dma_st(sync, out_dram[j * 128:(j + 1) * 128, :],
                     mf2[:, (j % 2) * T:(j % 2 + 1) * T])
    block.sync(s_stream)


def emit_store_wait(p):
    def f(g):
        g.wait_ge(p.sth, p.st)
        return g.engine_nop()
    p.do(p.block.gpsimd, f)



def build_l1ffn(fuse=None):
    if fuse is None:
        p = Prog().start()
        own = True
    else:
        p, own = fuse["p"], False
    nc = p.nc
    pre = "" if own else fuse["pre"]
    if own:
        x_in = nc.dram_tensor("x_in", [D, T], F32, kind="ExternalInput").ap()
    else:
        x_in = fuse["x_in"]
    wg = nc.dram_tensor("wg", [D, HID1], U8, kind="ExternalInput").ap()
    wu = nc.dram_tensor("wu", [D, HID1], U8, kind="ExternalInput").ap()
    wd = nc.dram_tensor("wd", [HID1, D], U8, kind="ExternalInput").ap()
    sg = nc.dram_tensor("sg", [1, HID1], F32, kind="ExternalInput").ap()
    su = nc.dram_tensor("su", [1, HID1], F32, kind="ExternalInput").ap()
    sd = nc.dram_tensor("sd", [1, D], F32, kind="ExternalInput").ap()
    if own:
        cdiv = nc.dram_tensor("cdiv", [128, 1], BF16, kind="ExternalInput").ap()
        crow = nc.dram_tensor("crow", [1, 128], F32, kind="ExternalInput").ap()
    cm128 = nc.dram_tensor("cm128", [128, 1], F32, kind="ExternalInput").ap()
    if own:
        x_out = nc.dram_tensor("x_out", [D, T], F32, kind="ExternalOutput").ap()
    else:
        x_out = fuse["x_out"]

    i_dram = nc.dram_tensor(pre + "i_dram", [HID1, T], BF16)
    wd_dram = nc.dram_tensor(pre + "wd_dram", [HID1, D], BF16)
    mix_dram = nc.dram_tensor(pre + "mix_dram", [D, T], F32)

    S = SAlloc(p, pre, fuse)
    if fuse is None:
        xbf = p.sb("xbf", [128, DC * T], BF16)
        aW = p.sb("aW", [128, 49152], U8)
    else:
        xbf = bview(fuse["P1"], 0, BF16, DC * T)
        aW = bview(fuse["P2"], 0, U8, 49152)
    aM = S.t("aM", [128, 24576], U8)
    xf = S.t("xf", [128, 2 * T], F32)
    mf2 = bview(aW, 0, F32, 2 * T)
    mbc = bview(aM, 0, F32, T)
    rbc = bview(aM, 8192, F32, T)
    sq = S.t("sq", [128, 2048], U8)
    srow = aW[0:1, 16384:16384 + 8 * T].bitcast(F32)
    rstd = aW[0:1, 32768:32768 + 4 * T].bitcast(F32)
    tmp = S.t("tmp", [128, 512], F32)
    sbS = S.t("sbS", [128, 2 * HID1 + D], BF16)
    scratch = dict(xf=xf, sq=sq, srow=srow, rstd=rstd, mbc=mbc, rbc=rbc,
                   tmp=tmp, mf2=mf2)

    C = {}
    block = p.block
    if fuse is None:
        C["ones_col_div"] = p.sb("c_div", [128, 1], BF16)
        C["ones_row_f32"] = p.sb("c_row", [1, 128], F32)
    else:
        C["ones_col_div"] = fuse["C"]["ones_col_div"]
        C["ones_row_f32"] = fuse["C"]["ones_row_f32"]
    C["m128_col"] = S.t("c_m128", [128, 1], F32)
    C["sg_row"] = aW[0:1, 0:4 * HID1].bitcast(F32)
    C["su_row"] = aW[0:1, 24576:24576 + 4 * HID1].bitcast(F32)
    C["sd_row"] = aM[0:1, 0:4 * D].bitcast(F32)

    def s_consts(sync):
        if fuse is None:
            p.dma(sync, C["ones_col_div"][:], cdiv)
            p.dma(sync, C["ones_row_f32"][:], crow)
        p.dma(sync, C["m128_col"][:], cm128)
        p.dma(sync, C["sg_row"], sg)
        p.dma(sync, C["su_row"], su)
        p.dma(sync, C["sd_row"], sd)
    block.sync(s_consts)
    ld_consts = p.ld

    def f(v):
        v.wait_ge(p.ldh, ld_consts)
        return v.engine_nop()
    p.do(block.vector, f)

    def sbS_view(which):
        if which == "g":
            return sbS[:, 0:HID1]
        if which == "u":
            return sbS[:, HID1:2 * HID1]
        return sbS[:, 2 * HID1:2 * HID1 + D]

    for which, row, n in (("g", "sg_row", HID1), ("u", "su_row", HID1), ("d", "sd_row", D)):
        for i in range(n // 512):
            def f(t, which=which, row=row, i=i):
                return t.matmul(p.pD[:, 0:512], C["ones_row_f32"][0:1, :],
                                C[row][0:1, i * 512:(i + 1) * 512], start=True, stop=True)
            p.do(block.tensor, f)

            def f(v, which=which, i=i):
                return v.tensor_copy(sbS_view(which)[:, i * 512:(i + 1) * 512], p.pD[:, 0:512])
            p.do(block.vector, f)

    emit_ln(p, x_in, xbf, C, scratch)

    cod_g = bview(aW, 0, U8, 16 * 512)
    cod_u = bview(aW, 8192, U8, 16 * 512)
    wg_bf = bview(aW, 16384, BF16, 16 * 512)
    wu_bf = bview(aW, 32768, BF16, 16 * 512)
    cod_d = bview(aM, 0, U8, 4 * 2048)
    wd_bf = bview(aM, 8192, BF16, 4 * 2048)
    gsil = S.t("gsil", [128, 1024], BF16)
    itile = S.t("itile", [128, 1024], BF16)
    it_st = []
    wd_st = []

    NHS = HID1 // 512
    for hs in range(NHS):
        ld0 = p.ld
        dep = p.step

        def s_w(sync, hs=hs, dep=dep):
            sync.wait_ge(p.fl, dep)
            p.dma(sync, cod_g.rearrange("p (c h) -> p c h", h=512),
                  wg[:, hs * 512:(hs + 1) * 512].rearrange("(c p) h -> p c h", p=128))
            p.dma(sync, cod_u.rearrange("p (c h) -> p c h", h=512),
                  wu[:, hs * 512:(hs + 1) * 512].rearrange("(c p) h -> p c h", p=128))
            p.dma(sync, cod_d.rearrange("p (c o) -> p c o", o=2048),
                  wd[hs * 512:(hs + 1) * 512, :].rearrange("(c p) o -> p c o", p=128))
        block.sync(s_w)

        def f(sc, hs=hs, ld0=ld0):
            sc.wait_ge(p.ldh, ld0 + 32)
            sc.activation(wg_bf[:], cod_g[:], AF.Copy, bias=-128.0, scale=1.0)
            return sc.activation(wu_bf[:], cod_u[:], AF.Copy, bias=-128.0, scale=1.0)
        p.do(block.scalar, f)

        def f(v, hs=hs):
            for c in range(DC):
                v.tensor_tensor(wg_bf[:, c * 512:(c + 1) * 512], wg_bf[:, c * 512:(c + 1) * 512],
                                sbS_view("g")[:, hs * 512:(hs + 1) * 512], ALU.mult)
                last = v.tensor_tensor(wu_bf[:, c * 512:(c + 1) * 512], wu_bf[:, c * 512:(c + 1) * 512],
                                       sbS_view("u")[:, hs * 512:(hs + 1) * 512], ALU.mult)
            return last
        p.do(block.vector, f)

        def f(sc, hs=hs, ld0=ld0):
            sc.wait_ge(p.ldh, ld0 + 48)
            if wd_st:
                sc.wait_ge(p.sth, wd_st[-1])
            return sc.activation(wd_bf[:], cod_d[:], AF.Copy, bias=-128.0, scale=1.0)
        p.do(block.scalar, f)

        def f(v, hs=hs):
            for c in range(4):
                last = v.tensor_tensor(wd_bf[:, c * 2048:(c + 1) * 2048],
                                       wd_bf[:, c * 2048:(c + 1) * 2048],
                                       sbS_view("d")[:, 0:2048], ALU.mult)
            return last
        p.do(block.vector, f)

        def s_wd(sync, hs=hs):
            sync.wait_ge(p.fl, p.step)
            p.dma_st(sync, wd_dram[hs * 512:(hs + 1) * 512, :].rearrange("(c p) o -> p c o", p=128),
                     wd_bf.rearrange("p (c o) -> p c o", o=2048))
        block.sync(s_wd)
        wd_st.append(p.st)

        for hc in range(4):
            for tt in range(NTT):
                def f(t, hs=hs, hc=hc, tt=tt):
                    for j in range(DC):
                        t.matmul(p.pA[:, 0:512],
                                 wg_bf[:, j * 512 + hc * 128:j * 512 + hc * 128 + 128],
                                 xbf[:, j * T + tt * 512:j * T + (tt + 1) * 512],
                                 start=(j == 0), stop=(j == DC - 1))
                    for j in range(DC):
                        mm = t.matmul(p.pB[:, 0:512],
                                      wu_bf[:, j * 512 + hc * 128:j * 512 + hc * 128 + 128],
                                      xbf[:, j * T + tt * 512:j * T + (tt + 1) * 512],
                                      start=(j == 0), stop=(j == DC - 1))
                    return mm
                p.do(block.tensor, f)

                def f(sc):
                    return sc.activation(gsil[:, 0:512], p.pA[:, 0:512], AF.Silu)
                p.do(block.scalar, f)

                k = len(it_st)
                slot = slice((k % 2) * 512, (k % 2) * 512 + 512)

                def f(v, k=k, slot=slot):
                    if k >= 2:
                        v.wait_ge(p.sth, it_st[k - 2])
                    return v.tensor_tensor(itile[:, slot], gsil[:, 0:512], p.pB[:, 0:512], ALU.mult)
                p.do(block.vector, f)

                def s_i(sync, hs=hs, hc=hc, tt=tt, slot=slot):
                    sync.wait_ge(p.fl, p.step)
                    p.dma_st(sync, i_dram[hs * 512 + hc * 128:hs * 512 + (hc + 1) * 128,
                                          tt * 512:(tt + 1) * 512], itile[:, slot])
                block.sync(s_i)
                it_st.append(p.st)

    icol = bview(aW, 0, BF16, HC1 * 512)
    wds = bview(aM, 0, BF16, HC1 * 128)
    mtile = S.t("mtile", [128, 1024], F32)
    mt_st = []
    st_wd_done = p.st
    for tt in range(NTT):
        ld0 = p.ld
        dep = p.step

        def s_i2(sync, tt=tt, dep=dep):
            sync.wait_ge(p.fl, dep)
            sync.wait_ge(p.sth, st_wd_done)
            p.dma(sync, icol.rearrange("p (c t) -> p c t", t=512),
                  i_dram[:, tt * 512:(tt + 1) * 512].rearrange("(c p) t -> p c t", p=128))
        block.sync(s_i2)

        for oc in range(DC):
            ld1 = p.ld
            dep = p.step

            def s_wd2(sync, tt=tt, oc=oc, dep=dep):
                sync.wait_ge(p.fl, dep)
                p.dma(sync, wds.rearrange("p (c o) -> p c o", o=128),
                      wd_dram[:, oc * 128:(oc + 1) * 128].rearrange("(c p) o -> p c o", p=128))
            block.sync(s_wd2)

            def f(t, tt=tt, oc=oc, ld0=ld0, ld1=ld1):
                t.wait_ge(p.ldh, ld0 + 16)
                t.wait_ge(p.ldh, ld1 + 16)
                for c in range(HC1):
                    mm = t.matmul(p.pC[:, 0:512], wds[:, c * 128:(c + 1) * 128],
                                  icol[:, c * 512:(c + 1) * 512],
                                  start=(c == 0), stop=(c == HC1 - 1))
                return mm
            p.do(block.tensor, f)

            k = len(mt_st)
            slot = slice((k % 2) * 512, (k % 2) * 512 + 512)

            def f(v, k=k, slot=slot):
                if k >= 2:
                    v.wait_ge(p.sth, mt_st[k - 2])
                return v.tensor_copy(mtile[:, slot], p.pC[:, 0:512])
            p.do(block.vector, f)

            def s_m(sync, tt=tt, oc=oc, slot=slot):
                sync.wait_ge(p.fl, p.step)
                p.dma_st(sync, mix_dram[oc * 128:(oc + 1) * 128, tt * 512:(tt + 1) * 512],
                         mtile[:, slot])
            block.sync(s_m)
            mt_st.append(p.st)

    def f(g):
        g.wait_ge(p.sth, p.st)
        return g.engine_nop()
    p.do(block.gpsimd, f)
    emit_postln_residual(p, mix_dram, x_in, x_out, C, scratch, xbf)
    emit_store_wait(p)
    return p



NC0 = MLP0 // 128
NSL0 = MLP0 // 512


def emit_decode_tile(p, v, wide, cod, bits, outW, B, n):
    v.tensor_copy(wide[:, 0:n], cod)
    v.tensor_scalar(bits[:, 0:n], wide[:, 0:n], 7, 6, ALU.bitwise_and, ALU.logical_shift_left)
    v.tensor_scalar(wide[:, n:2 * n], wide[:, 0:n], 8, 12, ALU.bitwise_and, ALU.logical_shift_left)
    v.tensor_tensor(bits[:, 0:n], bits[:, 0:n], wide[:, n:2 * n], ALU.bitwise_or)
    v.tensor_tensor(outW(0), bits[:, 0:n].bitcast(BF16), B, ALU.mult)
    v.tensor_scalar(bits[:, 0:n], wide[:, 0:n], 0x70, 2, ALU.bitwise_and, ALU.logical_shift_left)
    v.tensor_scalar(wide[:, n:2 * n], wide[:, 0:n], 0x80, 8, ALU.bitwise_and, ALU.logical_shift_left)
    v.tensor_tensor(bits[:, 0:n], bits[:, 0:n], wide[:, n:2 * n], ALU.bitwise_or)
    return v.tensor_tensor(outW(1), bits[:, 0:n].bitcast(BF16), B, ALU.mult)


def build_l0ffn(debug=False, fuse=None):
    if fuse is None:
        p = Prog().start()
        own = True
    else:
        p, own = fuse["p"], False
    nc = p.nc
    pre = "" if own else fuse["pre"]
    if own:
        x_in = nc.dram_tensor("x_in", [D, T], F32, kind="ExternalInput").ap()
    else:
        x_in = fuse["x_in"]
    cg = nc.dram_tensor("cg", [1024, MLP0], U8, kind="ExternalInput").ap()
    cu = nc.dram_tensor("cu", [1024, MLP0], U8, kind="ExternalInput").ap()
    cd = nc.dram_tensor("cd", [MLP0 // 2, D], U8, kind="ExternalInput").ap()
    scg = nc.dram_tensor("scg", [64, MLP0], F32, kind="ExternalInput").ap()
    scu = nc.dram_tensor("scu", [64, MLP0], F32, kind="ExternalInput").ap()
    scd = nc.dram_tensor("scd", [512, D], F32, kind="ExternalInput").ap()
    if own:
        cdiv = nc.dram_tensor("cdiv", [128, 1], BF16, kind="ExternalInput").ap()
        crow = nc.dram_tensor("crow", [1, 128], F32, kind="ExternalInput").ap()
    ceall = nc.dram_tensor("ceall", [64, 1024], BF16, kind="ExternalInput").ap()
    ce8 = nc.dram_tensor("ce8", [8, 128], BF16, kind="ExternalInput").ap()
    cperm = nc.dram_tensor("cperm", [128, 512], BF16, kind="ExternalInput").ap()
    if own:
        x_out = nc.dram_tensor("x_out", [D, T], F32, kind="ExternalOutput").ap()
    else:
        x_out = fuse["x_out"]
    if debug:
        dbg_mix = nc.dram_tensor("dbg_mix", [D, T], F32, kind="ExternalOutput").ap()
        dbg_kv0 = nc.dram_tensor("dbg_kv0", [2048, 132], F32, kind="ExternalOutput").ap()
        dbg_qs0 = nc.dram_tensor("dbg_qs0", [128, 16], F32, kind="ExternalOutput").ap()
        dbg_c = nc.dram_tensor("dbg_c", [128, 1024], F32, kind="ExternalOutput").ap()
        dbg_c2 = nc.dram_tensor("dbg_c2", [128, 132], F32, kind="ExternalOutput").ap()
        dbg_xp = nc.dram_tensor("dbg_xp", [D, T], BF16, kind="ExternalOutput").ap()
        dbg_wg = nc.dram_tensor("dbg_wg", [D, 512], BF16, kind="ExternalOutput").ap()
        dbg_i = nc.dram_tensor("dbg_i", [512, T], BF16, kind="ExternalOutput").ap()
        dbg_wd = nc.dram_tensor("dbg_wd", [512, D], BF16, kind="ExternalOutput").ap()

    i_dram = nc.dram_tensor(pre + "i_dram", [MLP0, T], BF16)
    wd_dram = nc.dram_tensor(pre + "wd_dram", [MLP0, D], BF16)
    mix_dram = nc.dram_tensor(pre + "mix_dram", [D, T], F32)

    S = SAlloc(p, pre, fuse)
    if fuse is None:
        xbf = p.sb("xbf", [128, DC * T], BF16)
        aW = p.sb("aW", [128, 49152], U8)
    else:
        xbf = bview(fuse["P1"], 0, BF16, DC * T)
        aW = bview(fuse["P2"], 0, U8, 49152)
    xpt = S.t("xpt", [128, 2 * T], BF16)
    xP = xbf
    aM = S.t("aM", [128, 24576], U8)
    xf = S.t("xf", [128, 2 * T], F32)
    sq = S.t("sq", [128, 4096], U8)
    tmp = S.t("tmp", [128, 512], F32)
    mf2 = bview(aW, 0, F32, 2 * T)
    mbc = bview(aM, 0, F32, T)
    rbc = bview(aM, 8192, F32, T)
    srow = aW[0:1, 16384:16384 + 8 * T].bitcast(F32)
    rstd = aW[0:1, 32768:32768 + 4 * T].bitcast(F32)
    scratch = dict(xf=xf, sq=sq, srow=srow, rstd=rstd, mbc=mbc, rbc=rbc,
                   tmp=tmp, mf2=mf2)

    C = {}
    if fuse is None:
        C["ones_col_div"] = p.sb("c_div", [128, 1], BF16)
        C["ones_row_f32"] = p.sb("c_row", [1, 128], F32)
    else:
        C["ones_col_div"] = fuse["C"]["ones_col_div"]
        C["ones_row_f32"] = fuse["C"]["ones_row_f32"]
    C["eall"] = S.t("c_eall", [64, 1024], BF16)
    C["e8"] = S.t("c_e8", [8, 128], BF16)
    C["perm"] = S.t("c_perm", [128, 512], BF16)
    block = p.block

    def s_consts(sync):
        if fuse is None:
            p.dma(sync, C["ones_col_div"][:], cdiv)
            p.dma(sync, C["ones_row_f32"][:], crow)
        p.dma(sync, C["eall"][:], ceall)
        p.dma(sync, C["e8"][:], ce8)
        p.dma(sync, C["perm"][:], cperm)
    block.sync(s_consts)
    ld_consts = p.ld

    def f(v):
        v.wait_ge(p.ldh, ld_consts)
        return v.engine_nop()
    p.do(block.vector, f)

    emit_ln(p, x_in, xbf, C, scratch)

    for jc in range(8):
        for s in range(2):
            for tt in range(NTT):
                def f(t, jc=jc, s=s, tt=tt):
                    t.matmul(p.pA[:, 0:512], C["perm"][:, (0 + s * 2) * 128:(1 + s * 2) * 128],
                             xbf[:, (2 * jc) * T + tt * 512:(2 * jc) * T + (tt + 1) * 512],
                             start=True, stop=False)
                    return t.matmul(p.pA[:, 0:512], C["perm"][:, (1 + s * 2) * 128:(2 + s * 2) * 128],
                                    xbf[:, (2 * jc + 1) * T + tt * 512:(2 * jc + 1) * T + (tt + 1) * 512],
                                    start=False, stop=True)
                p.do(block.tensor, f)

                def f(v, s=s, tt=tt):
                    return v.tensor_copy(xpt[:, s * T + tt * 512:s * T + (tt + 1) * 512],
                                         p.pA[:, 0:512])
                p.do(block.vector, f)

        def f(v, jc=jc):
            v.tensor_copy(xbf[:, (2 * jc) * T:(2 * jc + 1) * T], xpt[:, 0:T])
            return v.tensor_copy(xbf[:, (2 * jc + 1) * T:(2 * jc + 2) * T], xpt[:, T:2 * T])
        p.do(block.vector, f)

    codg = bview(aW, 0, U8, 8 * 512)
    codu = bview(aW, 4096, U8, 8 * 512)
    wgd = bview(aW, 8192, BF16, 16 * 512)
    wud = bview(aW, 24576, BF16, 16 * 512)
    wide = bview(aW, 40960, U16, 1024)
    bits = bview(aW, 43008, U16, 1024)
    bsl = bview(aW, 45056, BF16, 1024)
    sgt = aM[0:64, 0:2048].bitcast(F32)
    sut = aM[0:64, 2048:4096].bitcast(F32)
    sgb = aM[0:64, 4096:5120].bitcast(BF16)
    sub = aM[0:64, 5120:6144].bitcast(BF16)
    itile = S.t("itile", [128, 1024], BF16)
    gsil = S.t("gsil", [128, 1024], BF16)
    it_st = []

    for hs in range(NSL0):
        ld0 = p.ld
        dep = p.step

        def s_w(sync, hs=hs, dep=dep):
            sync.wait_ge(p.fl, dep)
            p.dma(sync, codg.rearrange("p (c n) -> p c n", n=512),
                  cg[:, hs * 512:(hs + 1) * 512].rearrange("(c p) n -> p c n", p=128))
            p.dma(sync, codu.rearrange("p (c n) -> p c n", n=512),
                  cu[:, hs * 512:(hs + 1) * 512].rearrange("(c p) n -> p c n", p=128))
            p.dma(sync, sgt, scg[:, hs * 512:(hs + 1) * 512])
            p.dma(sync, sut, scu[:, hs * 512:(hs + 1) * 512])
        block.sync(s_w)

        def f(v, ld0=ld0):
            v.wait_ge(p.ldh, ld0 + 64)
            v.tensor_copy(sgb[:, 0:512], sgt)
            return v.tensor_copy(sub[:, 0:512], sut)
        p.do(block.vector, f)

        for jc in range(8):
            def f(t, jc=jc):
                t.matmul(p.pD[:, 0:512], C["eall"][:, jc * 128:(jc + 1) * 128],
                         sgb[:, 0:512], start=True, stop=True)
                return t.matmul(p.pD[:, 512:1024], C["eall"][:, jc * 128:(jc + 1) * 128],
                                sub[:, 0:512], start=True, stop=True)
            p.do(block.tensor, f)

            def f(v):
                return v.tensor_copy(bsl[:, 0:1024], p.pD[:, 0:1024])
            p.do(block.vector, f)

            def f(v, jc=jc):
                emit_decode_tile(p, v, wide, codg[:, jc * 512:(jc + 1) * 512], bits,
                                 lambda s, jc=jc: wgd[:, (2 * jc + s) * 512:(2 * jc + s + 1) * 512],
                                 bsl[:, 0:512], 512)
                return emit_decode_tile(p, v, wide, codu[:, jc * 512:(jc + 1) * 512], bits,
                                        lambda s, jc=jc: wud[:, (2 * jc + s) * 512:(2 * jc + s + 1) * 512],
                                        bsl[:, 512:1024], 512)
            p.do(block.vector, f)

        if debug and hs == 0:
            def s_dw(sync):
                sync.wait_ge(p.fl, p.step)
                for pc2 in range(16):
                    p.dma_st(sync, dbg_wg[pc2 * 128:(pc2 + 1) * 128, :],
                             wgd[:, pc2 * 512:(pc2 + 1) * 512])
            block.sync(s_dw)

        for w in range(2):
            for s in range(2):
                for tp in range(2):
                    def f(t, hs=hs, w=w, s=s, tp=tp):
                        for half, (pg, pu) in enumerate(((p.pA, p.pB), (p.pC, p.pE))):
                            tt = tp * 2 + half
                            for pc in range(16):
                                t.matmul(pg[:, 0:512],
                                         wgd[:, pc * 512 + w * 256 + s: pc * 512 + (w + 1) * 256: 2],
                                         xP[:, pc * T + tt * 512:pc * T + (tt + 1) * 512],
                                         start=(pc == 0), stop=(pc == 15))
                            for pc in range(16):
                                mm = t.matmul(pu[:, 0:512],
                                              wud[:, pc * 512 + w * 256 + s: pc * 512 + (w + 1) * 256: 2],
                                              xP[:, pc * T + tt * 512:pc * T + (tt + 1) * 512],
                                              start=(pc == 0), stop=(pc == 15))
                        return mm
                    p.do(block.tensor, f)

                    def f(sc):
                        sc.activation(gsil[:, 0:512], p.pA[:, 0:512], AF.Gelu_apprx_tanh)
                        return sc.activation(gsil[:, 512:1024], p.pC[:, 0:512], AF.Gelu_apprx_tanh)
                    p.do(block.scalar, f)

                    k = len(it_st)

                    def f(v, k=k):
                        if k >= 2:
                            v.wait_ge(p.sth, it_st[k - 1])
                        v.tensor_tensor(itile[:, 0:512], gsil[:, 0:512], p.pB[:, 0:512], ALU.mult)
                        return v.tensor_tensor(itile[:, 512:1024], gsil[:, 512:1024], p.pE[:, 0:512], ALU.mult)
                    p.do(block.vector, f)

                    rbase = 128 * (4 * hs + 2 * w + s)

                    def s_i(sync, rbase=rbase, tp=tp):
                        sync.wait_ge(p.fl, p.step)
                        p.dma_st(sync, i_dram[rbase:rbase + 128, tp * 1024:tp * 1024 + 512],
                                 itile[:, 0:512])
                        p.dma_st(sync, i_dram[rbase:rbase + 128, tp * 1024 + 512:(tp + 1) * 1024],
                                 itile[:, 512:1024])
                    block.sync(s_i)
                    it_st.append(p.st)

    codd = bview(aW, 0, U8, 4 * 2048)
    widd = bview(aW, 16384, U16, 2048)
    bitd = bview(aW, 20480, U16, 2048)
    wdd = bview(aW, 24576, BF16, 2048)
    sdt = aM[0:8, 0:8192].bitcast(F32)
    sdb = aM[0:8, 8192:12288].bitcast(BF16)
    bdt = bview(aM, 16384, BF16, 2048)
    wd_st2 = []
    for ds in range(16):
        ld0 = p.ld
        dep = p.step

        def s_w(sync, ds=ds, dep=dep):
            sync.wait_ge(p.fl, dep)
            p.dma(sync, codd.rearrange("p (c o) -> p c o", o=2048),
                  cd[ds * 512:(ds + 1) * 512, :].rearrange("(c p) o -> p c o", p=128))
        block.sync(s_w)

        for jq in range(4):
            jcp = ds * 4 + jq
            def s_s(sync, jcp=jcp):
                sync.wait_ge(p.fl, p.step)
                p.dma(sync, sdt, scd[jcp * 8:(jcp + 1) * 8, :])
            block.sync(s_s)
            ld1 = p.ld

            def f(v, ld1=ld1):
                v.wait_ge(p.ldh, ld1)
                return v.tensor_copy(sdb[:, 0:2048], sdt)
            p.do(block.vector, f)

            def f(t):
                for q in range(4):
                    mm = t.matmul(p.pD[:, 0:512] if q == 0 else [p.pD[:, 512:1024], p.pE[:, 0:512], p.pF[:, 0:512]][q - 1],
                                  C["e8"][:, 0:128], sdb[:, q * 512:(q + 1) * 512],
                                  start=True, stop=True)
                return mm
            p.do(block.tensor, f)

            def f(v):
                v.tensor_copy(bdt[:, 0:512], p.pD[:, 0:512])
                v.tensor_copy(bdt[:, 512:1024], p.pD[:, 512:1024])
                v.tensor_copy(bdt[:, 1024:1536], p.pE[:, 0:512])
                return v.tensor_copy(bdt[:, 1536:2048], p.pF[:, 0:512])
            p.do(block.vector, f)

            for s in range(2):
                k2 = len(wd_st2)

                def f(v, jq=jq, s=s, k2=k2, ld0=ld0):
                    v.wait_ge(p.ldh, ld0 + 16)
                    if k2 >= 1:
                        v.wait_ge(p.sth, wd_st2[k2 - 1])
                    cod = codd[:, jq * 2048:(jq + 1) * 2048]
                    v.tensor_copy(widd[:, 0:2048], cod)
                    if s == 0:
                        v.tensor_scalar(bitd[:, 0:2048], widd[:, 0:2048], 7, 6, ALU.bitwise_and, ALU.logical_shift_left)
                        v.tensor_scalar(widd[:, 0:2048], widd[:, 0:2048], 8, 12, ALU.bitwise_and, ALU.logical_shift_left)
                    else:
                        v.tensor_scalar(bitd[:, 0:2048], widd[:, 0:2048], 0x70, 2, ALU.bitwise_and, ALU.logical_shift_left)
                        v.tensor_scalar(widd[:, 0:2048], widd[:, 0:2048], 0x80, 8, ALU.bitwise_and, ALU.logical_shift_left)
                    v.tensor_tensor(bitd[:, 0:2048], bitd[:, 0:2048], widd[:, 0:2048], ALU.bitwise_or)
                    return v.tensor_tensor(wdd[:, 0:2048], bitd[:, 0:2048].bitcast(BF16), bdt[:, 0:2048], ALU.mult)
                p.do(block.vector, f)

                rbase = 128 * (2 * jcp + s)

                def s_wd(sync, rbase=rbase):
                    sync.wait_ge(p.fl, p.step)
                    p.dma_st(sync, wd_dram[rbase:rbase + 128, :], wdd[:, 0:2048])
                block.sync(s_wd)
                wd_st2.append(p.st)

    if debug:
        def s_dbg(sync):
            sync.wait_ge(p.fl, p.step)
            for j in range(DC):
                p.dma_st(sync, dbg_xp[j * 128:(j + 1) * 128, :], xbf[:, j * T:(j + 1) * T])
        block.sync(s_dbg)

        def s_dbg2(sync):
            p.dma_st(sync, dbg_i[:, :], i_dram[0:512, :])
            p.dma_st(sync, dbg_wd[:, :], wd_dram[0:512, :])
        block.sync(s_dbg2)

    ibuf = xbf
    wcol = bview(aW, 0, BF16, 16 * 128)
    mtile = S.t("mtile", [128, 1024], F32)
    mt_st = []
    acc_prev = {}
    st_before = p.st
    for g in range(8):
        ld0 = p.ld
        dep = p.step

        def s_i2(sync, g=g, dep=dep, st_before=st_before):
            sync.wait_ge(p.fl, dep)
            sync.wait_ge(p.sth, st_before)
            p.dma(sync, ibuf[:].rearrange("p (c t) -> p c t", t=T),
                  i_dram[g * 2048:(g + 1) * 2048, :].rearrange("(c p) t -> p c t", p=128))
        block.sync(s_i2)

        for oc in range(DC):
            ld1 = p.ld
            dep = p.step

            def s_wc(sync, g=g, oc=oc, dep=dep):
                sync.wait_ge(p.fl, dep)
                p.dma(sync, wcol.rearrange("p (c o) -> p c o", o=128),
                      wd_dram[g * 2048:(g + 1) * 2048, oc * 128:(oc + 1) * 128]
                      .rearrange("(c p) o -> p c o", p=128))
            block.sync(s_wc)

            for tt in range(NTT):
                def f(t, g=g, oc=oc, tt=tt, ld0=ld0, ld1=ld1):
                    t.wait_ge(p.ldh, ld0 + 16)
                    t.wait_ge(p.ldh, ld1 + 16)
                    for c in range(16):
                        mm = t.matmul(p.pC[:, 0:512], wcol[:, c * 128:(c + 1) * 128],
                                      ibuf[:, c * T + tt * 512:c * T + (tt + 1) * 512],
                                      start=(c == 0), stop=(c == 15))
                    return mm
                p.do(block.tensor, f)

                k = len(mt_st)
                slot = slice((k % 2) * 512, (k % 2) * 512 + 512)

                def f(v, k=k, slot=slot):
                    if k >= 2:
                        v.wait_ge(p.sth, mt_st[k - 2])
                    return v.tensor_copy(mtile[:, slot], p.pC[:, 0:512])
                p.do(block.vector, f)

                def s_m(gq, g=g, oc=oc, tt=tt, slot=slot):
                    gq.wait_ge(p.fl, p.step)
                    if g > 0:
                        gq.wait_ge(p.sth, acc_prev[(oc, tt)])
                    p.dma_st(gq, mix_dram[oc * 128:(oc + 1) * 128, tt * 512:(tt + 1) * 512],
                             mtile[:, slot], accum=(g > 0))
                block.gpsimd(s_m)
                mt_st.append(p.st)
                acc_prev[(oc, tt)] = p.st

    def f(g):
        g.wait_ge(p.sth, p.st)
        return g.engine_nop()
    p.do(block.gpsimd, f)
    if debug:
        def s_dm2(sync):
            p.dma_st(sync, dbg_mix[:, :], mix_dram[:, :])
        block.sync(s_dm2)
    emit_postln_residual(p, mix_dram, x_in, x_out, C, scratch, xP)
    emit_store_wait(p)
    return p


def prep_l0ffn_inputs(full):
    def codesT(w):
        return np.ascontiguousarray(_u8codes(np.asarray(w)).T)
    perm = np.zeros((128, 512), BF16NP)
    for s in range(2):
        for i in range(64):
            perm[2 * i + s, s * 256 + i] = 1
            perm[2 * i + s, s * 256 + 128 + 64 + i] = 1
    eall = np.zeros((64, 1024), BF16NP)
    for jc in range(8):
        for i in range(128):
            eall[(128 * jc + i) // 16, jc * 128 + i] = BF16NP(TWO126)
    e8 = np.zeros((8, 128), BF16NP)
    for i in range(128):
        e8[i // 16, i] = BF16NP(TWO126)
    shared = {
        "cg": codesT(full["L0_ffn_gate_W_packed"]),
        "cu": codesT(full["L0_ffn_up_W_packed"]),
        "cd": codesT(full["L0_ffn_down_W_packed"]),
        "scg": np.ascontiguousarray(np.asarray(full["L0_ffn_gate_scales"], np.float32).T),
        "scu": np.ascontiguousarray(np.asarray(full["L0_ffn_up_scales"], np.float32).T),
        "scd": np.ascontiguousarray(np.asarray(full["L0_ffn_down_scales"], np.float32).T),
        "cdiv": np.full((128, 1), 1.0 / 2048.0, BF16NP),
        "crow": np.ones((1, 128), np.float32),
        "ceall": eall, "ce8": e8, "cperm": perm,
    }
    return shared


def run_l0ffn(x_flat, full, debug=False):
    shared = prep_l0ffn_inputs(full)
    shards = shard_fm(x_flat)
    p = build_l0ffn(debug)
    in_maps = [dict(shared, x_in=s) for s in shards]
    res = run_bass_kernel_spmd(p.nc, in_maps, core_ids=list(range(8)))
    p.close()
    if debug:
        return res
    return unshard_fm([res.results[c]["x_out"] for c in range(8)])



def build_l0attn(debug=False, fuse=None):
    if fuse is None:
        p = Prog().start()
        own = True
    else:
        p, own = fuse["p"], False
    nc = p.nc
    pre = "" if own else fuse["pre"]
    if own:
        x_in = nc.dram_tensor("x_in", [D, T], F32, kind="ExternalInput").ap()
    else:
        x_in = fuse["x_in"]
    wqkvT = nc.dram_tensor("wqkvT", [D, 3 * D], F32, kind="ExternalInput").ap()
    woutT = nc.dram_tensor("woutT", [D, D], F32, kind="ExternalInput").ap()
    cbias = nc.dram_tensor("cbias", [128, 32], F32, kind="ExternalInput").ap()
    cbrow = nc.dram_tensor("cbrow", [1, D], F32, kind="ExternalInput").ap()
    cbout = nc.dram_tensor("cbout", [128, 16], F32, kind="ExternalInput").ap()
    if own:
        cdiv = nc.dram_tensor("cdiv", [128, 1], BF16, kind="ExternalInput").ap()
        crow = nc.dram_tensor("crow", [1, 128], F32, kind="ExternalInput").ap()
        cid = nc.dram_tensor("cid", [128, 128], BF16, kind="ExternalInput").ap()
    if own:
        x_out = nc.dram_tensor("x_out", [D, T], F32, kind="ExternalOutput").ap()
    else:
        x_out = fuse["x_out"]
    if debug:
        dbg_q = nc.dram_tensor("dbg_q", [D, T], BF16, kind="ExternalOutput").ap()
        dbg_s = nc.dram_tensor("dbg_s", [T, 1024], F32, kind="ExternalOutput").ap()
        dbg_o = nc.dram_tensor("dbg_o", [T, D], F32, kind="ExternalOutput").ap()
        dbg_mx = nc.dram_tensor("dbg_mx", [128, 48], F32, kind="ExternalOutput").ap()
        dbg_mix = nc.dram_tensor("dbg_mix", [D, T], F32, kind="ExternalOutput").ap()
        dbg_kv0 = nc.dram_tensor("dbg_kv0", [2048, 132], F32, kind="ExternalOutput").ap()
        dbg_qs0 = nc.dram_tensor("dbg_qs0", [128, 16], F32, kind="ExternalOutput").ap()
        dbg_c = nc.dram_tensor("dbg_c", [128, 1024], F32, kind="ExternalOutput").ap()
        dbg_c2 = nc.dram_tensor("dbg_c2", [128, 132], F32, kind="ExternalOutput").ap()
        dbg_kg = nc.dram_tensor("dbg_kg", [512, T], BF16, kind="ExternalOutput").ap()

    kdram = nc.dram_tensor(pre + "kdram", [D, T], BF16)
    vdram = nc.dram_tensor(pre + "vdram", [T, D], BF16)
    kg = nc.dram_tensor(pre + "kg", [4 * D, T], BF16)
    vg = nc.dram_tensor(pre + "vg", [4 * T, D], BF16)
    sdram = nc.dram_tensor(pre + "sdram", [T, 4 * T], F32)
    odram = nc.dram_tensor(pre + "odram", [T, D], F32)
    mix_dram = nc.dram_tensor(pre + "mix_dram", [D, T], F32)

    S = SAlloc(p, pre, fuse)
    if fuse is None:
        xbf = p.sb("xbf", [128, DC * T], BF16)
        qfm = p.sb("qfm", [128, DC * T], BF16)
    else:
        xbf = bview(fuse["P1"], 0, BF16, DC * T)
        qfm = bview(fuse["P2"], 0, BF16, DC * T)
    aW = S.t("aW", [128, 24576], U8)
    aM = S.t("aM", [128, 16384], U8)
    xf = S.t("xf", [128, 2 * T], F32)
    sq = S.t("sq", [128, 4096], U8)
    tmp = S.t("tmp", [128, 512], F32)
    mx = S.t("mx", [128, 16], F32)
    lsum = S.t("lsum", [128, 16], F32)
    rs1 = S.t("rs1", [128, 1], F32)
    bias_sb = S.t("bias_sb", [128, 32], F32)
    bout_sb = S.t("bout_sb", [128, 16], F32)

    mbc = bview(aM, 0, F32, T)
    rbc = bview(aM, 8192, F32, T)
    vbias = bview(aM, 0, F32, 2048)
    srow = qfm[0:1, 0:8192].bitcast(F32)
    rstd = qfm[0:1, 8192:12288].bitcast(F32)
    mf2 = bview(qfm, 32768, F32, 2 * T)
    scratch = dict(xf=xf, sq=sq, srow=srow, rstd=rstd, mbc=mbc, rbc=rbc,
                   tmp=tmp, mf2=mf2)

    C = {}
    if fuse is None:
        C["ones_col_div"] = p.sb("c_div", [128, 1], BF16)
        C["ones_row_f32"] = p.sb("c_row", [1, 128], F32)
        C["id128"] = p.sb("c_id", [128, 128], BF16)
    else:
        C["ones_col_div"] = fuse["C"]["ones_col_div"]
        C["ones_row_f32"] = fuse["C"]["ones_row_f32"]
        C["id128"] = fuse["C"]["id128"]
    block = p.block

    def s_consts(sync):
        if fuse is None:
            p.dma(sync, C["ones_col_div"][:], cdiv)
            p.dma(sync, C["ones_row_f32"][:], crow)
            p.dma(sync, C["id128"][:], cid)
        p.dma(sync, bias_sb[:], cbias)
        p.dma(sync, bout_sb[:], cbout)
    block.sync(s_consts)
    ld_consts = p.ld

    def f(v):
        v.wait_ge(p.ldh, ld_consts)
        return v.engine_nop()
    p.do(block.vector, f)

    emit_ln(p, x_in, xbf, C, scratch)

    vbrow = aW[0:1, 0:8192].bitcast(F32)

    def s_vb(sync):
        p.dma(sync, vbrow, cbrow)
    block.sync(s_vb)
    ld_vb = p.ld

    def f(t):
        t.wait_ge(p.ldh, ld_vb)
        for et in range(4):
            mm = t.matmul(p.pD[:, 0:512], C["ones_row_f32"][0:1, :],
                          vbrow[0:1, et * 512:(et + 1) * 512], start=True, stop=True)
        return mm
    for et in range(4):
        def f(t, et=et):
            if et == 0:
                t.wait_ge(p.ldh, ld_vb)
            return t.matmul(p.pD[:, 0:512], C["ones_row_f32"][0:1, :],
                            vbrow[0:1, et * 512:(et + 1) * 512], start=True, stop=True)
        p.do(block.tensor, f)

        def f(v, et=et):
            return v.tensor_copy(vbias[:, et * 512:(et + 1) * 512], p.pD[:, 0:512])
        p.do(block.vector, f)

    wsl = bview(aW, 8192, BF16, 16 * 512)
    ktile = S.t("ktile", [128, 1024], BF16)
    vtile = S.t("vtile", [128, 1024], BF16)
    k_st = []
    v_st = []

    for ec in range(32):
        dep = p.step

        def s_w(gq, ec=ec, dep=dep):
            gq.wait_ge(p.fl, dep)
            p.dma(gq, wsl[:, 0:16 * 128].rearrange("p (c o) -> p c o", o=128),
                  wqkvT[:, ec * 128:(ec + 1) * 128].rearrange("(c p) o -> p c o", p=128))
        block.gpsimd(s_w)
        ld0 = p.ld

        for tt in range(NTT):
            def f(t, ec=ec, tt=tt, ld0=ld0):
                t.wait_ge(p.ldh, ld0)
                for j in range(DC):
                    mm = t.matmul(p.pA[:, 0:512], wsl[:, j * 128:(j + 1) * 128],
                                  xbf[:, j * T + tt * 512:j * T + (tt + 1) * 512],
                                  start=(j == 0), stop=(j == DC - 1))
                return mm
            p.do(block.tensor, f)

            if ec < 16:
                def f(v, ec=ec, tt=tt):
                    return v.tensor_scalar(qfm[:, ec * T + tt * 512:ec * T + (tt + 1) * 512],
                                           p.pA[:, 0:512], bias_sb[:, ec:ec + 1], None, ALU.add)
                p.do(block.vector, f)
            else:
                kk = len(k_st)
                slot = slice((kk % 2) * 512, (kk % 2) * 512 + 512)

                def f(v, ec=ec, tt=tt, kk=kk, slot=slot):
                    if kk >= 2:
                        v.wait_ge(p.sth, k_st[kk - 2])
                    return v.tensor_scalar(ktile[:, slot], p.pA[:, 0:512],
                                           bias_sb[:, ec:ec + 1], None, ALU.add)
                p.do(block.vector, f)

                def s_k(sync, ec=ec, tt=tt, slot=slot):
                    sync.wait_ge(p.fl, p.step)
                    p.dma_st(sync, kdram[(ec - 16) * 128:(ec - 15) * 128, tt * 512:(tt + 1) * 512],
                             ktile[:, slot])
                block.sync(s_k)
                k_st.append(p.st)

    vsl = bview(aW, 8192, BF16, 16 * 512)
    for et in range(4):
        dep = p.step

        def s_w(gq, et=et, dep=dep):
            gq.wait_ge(p.fl, dep)
            p.dma(gq, vsl[:].rearrange("p (c o) -> p c o", o=512),
                  wqkvT[:, 2 * D + et * 512:2 * D + (et + 1) * 512].rearrange("(c p) o -> p c o", p=128))
        block.gpsimd(s_w)
        ld0 = p.ld

        for tc in range(DC):
            def f(t, et=et, tc=tc, ld0=ld0):
                t.wait_ge(p.ldh, ld0)
                for j in range(DC):
                    mm = t.matmul(p.pA[:, 0:512], xbf[:, j * T + tc * 128:j * T + tc * 128 + 128],
                                  vsl[:, j * 512:(j + 1) * 512],
                                  start=(j == 0), stop=(j == DC - 1))
                return mm
            p.do(block.tensor, f)

            kk = len(v_st)
            slot = slice((kk % 2) * 512, (kk % 2) * 512 + 512)

            def f(v, et=et, kk=kk, slot=slot):
                if kk >= 2:
                    v.wait_ge(p.sth, v_st[kk - 2])
                return v.tensor_tensor(vtile[:, slot], p.pA[:, 0:512],
                                       vbias[:, et * 512:(et + 1) * 512], ALU.add)
            p.do(block.vector, f)

            def s_v(sync, et=et, tc=tc, slot=slot):
                sync.wait_ge(p.fl, p.step)
                p.dma_st(sync, vdram[tc * 128:(tc + 1) * 128, et * 512:(et + 1) * 512],
                         vtile[:, slot])
            block.sync(s_v)
            v_st.append(p.st)

    st_kv = p.st

    def s_cc(gq):
        gq.wait_ge(p.fl, p.step)
        gq.wait_ge(p.sth, st_kv)
        gq.collective_compute("AllGather", ALU.bypass, replica_groups=RG4,
                              ins=[kdram.ap().opt()], outs=[kg.ap().opt()]).then_inc(p.cch)
        gq.collective_compute("AllGather", ALU.bypass, replica_groups=RG4,
                              ins=[vdram.ap().opt()], outs=[vg.ap().opt()]).then_inc(p.cch)
    block.gpsimd(s_cc)
    p.cc += 2
    cc_after = p.cc

    kslab = bview(xbf, 0, BF16, 16 * 512)
    kslab2 = bview(xbf, 16384, BF16, 16 * 512)
    stile = bview(xbf, 32768, F32, 1024)
    s_st = []

    def f(v):
        return v.memset(mx[:], -3.0e38)
    p.do(block.vector, f)

    for kt in range(16):
        dep = p.step
        ksl = [kslab, kslab2][kt % 2]

        def s_ks(sync, kt=kt, dep=dep, ksl=ksl):
            sync.wait_ge(p.fl, max(0, dep - 2))
            if kt == 0:
                sync.wait_ge(p.cch, cc_after)
            r, tl = kt // 4, kt % 4
            p.dma(sync, ksl[:].rearrange("p (c t) -> p c t", t=512),
                  kg[r * D:(r + 1) * D, tl * 512:(tl + 1) * 512].rearrange("(c p) t -> p c t", p=128))
        block.sync(s_ks)
        ld0 = p.ld

        for qc in range(16):
            def f(t, kt=kt, qc=qc, ld0=ld0, ksl=ksl):
                t.wait_ge(p.ldh, ld0)
                for ec in range(DC):
                    mm = t.matmul(p.pA[:, 0:512], qfm[:, ec * T + qc * 128:ec * T + qc * 128 + 128],
                                  ksl[:, ec * 512:(ec + 1) * 512],
                                  start=(ec == 0), stop=(ec == DC - 1))
                return mm
            p.do(block.tensor, f)

            kk = len(s_st)
            slot = slice((kk % 2) * 512, (kk % 2) * 512 + 512)

            def f(v, kk=kk, slot=slot, qc=qc):
                if kk >= 2:
                    v.wait_ge(p.sth, s_st[kk - 2])
                v.tensor_copy(stile[:, slot], p.pA[:, 0:512])
                return v.tensor_reduce(rs1[:], p.pA[:, 0:512], AX.X, ALU.max)
            p.do(block.vector, f)

            def f(v, qc=qc):
                return v.tensor_tensor(mx[:, qc:qc + 1], mx[:, qc:qc + 1], rs1[:], ALU.max)
            p.do(block.vector, f)

            def s_s(sync, kt=kt, qc=qc, slot=slot):
                sync.wait_ge(p.fl, p.step)
                p.dma_st(sync, sdram[qc * 128:(qc + 1) * 128, kt * 512:(kt + 1) * 512],
                         stile[:, slot])
            block.sync(s_s)
            s_st.append(p.st)

    SCALE0 = float(1.0 / np.sqrt(2048.0))
    bmax = S.t("bmax", [128, 16], F32)
    st_ph3 = p.st

    def f(v):
        v.wait_ge(p.sth, st_ph3)
        v.memset(lsum[:], 0.0)
        return v.tensor_scalar(bmax[:], mx[:], float(-SCALE0), None, ALU.mult)
    p.do(block.vector, f)

    vslab = bview(xbf, 0, BF16, 16 * T)
    sf = bview(aW, 0, F32, 1024)
    ptb = bview(aW, 8192, BF16, 16 * 128)
    ptile = bview(aW, 12288, BF16, 512)
    otile = bview(aW, 16384, F32, 1024)
    o_acc_prev = {}
    o_st = []
    sload = []

    for grp in range(4):
        dep = p.step

        def s_vs(sync, grp=grp, dep=dep):
            sync.wait_ge(p.fl, dep)
            p.dma(sync, vslab[:].rearrange("p (c e) -> p c e", e=D),
                  vg[grp * T:(grp + 1) * T, :].rearrange("(c p) e -> p c e", p=128))
        block.sync(s_vs)
        ld_v = p.ld

        for qc in range(16):
            for k4 in range(4):
                kt = grp * 4 + k4
                dep = p.step

                def s_sf(sync, kt=kt, qc=qc, dep=dep):
                    sync.wait_ge(p.fl, max(0, dep - 1))
                    p.dma(sync, sf[:, (kt % 2) * 512:(kt % 2) * 512 + 512],
                          sdram[qc * 128:(qc + 1) * 128, kt * 512:(kt + 1) * 512])
                block.sync(s_sf)
                ld0 = p.ld

                def f(sc, kt=kt, qc=qc, ld0=ld0):
                    sc.wait_ge(p.ldh, ld0)
                    return sc.activation(ptile[:, 0:512], sf[:, (kt % 2) * 512:(kt % 2) * 512 + 512],
                                         AF.Exp, bias=bmax[:, qc:qc + 1], scale=np.float32(SCALE0),
                                         accum_out=rs1[:])
                p.do(block.scalar, f)

                def f(v, qc=qc):
                    return v.tensor_tensor(lsum[:, qc:qc + 1], lsum[:, qc:qc + 1], rs1[:], ALU.add)
                p.do(block.vector, f)

                def f(t):
                    for i in range(4):
                        last = t.transpose(p.pT[:, i * 128:(i + 1) * 128],
                                           ptile[:, i * 128:(i + 1) * 128], C["id128"][:])
                    return last
                p.do(block.tensor, f)

                def f(v, k4=k4):
                    for i in range(4):
                        last = v.tensor_copy(ptb[:, (k4 * 4 + i) * 128:(k4 * 4 + i + 1) * 128],
                                             p.pT[:, i * 128:(i + 1) * 128])
                    return last
                p.do(block.vector, f)

            for et in range(4):
                def f(t, et=et, ld_v=ld_v):
                    t.wait_ge(p.ldh, ld_v)
                    for c in range(16):
                        mm = t.matmul(p.pB[:, 0:512], ptb[:, c * 128:(c + 1) * 128],
                                      vslab[:, c * D + et * 512:c * D + (et + 1) * 512],
                                      start=(c == 0), stop=(c == 15))
                    return mm
                p.do(block.tensor, f)

                kk = len(o_st)
                slot = slice((kk % 2) * 512, (kk % 2) * 512 + 512)

                def f(v, kk=kk, slot=slot):
                    if kk >= 2:
                        v.wait_ge(p.sth, o_st[kk - 2])
                    return v.tensor_copy(otile[:, slot], p.pB[:, 0:512])
                p.do(block.vector, f)

                def s_o(gq, grp=grp, qc=qc, et=et, slot=slot):
                    gq.wait_ge(p.fl, p.step)
                    if grp > 0:
                        gq.wait_ge(p.sth, o_acc_prev[(qc, et)])
                    p.dma_st(gq, odram[qc * 128:(qc + 1) * 128, et * 512:(et + 1) * 512],
                             otile[:, slot], accum=(grp > 0))
                block.gpsimd(s_o)
                o_st.append(p.st)
                o_acc_prev[(qc, et)] = p.st

    if debug:
        def s_d1(sync):
            sync.wait_ge(p.fl, p.step)
            sync.wait_ge(p.sth, p.st)
            for ec2 in range(DC):
                p.dma_st(sync, dbg_q[ec2 * 128:(ec2 + 1) * 128, :], qfm[:, ec2 * T:(ec2 + 1) * T])
            p.dma_st(sync, dbg_s[:, :], sdram[:, 0:1024])
            p.dma_st(sync, dbg_o[:, :], odram[:, :])
            p.dma_st(sync, dbg_mx[:, 0:16], mx[:])
            p.dma_st(sync, dbg_mx[:, 16:32], lsum[:])
            p.dma_st(sync, dbg_mx[:, 32:48], bmax[:])
            p.dma_st(sync, dbg_kg[:, :], kg[0:512, :])
        block.sync(s_d1)

    linv = S.t("linv", [128, 16], F32)

    def f(v):
        return v.reciprocal(linv[:], lsum[:])
    p.do(block.vector, f)

    ctxT = bview(xbf, 0, BF16, DC * T)
    obuf = bview(aW, 0, F32, 2 * T)
    otm = bview(aW, 16384, BF16, T)
    st_o_done = p.st
    for qc in range(16):
        dep = p.step

        def s_ob(sync, qc=qc, dep=dep):
            sync.wait_ge(p.fl, max(0, dep - 1))
            if qc == 0:
                sync.wait_ge(p.sth, st_o_done)
            p.dma(sync, obuf[:, (qc % 2) * T:(qc % 2 + 1) * T], odram[qc * 128:(qc + 1) * 128, :])
        block.sync(s_ob)
        ld0 = p.ld

        def f(v, qc=qc, ld0=ld0):
            v.wait_ge(p.ldh, ld0)
            return v.tensor_scalar(otm[:, 0:T], obuf[:, (qc % 2) * T:(qc % 2 + 1) * T],
                                   linv[:, qc:qc + 1], None, ALU.mult)
        p.do(block.vector, f)

        for eg in range(2):
            def f(t, eg=eg):
                for i in range(8):
                    last = t.transpose(p.pT[:, i * 128:(i + 1) * 128],
                                       otm[:, (eg * 8 + i) * 128:(eg * 8 + i + 1) * 128],
                                       C["id128"][:])
                return last
            p.do(block.tensor, f)

            def f(v, qc=qc, eg=eg):
                for i in range(8):
                    ec = eg * 8 + i
                    last = v.tensor_copy(ctxT[:, ec * T + qc * 128:ec * T + qc * 128 + 128],
                                         p.pT[:, i * 128:(i + 1) * 128])
                return last
            p.do(block.vector, f)

    wosl = bview(aW, 0, BF16, 16 * 128)
    mtile = S.t("mtile", [128, 1024], F32)
    mt_st = []
    for oc in range(16):
        dep = p.step

        def s_w(gq, oc=oc, dep=dep):
            gq.wait_ge(p.fl, dep)
            p.dma(gq, wosl[:].rearrange("p (c o) -> p c o", o=128),
                  woutT[:, oc * 128:(oc + 1) * 128].rearrange("(c p) o -> p c o", p=128))
        block.gpsimd(s_w)
        ld0 = p.ld

        for tt in range(NTT):
            def f(t, oc=oc, tt=tt, ld0=ld0):
                t.wait_ge(p.ldh, ld0)
                for ec in range(DC):
                    mm = t.matmul(p.pC[:, 0:512], wosl[:, ec * 128:(ec + 1) * 128],
                                  ctxT[:, ec * T + tt * 512:ec * T + (tt + 1) * 512],
                                  start=(ec == 0), stop=(ec == DC - 1))
                return mm
            p.do(block.tensor, f)

            kk = len(mt_st)
            slot = slice((kk % 2) * 512, (kk % 2) * 512 + 512)

            def f(v, oc=oc, kk=kk, slot=slot):
                if kk >= 2:
                    v.wait_ge(p.sth, mt_st[kk - 2])
                return v.tensor_scalar(mtile[:, slot], p.pC[:, 0:512],
                                       bout_sb[:, oc:oc + 1], None, ALU.add)
            p.do(block.vector, f)

            def s_m(sync, oc=oc, tt=tt, slot=slot):
                sync.wait_ge(p.fl, p.step)
                p.dma_st(sync, mix_dram[oc * 128:(oc + 1) * 128, tt * 512:(tt + 1) * 512],
                         mtile[:, slot])
            block.sync(s_m)
            mt_st.append(p.st)

    def f(g):
        g.wait_ge(p.sth, p.st)
        return g.engine_nop()
    p.do(block.gpsimd, f)
    if debug:
        def s_d2(sync):
            p.dma_st(sync, dbg_mix[:, :], mix_dram[:, :])
        block.sync(s_d2)
    emit_postln_residual(p, mix_dram, x_in, x_out, C, scratch, xbf)
    emit_store_wait(p)
    return p


def prep_l0attn_inputs(full):
    wqkv = np.asarray(full["L0_attn_w_qkv"], np.float32)
    wout = np.asarray(full["L0_attn_w_out"], np.float32)
    bqkv = np.asarray(full["L0_attn_b_qkv"], np.float32)
    bout = np.asarray(full["L0_attn_b_out"], np.float32)
    shared = {
        "wqkvT": np.ascontiguousarray(wqkv.T),
        "woutT": np.ascontiguousarray(wout.T),
        "cbias": np.ascontiguousarray(bqkv[0:4096].reshape(32, 128).T),
        "cbrow": bqkv[4096:6144].reshape(1, 2048),
        "cbout": np.ascontiguousarray(bout.reshape(16, 128).T),
        "cdiv": np.full((128, 1), 1.0 / 2048.0, BF16NP),
        "crow": np.ones((1, 128), np.float32),
        "cid": np.eye(128, dtype=BF16NP),
    }
    return shared


def run_l0attn(x_flat, full, debug=False):
    shared = prep_l0attn_inputs(full)
    shards = shard_fm(x_flat)
    p = build_l0attn(debug)
    in_maps = [dict(shared, x_in=s) for s in shards]
    res = run_bass_kernel_spmd(p.nc, in_maps, core_ids=list(range(8)))
    p.close()
    if debug:
        return res
    return unshard_fm([res.results[c]["x_out"] for c in range(8)])



def build_l1attn(debug=False, fuse=None):
    if fuse is None:
        p = Prog().start()
        own = True
    else:
        p, own = fuse["p"], False
    nc = p.nc
    pre = "" if own else fuse["pre"]
    if own:
        x_in = nc.dram_tensor("x_in", [D, T], F32, kind="ExternalInput").ap()
    else:
        x_in = fuse["x_in"]
    wqT = nc.dram_tensor("wqT", [D, D], F32, kind="ExternalInput").ap()
    wkT = nc.dram_tensor("wkT", [D, D], F32, kind="ExternalInput").ap()
    wvT = nc.dram_tensor("wvT", [D, D], F32, kind="ExternalInput").ap()
    woT = nc.dram_tensor("woT", [D, D], F32, kind="ExternalInput").ap()
    memaug = nc.dram_tensor("memaug", [64 * 128, 132], F32, kind="ExternalInput").ap()
    mnormT = nc.dram_tensor("mnormT", [128, 64], F32, kind="ExternalInput").ap()
    gatein = nc.dram_tensor("gatein", [1, 16], F32, kind="ExternalInput").ap()
    maskpre = nc.dram_tensor("maskpre", [128, 4], F32, kind="ExternalInput").ap()
    ctril = nc.dram_tensor("ctril", [128, 128], BF16, kind="ExternalInput").ap()
    if own:
        cdiv = nc.dram_tensor("cdiv", [128, 1], BF16, kind="ExternalInput").ap()
        crow = nc.dram_tensor("crow", [1, 128], F32, kind="ExternalInput").ap()
        cid = nc.dram_tensor("cid", [128, 128], BF16, kind="ExternalInput").ap()
    conebf = nc.dram_tensor("conebf", [128, 1], BF16, kind="ExternalInput").ap()
    crelrow = nc.dram_tensor("crelrow", [128, 1], F32, kind="ExternalInput").ap()
    if own:
        x_out = nc.dram_tensor("x_out", [D, T], F32, kind="ExternalOutput").ap()
    else:
        x_out = fuse["x_out"]
    if debug:
        dbg_sk = nc.dram_tensor("dbg_sk", [T, D], BF16, kind="ExternalOutput").ap()
        dbg_v = nc.dram_tensor("dbg_v", [T, D], BF16, kind="ExternalOutput").ap()
        dbg_kvg = nc.dram_tensor("dbg_kvg", [4 * 2048, 132], F32, kind="ExternalOutput").ap()
        dbg_qs = nc.dram_tensor("dbg_qs", [128, 16], F32, kind="ExternalOutput").ap()
        dbg_w = nc.dram_tensor("dbg_w", [128, 64 + 16 + 132], F32, kind="ExternalOutput").ap()
        dbg_mix = nc.dram_tensor("dbg_mix", [D, T], F32, kind="ExternalOutput").ap()
        dbg_kv0 = nc.dram_tensor("dbg_kv0", [2048, 132], F32, kind="ExternalOutput").ap()
        dbg_qs0 = nc.dram_tensor("dbg_qs0", [128, 16], F32, kind="ExternalOutput").ap()
        dbg_c = nc.dram_tensor("dbg_c", [128, 1024], F32, kind="ExternalOutput").ap()
        dbg_c2 = nc.dram_tensor("dbg_c2", [128, 132], F32, kind="ExternalOutput").ap()

    sk_dram = nc.dram_tensor(pre + "sk_dram", [T, D], BF16)
    v_dram = nc.dram_tensor(pre + "v_dram", [T, D], BF16)
    kv_dram = nc.dram_tensor(pre + "kv_dram", [16 * 128, 132], F32)
    kvg_dram = nc.dram_tensor(pre + "kvg_dram", [4 * 16 * 128, 132], F32)
    qs_dram = nc.dram_tensor(pre + "qs_dram", [128, 16], F32)
    qsg_dram = nc.dram_tensor(pre + "qsg_dram", [128, 16], F32)
    w_dram = nc.dram_tensor(pre + "w_dram", [16, 4], F32)
    mix_dram = nc.dram_tensor(pre + "mix_dram", [D, T], F32)

    S = SAlloc(p, pre, fuse)
    if fuse is None:
        xbf = p.sb("xbf", [128, DC * T], BF16)
        aG = p.sb("aG", [128, 65536], U8)
    else:
        xbf = bview(fuse["P1"], 0, BF16, DC * T)
        aG = bview(fuse["P2"], 0, U8, 65536)
    aM = S.t("aM", [128, 16384], U8)
    xf = S.t("xf", [128, 2 * T], F32)
    sq = S.t("sq", [128, 4096], U8)
    tmp = S.t("tmp", [128, 512], F32)
    memb = S.t("memb", [128, 64 * 132], BF16)
    spv = S.t("spv", [128, 16 * 132], F32)
    stbf = S.t("stbf", [128, 132], BF16)
    mtb = S.t("mtb", [128, 132], BF16)
    sqt = S.t("sqt", [128, 128], BF16)
    wsl = S.t("wsl", [128, 16 * 128], BF16)
    qsum = S.t("qsum", [128, 16], F32)
    xsb = S.t("xsb", [128, 16], BF16)
    wcol = S.t("wcol", [128, 64], F32)
    gcol = S.t("gcol", [128, 16], F32)
    relc = S.t("relc", [16, 8], F32)
    rs1 = S.t("rs1", [128, 1], F32)
    accmem = S.t("accmem", [128, 128], F32)
    rs2 = S.t("rs2", [128, 1], F32)
    rs4a = S.t("rs4a", [128, 4], F32)
    rs4b = S.t("rs4b", [128, 4], F32)
    oloc = S.t("oloc", [128, 128], F32)
    t128 = S.t("t128", [128, 128], F32)
    sktm = S.t("sktm", [128, 128], BF16)
    stm = S.t("stm", [128, 128], BF16)
    otm = S.t("otm", [128, 128], BF16)

    kq_buf = bview(aG, 0, BF16, 16 * 512)
    skt_buf = bview(aG, 16384, BF16, 4 * 2048)
    vg_buf = bview(aG, 32768, BF16, 4 * 2048)
    ofm_buf = bview(aG, 49152, BF16, 16 * 512)
    mf2 = bview(aG, 0, F32, 2 * T)
    srow = aG[0:1, 16384:32768].bitcast(F32)
    rstd = aG[0:1, 32768:40960].bitcast(F32)
    mbc = bview(aM, 0, F32, T)
    rbc = bview(aM, 8192, F32, T)
    scratch = dict(xf=xf, sq=sq, srow=srow, rstd=rstd, mbc=mbc, rbc=rbc,
                   tmp=tmp, mf2=mf2)

    C = {}
    if fuse is None:
        C["ones_col_div"] = p.sb("c_div", [128, 1], BF16)
        C["ones_row_f32"] = p.sb("c_row", [1, 128], F32)
        C["id128"] = p.sb("c_id", [128, 128], BF16)
    else:
        C["ones_col_div"] = fuse["C"]["ones_col_div"]
        C["ones_row_f32"] = fuse["C"]["ones_row_f32"]
        C["id128"] = fuse["C"]["id128"]
    C["onebf"] = S.t("c_onebf", [128, 1], BF16)
    C["relrow"] = S.t("c_relrow", [128, 1], F32)
    C["tril"] = S.t("c_tril", [128, 128], BF16)
    C["maskpre"] = S.t("c_maskpre", [128, 4], F32)
    mnorm_sb = S.t("mnorm_sb", [128, 64], F32)
    grow = S.t("grow", [1, 16], F32)
    block = p.block

    def s_consts(sync):
        if fuse is None:
            p.dma(sync, C["ones_col_div"][:], cdiv)
            p.dma(sync, C["ones_row_f32"][:], crow)
            p.dma(sync, C["id128"][:], cid)
        p.dma(sync, C["onebf"][:], conebf)
        p.dma(sync, C["relrow"][:], crelrow)
        p.dma(sync, C["tril"][:], ctril)
        p.dma(sync, C["maskpre"][:], maskpre)
        p.dma(sync, mnorm_sb[:], mnormT)
        p.dma(sync, grow[:], gatein)
    block.sync(s_consts)

    def s_memb(gq):
        p.dma(gq, memb[:].rearrange("p (c e) -> p c e", e=132),
              memaug.rearrange("(c p) e -> p c e", p=128))
    block.gpsimd(s_memb)
    ld_consts = p.ld

    def f(v):
        v.wait_ge(p.ldh, ld_consts)
        return v.engine_nop()
    p.do(block.vector, f)

    emit_ln(p, x_in, xbf, C, scratch)

    def f(v):
        for j in range(DC):
            last = v.tensor_reduce(qsum[:, j:j + 1], xbf[:, j * T:(j + 1) * T], AX.X, ALU.add)
        return last
    p.do(block.vector, f)

    def f(v):
        return v.tensor_copy(xsb[:], qsum[:])
    p.do(block.vector, f)

    for ec in range(DC):
        dep = p.step

        def s_w(gq, ec=ec, dep=dep):
            gq.wait_ge(p.fl, dep)
            p.dma(gq, wsl[:].rearrange("p (c o) -> p c o", o=128),
                  wqT[:, ec * 128:(ec + 1) * 128].rearrange("(c p) o -> p c o", p=128))
        block.gpsimd(s_w)
        ld0 = p.ld

        def f(t, ec=ec, ld0=ld0):
            t.wait_ge(p.ldh, ld0)
            for j in range(DC):
                mm = t.matmul(p.pA[:, 0:1], wsl[:, j * 128:(j + 1) * 128], xsb[:, j:j + 1],
                              start=(j == 0), stop=(j == DC - 1))
            return mm
        p.do(block.tensor, f)

        def f(v, ec=ec):
            return v.tensor_copy(qsum[:, ec:ec + 1], p.pA[:, 0:1])
        p.do(block.vector, f)

    def s_q(sync):
        sync.wait_ge(p.fl, p.step)
        p.dma_st(sync, qs_dram.ap(), qsum[:])
    block.sync(s_q)
    st_q = p.st

    def s_cc1(gq):
        gq.wait_ge(p.sth, st_q)
        gq.collective_compute("AllReduce", ALU.add, replica_groups=RG4,
                              ins=[qs_dram.ap().opt()], outs=[qsg_dram.ap().opt()]).then_inc(p.cch)
    block.gpsimd(s_cc1)
    p.cc += 1

    def s_qg(sync):
        sync.wait_ge(p.cch, p.cc)
        p.dma(sync, qsum[:], qsg_dram.ap())
    block.sync(s_qg)
    ld_qg = p.ld

    for m in range(4):
        def f(v, m=m, ld_qg=ld_qg):
            v.wait_ge(p.ldh, ld_qg)
            return v.tensor_tensor(t128[:, 0:16], qsum[:], mnorm_sb[:, m * 16:(m + 1) * 16], ALU.mult)
        p.do(block.vector, f)

        def f(t, m=m):
            t.matmul(p.pA[0:1, 0:16], C["relrow"][:, 0:1], t128[:, 0:16], start=True, stop=True)
            return t.transpose(p.pB[0:16, 0:1], p.pA[0:1, 0:16], C["ones_row_f32"][0:1, 0:1])
        break
    relrow_sb = S.t("relrow_sb", [1, 16], F32)
    for m in range(4):
        def f(v, m=m, ld_qg=ld_qg):
            v.wait_ge(p.ldh, ld_qg)
            return v.tensor_tensor(t128[:, 0:16], qsum[:], mnorm_sb[:, m * 16:(m + 1) * 16], ALU.mult)
        p.do(block.vector, f)

        def f(t, m=m):
            return t.matmul(p.pA[0:1, 0:16], C["relrow"][:, 0:1], t128[:, 0:16], start=True, stop=True)
        p.do(block.tensor, f)

        def f(v, m=m):
            return v.tensor_copy(relrow_sb[0:1, :], p.pA[0:1, 0:16])
        p.do(block.vector, f)

        def f(t, m=m):
            return t.transpose(p.pB[0:16, 0:1], relrow_sb[0:1, 0:16], C["ones_row_f32"][0:1, 0:1])
        p.do(block.tensor, f)

        def f(v, m=m):
            return v.tensor_copy(relc[0:16, m:m + 1], p.pB[0:16, 0:1])
        p.do(block.vector, f)

    def f(v):
        v.tensor_reduce(relc[0:16, 4:5], relc[0:16, 0:4], AX.X, ALU.max, negate=True)
        return v.engine_nop()
    p.do(block.vector, f)

    def f(sc):
        return sc.activation(relc[0:16, 0:4], relc[0:16, 0:4], AF.Exp, bias=relc[0:16, 4:5],
                             scale=np.float32(1.0))
    p.do(block.scalar, f)

    def f(v):
        return v.tensor_reduce(relc[0:16, 5:6], relc[0:16, 0:4], AX.X, ALU.add)
    p.do(block.vector, f)

    def f(v):
        return v.reciprocal(relc[0:16, 6:7], relc[0:16, 5:6])
    p.do(block.vector, f)

    def f(v):
        return v.tensor_scalar(relc[0:16, 0:4], relc[0:16, 0:4], relc[0:16, 6:7], None, ALU.mult)
    p.do(block.vector, f)

    def s_wd(sync):
        sync.wait_ge(p.fl, p.step)
        p.dma_st(sync, w_dram.ap(), relc[0:16, 0:4])
    block.sync(s_wd)
    st_w = p.st

    def s_wl(sync):
        sync.wait_ge(p.sth, st_w)
        p.dma(sync, t128[0:1, 0:64].bitcast(F32), w_dram.ap().rearrange("(o h) m -> o (h m)", o=1))
    block.sync(s_wl)
    ld_w = p.ld

    def f(t, ld_w=ld_w):
        t.wait_ge(p.ldh, ld_w)
        return t.matmul(p.pA[:, 0:64], C["ones_row_f32"][0:1, :], t128[0:1, 0:64],
                        start=True, stop=True)
    p.do(block.tensor, f)

    def f(v):
        return v.tensor_copy(wcol[:], p.pA[:, 0:64])
    p.do(block.vector, f)

    def f(sc):
        return sc.activation(grow[0:1, :], grow[0:1, :], AF.Sigmoid)
    p.do(block.scalar, f)

    def f(t):
        return t.matmul(p.pA[:, 0:16], C["ones_row_f32"][0:1, :], grow[0:1, 0:16],
                        start=True, stop=True)
    p.do(block.tensor, f)

    def f(v):
        return v.tensor_copy(gcol[:], p.pA[:, 0:16])
    p.do(block.vector, f)

    def f(v):
        return v.memset(t128[:], 0.0)
    p.do(block.vector, f)

    totkv = spv

    def f(v):
        return v.memset(totkv[:], 0.0)
    p.do(block.vector, f)

    sct = S.t("sct", [128, 512], F32)
    sp_st = []

    def emit_sigma(v, buf, ec):
        sl = slice(ec * 512, (ec + 1) * 512)
        v.tensor_scalar(sct[:, 0:512], buf[:, sl], 0.0, None, ALU.min)
        return sl

    for g in range(4):
        for ec in range(DC):
            dep = p.step

            def s_w(gq, g=g, ec=ec, dep=dep):
                gq.wait_ge(p.fl, dep)
                p.dma(gq, wsl[:].rearrange("p (c o) -> p c o", o=128),
                      wkT[:, ec * 128:(ec + 1) * 128].rearrange("(c p) o -> p c o", p=128))
            block.gpsimd(s_w)
            ld0 = p.ld

            def f(t, g=g, ec=ec, ld0=ld0):
                t.wait_ge(p.ldh, ld0)
                for j in range(DC):
                    mm = t.matmul(p.pA[:, 0:512], wsl[:, j * 128:(j + 1) * 128],
                                  xbf[:, j * T + g * 512:j * T + (g + 1) * 512],
                                  start=(j == 0), stop=(j == DC - 1))
                return mm
            p.do(block.tensor, f)

            def f(v, ec=ec):
                return v.tensor_scalar(sct[:, 0:512], p.pA[:, 0:512], 0.0, None, ALU.min)
            p.do(block.vector, f)

            def f(sc, ec=ec):
                return sc.activation(sct[:, 0:512], sct[:, 0:512], AF.Exp)
            p.do(block.scalar, f)

            def f(v, ec=ec):
                v.tensor_scalar(kq_buf[:, ec * 512:(ec + 1) * 512], p.pA[:, 0:512], 0.0, None, ALU.max)
                return v.tensor_tensor(kq_buf[:, ec * 512:(ec + 1) * 512],
                                       kq_buf[:, ec * 512:(ec + 1) * 512], sct[:, 0:512], ALU.add)
            p.do(block.vector, f)

        for et in range(4):
            dep = p.step

            def s_w(gq, g=g, et=et, dep=dep):
                gq.wait_ge(p.fl, dep)
                p.dma(gq, bview(aG, 49152, BF16, 16 * 512)[:].rearrange("p (c o) -> p c o", o=512),
                      wvT[:, et * 512:(et + 1) * 512].rearrange("(c p) o -> p c o", p=128))
            block.gpsimd(s_w)
            ld0 = p.ld
            vsl = bview(aG, 49152, BF16, 16 * 512)

            for tc in range(4):
                def f(t, g=g, et=et, tc=tc, ld0=ld0, vsl=vsl):
                    t.wait_ge(p.ldh, ld0)
                    for j in range(DC):
                        mm = t.matmul(p.pB[:, 0:512],
                                      xbf[:, j * T + g * 512 + tc * 128:j * T + g * 512 + tc * 128 + 128],
                                      vsl[:, j * 512:(j + 1) * 512],
                                      start=(j == 0), stop=(j == DC - 1))
                    return mm
                p.do(block.tensor, f)

                def f(v, tc=tc, et=et):
                    if tc == 0 and et == 0 and sp_st:
                        v.wait_ge(p.sth, sp_st[-1])
                    return v.tensor_copy(vg_buf[:, tc * 2048 + et * 512:tc * 2048 + (et + 1) * 512],
                                         p.pB[:, 0:512])
                p.do(block.vector, f)

        for tc in range(4):
            for hg in range(2):
                def f(t, tc=tc, hg=hg):
                    for i in range(8):
                        h2 = hg * 8 + i
                        last = t.transpose(p.pT[:, i * 128:(i + 1) * 128],
                                           kq_buf[:, h2 * 512 + tc * 128:h2 * 512 + tc * 128 + 128],
                                           C["id128"][:])
                    return last
                p.do(block.tensor, f)

                def f(v, tc=tc, hg=hg):
                    if tc == 0 and hg == 0 and sp_st:
                        v.wait_ge(p.sth, sp_st[-1])
                    for i in range(8):
                        h2 = hg * 8 + i
                        last = v.tensor_copy(skt_buf[:, tc * 2048 + h2 * 128:tc * 2048 + (h2 + 1) * 128],
                                             p.pT[:, i * 128:(i + 1) * 128])
                    return last
                p.do(block.vector, f)

        def s_sp(sync, g=g):
            sync.wait_ge(p.fl, p.step)
            for tc in range(4):
                p.dma_st(sync, sk_dram[(g * 4 + tc) * 128:(g * 4 + tc + 1) * 128, :],
                         skt_buf[:, tc * 2048:(tc + 1) * 2048])
                p.dma_st(sync, v_dram[(g * 4 + tc) * 128:(g * 4 + tc + 1) * 128, :],
                         vg_buf[:, tc * 2048:(tc + 1) * 2048])
        block.sync(s_sp)
        sp_st.append(p.st)

        for h in range(DC):
            def f(t, g=g, h=h):
                for tc in range(4):
                    t.matmul(p.pD[:, 0:128], skt_buf[:, tc * 2048 + h * 128:tc * 2048 + (h + 1) * 128],
                             vg_buf[:, tc * 2048 + h * 128:tc * 2048 + (h + 1) * 128],
                             start=(tc == 0), stop=False)
                    mm = t.matmul(p.pD[:, 128:129], skt_buf[:, tc * 2048 + h * 128:tc * 2048 + (h + 1) * 128],
                                  C["onebf"][:, 0:1], start=False, stop=(tc == 3))
                return mm
            p.do(block.tensor, f)

            def f(v, h=h):
                return v.tensor_tensor(totkv[:, h * 132:h * 132 + 129], totkv[:, h * 132:h * 132 + 129],
                                       p.pD[:, 0:129], ALU.add)
            p.do(block.vector, f)

    def s_kv(sync):
        sync.wait_ge(p.fl, p.step)
        p.dma_st(sync, kv_dram.ap().rearrange("(h p) c -> p h c", p=128),
                 totkv[:].rearrange("p (h c) -> p h c", c=132))
    block.sync(s_kv)
    st_kv = p.st

    def s_cc2(gq):
        gq.wait_ge(p.sth, st_kv)
        gq.collective_compute("AllGather", ALU.bypass, replica_groups=RG4,
                              ins=[kv_dram.ap().opt()], outs=[kvg_dram.ap().opt()]).then_inc(p.cch)
    block.gpsimd(s_cc2)
    p.cc += 1

    def f(v):
        v.wait_ge(p.sth, st_kv)
        v.wait_ge(p.cch, p.cc)
        return v.memset(spv[:], 0.0)
    p.do(block.vector, f)

    rbuf = bview(aG, 49152, F32, 16 * 132)
    for r in range(4):
        dep = p.step

        def s_r(sync, r=r, dep=dep):
            sync.wait_ge(p.fl, dep)
            if r == 0:
                sync.wait_ge(p.cch, p.cc)
            p.dma(sync, rbuf[:].rearrange("p (h c) -> p h c", c=132),
                  kvg_dram[r * 2048:(r + 1) * 2048, :].rearrange("(h p) c -> p h c", p=128))
        block.sync(s_r)
        ld0 = p.ld

        def f(v, r=r, ld0=ld0):
            v.wait_ge(p.ldh, ld0)
            return v.tensor_scalar(rbuf[:], rbuf[:], C["maskpre"][:, r:r + 1], None, ALU.mult)
        p.do(block.vector, f)

        def f(v, r=r):
            return v.tensor_tensor(spv[:], spv[:], rbuf[:], ALU.add)
        p.do(block.vector, f)

    if debug:
        def s_dbg(sync):
            sync.wait_ge(p.fl, p.step)
            sync.wait_ge(p.sth, p.st)
            p.dma_st(sync, dbg_sk[:, :], sk_dram.ap())
            p.dma_st(sync, dbg_v[:, :], v_dram.ap())
            p.dma_st(sync, dbg_kvg[:, :], kvg_dram.ap())
            p.dma_st(sync, dbg_qs[:, :], qsum[:])
            p.dma_st(sync, dbg_w[:, 0:64], wcol[:])
            p.dma_st(sync, dbg_w[:, 64:80], gcol[:])
            p.dma_st(sync, dbg_w[:, 80:212], spv[:, 0:132])
            p.dma_st(sync, dbg_kv0[:, :], kv_dram.ap())
            p.dma_st(sync, dbg_qs0[:, :], qs_dram.ap())
        block.sync(s_dbg)

    EPS1 = 1e-6
    for g in range(4):
        for ec in range(DC):
            dep = p.step

            def s_w(gq, g=g, ec=ec, dep=dep):
                gq.wait_ge(p.fl, dep)
                p.dma(gq, wsl[:].rearrange("p (c o) -> p c o", o=128),
                      wqT[:, ec * 128:(ec + 1) * 128].rearrange("(c p) o -> p c o", p=128))
            block.gpsimd(s_w)
            ld0 = p.ld

            def f(t, g=g, ec=ec, ld0=ld0):
                t.wait_ge(p.ldh, ld0)
                for j in range(DC):
                    mm = t.matmul(p.pA[:, 0:512], wsl[:, j * 128:(j + 1) * 128],
                                  xbf[:, j * T + g * 512:j * T + (g + 1) * 512],
                                  start=(j == 0), stop=(j == DC - 1))
                return mm
            p.do(block.tensor, f)

            def f(v, ec=ec):
                return v.tensor_scalar(sct[:, 0:512], p.pA[:, 0:512], 0.0, None, ALU.min)
            p.do(block.vector, f)

            def f(sc, ec=ec):
                return sc.activation(sct[:, 0:512], sct[:, 0:512], AF.Exp)
            p.do(block.scalar, f)

            def f(v, ec=ec):
                v.tensor_scalar(kq_buf[:, ec * 512:(ec + 1) * 512], p.pA[:, 0:512], 0.0, None, ALU.max)
                return v.tensor_tensor(kq_buf[:, ec * 512:(ec + 1) * 512],
                                       kq_buf[:, ec * 512:(ec + 1) * 512], sct[:, 0:512], ALU.add)
            p.do(block.vector, f)

        dep = p.step

        def s_rl(sync, g=g, dep=dep):
            sync.wait_ge(p.fl, dep)
            if sp_st:
                sync.wait_ge(p.sth, sp_st[-1])
            p.dma(sync, skt_buf[:].rearrange("p (c e) -> p c e", e=2048),
                  sk_dram[g * 512:(g + 1) * 512, :].rearrange("(c p) e -> p c e", p=128))
            p.dma(sync, vg_buf[:].rearrange("p (c e) -> p c e", e=2048),
                  v_dram[g * 512:(g + 1) * 512, :].rearrange("(c p) e -> p c e", p=128))
        block.sync(s_rl)
        ld_rl = p.ld

        for tc in range(4):
            for h in range(DC):
                skt_sl = skt_buf[:, tc * 2048 + h * 128:tc * 2048 + (h + 1) * 128]
                vg_sl = vg_buf[:, tc * 2048 + h * 128:tc * 2048 + (h + 1) * 128]
                sq_sl = kq_buf[:, h * 512 + tc * 128:h * 512 + tc * 128 + 128]

                def f(v, h=h):
                    return v.tensor_copy(stbf[:, 0:132], spv[:, h * 132:(h + 1) * 132])
                p.do(block.vector, f)

                def f(t, tc=tc, h=h, ld_rl=ld_rl, skt_sl=skt_sl):
                    t.wait_ge(p.ldh, ld_rl)
                    return t.transpose(p.pT[:, 0:128], skt_sl, C["id128"][:])
                p.do(block.tensor, f)

                def f(v):
                    return v.tensor_copy(sktm[:], p.pT[:, 0:128])
                p.do(block.vector, f)

                def f(t, sq_sl=sq_sl):
                    return t.matmul(p.pE[:, 0:128], sktm[:], sq_sl, start=True, stop=True)
                p.do(block.tensor, f)

                def f(v):
                    return v.scalar_tensor_tensor(stm[:], p.pE[:, 0:128], 1.0, C["tril"][:],
                                                  ALU.mult, ALU.mult)
                p.do(block.vector, f)

                def f(t, sq_sl=sq_sl, vg_sl=vg_sl):
                    t.matmul(p.pD[:, 0:129], sq_sl, stbf[:, 0:129], start=True, stop=False)
                    t.matmul(p.pD[:, 0:128], stm[:], vg_sl, start=False, stop=False)
                    return t.matmul(p.pD[:, 128:129], stm[:], C["onebf"][:, 0:1],
                                    start=False, stop=True)
                p.do(block.tensor, f)

                if debug and g == 0 and tc == 0 and h == 0:
                    dbgc = bview(aM, 0, F32, 1024)

                    def f(v):
                        v.tensor_copy(dbgc[:, 0:129], p.pD[:, 0:129])
                        return v.tensor_copy(dbgc[:, 256:256 + 128], stm[:])
                    p.do(block.vector, f)

                mem_ps = [(p.pB, 0), (p.pB, 256), (p.pE, 0), (p.pE, 256)]

                def f(t, h=h, sq_sl=sq_sl):
                    for m in range(4):
                        ps, off = mem_ps[m]
                        mm = t.matmul(ps[:, off:off + 132], sq_sl,
                                      memb[:, (m * 16 + h) * 132:(m * 16 + h + 1) * 132],
                                      start=True, stop=True)
                    return mm
                p.do(block.tensor, f)

                def f(v, h=h):
                    for m in range(4):
                        ps, off = mem_ps[m]
                        last = v.tensor_scalar(rs4a[:, m:m + 1], ps[:, off + 128:off + 129],
                                               float(EPS1), None, ALU.max)
                    return last
                p.do(block.vector, f)

                def f(v):
                    return v.reciprocal(rs4b[:], rs4a[:])
                p.do(block.vector, f)

                def f(v, h=h):
                    return v.tensor_tensor(rs4b[:], rs4b[:], wcol[:, h * 4:h * 4 + 4], ALU.mult)
                p.do(block.vector, f)

                def f(v, h=h):
                    for m in range(4):
                        ps, off = mem_ps[m]
                        last = v.tensor_scalar(sct[:, m * 128:(m + 1) * 128], ps[:, off:off + 128],
                                               rs4b[:, m:m + 1], None, ALU.mult)
                    return last
                p.do(block.vector, f)

                def f(v):
                    v.tensor_tensor(oloc[:], sct[:, 0:128], sct[:, 128:256], ALU.add)
                    v.tensor_tensor(t128[:], sct[:, 256:384], sct[:, 384:512], ALU.add)
                    return v.tensor_tensor(accmem[:], oloc[:], t128[:], ALU.add)
                p.do(block.vector, f)

                def f(v, h=h):
                    return v.tensor_scalar(rs1[:], p.pD[:, 128:129], float(EPS1), None, ALU.max)
                p.do(block.vector, f)

                def f(v):
                    return v.reciprocal(rs2[:], rs1[:])
                p.do(block.vector, f)

                def f(v, h=h):
                    v.tensor_scalar(oloc[:], p.pD[:, 0:128], rs2[:], None, ALU.mult)
                    v.tensor_tensor(t128[:], accmem[:], oloc[:], ALU.subtract)
                    return v.scalar_tensor_tensor(otm[:], t128[:], gcol[:, h:h + 1], oloc[:],
                                                  ALU.mult, ALU.add)
                p.do(block.vector, f)

                if debug and g == 0 and tc == 0 and h == 0:
                    def f(v):
                        v.tensor_copy(dbgc[:, 384:384 + 128], accmem[:])
                        v.tensor_copy(dbgc[:, 512:512 + 128], oloc[:])
                        v.tensor_copy(dbgc[:, 640:640 + 128], otm[:])
                        return v.tensor_copy(dbgc[:, 768:768 + 132], stbf[:, 0:132])
                    p.do(block.vector, f)

                def f(t, skt_sl=skt_sl, vg_sl=vg_sl):
                    t.transpose(p.pT[:, 128:256], otm[:], C["id128"][:])
                    t.matmul(p.pF[:, 0:128], skt_sl, vg_sl, start=True, stop=True)
                    return t.matmul(p.pF[:, 128:129], skt_sl, C["onebf"][:, 0:1],
                                    start=True, stop=True)
                p.do(block.tensor, f)

                def f(v, tc=tc, h=h):
                    v.tensor_copy(ofm_buf[:, h * 512 + tc * 128:h * 512 + tc * 128 + 128],
                                  p.pT[:, 128:256])
                    return v.tensor_tensor(spv[:, h * 132:h * 132 + 129],
                                           spv[:, h * 132:h * 132 + 129], p.pF[:, 0:129], ALU.add)
                p.do(block.vector, f)

        mt_st = []
        for oc in range(DC):
            dep = p.step

            def s_w(gq, oc=oc, dep=dep):
                gq.wait_ge(p.fl, dep)
                p.dma(gq, wsl[:].rearrange("p (c o) -> p c o", o=128),
                      woT[:, oc * 128:(oc + 1) * 128].rearrange("(c p) o -> p c o", p=128))
            block.gpsimd(s_w)
            ld0 = p.ld

            def f(t, oc=oc, ld0=ld0):
                t.wait_ge(p.ldh, ld0)
                for ec in range(DC):
                    mm = t.matmul(p.pC[:, 0:512], wsl[:, ec * 128:(ec + 1) * 128],
                                  ofm_buf[:, ec * 512:(ec + 1) * 512],
                                  start=(ec == 0), stop=(ec == DC - 1))
                return mm
            p.do(block.tensor, f)

            def f(v):
                return v.tensor_copy(tmp[:, 0:512], p.pC[:, 0:512])
            p.do(block.vector, f)

            def s_m(sync, g=g, oc=oc):
                sync.wait_ge(p.fl, p.step)
                p.dma_st(sync, mix_dram[oc * 128:(oc + 1) * 128, g * 512:(g + 1) * 512],
                         tmp[:, 0:512])
            block.sync(s_m)
            def f(v):
                v.wait_ge(p.sth, p.st)
                return v.engine_nop()
            p.do(block.vector, f)

    if debug:
        def s_dm(sync):
            sync.wait_ge(p.fl, p.step)
            p.dma_st(sync, dbg_mix[:, :], mix_dram.ap())
            p.dma_st(sync, dbg_c[:, :], bview(aM, 0, F32, 1024))
            p.dma_st(sync, dbg_c2[:, :], bview(aM, 8192, F32, 132))
        block.sync(s_dm)

    def f(g2):
        g2.wait_ge(p.sth, p.st)
        return g2.engine_nop()
    p.do(block.gpsimd, f)
    emit_postln_residual(p, mix_dram, x_in, x_out, C, scratch, xbf)
    emit_store_wait(p)
    return p


def prep_l1attn_inputs(full, core):
    mem = np.asarray(full["L1_attn_memories"], np.float32)
    mn = np.asarray(full["L1_attn_memory_norms"], np.float32)
    memaug = np.concatenate([mem, mn[..., None], np.zeros((4, 16, 128, 3), np.float32)], axis=3)
    memaug = memaug.reshape(64 * 128, 132)
    mnormT = np.ascontiguousarray(mn.transpose(2, 0, 1).reshape(128, 64))
    tril = np.triu(np.ones((128, 128), np.float32)).astype(BF16NP)
    maskpre = np.zeros((128, 4), np.float32)
    pos = core % 4
    maskpre[:, 0:pos] = 1.0
    shared = {
        "wqT": np.ascontiguousarray(np.asarray(full["L1_attn_w_q"], np.float32).T),
        "wkT": np.ascontiguousarray(np.asarray(full["L1_attn_w_k"], np.float32).T),
        "wvT": np.ascontiguousarray(np.asarray(full["L1_attn_w_v"], np.float32).T),
        "woT": np.ascontiguousarray(np.asarray(full["L1_attn_w_o"], np.float32).T),
        "memaug": memaug, "mnormT": mnormT,
        "gatein": np.asarray(full["L1_attn_gate"], np.float32).reshape(1, 16),
        "maskpre": maskpre,
        "ctril": tril,
        "cdiv": np.full((128, 1), 1.0 / 2048.0, BF16NP),
        "crow": np.ones((1, 128), np.float32),
        "cid": np.eye(128, dtype=BF16NP),
        "conebf": np.ones((128, 1), BF16NP),
        "crelrow": np.full((128, 1), (128.0 ** -0.5) / 8192.0, np.float32),
    }
    return shared


def run_l1attn(x_flat, full, debug=False):
    shards = shard_fm(x_flat)
    p = build_l1attn(debug)
    in_maps = [dict(prep_l1attn_inputs(full, c), x_in=s) for c, s in enumerate(shards)]
    res = run_bass_kernel_spmd(p.nc, in_maps, core_ids=list(range(8)))
    p.close()
    if debug:
        return res
    return unshard_fm([res.results[c]["x_out"] for c in range(8)])



def _u8codes(w_int32):
    return np.ascontiguousarray(w_int32.view(np.uint8).reshape(w_int32.shape[0], -1)[:, 0::4])


def prep_l1ffn_inputs(full):
    wg = np.ascontiguousarray(_u8codes(np.asarray(full["L1_ffn_wq_gate"])).T)
    wu = np.ascontiguousarray(_u8codes(np.asarray(full["L1_ffn_wq_up"])).T)
    wd = np.ascontiguousarray(_u8codes(np.asarray(full["L1_ffn_wq_down"])).T)
    shared = {
        "wg": wg, "wu": wu, "wd": wd,
        "sg": np.asarray(full["L1_ffn_s_gate"], np.float32).reshape(1, -1),
        "su": np.asarray(full["L1_ffn_s_up"], np.float32).reshape(1, -1),
        "sd": np.asarray(full["L1_ffn_s_down"], np.float32).reshape(1, -1),
        "cdiv": np.full((128, 1), 1.0 / 2048.0, BF16NP),
        "crow": np.ones((1, 128), np.float32),
        "cm128": np.full((128, 1), -128.0, np.float32),
    }
    return shared


def shard_fm(x_flat):
    return [np.ascontiguousarray(x_flat[c * T:(c + 1) * T].T) for c in range(8)]


def unshard_fm(shards):
    return np.concatenate([s.T for s in shards], axis=0)


def run_l1ffn(x_flat, full):
    shared = prep_l1ffn_inputs(full)
    shards = shard_fm(x_flat)
    p = build_l1ffn()
    in_maps = [dict(shared, x_in=s) for s in shards]
    res = run_bass_kernel_spmd(p.nc, in_maps, core_ids=list(range(8)))
    p.close()
    return unshard_fm([res.results[c]["x_out"] for c in range(8)])


def build_fused():
    p = Prog().start()
    nc = p.nc
    x0 = nc.dram_tensor("x_in", [D, T], F32, kind="ExternalInput").ap()
    xo = nc.dram_tensor("x_out", [D, T], F32, kind="ExternalOutput").ap()
    x1 = nc.dram_tensor("x1", [D, T], F32).ap()
    x2 = nc.dram_tensor("x2", [D, T], F32).ap()
    x3 = nc.dram_tensor("x3", [D, T], F32).ap()
    cdiv = nc.dram_tensor("cdiv", [128, 1], BF16, kind="ExternalInput").ap()
    crow = nc.dram_tensor("crow", [1, 128], F32, kind="ExternalInput").ap()
    cid = nc.dram_tensor("cid", [128, 128], BF16, kind="ExternalInput").ap()
    P1 = p.sb("P1", [128, 65536], U8)
    P2 = p.sb("P2", [128, 65536], U8)
    REST = p.sb("REST", [128, 80896], U8)
    C = {
        "ones_col_div": p.sb("g_div", [128, 1], BF16),
        "ones_row_f32": p.sb("g_row", [1, 128], F32),
        "id128": p.sb("g_id", [128, 128], BF16),
    }
    block = p.block

    def s_c(sync):
        p.dma(sync, C["ones_col_div"][:], cdiv)
        p.dma(sync, C["ones_row_f32"][:], crow)
        p.dma(sync, C["id128"][:], cid)
    block.sync(s_c)
    ldc = p.ld

    def f(v):
        v.wait_ge(p.ldh, ldc)
        return v.engine_nop()
    p.do(block.vector, f)

    def barrier():
        def f(g):
            g.wait_ge(p.sth, p.st)
            return g.engine_nop()
        p.do(block.gpsimd, f)

    base = dict(p=p, P1=P1, P2=P2, REST=REST, C=C)
    build_l0attn(fuse=dict(base, pre="a_", x_in=x0, x_out=x1))
    barrier()
    build_l0ffn(fuse=dict(base, pre="f_", x_in=x1, x_out=x2))
    barrier()
    build_l1attn(fuse=dict(base, pre="b_", x_in=x2, x_out=x3))
    barrier()
    build_l1ffn(fuse=dict(base, pre="g_", x_in=x3, x_out=xo))
    return p


def kernel(**inputs):
    x = np.asarray(inputs["L0_attn_hidden_states"], np.float32).reshape(16384, 2048)
    shards = shard_fm(x)
    shared = {}
    shared.update(prep_l0attn_inputs(inputs))
    shared.update(prep_l0ffn_inputs(inputs))
    shared.update(prep_l1ffn_inputs(inputs))
    p = build_fused()
    in_maps = []
    for c in range(8):
        m = dict(shared)
        m.update(prep_l1attn_inputs(inputs, c))
        m["x_in"] = shards[c]
        in_maps.append(m)
    res = run_bass_kernel_spmd(p.nc, in_maps, core_ids=list(range(8)))
    p.close()
    out = unshard_fm([res.results[c]["x_out"] for c in range(8)])
    return np.ascontiguousarray(out.reshape(2, 8192, 2048).astype(np.float32))


def kernel_multilaunch(**inputs):
    x = np.asarray(inputs["L0_attn_hidden_states"], np.float32).reshape(16384, 2048)
    x = run_l0attn(x, inputs)
    x = run_l0ffn(x, inputs)
    x = run_l1attn(x, inputs)
    x = run_l1ffn(x, inputs)
    return np.ascontiguousarray(x.reshape(2, 8192, 2048).astype(np.float32))


# version 04
# speedup vs baseline: 1.1770x; 1.0476x over previous
import numpy as np
import ml_dtypes
from contextlib import ExitStack

import concourse.bass as bass
import concourse.mybir as mybir
from concourse.bass_utils import run_bass_kernel_spmd

F32 = mybir.dt.float32
BF16 = mybir.dt.bfloat16
U8 = mybir.dt.uint8
U16 = mybir.dt.uint16
AF = mybir.ActivationFunctionType
ALU = mybir.AluOpType
AX = mybir.AxisListType
BF16NP = ml_dtypes.bfloat16

T = 2048
D = 2048
DC = 16
NTT = 4
EPS = 1e-5
TWO126 = float(2.0 ** 126)
RG4 = [[0, 1, 2, 3], [4, 5, 6, 7]]
RG8 = [[0, 1, 2, 3, 4, 5, 6, 7]]
HID1 = 5632
HC1 = HID1 // 128
MLP0 = 16384


class Prog:
    def __init__(self):
        self.nc = bass.Bass("TRN2", target_bir_lowering=False)
        self.ctx = ExitStack()
        self.step = 0
        self.ld = 0
        self.st = 0
        self.cc = 0

    def start(self):
        c = self.ctx.enter_context
        self.fl = c(self.nc.semaphore("fl"))
        self.ldh = c(self.nc.semaphore("ldh"))
        self.sth = c(self.nc.semaphore("sth"))
        self.cch = c(self.nc.semaphore("cch"))
        self.block = c(self.nc.Block())
        self.pA = c(self.nc.psum_tensor("pA", [128, 512], F32))
        self.pB = c(self.nc.psum_tensor("pB", [128, 512], F32))
        self.pC = c(self.nc.psum_tensor("pC", [128, 512], F32))
        self.pD = c(self.nc.psum_tensor("pD", [128, 1024], F32))
        self.pE = c(self.nc.psum_tensor("pE", [128, 512], F32))
        self.pF = c(self.nc.psum_tensor("pF", [128, 512], F32))
        self.pT = c(self.nc.psum_tensor("pT", [128, 1024], BF16))
        return self

    def sb(self, name, shape, dtype):
        return self.ctx.enter_context(self.nc.sbuf_tensor(name, shape, dtype))

    def do(self, section, emit):
        dep = self.step

        def wrapped(eng):
            eng.wait_ge(self.fl, dep)
            last = emit(eng)
            last.then_inc(self.fl, 1)
        section(wrapped)
        self.step += 1
        return self.step

    def load(self, emit, n=1):
        self.block.sync(emit)
        return self.ld

    def dma(self, q, out, in_, accum=False):
        kw = {"accum_op": ALU.add} if accum else {}
        q.dma_start(out=out, in_=in_, **kw).then_inc(self.ldh, 16)
        self.ld += 16
        return self.ld

    def dma_st(self, q, out, in_, accum=False):
        kw = {"accum_op": ALU.add} if accum else {}
        q.dma_start(out=out, in_=in_, **kw).then_inc(self.sth, 16)
        self.st += 16
        return self.st

    def close(self):
        self.ctx.close()


_ESZ = {F32: 4, BF16: 2, U8: 1, U16: 2}


class SAlloc:

    def __init__(self, p, pre, fuse):
        self.p = p
        self.pre = pre
        self.fuse = fuse
        self.off = 0

    def t(self, name, shape, dtype):
        if self.fuse is None:
            return self.p.sb(self.pre + name, shape, dtype)
        nb = shape[1] * _ESZ[dtype]
        nb_pad = (nb + 31) // 32 * 32
        o = self.off
        self.off += nb_pad
        R = self.fuse["REST"]
        assert self.off <= R.shape[1], f"REST overflow at {name}: {self.off}"
        v = R[:, o:o + nb].bitcast(dtype)
        if shape[0] != 128:
            v = v[0:shape[0], :]
        return v



def bview(buf, byte_off, dtype, cols):
    esz = _ESZ[dtype]
    bsz = _ESZ[buf.dtype]
    assert byte_off % bsz == 0 and (cols * esz) % bsz == 0
    return buf[:, byte_off // bsz:(byte_off + cols * esz) // bsz].bitcast(dtype)



def emit_stats(p, src_bf, C, srow, rstd, sq):
    block = p.block
    onesd = C["ones_col_div"]

    for tt in range(NTT):
        def f(t, tt=tt):
            for j in range(DC):
                mm = t.matmul(p.pE[0:1, 0:512], onesd[:, 0:1],
                              src_bf[:, j * T + tt * 512:j * T + (tt + 1) * 512],
                              start=(j == 0), stop=(j == DC - 1))
            return mm
        p.do(block.tensor, f)

        def f(v, tt=tt):
            return v.tensor_copy(srow[0:1, tt * 512:(tt + 1) * 512], p.pE[0:1, 0:512])
        p.do(block.vector, f)

    nsl = min(4, sq.shape[1] // 1024)
    sqv = sq[:, 0:nsl * 1024].bitcast(BF16)
    for tt in range(NTT):
        for jg in range(DC // nsl):
            def f(sc, tt=tt, jg=jg):
                for i in range(nsl):
                    j2 = jg * nsl + i
                    last = sc.activation(sqv[:, i * 512:(i + 1) * 512],
                                         src_bf[:, j2 * T + tt * 512:j2 * T + (tt + 1) * 512],
                                         AF.Square)
                return last
            p.do(block.scalar, f)

            def f(t, tt=tt, jg=jg):
                for i in range(nsl):
                    j2 = jg * nsl + i
                    last = t.matmul(p.pF[0:1, 0:512], onesd[:, 0:1],
                                    sqv[:, i * 512:(i + 1) * 512],
                                    start=(j2 == 0), stop=(j2 == DC - 1))
                return last
            p.do(block.tensor, f)

        def f(v, tt=tt):
            return v.tensor_copy(srow[0:1, T + tt * 512:T + (tt + 1) * 512], p.pF[0:1, 0:512])
        p.do(block.vector, f)

    def f(v):
        v.tensor_tensor(rstd[0:1, 0:T], srow[0:1, 0:T], srow[0:1, 0:T], ALU.mult)
        return v.tensor_tensor(srow[0:1, T:2 * T], srow[0:1, T:2 * T], rstd[0:1, 0:T], ALU.subtract)
    p.do(block.vector, f)

    def f(sc):
        return sc.activation(rstd[0:1, 0:T], srow[0:1, T:2 * T], AF.Sqrt, bias=np.float32(EPS), scale=np.float32(1.0))
    p.do(block.scalar, f)

    def f(v):
        return v.reciprocal(rstd[0:1, 0:T], rstd[0:1, 0:T])
    p.do(block.vector, f)


def emit_bcast_stats(p, C, srow, rstd, mbc, rbc):
    block = p.block
    ones_row = C["ones_row_f32"]
    for tt in range(NTT):
        def f(t, tt=tt):
            t.matmul(p.pD[:, 0:512], ones_row[0:1, :],
                     srow[0:1, tt * 512:(tt + 1) * 512], start=True, stop=True)
            return t.matmul(p.pD[:, 512:1024], ones_row[0:1, :],
                            rstd[0:1, tt * 512:(tt + 1) * 512], start=True, stop=True)
        p.do(block.tensor, f)

        def f(v, tt=tt):
            v.tensor_copy(mbc[:, tt * 512:(tt + 1) * 512], p.pD[:, 0:512])
            return v.tensor_copy(rbc[:, tt * 512:(tt + 1) * 512], p.pD[:, 512:1024])
        p.do(block.vector, f)


def emit_ln(p, x_dram, xbf, C, scratch):
    block = p.block
    xf, sq = scratch["xf"], scratch["sq"]
    srow, rstd = scratch["srow"], scratch["rstd"]
    mbc, rbc, tmp = scratch["mbc"], scratch["rbc"], scratch["tmp"]

    ld0 = p.ld
    copy_step0 = p.step

    def s_load(sync):
        sync.wait_ge(p.fl, copy_step0)
        for j in range(DC):
            if j >= 2:
                sync.wait_ge(p.fl, copy_step0 + (j - 2) + 1)
            p.dma(sync, xf[:, (j % 2) * T:(j % 2 + 1) * T], x_dram[j * 128:(j + 1) * 128, :])
    p.block.sync(s_load)

    for j in range(DC):
        def f(v, j=j):
            v.wait_ge(p.ldh, ld0 + 16 * (j + 1))
            return v.tensor_copy(xbf[:, j * T:(j + 1) * T], xf[:, (j % 2) * T:(j % 2 + 1) * T])
        p.do(block.vector, f)

    emit_stats(p, xbf, C, srow, rstd, sq)
    emit_bcast_stats(p, C, srow, rstd, mbc, rbc)

    for j in range(DC):
        def f(v, j=j):
            for tt in range(NTT):
                sl = slice(j * T + tt * 512, j * T + (tt + 1) * 512)
                st = slice(tt * 512, (tt + 1) * 512)
                v.tensor_tensor(tmp[:, 0:512], xbf[:, sl], mbc[:, st], ALU.subtract)
                last = v.tensor_tensor(xbf[:, sl], tmp[:, 0:512], rbc[:, st], ALU.mult)
            return last
        p.do(block.vector, f)


def emit_postln_residual(p, mix_dram, x_dram, out_dram, C, scratch, mbf):
    block = p.block
    xf, sq = scratch["xf"], scratch["sq"]
    srow, rstd = scratch["srow"], scratch["rstd"]
    mbc, rbc, tmp = scratch["mbc"], scratch["rbc"], scratch["tmp"]
    mf2 = scratch["mf2"]

    ld0 = p.ld
    copy_step0 = p.step

    def s_load(sync):
        sync.wait_ge(p.fl, copy_step0)
        for j in range(DC):
            if j >= 2:
                sync.wait_ge(p.fl, copy_step0 + (j - 2) + 1)
            p.dma(sync, xf[:, (j % 2) * T:(j % 2 + 1) * T], mix_dram[j * 128:(j + 1) * 128, :])
    block.sync(s_load)

    for j in range(DC):
        def f(v, j=j):
            v.wait_ge(p.ldh, ld0 + 16 * (j + 1))
            return v.tensor_copy(mbf[:, j * T:(j + 1) * T], xf[:, (j % 2) * T:(j % 2 + 1) * T])
        p.do(block.vector, f)

    emit_stats(p, mbf, C, srow, rstd, sq)
    emit_bcast_stats(p, C, srow, rstd, mbc, rbc)

    ld1 = p.ld
    norm_step0 = p.step

    for j in range(DC):
        def f(v, j=j):
            v.wait_ge(p.ldh, ld1 + 32 * (j + 1))
            for tt in range(NTT):
                slm = slice((j % 2) * T + tt * 512, (j % 2) * T + (tt + 1) * 512)
                st = slice(tt * 512, (tt + 1) * 512)
                v.tensor_tensor(tmp[:, 0:512], mf2[:, slm], mbc[:, st], ALU.subtract)
                v.tensor_tensor(tmp[:, 0:512], tmp[:, 0:512], rbc[:, st], ALU.mult)
                last = v.tensor_tensor(mf2[:, slm], xf[:, slm], tmp[:, 0:512], ALU.add)
            return last
        p.do(block.vector, f)

    def s_stream(sync):
        st_after = {}
        for j in range(DC):
            if j >= 2:
                sync.wait_ge(p.fl, norm_step0 + (j - 2) + 1)
                p.dma_st(sync, out_dram[(j - 2) * 128:(j - 1) * 128, :],
                         mf2[:, (j % 2) * T:(j % 2 + 1) * T])
                st_after[j - 2] = p.st
            if j >= 4:
                sync.wait_ge(p.sth, st_after[j - 4])
            p.dma(sync, xf[:, (j % 2) * T:(j % 2 + 1) * T], x_dram[j * 128:(j + 1) * 128, :])
            p.dma(sync, mf2[:, (j % 2) * T:(j % 2 + 1) * T], mix_dram[j * 128:(j + 1) * 128, :])
        for j in (DC - 2, DC - 1):
            sync.wait_ge(p.fl, norm_step0 + j + 1)
            p.dma_st(sync, out_dram[j * 128:(j + 1) * 128, :],
                     mf2[:, (j % 2) * T:(j % 2 + 1) * T])
    block.sync(s_stream)


def emit_store_wait(p):
    def f(g):
        g.wait_ge(p.sth, p.st)
        return g.engine_nop()
    p.do(p.block.gpsimd, f)



def build_l1ffn(fuse=None):
    if fuse is None:
        p = Prog().start()
        own = True
    else:
        p, own = fuse["p"], False
    nc = p.nc
    pre = "" if own else fuse["pre"]
    if own:
        x_in = nc.dram_tensor("x_in", [D, T], F32, kind="ExternalInput").ap()
    else:
        x_in = fuse["x_in"]
    wg = nc.dram_tensor("wg", [D, HID1], U8, kind="ExternalInput").ap()
    wu = nc.dram_tensor("wu", [D, HID1], U8, kind="ExternalInput").ap()
    wd = nc.dram_tensor("wd", [HID1, D], U8, kind="ExternalInput").ap()
    sg = nc.dram_tensor("sg", [1, HID1], F32, kind="ExternalInput").ap()
    su = nc.dram_tensor("su", [1, HID1], F32, kind="ExternalInput").ap()
    sd = nc.dram_tensor("sd", [1, D], F32, kind="ExternalInput").ap()
    if own:
        cdiv = nc.dram_tensor("cdiv", [128, 1], BF16, kind="ExternalInput").ap()
        crow = nc.dram_tensor("crow", [1, 128], F32, kind="ExternalInput").ap()
    cm128 = nc.dram_tensor("cm128", [128, 1], F32, kind="ExternalInput").ap()
    if own:
        x_out = nc.dram_tensor("x_out", [D, T], F32, kind="ExternalOutput").ap()
    else:
        x_out = fuse["x_out"]

    i_dram = nc.dram_tensor(pre + "i_dram", [HID1, T], BF16)
    wd_dram = nc.dram_tensor(pre + "wd_dram", [HID1, D], BF16)
    mix_dram = nc.dram_tensor(pre + "mix_dram", [D, T], F32)

    S = SAlloc(p, pre, fuse)
    if fuse is None:
        xbf = p.sb("xbf", [128, DC * T], BF16)
        aW = p.sb("aW", [128, 49152], U8)
    else:
        xbf = bview(fuse["P1"], 0, BF16, DC * T)
        aW = bview(fuse["P2"], 0, U8, 49152)
    aM = S.t("aM", [128, 24576], U8)
    xf = S.t("xf", [128, 2 * T], F32)
    mf2 = bview(aW, 0, F32, 2 * T)
    mbc = bview(aM, 0, F32, T)
    rbc = bview(aM, 8192, F32, T)
    sq = S.t("sq", [128, 2048], U8)
    srow = aW[0:1, 16384:16384 + 8 * T].bitcast(F32)
    rstd = aW[0:1, 32768:32768 + 4 * T].bitcast(F32)
    tmp = S.t("tmp", [128, 512], F32)
    sbS = S.t("sbS", [128, 2 * HID1 + D], BF16)
    scratch = dict(xf=xf, sq=sq, srow=srow, rstd=rstd, mbc=mbc, rbc=rbc,
                   tmp=tmp, mf2=mf2)

    C = {}
    block = p.block
    if fuse is None:
        C["ones_col_div"] = p.sb("c_div", [128, 1], BF16)
        C["ones_row_f32"] = p.sb("c_row", [1, 128], F32)
    else:
        C["ones_col_div"] = fuse["C"]["ones_col_div"]
        C["ones_row_f32"] = fuse["C"]["ones_row_f32"]
    C["m128_col"] = S.t("c_m128", [128, 1], F32)
    C["sg_row"] = aW[0:1, 0:4 * HID1].bitcast(F32)
    C["su_row"] = aW[0:1, 24576:24576 + 4 * HID1].bitcast(F32)
    C["sd_row"] = aM[0:1, 0:4 * D].bitcast(F32)

    def s_consts(sync):
        if fuse is None:
            p.dma(sync, C["ones_col_div"][:], cdiv)
            p.dma(sync, C["ones_row_f32"][:], crow)
        p.dma(sync, C["m128_col"][:], cm128)
        p.dma(sync, C["sg_row"], sg)
        p.dma(sync, C["su_row"], su)
        p.dma(sync, C["sd_row"], sd)
    block.sync(s_consts)
    ld_consts = p.ld

    def f(v):
        v.wait_ge(p.ldh, ld_consts)
        return v.engine_nop()
    p.do(block.vector, f)

    def sbS_view(which):
        if which == "g":
            return sbS[:, 0:HID1]
        if which == "u":
            return sbS[:, HID1:2 * HID1]
        return sbS[:, 2 * HID1:2 * HID1 + D]

    for which, row, n in (("g", "sg_row", HID1), ("u", "su_row", HID1), ("d", "sd_row", D)):
        for i in range(n // 512):
            def f(t, which=which, row=row, i=i):
                return t.matmul(p.pD[:, 0:512], C["ones_row_f32"][0:1, :],
                                C[row][0:1, i * 512:(i + 1) * 512], start=True, stop=True)
            p.do(block.tensor, f)

            def f(v, which=which, i=i):
                return v.tensor_copy(sbS_view(which)[:, i * 512:(i + 1) * 512], p.pD[:, 0:512])
            p.do(block.vector, f)

    emit_ln(p, x_in, xbf, C, scratch)

    cod_g = bview(aW, 0, U8, 16 * 512)
    cod_u = bview(aW, 8192, U8, 16 * 512)
    wg_bf = bview(aW, 16384, BF16, 16 * 512)
    wu_bf = bview(aW, 32768, BF16, 16 * 512)
    cod_d = bview(aM, 0, U8, 4 * 2048)
    wd_bf = bview(aM, 8192, BF16, 4 * 2048)
    gsil = S.t("gsil", [128, 1024], BF16)
    itile = S.t("itile", [128, 1024], BF16)
    it_st = []
    wd_st = []

    NHS = HID1 // 512
    for hs in range(NHS):
        ld0 = p.ld
        dep = p.step

        def s_w(sync, hs=hs, dep=dep):
            sync.wait_ge(p.fl, dep)
            p.dma(sync, cod_g.rearrange("p (c h) -> p c h", h=512),
                  wg[:, hs * 512:(hs + 1) * 512].rearrange("(c p) h -> p c h", p=128))
            p.dma(sync, cod_u.rearrange("p (c h) -> p c h", h=512),
                  wu[:, hs * 512:(hs + 1) * 512].rearrange("(c p) h -> p c h", p=128))
            p.dma(sync, cod_d.rearrange("p (c o) -> p c o", o=2048),
                  wd[hs * 512:(hs + 1) * 512, :].rearrange("(c p) o -> p c o", p=128))
        block.sync(s_w)

        def f(sc, hs=hs, ld0=ld0):
            sc.wait_ge(p.ldh, ld0 + 32)
            sc.activation(wg_bf[:], cod_g[:], AF.Copy, bias=-128.0, scale=1.0)
            return sc.activation(wu_bf[:], cod_u[:], AF.Copy, bias=-128.0, scale=1.0)
        p.do(block.scalar, f)

        def f(v, hs=hs):
            for c in range(DC):
                v.tensor_tensor(wg_bf[:, c * 512:(c + 1) * 512], wg_bf[:, c * 512:(c + 1) * 512],
                                sbS_view("g")[:, hs * 512:(hs + 1) * 512], ALU.mult)
                last = v.tensor_tensor(wu_bf[:, c * 512:(c + 1) * 512], wu_bf[:, c * 512:(c + 1) * 512],
                                       sbS_view("u")[:, hs * 512:(hs + 1) * 512], ALU.mult)
            return last
        p.do(block.vector, f)

        def f(sc, hs=hs, ld0=ld0):
            sc.wait_ge(p.ldh, ld0 + 48)
            if wd_st:
                sc.wait_ge(p.sth, wd_st[-1])
            return sc.activation(wd_bf[:], cod_d[:], AF.Copy, bias=-128.0, scale=1.0)
        p.do(block.scalar, f)

        def f(v, hs=hs):
            for c in range(4):
                last = v.tensor_tensor(wd_bf[:, c * 2048:(c + 1) * 2048],
                                       wd_bf[:, c * 2048:(c + 1) * 2048],
                                       sbS_view("d")[:, 0:2048], ALU.mult)
            return last
        p.do(block.vector, f)

        def s_wd(sync, hs=hs):
            sync.wait_ge(p.fl, p.step)
            p.dma_st(sync, wd_dram[hs * 512:(hs + 1) * 512, :].rearrange("(c p) o -> p c o", p=128),
                     wd_bf.rearrange("p (c o) -> p c o", o=2048))
        block.sync(s_wd)
        wd_st.append(p.st)

        for hc in range(4):
            for tt in range(NTT):
                def f(t, hs=hs, hc=hc, tt=tt):
                    for j in range(DC):
                        t.matmul(p.pA[:, 0:512],
                                 wg_bf[:, j * 512 + hc * 128:j * 512 + hc * 128 + 128],
                                 xbf[:, j * T + tt * 512:j * T + (tt + 1) * 512],
                                 start=(j == 0), stop=(j == DC - 1))
                    for j in range(DC):
                        mm = t.matmul(p.pB[:, 0:512],
                                      wu_bf[:, j * 512 + hc * 128:j * 512 + hc * 128 + 128],
                                      xbf[:, j * T + tt * 512:j * T + (tt + 1) * 512],
                                      start=(j == 0), stop=(j == DC - 1))
                    return mm
                p.do(block.tensor, f)

                def f(sc):
                    return sc.activation(gsil[:, 0:512], p.pA[:, 0:512], AF.Silu)
                p.do(block.scalar, f)

                k = len(it_st)
                slot = slice((k % 2) * 512, (k % 2) * 512 + 512)

                def f(v, k=k, slot=slot):
                    if k >= 2:
                        v.wait_ge(p.sth, it_st[k - 2])
                    return v.tensor_tensor(itile[:, slot], gsil[:, 0:512], p.pB[:, 0:512], ALU.mult)
                p.do(block.vector, f)

                def s_i(sync, hs=hs, hc=hc, tt=tt, slot=slot):
                    sync.wait_ge(p.fl, p.step)
                    p.dma_st(sync, i_dram[hs * 512 + hc * 128:hs * 512 + (hc + 1) * 128,
                                          tt * 512:(tt + 1) * 512], itile[:, slot])
                block.sync(s_i)
                it_st.append(p.st)

    icol = bview(aW, 0, BF16, HC1 * 512)
    wds = bview(aM, 0, BF16, HC1 * 128)
    mtile = S.t("mtile", [128, 1024], F32)
    mt_st = []
    st_wd_done = p.st
    for tt in range(NTT):
        ld0 = p.ld
        dep = p.step

        def s_i2(sync, tt=tt, dep=dep):
            sync.wait_ge(p.fl, dep)
            sync.wait_ge(p.sth, st_wd_done)
            p.dma(sync, icol.rearrange("p (c t) -> p c t", t=512),
                  i_dram[:, tt * 512:(tt + 1) * 512].rearrange("(c p) t -> p c t", p=128))
        block.sync(s_i2)

        for oc in range(DC):
            ld1 = p.ld
            dep = p.step

            def s_wd2(sync, tt=tt, oc=oc, dep=dep):
                sync.wait_ge(p.fl, dep)
                p.dma(sync, wds.rearrange("p (c o) -> p c o", o=128),
                      wd_dram[:, oc * 128:(oc + 1) * 128].rearrange("(c p) o -> p c o", p=128))
            block.sync(s_wd2)

            def f(t, tt=tt, oc=oc, ld0=ld0, ld1=ld1):
                t.wait_ge(p.ldh, ld0 + 16)
                t.wait_ge(p.ldh, ld1 + 16)
                for c in range(HC1):
                    mm = t.matmul(p.pC[:, 0:512], wds[:, c * 128:(c + 1) * 128],
                                  icol[:, c * 512:(c + 1) * 512],
                                  start=(c == 0), stop=(c == HC1 - 1))
                return mm
            p.do(block.tensor, f)

            k = len(mt_st)
            slot = slice((k % 2) * 512, (k % 2) * 512 + 512)

            def f(v, k=k, slot=slot):
                if k >= 2:
                    v.wait_ge(p.sth, mt_st[k - 2])
                return v.tensor_copy(mtile[:, slot], p.pC[:, 0:512])
            p.do(block.vector, f)

            def s_m(sync, tt=tt, oc=oc, slot=slot):
                sync.wait_ge(p.fl, p.step)
                p.dma_st(sync, mix_dram[oc * 128:(oc + 1) * 128, tt * 512:(tt + 1) * 512],
                         mtile[:, slot])
            block.sync(s_m)
            mt_st.append(p.st)

    def f(g):
        g.wait_ge(p.sth, p.st)
        return g.engine_nop()
    p.do(block.gpsimd, f)
    emit_postln_residual(p, mix_dram, x_in, x_out, C, scratch, xbf)
    emit_store_wait(p)
    return p



NC0 = MLP0 // 128
NSL0 = MLP0 // 512


def emit_decode_tile(p, v, wide, cod, bits, outW, B, n):
    v.tensor_copy(wide[:, 0:n], cod)
    v.tensor_scalar(bits[:, 0:n], wide[:, 0:n], 7, 6, ALU.bitwise_and, ALU.logical_shift_left)
    v.tensor_scalar(wide[:, n:2 * n], wide[:, 0:n], 8, 12, ALU.bitwise_and, ALU.logical_shift_left)
    v.tensor_tensor(bits[:, 0:n], bits[:, 0:n], wide[:, n:2 * n], ALU.bitwise_or)
    v.tensor_tensor(outW(0), bits[:, 0:n].bitcast(BF16), B, ALU.mult)
    v.tensor_scalar(bits[:, 0:n], wide[:, 0:n], 0x70, 2, ALU.bitwise_and, ALU.logical_shift_left)
    v.tensor_scalar(wide[:, n:2 * n], wide[:, 0:n], 0x80, 8, ALU.bitwise_and, ALU.logical_shift_left)
    v.tensor_tensor(bits[:, 0:n], bits[:, 0:n], wide[:, n:2 * n], ALU.bitwise_or)
    return v.tensor_tensor(outW(1), bits[:, 0:n].bitcast(BF16), B, ALU.mult)


def build_l0ffn(debug=False, fuse=None):
    if fuse is None:
        p = Prog().start()
        own = True
    else:
        p, own = fuse["p"], False
    nc = p.nc
    pre = "" if own else fuse["pre"]
    if own:
        x_in = nc.dram_tensor("x_in", [D, T], F32, kind="ExternalInput").ap()
    else:
        x_in = fuse["x_in"]
    cg = nc.dram_tensor("cg", [1024, MLP0], U8, kind="ExternalInput").ap()
    cu = nc.dram_tensor("cu", [1024, MLP0], U8, kind="ExternalInput").ap()
    cd = nc.dram_tensor("cd", [MLP0 // 2, D], U8, kind="ExternalInput").ap()
    scg = nc.dram_tensor("scg", [64, MLP0], F32, kind="ExternalInput").ap()
    scu = nc.dram_tensor("scu", [64, MLP0], F32, kind="ExternalInput").ap()
    scd = nc.dram_tensor("scd", [512, D], F32, kind="ExternalInput").ap()
    if own:
        cdiv = nc.dram_tensor("cdiv", [128, 1], BF16, kind="ExternalInput").ap()
        crow = nc.dram_tensor("crow", [1, 128], F32, kind="ExternalInput").ap()
    ceall = nc.dram_tensor("ceall", [64, 1024], BF16, kind="ExternalInput").ap()
    ce8 = nc.dram_tensor("ce8", [8, 128], BF16, kind="ExternalInput").ap()
    cperm = nc.dram_tensor("cperm", [128, 512], BF16, kind="ExternalInput").ap()
    if own:
        x_out = nc.dram_tensor("x_out", [D, T], F32, kind="ExternalOutput").ap()
    else:
        x_out = fuse["x_out"]
    if debug:
        dbg_mix = nc.dram_tensor("dbg_mix", [D, T], F32, kind="ExternalOutput").ap()
        dbg_kv0 = nc.dram_tensor("dbg_kv0", [2048, 132], F32, kind="ExternalOutput").ap()
        dbg_qs0 = nc.dram_tensor("dbg_qs0", [128, 16], F32, kind="ExternalOutput").ap()
        dbg_c = nc.dram_tensor("dbg_c", [128, 1024], F32, kind="ExternalOutput").ap()
        dbg_c2 = nc.dram_tensor("dbg_c2", [128, 132], F32, kind="ExternalOutput").ap()
        dbg_xp = nc.dram_tensor("dbg_xp", [D, T], BF16, kind="ExternalOutput").ap()
        dbg_wg = nc.dram_tensor("dbg_wg", [D, 512], BF16, kind="ExternalOutput").ap()
        dbg_i = nc.dram_tensor("dbg_i", [512, T], BF16, kind="ExternalOutput").ap()
        dbg_wd = nc.dram_tensor("dbg_wd", [512, D], BF16, kind="ExternalOutput").ap()

    i_dram = nc.dram_tensor(pre + "i_dram", [MLP0, T], BF16)
    wd_dram = nc.dram_tensor(pre + "wd_dram", [MLP0, D], BF16)
    mix_dram = nc.dram_tensor(pre + "mix_dram", [D, T], F32)

    S = SAlloc(p, pre, fuse)
    if fuse is None:
        xbf = p.sb("xbf", [128, DC * T], BF16)
        aW = p.sb("aW", [128, 49152], U8)
    else:
        xbf = bview(fuse["P1"], 0, BF16, DC * T)
        aW = bview(fuse["P2"], 0, U8, 49152)
    xpt = S.t("xpt", [128, 2 * T], BF16)
    xP = xbf
    aM = S.t("aM", [128, 24576], U8)
    xf = S.t("xf", [128, 2 * T], F32)
    sq = S.t("sq", [128, 4096], U8)
    tmp = S.t("tmp", [128, 512], F32)
    mf2 = bview(aW, 0, F32, 2 * T)
    mbc = bview(aM, 0, F32, T)
    rbc = bview(aM, 8192, F32, T)
    srow = aW[0:1, 16384:16384 + 8 * T].bitcast(F32)
    rstd = aW[0:1, 32768:32768 + 4 * T].bitcast(F32)
    scratch = dict(xf=xf, sq=sq, srow=srow, rstd=rstd, mbc=mbc, rbc=rbc,
                   tmp=tmp, mf2=mf2)

    C = {}
    if fuse is None:
        C["ones_col_div"] = p.sb("c_div", [128, 1], BF16)
        C["ones_row_f32"] = p.sb("c_row", [1, 128], F32)
    else:
        C["ones_col_div"] = fuse["C"]["ones_col_div"]
        C["ones_row_f32"] = fuse["C"]["ones_row_f32"]
    C["eall"] = S.t("c_eall", [64, 1024], BF16)
    C["e8"] = S.t("c_e8", [8, 128], BF16)
    C["perm"] = S.t("c_perm", [128, 512], BF16)
    block = p.block

    def s_consts(sync):
        if fuse is None:
            p.dma(sync, C["ones_col_div"][:], cdiv)
            p.dma(sync, C["ones_row_f32"][:], crow)
        p.dma(sync, C["eall"][:], ceall)
        p.dma(sync, C["e8"][:], ce8)
        p.dma(sync, C["perm"][:], cperm)
    block.sync(s_consts)
    ld_consts = p.ld

    def f(v):
        v.wait_ge(p.ldh, ld_consts)
        return v.engine_nop()
    p.do(block.vector, f)

    emit_ln(p, x_in, xbf, C, scratch)

    for jc in range(8):
        for s in range(2):
            for tt in range(NTT):
                def f(t, jc=jc, s=s, tt=tt):
                    t.matmul(p.pA[:, 0:512], C["perm"][:, (0 + s * 2) * 128:(1 + s * 2) * 128],
                             xbf[:, (2 * jc) * T + tt * 512:(2 * jc) * T + (tt + 1) * 512],
                             start=True, stop=False)
                    return t.matmul(p.pA[:, 0:512], C["perm"][:, (1 + s * 2) * 128:(2 + s * 2) * 128],
                                    xbf[:, (2 * jc + 1) * T + tt * 512:(2 * jc + 1) * T + (tt + 1) * 512],
                                    start=False, stop=True)
                p.do(block.tensor, f)

                def f(v, s=s, tt=tt):
                    return v.tensor_copy(xpt[:, s * T + tt * 512:s * T + (tt + 1) * 512],
                                         p.pA[:, 0:512])
                p.do(block.vector, f)

        def f(v, jc=jc):
            v.tensor_copy(xbf[:, (2 * jc) * T:(2 * jc + 1) * T], xpt[:, 0:T])
            return v.tensor_copy(xbf[:, (2 * jc + 1) * T:(2 * jc + 2) * T], xpt[:, T:2 * T])
        p.do(block.vector, f)

    codg = bview(aW, 0, U8, 8 * 512)
    codu = bview(aW, 4096, U8, 8 * 512)
    wgd = bview(aW, 8192, BF16, 16 * 512)
    wud = bview(aW, 24576, BF16, 16 * 512)
    wide = bview(aW, 40960, U16, 1024)
    bits = bview(aW, 43008, U16, 1024)
    bsl = bview(aW, 45056, BF16, 1024)
    sgt = aM[0:64, 0:2048].bitcast(F32)
    sut = aM[0:64, 2048:4096].bitcast(F32)
    sgb = aM[0:64, 4096:5120].bitcast(BF16)
    sub = aM[0:64, 5120:6144].bitcast(BF16)
    itile = S.t("itile", [128, 1024], BF16)
    gsil = S.t("gsil", [128, 1024], BF16)
    it_st = []

    for hs in range(NSL0):
        ld0 = p.ld
        dep = p.step

        def s_w(sync, hs=hs, dep=dep):
            sync.wait_ge(p.fl, dep)
            p.dma(sync, codg.rearrange("p (c n) -> p c n", n=512),
                  cg[:, hs * 512:(hs + 1) * 512].rearrange("(c p) n -> p c n", p=128))
            p.dma(sync, codu.rearrange("p (c n) -> p c n", n=512),
                  cu[:, hs * 512:(hs + 1) * 512].rearrange("(c p) n -> p c n", p=128))
            p.dma(sync, sgt, scg[:, hs * 512:(hs + 1) * 512])
            p.dma(sync, sut, scu[:, hs * 512:(hs + 1) * 512])
        block.sync(s_w)

        def f(v, ld0=ld0):
            v.wait_ge(p.ldh, ld0 + 64)
            v.tensor_copy(sgb[:, 0:512], sgt)
            return v.tensor_copy(sub[:, 0:512], sut)
        p.do(block.vector, f)

        for jc in range(8):
            def f(t, jc=jc):
                t.matmul(p.pD[:, 0:512], C["eall"][:, jc * 128:(jc + 1) * 128],
                         sgb[:, 0:512], start=True, stop=True)
                return t.matmul(p.pD[:, 512:1024], C["eall"][:, jc * 128:(jc + 1) * 128],
                                sub[:, 0:512], start=True, stop=True)
            p.do(block.tensor, f)

            def f(v):
                return v.tensor_copy(bsl[:, 0:1024], p.pD[:, 0:1024])
            p.do(block.vector, f)

            def f(v, jc=jc):
                emit_decode_tile(p, v, wide, codg[:, jc * 512:(jc + 1) * 512], bits,
                                 lambda s, jc=jc: wgd[:, (2 * jc + s) * 512:(2 * jc + s + 1) * 512],
                                 bsl[:, 0:512], 512)
                return emit_decode_tile(p, v, wide, codu[:, jc * 512:(jc + 1) * 512], bits,
                                        lambda s, jc=jc: wud[:, (2 * jc + s) * 512:(2 * jc + s + 1) * 512],
                                        bsl[:, 512:1024], 512)
            p.do(block.vector, f)

        if debug and hs == 0:
            def s_dw(sync):
                sync.wait_ge(p.fl, p.step)
                for pc2 in range(16):
                    p.dma_st(sync, dbg_wg[pc2 * 128:(pc2 + 1) * 128, :],
                             wgd[:, pc2 * 512:(pc2 + 1) * 512])
            block.sync(s_dw)

        for w in range(2):
            for s in range(2):
                for tp in range(2):
                    def f(t, hs=hs, w=w, s=s, tp=tp):
                        for half, (pg, pu) in enumerate(((p.pA, p.pB), (p.pC, p.pE))):
                            tt = tp * 2 + half
                            for pc in range(16):
                                t.matmul(pg[:, 0:512],
                                         wgd[:, pc * 512 + w * 256 + s: pc * 512 + (w + 1) * 256: 2],
                                         xP[:, pc * T + tt * 512:pc * T + (tt + 1) * 512],
                                         start=(pc == 0), stop=(pc == 15))
                            for pc in range(16):
                                mm = t.matmul(pu[:, 0:512],
                                              wud[:, pc * 512 + w * 256 + s: pc * 512 + (w + 1) * 256: 2],
                                              xP[:, pc * T + tt * 512:pc * T + (tt + 1) * 512],
                                              start=(pc == 0), stop=(pc == 15))
                        return mm
                    p.do(block.tensor, f)

                    def f(sc):
                        sc.activation(gsil[:, 0:512], p.pA[:, 0:512], AF.Gelu_apprx_tanh)
                        return sc.activation(gsil[:, 512:1024], p.pC[:, 0:512], AF.Gelu_apprx_tanh)
                    p.do(block.scalar, f)

                    k = len(it_st)

                    def f(v, k=k):
                        if k >= 2:
                            v.wait_ge(p.sth, it_st[k - 1])
                        v.tensor_tensor(itile[:, 0:512], gsil[:, 0:512], p.pB[:, 0:512], ALU.mult)
                        return v.tensor_tensor(itile[:, 512:1024], gsil[:, 512:1024], p.pE[:, 0:512], ALU.mult)
                    p.do(block.vector, f)

                    rbase = 128 * (4 * hs + 2 * w + s)

                    def s_i(sync, rbase=rbase, tp=tp):
                        sync.wait_ge(p.fl, p.step)
                        p.dma_st(sync, i_dram[rbase:rbase + 128, tp * 1024:tp * 1024 + 512],
                                 itile[:, 0:512])
                        p.dma_st(sync, i_dram[rbase:rbase + 128, tp * 1024 + 512:(tp + 1) * 1024],
                                 itile[:, 512:1024])
                    block.sync(s_i)
                    it_st.append(p.st)

    codd = bview(aW, 0, U8, 4 * 2048)
    widd = bview(aW, 16384, U16, 2048)
    bitd = bview(aW, 20480, U16, 2048)
    wdd = bview(aW, 24576, BF16, 2048)
    sdt = aM[0:8, 0:8192].bitcast(F32)
    sdb = aM[0:8, 8192:12288].bitcast(BF16)
    bdt = bview(aM, 16384, BF16, 2048)
    wd_st2 = []
    for ds in range(16):
        ld0 = p.ld
        dep = p.step

        def s_w(sync, ds=ds, dep=dep):
            sync.wait_ge(p.fl, dep)
            p.dma(sync, codd.rearrange("p (c o) -> p c o", o=2048),
                  cd[ds * 512:(ds + 1) * 512, :].rearrange("(c p) o -> p c o", p=128))
        block.sync(s_w)

        for jq in range(4):
            jcp = ds * 4 + jq
            def s_s(sync, jcp=jcp):
                sync.wait_ge(p.fl, p.step)
                p.dma(sync, sdt, scd[jcp * 8:(jcp + 1) * 8, :])
            block.sync(s_s)
            ld1 = p.ld

            def f(v, ld1=ld1):
                v.wait_ge(p.ldh, ld1)
                return v.tensor_copy(sdb[:, 0:2048], sdt)
            p.do(block.vector, f)

            def f(t):
                for q in range(4):
                    mm = t.matmul(p.pD[:, 0:512] if q == 0 else [p.pD[:, 512:1024], p.pE[:, 0:512], p.pF[:, 0:512]][q - 1],
                                  C["e8"][:, 0:128], sdb[:, q * 512:(q + 1) * 512],
                                  start=True, stop=True)
                return mm
            p.do(block.tensor, f)

            def f(v):
                v.tensor_copy(bdt[:, 0:512], p.pD[:, 0:512])
                v.tensor_copy(bdt[:, 512:1024], p.pD[:, 512:1024])
                v.tensor_copy(bdt[:, 1024:1536], p.pE[:, 0:512])
                return v.tensor_copy(bdt[:, 1536:2048], p.pF[:, 0:512])
            p.do(block.vector, f)

            for s in range(2):
                k2 = len(wd_st2)

                def f(v, jq=jq, s=s, k2=k2, ld0=ld0):
                    v.wait_ge(p.ldh, ld0 + 16)
                    if k2 >= 1:
                        v.wait_ge(p.sth, wd_st2[k2 - 1])
                    cod = codd[:, jq * 2048:(jq + 1) * 2048]
                    v.tensor_copy(widd[:, 0:2048], cod)
                    if s == 0:
                        v.tensor_scalar(bitd[:, 0:2048], widd[:, 0:2048], 7, 6, ALU.bitwise_and, ALU.logical_shift_left)
                        v.tensor_scalar(widd[:, 0:2048], widd[:, 0:2048], 8, 12, ALU.bitwise_and, ALU.logical_shift_left)
                    else:
                        v.tensor_scalar(bitd[:, 0:2048], widd[:, 0:2048], 0x70, 2, ALU.bitwise_and, ALU.logical_shift_left)
                        v.tensor_scalar(widd[:, 0:2048], widd[:, 0:2048], 0x80, 8, ALU.bitwise_and, ALU.logical_shift_left)
                    v.tensor_tensor(bitd[:, 0:2048], bitd[:, 0:2048], widd[:, 0:2048], ALU.bitwise_or)
                    return v.tensor_tensor(wdd[:, 0:2048], bitd[:, 0:2048].bitcast(BF16), bdt[:, 0:2048], ALU.mult)
                p.do(block.vector, f)

                rbase = 128 * (2 * jcp + s)

                def s_wd(sync, rbase=rbase):
                    sync.wait_ge(p.fl, p.step)
                    p.dma_st(sync, wd_dram[rbase:rbase + 128, :], wdd[:, 0:2048])
                block.sync(s_wd)
                wd_st2.append(p.st)

    if debug:
        def s_dbg(sync):
            sync.wait_ge(p.fl, p.step)
            for j in range(DC):
                p.dma_st(sync, dbg_xp[j * 128:(j + 1) * 128, :], xbf[:, j * T:(j + 1) * T])
        block.sync(s_dbg)

        def s_dbg2(sync):
            p.dma_st(sync, dbg_i[:, :], i_dram[0:512, :])
            p.dma_st(sync, dbg_wd[:, :], wd_dram[0:512, :])
        block.sync(s_dbg2)

    ibuf = xbf
    wcol = bview(aW, 0, BF16, 16 * 128)
    mtile = S.t("mtile", [128, 1024], F32)
    mt_st = []
    acc_prev = {}
    st_before = p.st
    for g in range(8):
        ld0 = p.ld
        dep = p.step

        def s_i2(sync, g=g, dep=dep, st_before=st_before):
            sync.wait_ge(p.fl, dep)
            sync.wait_ge(p.sth, st_before)
            p.dma(sync, ibuf[:].rearrange("p (c t) -> p c t", t=T),
                  i_dram[g * 2048:(g + 1) * 2048, :].rearrange("(c p) t -> p c t", p=128))
        block.sync(s_i2)

        dmm_steps = {}
        for oc in range(DC):
            ld1 = p.ld
            dep = p.step if oc < 2 else dmm_steps[oc - 2]
            wso = (oc % 2) * 4096

            def s_wc(sync, g=g, oc=oc, dep=dep, wso=wso):
                sync.wait_ge(p.fl, dep)
                p.dma(sync, bview(aW, wso, BF16, 16 * 128).rearrange("p (c o) -> p c o", o=128),
                      wd_dram[g * 2048:(g + 1) * 2048, oc * 128:(oc + 1) * 128]
                      .rearrange("(c p) o -> p c o", p=128))
            block.sync(s_wc)

            for tt in range(NTT):
                def f(t, g=g, oc=oc, tt=tt, ld0=ld0, ld1=ld1, wso=wso):
                    t.wait_ge(p.ldh, ld0 + 16)
                    t.wait_ge(p.ldh, ld1 + 16)
                    wv = bview(aW, wso, BF16, 16 * 128)
                    for c in range(16):
                        mm = t.matmul(p.pC[:, 0:512], wv[:, c * 128:(c + 1) * 128],
                                      ibuf[:, c * T + tt * 512:c * T + (tt + 1) * 512],
                                      start=(c == 0), stop=(c == 15))
                    if tt == NTT - 1:
                        dmm_steps[oc] = p.step + 1
                    return mm
                p.do(block.tensor, f)

                k = len(mt_st)
                slot = slice((k % 2) * 512, (k % 2) * 512 + 512)

                def f(v, k=k, slot=slot):
                    if k >= 2:
                        v.wait_ge(p.sth, mt_st[k - 2])
                    return v.tensor_copy(mtile[:, slot], p.pC[:, 0:512])
                p.do(block.vector, f)

                def s_m(gq, g=g, oc=oc, tt=tt, slot=slot):
                    gq.wait_ge(p.fl, p.step)
                    if g > 0:
                        gq.wait_ge(p.sth, acc_prev[(oc, tt)])
                    p.dma_st(gq, mix_dram[oc * 128:(oc + 1) * 128, tt * 512:(tt + 1) * 512],
                             mtile[:, slot], accum=(g > 0))
                block.gpsimd(s_m)
                mt_st.append(p.st)
                acc_prev[(oc, tt)] = p.st

    def f(g):
        g.wait_ge(p.sth, p.st)
        return g.engine_nop()
    p.do(block.gpsimd, f)
    if debug:
        def s_dm2(sync):
            p.dma_st(sync, dbg_mix[:, :], mix_dram[:, :])
        block.sync(s_dm2)
    emit_postln_residual(p, mix_dram, x_in, x_out, C, scratch, xP)
    emit_store_wait(p)
    return p


def prep_l0ffn_inputs(full):
    def codesT(w):
        return np.ascontiguousarray(_u8codes(np.asarray(w)).T)
    perm = np.zeros((128, 512), BF16NP)
    for s in range(2):
        for i in range(64):
            perm[2 * i + s, s * 256 + i] = 1
            perm[2 * i + s, s * 256 + 128 + 64 + i] = 1
    eall = np.zeros((64, 1024), BF16NP)
    for jc in range(8):
        for i in range(128):
            eall[(128 * jc + i) // 16, jc * 128 + i] = BF16NP(TWO126)
    e8 = np.zeros((8, 128), BF16NP)
    for i in range(128):
        e8[i // 16, i] = BF16NP(TWO126)
    shared = {
        "cg": codesT(full["L0_ffn_gate_W_packed"]),
        "cu": codesT(full["L0_ffn_up_W_packed"]),
        "cd": codesT(full["L0_ffn_down_W_packed"]),
        "scg": np.ascontiguousarray(np.asarray(full["L0_ffn_gate_scales"], np.float32).T),
        "scu": np.ascontiguousarray(np.asarray(full["L0_ffn_up_scales"], np.float32).T),
        "scd": np.ascontiguousarray(np.asarray(full["L0_ffn_down_scales"], np.float32).T),
        "cdiv": np.full((128, 1), 1.0 / 2048.0, BF16NP),
        "crow": np.ones((1, 128), np.float32),
        "ceall": eall, "ce8": e8, "cperm": perm,
    }
    return shared


def run_l0ffn(x_flat, full, debug=False):
    shared = prep_l0ffn_inputs(full)
    shards = shard_fm(x_flat)
    p = build_l0ffn(debug)
    in_maps = [dict(shared, x_in=s) for s in shards]
    res = run_bass_kernel_spmd(p.nc, in_maps, core_ids=list(range(8)))
    p.close()
    if debug:
        return res
    return unshard_fm([res.results[c]["x_out"] for c in range(8)])



def build_l0attn(debug=False, fuse=None):
    if fuse is None:
        p = Prog().start()
        own = True
    else:
        p, own = fuse["p"], False
    nc = p.nc
    pre = "" if own else fuse["pre"]
    if own:
        x_in = nc.dram_tensor("x_in", [D, T], F32, kind="ExternalInput").ap()
    else:
        x_in = fuse["x_in"]
    wqkvT = nc.dram_tensor("wqkvT", [D, 3 * D], F32, kind="ExternalInput").ap()
    woutT = nc.dram_tensor("woutT", [D, D], F32, kind="ExternalInput").ap()
    cbias = nc.dram_tensor("cbias", [128, 32], F32, kind="ExternalInput").ap()
    cbrow = nc.dram_tensor("cbrow", [1, D], F32, kind="ExternalInput").ap()
    cbout = nc.dram_tensor("cbout", [128, 16], F32, kind="ExternalInput").ap()
    if own:
        cdiv = nc.dram_tensor("cdiv", [128, 1], BF16, kind="ExternalInput").ap()
        crow = nc.dram_tensor("crow", [1, 128], F32, kind="ExternalInput").ap()
        cid = nc.dram_tensor("cid", [128, 128], BF16, kind="ExternalInput").ap()
    if own:
        x_out = nc.dram_tensor("x_out", [D, T], F32, kind="ExternalOutput").ap()
    else:
        x_out = fuse["x_out"]
    if debug:
        dbg_q = nc.dram_tensor("dbg_q", [D, T], BF16, kind="ExternalOutput").ap()
        dbg_s = nc.dram_tensor("dbg_s", [T, 1024], F32, kind="ExternalOutput").ap()
        dbg_o = nc.dram_tensor("dbg_o", [T, D], F32, kind="ExternalOutput").ap()
        dbg_mx = nc.dram_tensor("dbg_mx", [128, 48], F32, kind="ExternalOutput").ap()
        dbg_mix = nc.dram_tensor("dbg_mix", [D, T], F32, kind="ExternalOutput").ap()
        dbg_kv0 = nc.dram_tensor("dbg_kv0", [2048, 132], F32, kind="ExternalOutput").ap()
        dbg_qs0 = nc.dram_tensor("dbg_qs0", [128, 16], F32, kind="ExternalOutput").ap()
        dbg_c = nc.dram_tensor("dbg_c", [128, 1024], F32, kind="ExternalOutput").ap()
        dbg_c2 = nc.dram_tensor("dbg_c2", [128, 132], F32, kind="ExternalOutput").ap()
        dbg_kg = nc.dram_tensor("dbg_kg", [512, T], BF16, kind="ExternalOutput").ap()

    kdram = nc.dram_tensor(pre + "kdram", [D, T], BF16)
    vdram = nc.dram_tensor(pre + "vdram", [T, D], BF16)
    kg = nc.dram_tensor(pre + "kg", [4 * D, T], BF16)
    vg = nc.dram_tensor(pre + "vg", [4 * T, D], BF16)
    sdram = nc.dram_tensor(pre + "sdram", [T, 4 * T], F32)
    odram = nc.dram_tensor(pre + "odram", [T, D], F32)
    mix_dram = nc.dram_tensor(pre + "mix_dram", [D, T], F32)

    S = SAlloc(p, pre, fuse)
    if fuse is None:
        xbf = p.sb("xbf", [128, DC * T], BF16)
        qfm = p.sb("qfm", [128, DC * T], BF16)
    else:
        xbf = bview(fuse["P1"], 0, BF16, DC * T)
        qfm = bview(fuse["P2"], 0, BF16, DC * T)
    aW = S.t("aW", [128, 24576], U8)
    aM = S.t("aM", [128, 16384], U8)
    xf = S.t("xf", [128, 2 * T], F32)
    sq = S.t("sq", [128, 4096], U8)
    tmp = S.t("tmp", [128, 512], F32)
    mx = S.t("mx", [128, 16], F32)
    lsum = S.t("lsum", [128, 16], F32)
    rs1 = S.t("rs1", [128, 1], F32)
    bias_sb = S.t("bias_sb", [128, 32], F32)
    bout_sb = S.t("bout_sb", [128, 16], F32)

    mbc = bview(aM, 0, F32, T)
    rbc = bview(aM, 8192, F32, T)
    vbias = bview(aM, 0, F32, 2048)
    srow = qfm[0:1, 0:8192].bitcast(F32)
    rstd = qfm[0:1, 8192:12288].bitcast(F32)
    mf2 = bview(qfm, 32768, F32, 2 * T)
    scratch = dict(xf=xf, sq=sq, srow=srow, rstd=rstd, mbc=mbc, rbc=rbc,
                   tmp=tmp, mf2=mf2)

    C = {}
    if fuse is None:
        C["ones_col_div"] = p.sb("c_div", [128, 1], BF16)
        C["ones_row_f32"] = p.sb("c_row", [1, 128], F32)
        C["id128"] = p.sb("c_id", [128, 128], BF16)
    else:
        C["ones_col_div"] = fuse["C"]["ones_col_div"]
        C["ones_row_f32"] = fuse["C"]["ones_row_f32"]
        C["id128"] = fuse["C"]["id128"]
    block = p.block

    def s_consts(sync):
        if fuse is None:
            p.dma(sync, C["ones_col_div"][:], cdiv)
            p.dma(sync, C["ones_row_f32"][:], crow)
            p.dma(sync, C["id128"][:], cid)
        p.dma(sync, bias_sb[:], cbias)
        p.dma(sync, bout_sb[:], cbout)
    block.sync(s_consts)
    ld_consts = p.ld

    def f(v):
        v.wait_ge(p.ldh, ld_consts)
        return v.engine_nop()
    p.do(block.vector, f)

    emit_ln(p, x_in, xbf, C, scratch)

    vbrow = aW[0:1, 0:8192].bitcast(F32)

    def s_vb(sync):
        p.dma(sync, vbrow, cbrow)
    block.sync(s_vb)
    ld_vb = p.ld

    def f(t):
        t.wait_ge(p.ldh, ld_vb)
        for et in range(4):
            mm = t.matmul(p.pD[:, 0:512], C["ones_row_f32"][0:1, :],
                          vbrow[0:1, et * 512:(et + 1) * 512], start=True, stop=True)
        return mm
    for et in range(4):
        def f(t, et=et):
            if et == 0:
                t.wait_ge(p.ldh, ld_vb)
            return t.matmul(p.pD[:, 0:512], C["ones_row_f32"][0:1, :],
                            vbrow[0:1, et * 512:(et + 1) * 512], start=True, stop=True)
        p.do(block.tensor, f)

        def f(v, et=et):
            return v.tensor_copy(vbias[:, et * 512:(et + 1) * 512], p.pD[:, 0:512])
        p.do(block.vector, f)

    wsl = bview(aW, 8192, BF16, 16 * 512)
    ktile = S.t("ktile", [128, 1024], BF16)
    vtile = S.t("vtile", [128, 1024], BF16)
    k_st = []
    v_st = []

    for ec in range(32):
        dep = p.step

        def s_w(gq, ec=ec, dep=dep):
            gq.wait_ge(p.fl, dep)
            p.dma(gq, wsl[:, 0:16 * 128].rearrange("p (c o) -> p c o", o=128),
                  wqkvT[:, ec * 128:(ec + 1) * 128].rearrange("(c p) o -> p c o", p=128))
        block.gpsimd(s_w)
        ld0 = p.ld

        for tt in range(NTT):
            def f(t, ec=ec, tt=tt, ld0=ld0):
                t.wait_ge(p.ldh, ld0)
                for j in range(DC):
                    mm = t.matmul(p.pA[:, 0:512], wsl[:, j * 128:(j + 1) * 128],
                                  xbf[:, j * T + tt * 512:j * T + (tt + 1) * 512],
                                  start=(j == 0), stop=(j == DC - 1))
                return mm
            p.do(block.tensor, f)

            if ec < 16:
                def f(v, ec=ec, tt=tt):
                    return v.tensor_scalar(qfm[:, ec * T + tt * 512:ec * T + (tt + 1) * 512],
                                           p.pA[:, 0:512], bias_sb[:, ec:ec + 1], None, ALU.add)
                p.do(block.vector, f)
            else:
                kk = len(k_st)
                slot = slice((kk % 2) * 512, (kk % 2) * 512 + 512)

                def f(v, ec=ec, tt=tt, kk=kk, slot=slot):
                    if kk >= 2:
                        v.wait_ge(p.sth, k_st[kk - 2])
                    return v.tensor_scalar(ktile[:, slot], p.pA[:, 0:512],
                                           bias_sb[:, ec:ec + 1], None, ALU.add)
                p.do(block.vector, f)

                def s_k(sync, ec=ec, tt=tt, slot=slot):
                    sync.wait_ge(p.fl, p.step)
                    p.dma_st(sync, kdram[(ec - 16) * 128:(ec - 15) * 128, tt * 512:(tt + 1) * 512],
                             ktile[:, slot])
                block.sync(s_k)
                k_st.append(p.st)

    vsl = bview(aW, 8192, BF16, 16 * 512)
    for et in range(4):
        dep = p.step

        def s_w(gq, et=et, dep=dep):
            gq.wait_ge(p.fl, dep)
            p.dma(gq, vsl[:].rearrange("p (c o) -> p c o", o=512),
                  wqkvT[:, 2 * D + et * 512:2 * D + (et + 1) * 512].rearrange("(c p) o -> p c o", p=128))
        block.gpsimd(s_w)
        ld0 = p.ld

        for tc in range(DC):
            def f(t, et=et, tc=tc, ld0=ld0):
                t.wait_ge(p.ldh, ld0)
                for j in range(DC):
                    mm = t.matmul(p.pA[:, 0:512], xbf[:, j * T + tc * 128:j * T + tc * 128 + 128],
                                  vsl[:, j * 512:(j + 1) * 512],
                                  start=(j == 0), stop=(j == DC - 1))
                return mm
            p.do(block.tensor, f)

            kk = len(v_st)
            slot = slice((kk % 2) * 512, (kk % 2) * 512 + 512)

            def f(v, et=et, kk=kk, slot=slot):
                if kk >= 2:
                    v.wait_ge(p.sth, v_st[kk - 2])
                return v.tensor_tensor(vtile[:, slot], p.pA[:, 0:512],
                                       vbias[:, et * 512:(et + 1) * 512], ALU.add)
            p.do(block.vector, f)

            def s_v(sync, et=et, tc=tc, slot=slot):
                sync.wait_ge(p.fl, p.step)
                p.dma_st(sync, vdram[tc * 128:(tc + 1) * 128, et * 512:(et + 1) * 512],
                         vtile[:, slot])
            block.sync(s_v)
            v_st.append(p.st)

    st_kv = p.st

    def s_cc(gq):
        gq.wait_ge(p.fl, p.step)
        gq.wait_ge(p.sth, st_kv)
        gq.collective_compute("AllGather", ALU.bypass, replica_groups=RG4,
                              ins=[kdram.ap().opt()], outs=[kg.ap().opt()]).then_inc(p.cch)
        gq.collective_compute("AllGather", ALU.bypass, replica_groups=RG4,
                              ins=[vdram.ap().opt()], outs=[vg.ap().opt()]).then_inc(p.cch)
    block.gpsimd(s_cc)
    p.cc += 2
    cc_after = p.cc

    kslab = bview(xbf, 0, BF16, 16 * 512)
    kslab2 = bview(xbf, 16384, BF16, 16 * 512)
    stile = bview(xbf, 32768, F32, 1024)
    s_st = []

    def f(v):
        return v.memset(mx[:], -3.0e38)
    p.do(block.vector, f)

    for kt in range(16):
        dep = p.step
        ksl = [kslab, kslab2][kt % 2]

        def s_ks(sync, kt=kt, dep=dep, ksl=ksl):
            sync.wait_ge(p.fl, max(0, dep - 2))
            if kt == 0:
                sync.wait_ge(p.cch, cc_after)
            r, tl = kt // 4, kt % 4
            p.dma(sync, ksl[:].rearrange("p (c t) -> p c t", t=512),
                  kg[r * D:(r + 1) * D, tl * 512:(tl + 1) * 512].rearrange("(c p) t -> p c t", p=128))
        block.sync(s_ks)
        ld0 = p.ld

        for qc in range(16):
            def f(t, kt=kt, qc=qc, ld0=ld0, ksl=ksl):
                t.wait_ge(p.ldh, ld0)
                for ec in range(DC):
                    mm = t.matmul(p.pA[:, 0:512], qfm[:, ec * T + qc * 128:ec * T + qc * 128 + 128],
                                  ksl[:, ec * 512:(ec + 1) * 512],
                                  start=(ec == 0), stop=(ec == DC - 1))
                return mm
            p.do(block.tensor, f)

            kk = len(s_st)
            slot = slice((kk % 2) * 512, (kk % 2) * 512 + 512)

            def f(v, kk=kk, slot=slot, qc=qc):
                if kk >= 2:
                    v.wait_ge(p.sth, s_st[kk - 2])
                v.tensor_copy(stile[:, slot], p.pA[:, 0:512])
                return v.tensor_reduce(rs1[:], p.pA[:, 0:512], AX.X, ALU.max)
            p.do(block.vector, f)

            def f(v, qc=qc):
                return v.tensor_tensor(mx[:, qc:qc + 1], mx[:, qc:qc + 1], rs1[:], ALU.max)
            p.do(block.vector, f)

            def s_s(sync, kt=kt, qc=qc, slot=slot):
                sync.wait_ge(p.fl, p.step)
                p.dma_st(sync, sdram[qc * 128:(qc + 1) * 128, kt * 512:(kt + 1) * 512],
                         stile[:, slot])
            block.sync(s_s)
            s_st.append(p.st)

    SCALE0 = float(1.0 / np.sqrt(2048.0))
    bmax = S.t("bmax", [128, 16], F32)
    st_ph3 = p.st

    def f(v):
        v.wait_ge(p.sth, st_ph3)
        v.memset(lsum[:], 0.0)
        return v.tensor_scalar(bmax[:], mx[:], float(-SCALE0), None, ALU.mult)
    p.do(block.vector, f)

    vslab = bview(xbf, 0, BF16, 16 * T)
    sf = bview(aW, 0, F32, 1024)
    ptb = bview(aW, 8192, BF16, 16 * 128)
    ptile = bview(aW, 12288, BF16, 512)
    otile = bview(aW, 16384, F32, 1024)
    o_acc_prev = {}
    o_st = []
    sload = []

    for grp in range(4):
        dep = p.step

        def s_vs(sync, grp=grp, dep=dep):
            sync.wait_ge(p.fl, dep)
            p.dma(sync, vslab[:].rearrange("p (c e) -> p c e", e=D),
                  vg[grp * T:(grp + 1) * T, :].rearrange("(c p) e -> p c e", p=128))
        block.sync(s_vs)
        ld_v = p.ld

        for qc in range(16):
            for k4 in range(4):
                kt = grp * 4 + k4
                dep = p.step

                def s_sf(sync, kt=kt, qc=qc, dep=dep):
                    sync.wait_ge(p.fl, max(0, dep - 1))
                    p.dma(sync, sf[:, (kt % 2) * 512:(kt % 2) * 512 + 512],
                          sdram[qc * 128:(qc + 1) * 128, kt * 512:(kt + 1) * 512])
                block.sync(s_sf)
                ld0 = p.ld

                def f(sc, kt=kt, qc=qc, ld0=ld0):
                    sc.wait_ge(p.ldh, ld0)
                    return sc.activation(ptile[:, 0:512], sf[:, (kt % 2) * 512:(kt % 2) * 512 + 512],
                                         AF.Exp, bias=bmax[:, qc:qc + 1], scale=np.float32(SCALE0),
                                         accum_out=rs1[:])
                p.do(block.scalar, f)

                def f(v, qc=qc):
                    return v.tensor_tensor(lsum[:, qc:qc + 1], lsum[:, qc:qc + 1], rs1[:], ALU.add)
                p.do(block.vector, f)

                def f(t):
                    for i in range(4):
                        last = t.transpose(p.pT[:, i * 128:(i + 1) * 128],
                                           ptile[:, i * 128:(i + 1) * 128], C["id128"][:])
                    return last
                p.do(block.tensor, f)

                def f(v, k4=k4):
                    for i in range(4):
                        last = v.tensor_copy(ptb[:, (k4 * 4 + i) * 128:(k4 * 4 + i + 1) * 128],
                                             p.pT[:, i * 128:(i + 1) * 128])
                    return last
                p.do(block.vector, f)

            for et in range(4):
                def f(t, et=et, ld_v=ld_v):
                    t.wait_ge(p.ldh, ld_v)
                    for c in range(16):
                        mm = t.matmul(p.pB[:, 0:512], ptb[:, c * 128:(c + 1) * 128],
                                      vslab[:, c * D + et * 512:c * D + (et + 1) * 512],
                                      start=(c == 0), stop=(c == 15))
                    return mm
                p.do(block.tensor, f)

                kk = len(o_st)
                slot = slice((kk % 2) * 512, (kk % 2) * 512 + 512)

                def f(v, kk=kk, slot=slot):
                    if kk >= 2:
                        v.wait_ge(p.sth, o_st[kk - 2])
                    return v.tensor_copy(otile[:, slot], p.pB[:, 0:512])
                p.do(block.vector, f)

                def s_o(gq, grp=grp, qc=qc, et=et, slot=slot):
                    gq.wait_ge(p.fl, p.step)
                    if grp > 0:
                        gq.wait_ge(p.sth, o_acc_prev[(qc, et)])
                    p.dma_st(gq, odram[qc * 128:(qc + 1) * 128, et * 512:(et + 1) * 512],
                             otile[:, slot], accum=(grp > 0))
                block.gpsimd(s_o)
                o_st.append(p.st)
                o_acc_prev[(qc, et)] = p.st

    if debug:
        def s_d1(sync):
            sync.wait_ge(p.fl, p.step)
            sync.wait_ge(p.sth, p.st)
            for ec2 in range(DC):
                p.dma_st(sync, dbg_q[ec2 * 128:(ec2 + 1) * 128, :], qfm[:, ec2 * T:(ec2 + 1) * T])
            p.dma_st(sync, dbg_s[:, :], sdram[:, 0:1024])
            p.dma_st(sync, dbg_o[:, :], odram[:, :])
            p.dma_st(sync, dbg_mx[:, 0:16], mx[:])
            p.dma_st(sync, dbg_mx[:, 16:32], lsum[:])
            p.dma_st(sync, dbg_mx[:, 32:48], bmax[:])
            p.dma_st(sync, dbg_kg[:, :], kg[0:512, :])
        block.sync(s_d1)

    linv = S.t("linv", [128, 16], F32)

    def f(v):
        return v.reciprocal(linv[:], lsum[:])
    p.do(block.vector, f)

    ctxT = bview(xbf, 0, BF16, DC * T)
    obuf = bview(aW, 0, F32, 2 * T)
    otm = bview(aW, 16384, BF16, T)
    st_o_done = p.st
    for qc in range(16):
        dep = p.step

        def s_ob(sync, qc=qc, dep=dep):
            sync.wait_ge(p.fl, max(0, dep - 1))
            if qc == 0:
                sync.wait_ge(p.sth, st_o_done)
            p.dma(sync, obuf[:, (qc % 2) * T:(qc % 2 + 1) * T], odram[qc * 128:(qc + 1) * 128, :])
        block.sync(s_ob)
        ld0 = p.ld

        def f(v, qc=qc, ld0=ld0):
            v.wait_ge(p.ldh, ld0)
            return v.tensor_scalar(otm[:, 0:T], obuf[:, (qc % 2) * T:(qc % 2 + 1) * T],
                                   linv[:, qc:qc + 1], None, ALU.mult)
        p.do(block.vector, f)

        for eg in range(2):
            def f(t, eg=eg):
                for i in range(8):
                    last = t.transpose(p.pT[:, i * 128:(i + 1) * 128],
                                       otm[:, (eg * 8 + i) * 128:(eg * 8 + i + 1) * 128],
                                       C["id128"][:])
                return last
            p.do(block.tensor, f)

            def f(v, qc=qc, eg=eg):
                for i in range(8):
                    ec = eg * 8 + i
                    last = v.tensor_copy(ctxT[:, ec * T + qc * 128:ec * T + qc * 128 + 128],
                                         p.pT[:, i * 128:(i + 1) * 128])
                return last
            p.do(block.vector, f)

    wosl = bview(aW, 0, BF16, 16 * 128)
    mtile = S.t("mtile", [128, 1024], F32)
    mt_st = []
    for oc in range(16):
        dep = p.step

        def s_w(gq, oc=oc, dep=dep):
            gq.wait_ge(p.fl, dep)
            p.dma(gq, wosl[:].rearrange("p (c o) -> p c o", o=128),
                  woutT[:, oc * 128:(oc + 1) * 128].rearrange("(c p) o -> p c o", p=128))
        block.gpsimd(s_w)
        ld0 = p.ld

        for tt in range(NTT):
            def f(t, oc=oc, tt=tt, ld0=ld0):
                t.wait_ge(p.ldh, ld0)
                for ec in range(DC):
                    mm = t.matmul(p.pC[:, 0:512], wosl[:, ec * 128:(ec + 1) * 128],
                                  ctxT[:, ec * T + tt * 512:ec * T + (tt + 1) * 512],
                                  start=(ec == 0), stop=(ec == DC - 1))
                return mm
            p.do(block.tensor, f)

            kk = len(mt_st)
            slot = slice((kk % 2) * 512, (kk % 2) * 512 + 512)

            def f(v, oc=oc, kk=kk, slot=slot):
                if kk >= 2:
                    v.wait_ge(p.sth, mt_st[kk - 2])
                return v.tensor_scalar(mtile[:, slot], p.pC[:, 0:512],
                                       bout_sb[:, oc:oc + 1], None, ALU.add)
            p.do(block.vector, f)

            def s_m(sync, oc=oc, tt=tt, slot=slot):
                sync.wait_ge(p.fl, p.step)
                p.dma_st(sync, mix_dram[oc * 128:(oc + 1) * 128, tt * 512:(tt + 1) * 512],
                         mtile[:, slot])
            block.sync(s_m)
            mt_st.append(p.st)

    def f(g):
        g.wait_ge(p.sth, p.st)
        return g.engine_nop()
    p.do(block.gpsimd, f)
    if debug:
        def s_d2(sync):
            p.dma_st(sync, dbg_mix[:, :], mix_dram[:, :])
        block.sync(s_d2)
    emit_postln_residual(p, mix_dram, x_in, x_out, C, scratch, xbf)
    emit_store_wait(p)
    return p


def prep_l0attn_inputs(full):
    wqkv = np.asarray(full["L0_attn_w_qkv"], np.float32)
    wout = np.asarray(full["L0_attn_w_out"], np.float32)
    bqkv = np.asarray(full["L0_attn_b_qkv"], np.float32)
    bout = np.asarray(full["L0_attn_b_out"], np.float32)
    shared = {
        "wqkvT": np.ascontiguousarray(wqkv.T),
        "woutT": np.ascontiguousarray(wout.T),
        "cbias": np.ascontiguousarray(bqkv[0:4096].reshape(32, 128).T),
        "cbrow": bqkv[4096:6144].reshape(1, 2048),
        "cbout": np.ascontiguousarray(bout.reshape(16, 128).T),
        "cdiv": np.full((128, 1), 1.0 / 2048.0, BF16NP),
        "crow": np.ones((1, 128), np.float32),
        "cid": np.eye(128, dtype=BF16NP),
    }
    return shared


def run_l0attn(x_flat, full, debug=False):
    shared = prep_l0attn_inputs(full)
    shards = shard_fm(x_flat)
    p = build_l0attn(debug)
    in_maps = [dict(shared, x_in=s) for s in shards]
    res = run_bass_kernel_spmd(p.nc, in_maps, core_ids=list(range(8)))
    p.close()
    if debug:
        return res
    return unshard_fm([res.results[c]["x_out"] for c in range(8)])



def build_l1attn(debug=False, fuse=None):
    if fuse is None:
        p = Prog().start()
        own = True
    else:
        p, own = fuse["p"], False
    nc = p.nc
    pre = "" if own else fuse["pre"]
    if own:
        x_in = nc.dram_tensor("x_in", [D, T], F32, kind="ExternalInput").ap()
    else:
        x_in = fuse["x_in"]
    wqT = nc.dram_tensor("wqT", [D, D], F32, kind="ExternalInput").ap()
    wkT = nc.dram_tensor("wkT", [D, D], F32, kind="ExternalInput").ap()
    wvT = nc.dram_tensor("wvT", [D, D], F32, kind="ExternalInput").ap()
    woT = nc.dram_tensor("woT", [D, D], F32, kind="ExternalInput").ap()
    memaug = nc.dram_tensor("memaug", [64 * 128, 132], F32, kind="ExternalInput").ap()
    mnormT = nc.dram_tensor("mnormT", [128, 64], F32, kind="ExternalInput").ap()
    gatein = nc.dram_tensor("gatein", [1, 16], F32, kind="ExternalInput").ap()
    maskpre = nc.dram_tensor("maskpre", [128, 4], F32, kind="ExternalInput").ap()
    ctril = nc.dram_tensor("ctril", [128, 128], BF16, kind="ExternalInput").ap()
    if own:
        cdiv = nc.dram_tensor("cdiv", [128, 1], BF16, kind="ExternalInput").ap()
        crow = nc.dram_tensor("crow", [1, 128], F32, kind="ExternalInput").ap()
        cid = nc.dram_tensor("cid", [128, 128], BF16, kind="ExternalInput").ap()
    conebf = nc.dram_tensor("conebf", [128, 1], BF16, kind="ExternalInput").ap()
    crelrow = nc.dram_tensor("crelrow", [128, 1], F32, kind="ExternalInput").ap()
    if own:
        x_out = nc.dram_tensor("x_out", [D, T], F32, kind="ExternalOutput").ap()
    else:
        x_out = fuse["x_out"]
    if debug:
        dbg_sk = nc.dram_tensor("dbg_sk", [T, D], BF16, kind="ExternalOutput").ap()
        dbg_v = nc.dram_tensor("dbg_v", [T, D], BF16, kind="ExternalOutput").ap()
        dbg_kvg = nc.dram_tensor("dbg_kvg", [4 * 2048, 132], F32, kind="ExternalOutput").ap()
        dbg_qs = nc.dram_tensor("dbg_qs", [128, 16], F32, kind="ExternalOutput").ap()
        dbg_w = nc.dram_tensor("dbg_w", [128, 64 + 16 + 132], F32, kind="ExternalOutput").ap()
        dbg_mix = nc.dram_tensor("dbg_mix", [D, T], F32, kind="ExternalOutput").ap()
        dbg_kv0 = nc.dram_tensor("dbg_kv0", [2048, 132], F32, kind="ExternalOutput").ap()
        dbg_qs0 = nc.dram_tensor("dbg_qs0", [128, 16], F32, kind="ExternalOutput").ap()
        dbg_c = nc.dram_tensor("dbg_c", [128, 1024], F32, kind="ExternalOutput").ap()
        dbg_c2 = nc.dram_tensor("dbg_c2", [128, 132], F32, kind="ExternalOutput").ap()

    sk_dram = nc.dram_tensor(pre + "sk_dram", [T, D], BF16)
    v_dram = nc.dram_tensor(pre + "v_dram", [T, D], BF16)
    kv_dram = nc.dram_tensor(pre + "kv_dram", [16 * 128, 132], F32)
    kvg_dram = nc.dram_tensor(pre + "kvg_dram", [4 * 16 * 128, 132], F32)
    qs_dram = nc.dram_tensor(pre + "qs_dram", [128, 16], F32)
    qsg_dram = nc.dram_tensor(pre + "qsg_dram", [128, 16], F32)
    w_dram = nc.dram_tensor(pre + "w_dram", [16, 4], F32)
    mix_dram = nc.dram_tensor(pre + "mix_dram", [D, T], F32)

    S = SAlloc(p, pre, fuse)
    if fuse is None:
        xbf = p.sb("xbf", [128, DC * T], BF16)
        aG = p.sb("aG", [128, 65536], U8)
    else:
        xbf = bview(fuse["P1"], 0, BF16, DC * T)
        aG = bview(fuse["P2"], 0, U8, 65536)
    aM = S.t("aM", [128, 16384], U8)
    xf = S.t("xf", [128, 2 * T], F32)
    sq = S.t("sq", [128, 4096], U8)
    tmp = S.t("tmp", [128, 512], F32)
    memb = S.t("memb", [128, 64 * 132], BF16)
    spv = S.t("spv", [128, 16 * 132], F32)
    stbf = S.t("stbf", [128, 132], BF16)
    mtb = S.t("mtb", [128, 132], BF16)
    sqt = S.t("sqt", [128, 128], BF16)
    wsl = S.t("wsl", [128, 2 * 16 * 128], BF16)
    qsum = S.t("qsum", [128, 16], F32)
    xsb = S.t("xsb", [128, 16], BF16)
    wcol = S.t("wcol", [128, 64], F32)
    gcol = S.t("gcol", [128, 16], F32)
    relc = S.t("relc", [16, 8], F32)
    rs1 = S.t("rs1", [128, 1], F32)
    accmem = S.t("accmem", [128, 128], F32)
    rs2 = S.t("rs2", [128, 1], F32)
    rs4a = S.t("rs4a", [128, 4], F32)
    rs4b = S.t("rs4b", [128, 4], F32)
    oloc = S.t("oloc", [128, 128], F32)
    t128 = S.t("t128", [128, 128], F32)
    sktm = S.t("sktm", [128, 128], BF16)
    stm = S.t("stm", [128, 128], BF16)
    otm = S.t("otm", [128, 128], BF16)

    kq_buf = bview(aG, 0, BF16, 16 * 512)
    skt_buf = bview(aG, 16384, BF16, 4 * 2048)
    vg_buf = bview(aG, 32768, BF16, 4 * 2048)
    ofm_buf = bview(aG, 49152, BF16, 16 * 512)
    mf2 = bview(aG, 0, F32, 2 * T)
    srow = aG[0:1, 16384:32768].bitcast(F32)
    rstd = aG[0:1, 32768:40960].bitcast(F32)
    mbc = bview(aM, 0, F32, T)
    rbc = bview(aM, 8192, F32, T)
    scratch = dict(xf=xf, sq=sq, srow=srow, rstd=rstd, mbc=mbc, rbc=rbc,
                   tmp=tmp, mf2=mf2)

    C = {}
    if fuse is None:
        C["ones_col_div"] = p.sb("c_div", [128, 1], BF16)
        C["ones_row_f32"] = p.sb("c_row", [1, 128], F32)
        C["id128"] = p.sb("c_id", [128, 128], BF16)
    else:
        C["ones_col_div"] = fuse["C"]["ones_col_div"]
        C["ones_row_f32"] = fuse["C"]["ones_row_f32"]
        C["id128"] = fuse["C"]["id128"]
    C["onebf"] = S.t("c_onebf", [128, 1], BF16)
    C["relrow"] = S.t("c_relrow", [128, 1], F32)
    C["tril"] = S.t("c_tril", [128, 128], BF16)
    C["maskpre"] = S.t("c_maskpre", [128, 4], F32)
    mnorm_sb = S.t("mnorm_sb", [128, 64], F32)
    grow = S.t("grow", [1, 16], F32)
    block = p.block

    def s_consts(sync):
        if fuse is None:
            p.dma(sync, C["ones_col_div"][:], cdiv)
            p.dma(sync, C["ones_row_f32"][:], crow)
            p.dma(sync, C["id128"][:], cid)
        p.dma(sync, C["onebf"][:], conebf)
        p.dma(sync, C["relrow"][:], crelrow)
        p.dma(sync, C["tril"][:], ctril)
        p.dma(sync, C["maskpre"][:], maskpre)
        p.dma(sync, mnorm_sb[:], mnormT)
        p.dma(sync, grow[:], gatein)
    block.sync(s_consts)

    def s_memb(gq):
        p.dma(gq, memb[:].rearrange("p (c e) -> p c e", e=132),
              memaug.rearrange("(c p) e -> p c e", p=128))
    block.gpsimd(s_memb)
    ld_consts = p.ld

    def f(v):
        v.wait_ge(p.ldh, ld_consts)
        return v.engine_nop()
    p.do(block.vector, f)

    emit_ln(p, x_in, xbf, C, scratch)

    def f(v):
        for j in range(DC):
            last = v.tensor_reduce(qsum[:, j:j + 1], xbf[:, j * T:(j + 1) * T], AX.X, ALU.add)
        return last
    p.do(block.vector, f)

    def f(v):
        return v.tensor_copy(xsb[:], qsum[:])
    p.do(block.vector, f)

    mm_steps = {}
    for ec in range(DC):
        dep = p.step if ec < 2 else mm_steps[ec - 2]
        so = (ec % 2) * 2048

        def s_w(gq, ec=ec, dep=dep, so=so):
            gq.wait_ge(p.fl, dep)
            p.dma(gq, wsl[:, so:so + 2048].rearrange("p (c o) -> p c o", o=128),
                  wqT[:, ec * 128:(ec + 1) * 128].rearrange("(c p) o -> p c o", p=128))
        block.gpsimd(s_w)
        ld0 = p.ld

        def f(t, ec=ec, ld0=ld0, so=so):
            t.wait_ge(p.ldh, ld0)
            for j in range(DC):
                mm = t.matmul(p.pA[:, 0:1], wsl[:, so + j * 128:so + (j + 1) * 128], xsb[:, j:j + 1],
                              start=(j == 0), stop=(j == DC - 1))
            return mm
        mm_steps[ec] = p.do(block.tensor, f)

        def f(v, ec=ec):
            return v.tensor_copy(qsum[:, ec:ec + 1], p.pA[:, 0:1])
        p.do(block.vector, f)

    def s_q(sync):
        sync.wait_ge(p.fl, p.step)
        p.dma_st(sync, qs_dram.ap(), qsum[:])
    block.sync(s_q)
    st_q = p.st

    def s_cc1(gq):
        gq.wait_ge(p.sth, st_q)
        gq.collective_compute("AllReduce", ALU.add, replica_groups=RG4,
                              ins=[qs_dram.ap().opt()], outs=[qsg_dram.ap().opt()]).then_inc(p.cch)
    block.gpsimd(s_cc1)
    p.cc += 1

    def s_qg(sync):
        sync.wait_ge(p.cch, p.cc)
        p.dma(sync, qsum[:], qsg_dram.ap())
    block.sync(s_qg)
    ld_qg = p.ld

    for m in range(4):
        def f(v, m=m, ld_qg=ld_qg):
            v.wait_ge(p.ldh, ld_qg)
            return v.tensor_tensor(t128[:, 0:16], qsum[:], mnorm_sb[:, m * 16:(m + 1) * 16], ALU.mult)
        p.do(block.vector, f)

        def f(t, m=m):
            t.matmul(p.pA[0:1, 0:16], C["relrow"][:, 0:1], t128[:, 0:16], start=True, stop=True)
            return t.transpose(p.pB[0:16, 0:1], p.pA[0:1, 0:16], C["ones_row_f32"][0:1, 0:1])
        break
    relrow_sb = S.t("relrow_sb", [1, 16], F32)
    for m in range(4):
        def f(v, m=m, ld_qg=ld_qg):
            v.wait_ge(p.ldh, ld_qg)
            return v.tensor_tensor(t128[:, 0:16], qsum[:], mnorm_sb[:, m * 16:(m + 1) * 16], ALU.mult)
        p.do(block.vector, f)

        def f(t, m=m):
            return t.matmul(p.pA[0:1, 0:16], C["relrow"][:, 0:1], t128[:, 0:16], start=True, stop=True)
        p.do(block.tensor, f)

        def f(v, m=m):
            return v.tensor_copy(relrow_sb[0:1, :], p.pA[0:1, 0:16])
        p.do(block.vector, f)

        def f(t, m=m):
            return t.transpose(p.pB[0:16, 0:1], relrow_sb[0:1, 0:16], C["ones_row_f32"][0:1, 0:1])
        p.do(block.tensor, f)

        def f(v, m=m):
            return v.tensor_copy(relc[0:16, m:m + 1], p.pB[0:16, 0:1])
        p.do(block.vector, f)

    def f(v):
        v.tensor_reduce(relc[0:16, 4:5], relc[0:16, 0:4], AX.X, ALU.max, negate=True)
        return v.engine_nop()
    p.do(block.vector, f)

    def f(sc):
        return sc.activation(relc[0:16, 0:4], relc[0:16, 0:4], AF.Exp, bias=relc[0:16, 4:5],
                             scale=np.float32(1.0))
    p.do(block.scalar, f)

    def f(v):
        return v.tensor_reduce(relc[0:16, 5:6], relc[0:16, 0:4], AX.X, ALU.add)
    p.do(block.vector, f)

    def f(v):
        return v.reciprocal(relc[0:16, 6:7], relc[0:16, 5:6])
    p.do(block.vector, f)

    def f(v):
        return v.tensor_scalar(relc[0:16, 0:4], relc[0:16, 0:4], relc[0:16, 6:7], None, ALU.mult)
    p.do(block.vector, f)

    def s_wd(sync):
        sync.wait_ge(p.fl, p.step)
        p.dma_st(sync, w_dram.ap(), relc[0:16, 0:4])
    block.sync(s_wd)
    st_w = p.st

    def s_wl(sync):
        sync.wait_ge(p.sth, st_w)
        p.dma(sync, t128[0:1, 0:64].bitcast(F32), w_dram.ap().rearrange("(o h) m -> o (h m)", o=1))
    block.sync(s_wl)
    ld_w = p.ld

    def f(t, ld_w=ld_w):
        t.wait_ge(p.ldh, ld_w)
        return t.matmul(p.pA[:, 0:64], C["ones_row_f32"][0:1, :], t128[0:1, 0:64],
                        start=True, stop=True)
    p.do(block.tensor, f)

    def f(v):
        return v.tensor_copy(wcol[:], p.pA[:, 0:64])
    p.do(block.vector, f)

    def f(sc):
        return sc.activation(grow[0:1, :], grow[0:1, :], AF.Sigmoid)
    p.do(block.scalar, f)

    def f(t):
        return t.matmul(p.pA[:, 0:16], C["ones_row_f32"][0:1, :], grow[0:1, 0:16],
                        start=True, stop=True)
    p.do(block.tensor, f)

    def f(v):
        return v.tensor_copy(gcol[:], p.pA[:, 0:16])
    p.do(block.vector, f)

    def f(v):
        return v.memset(t128[:], 0.0)
    p.do(block.vector, f)

    totkv = spv

    def f(v):
        return v.memset(totkv[:], 0.0)
    p.do(block.vector, f)

    sct = S.t("sct", [128, 512], F32)
    sp_st = []

    def emit_sigma(v, buf, ec):
        sl = slice(ec * 512, (ec + 1) * 512)
        v.tensor_scalar(sct[:, 0:512], buf[:, sl], 0.0, None, ALU.min)
        return sl

    for g in range(4):
        kmm_steps = {}
        for ec in range(DC):
            dep = p.step if ec < 2 else kmm_steps[ec - 2]
            so = (ec % 2) * 2048

            def s_w(gq, g=g, ec=ec, dep=dep, so=so):
                gq.wait_ge(p.fl, dep)
                p.dma(gq, wsl[:, so:so + 2048].rearrange("p (c o) -> p c o", o=128),
                      wkT[:, ec * 128:(ec + 1) * 128].rearrange("(c p) o -> p c o", p=128))
            block.gpsimd(s_w)
            ld0 = p.ld

            def f(t, g=g, ec=ec, ld0=ld0, so=so):
                t.wait_ge(p.ldh, ld0)
                for j in range(DC):
                    mm = t.matmul(p.pA[:, 0:512], wsl[:, so + j * 128:so + (j + 1) * 128],
                                  xbf[:, j * T + g * 512:j * T + (g + 1) * 512],
                                  start=(j == 0), stop=(j == DC - 1))
                return mm
            kmm_steps[ec] = p.do(block.tensor, f)

            def f(v, ec=ec):
                return v.tensor_scalar(sct[:, 0:512], p.pA[:, 0:512], 0.0, None, ALU.min)
            p.do(block.vector, f)

            def f(sc, ec=ec):
                return sc.activation(sct[:, 0:512], sct[:, 0:512], AF.Exp)
            p.do(block.scalar, f)

            def f(v, ec=ec):
                v.tensor_scalar(kq_buf[:, ec * 512:(ec + 1) * 512], p.pA[:, 0:512], 0.0, None, ALU.max)
                return v.tensor_tensor(kq_buf[:, ec * 512:(ec + 1) * 512],
                                       kq_buf[:, ec * 512:(ec + 1) * 512], sct[:, 0:512], ALU.add)
            p.do(block.vector, f)

        for et in range(4):
            dep = p.step

            def s_w(gq, g=g, et=et, dep=dep):
                gq.wait_ge(p.fl, dep)
                p.dma(gq, bview(aG, 49152, BF16, 16 * 512)[:].rearrange("p (c o) -> p c o", o=512),
                      wvT[:, et * 512:(et + 1) * 512].rearrange("(c p) o -> p c o", p=128))
            block.gpsimd(s_w)
            ld0 = p.ld
            vsl = bview(aG, 49152, BF16, 16 * 512)

            for tc in range(4):
                def f(t, g=g, et=et, tc=tc, ld0=ld0, vsl=vsl):
                    t.wait_ge(p.ldh, ld0)
                    for j in range(DC):
                        mm = t.matmul(p.pB[:, 0:512],
                                      xbf[:, j * T + g * 512 + tc * 128:j * T + g * 512 + tc * 128 + 128],
                                      vsl[:, j * 512:(j + 1) * 512],
                                      start=(j == 0), stop=(j == DC - 1))
                    return mm
                p.do(block.tensor, f)

                def f(v, tc=tc, et=et):
                    if tc == 0 and et == 0 and sp_st:
                        v.wait_ge(p.sth, sp_st[-1])
                    return v.tensor_copy(vg_buf[:, tc * 2048 + et * 512:tc * 2048 + (et + 1) * 512],
                                         p.pB[:, 0:512])
                p.do(block.vector, f)

        for tc in range(4):
            for hg in range(2):
                def f(t, tc=tc, hg=hg):
                    for i in range(8):
                        h2 = hg * 8 + i
                        last = t.transpose(p.pT[:, i * 128:(i + 1) * 128],
                                           kq_buf[:, h2 * 512 + tc * 128:h2 * 512 + tc * 128 + 128],
                                           C["id128"][:])
                    return last
                p.do(block.tensor, f)

                def f(v, tc=tc, hg=hg):
                    if tc == 0 and hg == 0 and sp_st:
                        v.wait_ge(p.sth, sp_st[-1])
                    for i in range(8):
                        h2 = hg * 8 + i
                        last = v.tensor_copy(skt_buf[:, tc * 2048 + h2 * 128:tc * 2048 + (h2 + 1) * 128],
                                             p.pT[:, i * 128:(i + 1) * 128])
                    return last
                p.do(block.vector, f)

        def s_sp(sync, g=g):
            sync.wait_ge(p.fl, p.step)
            for tc in range(4):
                p.dma_st(sync, sk_dram[(g * 4 + tc) * 128:(g * 4 + tc + 1) * 128, :],
                         skt_buf[:, tc * 2048:(tc + 1) * 2048])
                p.dma_st(sync, v_dram[(g * 4 + tc) * 128:(g * 4 + tc + 1) * 128, :],
                         vg_buf[:, tc * 2048:(tc + 1) * 2048])
        block.sync(s_sp)
        sp_st.append(p.st)

        for h in range(DC):
            def f(t, g=g, h=h):
                for tc in range(4):
                    t.matmul(p.pD[:, 0:128], skt_buf[:, tc * 2048 + h * 128:tc * 2048 + (h + 1) * 128],
                             vg_buf[:, tc * 2048 + h * 128:tc * 2048 + (h + 1) * 128],
                             start=(tc == 0), stop=False)
                    mm = t.matmul(p.pD[:, 128:129], skt_buf[:, tc * 2048 + h * 128:tc * 2048 + (h + 1) * 128],
                                  C["onebf"][:, 0:1], start=False, stop=(tc == 3))
                return mm
            p.do(block.tensor, f)

            def f(v, h=h):
                return v.tensor_tensor(totkv[:, h * 132:h * 132 + 129], totkv[:, h * 132:h * 132 + 129],
                                       p.pD[:, 0:129], ALU.add)
            p.do(block.vector, f)

    def s_kv(sync):
        sync.wait_ge(p.fl, p.step)
        p.dma_st(sync, kv_dram.ap().rearrange("(h p) c -> p h c", p=128),
                 totkv[:].rearrange("p (h c) -> p h c", c=132))
    block.sync(s_kv)
    st_kv = p.st

    def s_cc2(gq):
        gq.wait_ge(p.sth, st_kv)
        gq.collective_compute("AllGather", ALU.bypass, replica_groups=RG4,
                              ins=[kv_dram.ap().opt()], outs=[kvg_dram.ap().opt()]).then_inc(p.cch)
    block.gpsimd(s_cc2)
    p.cc += 1

    def f(v):
        v.wait_ge(p.sth, st_kv)
        v.wait_ge(p.cch, p.cc)
        return v.memset(spv[:], 0.0)
    p.do(block.vector, f)

    rbuf = bview(aG, 49152, F32, 16 * 132)
    for r in range(4):
        dep = p.step

        def s_r(sync, r=r, dep=dep):
            sync.wait_ge(p.fl, dep)
            if r == 0:
                sync.wait_ge(p.cch, p.cc)
            p.dma(sync, rbuf[:].rearrange("p (h c) -> p h c", c=132),
                  kvg_dram[r * 2048:(r + 1) * 2048, :].rearrange("(h p) c -> p h c", p=128))
        block.sync(s_r)
        ld0 = p.ld

        def f(v, r=r, ld0=ld0):
            v.wait_ge(p.ldh, ld0)
            return v.tensor_scalar(rbuf[:], rbuf[:], C["maskpre"][:, r:r + 1], None, ALU.mult)
        p.do(block.vector, f)

        def f(v, r=r):
            return v.tensor_tensor(spv[:], spv[:], rbuf[:], ALU.add)
        p.do(block.vector, f)

    if debug:
        def s_dbg(sync):
            sync.wait_ge(p.fl, p.step)
            sync.wait_ge(p.sth, p.st)
            p.dma_st(sync, dbg_sk[:, :], sk_dram.ap())
            p.dma_st(sync, dbg_v[:, :], v_dram.ap())
            p.dma_st(sync, dbg_kvg[:, :], kvg_dram.ap())
            p.dma_st(sync, dbg_qs[:, :], qsum[:])
            p.dma_st(sync, dbg_w[:, 0:64], wcol[:])
            p.dma_st(sync, dbg_w[:, 64:80], gcol[:])
            p.dma_st(sync, dbg_w[:, 80:212], spv[:, 0:132])
            p.dma_st(sync, dbg_kv0[:, :], kv_dram.ap())
            p.dma_st(sync, dbg_qs0[:, :], qs_dram.ap())
        block.sync(s_dbg)

    EPS1 = 1e-6
    for g in range(4):
        qmm_steps = {}
        for ec in range(DC):
            dep = p.step if ec < 2 else qmm_steps[ec - 2]
            so = (ec % 2) * 2048

            def s_w(gq, g=g, ec=ec, dep=dep, so=so):
                gq.wait_ge(p.fl, dep)
                p.dma(gq, wsl[:, so:so + 2048].rearrange("p (c o) -> p c o", o=128),
                      wqT[:, ec * 128:(ec + 1) * 128].rearrange("(c p) o -> p c o", p=128))
            block.gpsimd(s_w)
            ld0 = p.ld

            def f(t, g=g, ec=ec, ld0=ld0, so=so):
                t.wait_ge(p.ldh, ld0)
                for j in range(DC):
                    mm = t.matmul(p.pA[:, 0:512], wsl[:, so + j * 128:so + (j + 1) * 128],
                                  xbf[:, j * T + g * 512:j * T + (g + 1) * 512],
                                  start=(j == 0), stop=(j == DC - 1))
                return mm
            qmm_steps[ec] = p.do(block.tensor, f)

            def f(v, ec=ec):
                return v.tensor_scalar(sct[:, 0:512], p.pA[:, 0:512], 0.0, None, ALU.min)
            p.do(block.vector, f)

            def f(sc, ec=ec):
                return sc.activation(sct[:, 0:512], sct[:, 0:512], AF.Exp)
            p.do(block.scalar, f)

            def f(v, ec=ec):
                v.tensor_scalar(kq_buf[:, ec * 512:(ec + 1) * 512], p.pA[:, 0:512], 0.0, None, ALU.max)
                return v.tensor_tensor(kq_buf[:, ec * 512:(ec + 1) * 512],
                                       kq_buf[:, ec * 512:(ec + 1) * 512], sct[:, 0:512], ALU.add)
            p.do(block.vector, f)

        dep = p.step

        def s_rl(sync, g=g, dep=dep):
            sync.wait_ge(p.fl, dep)
            if sp_st:
                sync.wait_ge(p.sth, sp_st[-1])
            p.dma(sync, skt_buf[:].rearrange("p (c e) -> p c e", e=2048),
                  sk_dram[g * 512:(g + 1) * 512, :].rearrange("(c p) e -> p c e", p=128))
            p.dma(sync, vg_buf[:].rearrange("p (c e) -> p c e", e=2048),
                  v_dram[g * 512:(g + 1) * 512, :].rearrange("(c p) e -> p c e", p=128))
        block.sync(s_rl)
        ld_rl = p.ld

        for tc in range(4):
            for h in range(DC):
                skt_sl = skt_buf[:, tc * 2048 + h * 128:tc * 2048 + (h + 1) * 128]
                vg_sl = vg_buf[:, tc * 2048 + h * 128:tc * 2048 + (h + 1) * 128]
                sq_sl = kq_buf[:, h * 512 + tc * 128:h * 512 + tc * 128 + 128]

                def f(v, h=h):
                    return v.tensor_copy(stbf[:, 0:132], spv[:, h * 132:(h + 1) * 132])
                p.do(block.vector, f)

                def f(t, tc=tc, h=h, ld_rl=ld_rl, skt_sl=skt_sl):
                    t.wait_ge(p.ldh, ld_rl)
                    return t.transpose(p.pT[:, 0:128], skt_sl, C["id128"][:])
                p.do(block.tensor, f)

                def f(v):
                    return v.tensor_copy(sktm[:], p.pT[:, 0:128])
                p.do(block.vector, f)

                def f(t, sq_sl=sq_sl):
                    return t.matmul(p.pE[:, 0:128], sktm[:], sq_sl, start=True, stop=True)
                p.do(block.tensor, f)

                def f(v):
                    return v.scalar_tensor_tensor(stm[:], p.pE[:, 0:128], 1.0, C["tril"][:],
                                                  ALU.mult, ALU.mult)
                p.do(block.vector, f)

                def f(t, sq_sl=sq_sl, vg_sl=vg_sl):
                    t.matmul(p.pD[:, 0:129], sq_sl, stbf[:, 0:129], start=True, stop=False)
                    t.matmul(p.pD[:, 0:128], stm[:], vg_sl, start=False, stop=False)
                    return t.matmul(p.pD[:, 128:129], stm[:], C["onebf"][:, 0:1],
                                    start=False, stop=True)
                p.do(block.tensor, f)

                if debug and g == 0 and tc == 0 and h == 0:
                    dbgc = bview(aM, 0, F32, 1024)

                    def f(v):
                        v.tensor_copy(dbgc[:, 0:129], p.pD[:, 0:129])
                        return v.tensor_copy(dbgc[:, 256:256 + 128], stm[:])
                    p.do(block.vector, f)

                mem_ps = [(p.pB, 0), (p.pB, 256), (p.pE, 0), (p.pE, 256)]

                def f(t, h=h, sq_sl=sq_sl):
                    for m in range(4):
                        ps, off = mem_ps[m]
                        mm = t.matmul(ps[:, off:off + 132], sq_sl,
                                      memb[:, (m * 16 + h) * 132:(m * 16 + h + 1) * 132],
                                      start=True, stop=True)
                    return mm
                p.do(block.tensor, f)

                def f(v, h=h):
                    for m in range(4):
                        ps, off = mem_ps[m]
                        last = v.tensor_scalar(rs4a[:, m:m + 1], ps[:, off + 128:off + 129],
                                               float(EPS1), None, ALU.max)
                    return last
                p.do(block.vector, f)

                def f(v):
                    return v.reciprocal(rs4b[:], rs4a[:])
                p.do(block.vector, f)

                def f(v, h=h):
                    return v.tensor_tensor(rs4b[:], rs4b[:], wcol[:, h * 4:h * 4 + 4], ALU.mult)
                p.do(block.vector, f)

                def f(v, h=h):
                    for m in range(4):
                        ps, off = mem_ps[m]
                        last = v.tensor_scalar(sct[:, m * 128:(m + 1) * 128], ps[:, off:off + 128],
                                               rs4b[:, m:m + 1], None, ALU.mult)
                    return last
                p.do(block.vector, f)

                def f(v):
                    v.tensor_tensor(oloc[:], sct[:, 0:128], sct[:, 128:256], ALU.add)
                    v.tensor_tensor(t128[:], sct[:, 256:384], sct[:, 384:512], ALU.add)
                    return v.tensor_tensor(accmem[:], oloc[:], t128[:], ALU.add)
                p.do(block.vector, f)

                def f(v, h=h):
                    return v.tensor_scalar(rs1[:], p.pD[:, 128:129], float(EPS1), None, ALU.max)
                p.do(block.vector, f)

                def f(v):
                    return v.reciprocal(rs2[:], rs1[:])
                p.do(block.vector, f)

                def f(v, h=h):
                    v.tensor_scalar(oloc[:], p.pD[:, 0:128], rs2[:], None, ALU.mult)
                    v.tensor_tensor(t128[:], accmem[:], oloc[:], ALU.subtract)
                    return v.scalar_tensor_tensor(otm[:], t128[:], gcol[:, h:h + 1], oloc[:],
                                                  ALU.mult, ALU.add)
                p.do(block.vector, f)

                if debug and g == 0 and tc == 0 and h == 0:
                    def f(v):
                        v.tensor_copy(dbgc[:, 384:384 + 128], accmem[:])
                        v.tensor_copy(dbgc[:, 512:512 + 128], oloc[:])
                        v.tensor_copy(dbgc[:, 640:640 + 128], otm[:])
                        return v.tensor_copy(dbgc[:, 768:768 + 132], stbf[:, 0:132])
                    p.do(block.vector, f)

                def f(t, skt_sl=skt_sl, vg_sl=vg_sl):
                    t.transpose(p.pT[:, 128:256], otm[:], C["id128"][:])
                    t.matmul(p.pF[:, 0:128], skt_sl, vg_sl, start=True, stop=True)
                    return t.matmul(p.pF[:, 128:129], skt_sl, C["onebf"][:, 0:1],
                                    start=True, stop=True)
                p.do(block.tensor, f)

                def f(v, tc=tc, h=h):
                    v.tensor_copy(ofm_buf[:, h * 512 + tc * 128:h * 512 + tc * 128 + 128],
                                  p.pT[:, 128:256])
                    return v.tensor_tensor(spv[:, h * 132:h * 132 + 129],
                                           spv[:, h * 132:h * 132 + 129], p.pF[:, 0:129], ALU.add)
                p.do(block.vector, f)

        mt_st = []
        omm_steps = {}
        for oc in range(DC):
            dep = p.step if oc < 2 else omm_steps[oc - 2]
            so = (oc % 2) * 2048

            def s_w(gq, oc=oc, dep=dep, so=so):
                gq.wait_ge(p.fl, dep)
                p.dma(gq, wsl[:, so:so + 2048].rearrange("p (c o) -> p c o", o=128),
                      woT[:, oc * 128:(oc + 1) * 128].rearrange("(c p) o -> p c o", p=128))
            block.gpsimd(s_w)
            ld0 = p.ld

            def f(t, oc=oc, ld0=ld0, so=so):
                t.wait_ge(p.ldh, ld0)
                for ec in range(DC):
                    mm = t.matmul(p.pC[:, 0:512], wsl[:, so + ec * 128:so + (ec + 1) * 128],
                                  ofm_buf[:, ec * 512:(ec + 1) * 512],
                                  start=(ec == 0), stop=(ec == DC - 1))
                return mm
            omm_steps[oc] = p.do(block.tensor, f)

            def f(v):
                return v.tensor_copy(tmp[:, 0:512], p.pC[:, 0:512])
            p.do(block.vector, f)

            def s_m(sync, g=g, oc=oc):
                sync.wait_ge(p.fl, p.step)
                p.dma_st(sync, mix_dram[oc * 128:(oc + 1) * 128, g * 512:(g + 1) * 512],
                         tmp[:, 0:512])
            block.sync(s_m)
            def f(v):
                v.wait_ge(p.sth, p.st)
                return v.engine_nop()
            p.do(block.vector, f)

    if debug:
        def s_dm(sync):
            sync.wait_ge(p.fl, p.step)
            p.dma_st(sync, dbg_mix[:, :], mix_dram.ap())
            p.dma_st(sync, dbg_c[:, :], bview(aM, 0, F32, 1024))
            p.dma_st(sync, dbg_c2[:, :], bview(aM, 8192, F32, 132))
        block.sync(s_dm)

    def f(g2):
        g2.wait_ge(p.sth, p.st)
        return g2.engine_nop()
    p.do(block.gpsimd, f)
    emit_postln_residual(p, mix_dram, x_in, x_out, C, scratch, xbf)
    emit_store_wait(p)
    return p


def prep_l1attn_inputs(full, core):
    mem = np.asarray(full["L1_attn_memories"], np.float32)
    mn = np.asarray(full["L1_attn_memory_norms"], np.float32)
    memaug = np.concatenate([mem, mn[..., None], np.zeros((4, 16, 128, 3), np.float32)], axis=3)
    memaug = memaug.reshape(64 * 128, 132)
    mnormT = np.ascontiguousarray(mn.transpose(2, 0, 1).reshape(128, 64))
    tril = np.triu(np.ones((128, 128), np.float32)).astype(BF16NP)
    maskpre = np.zeros((128, 4), np.float32)
    pos = core % 4
    maskpre[:, 0:pos] = 1.0
    shared = {
        "wqT": np.ascontiguousarray(np.asarray(full["L1_attn_w_q"], np.float32).T),
        "wkT": np.ascontiguousarray(np.asarray(full["L1_attn_w_k"], np.float32).T),
        "wvT": np.ascontiguousarray(np.asarray(full["L1_attn_w_v"], np.float32).T),
        "woT": np.ascontiguousarray(np.asarray(full["L1_attn_w_o"], np.float32).T),
        "memaug": memaug, "mnormT": mnormT,
        "gatein": np.asarray(full["L1_attn_gate"], np.float32).reshape(1, 16),
        "maskpre": maskpre,
        "ctril": tril,
        "cdiv": np.full((128, 1), 1.0 / 2048.0, BF16NP),
        "crow": np.ones((1, 128), np.float32),
        "cid": np.eye(128, dtype=BF16NP),
        "conebf": np.ones((128, 1), BF16NP),
        "crelrow": np.full((128, 1), (128.0 ** -0.5) / 8192.0, np.float32),
    }
    return shared


def run_l1attn(x_flat, full, debug=False):
    shards = shard_fm(x_flat)
    p = build_l1attn(debug)
    in_maps = [dict(prep_l1attn_inputs(full, c), x_in=s) for c, s in enumerate(shards)]
    res = run_bass_kernel_spmd(p.nc, in_maps, core_ids=list(range(8)))
    p.close()
    if debug:
        return res
    return unshard_fm([res.results[c]["x_out"] for c in range(8)])



def _u8codes(w_int32):
    return np.ascontiguousarray(w_int32.view(np.uint8).reshape(w_int32.shape[0], -1)[:, 0::4])


def prep_l1ffn_inputs(full):
    wg = np.ascontiguousarray(_u8codes(np.asarray(full["L1_ffn_wq_gate"])).T)
    wu = np.ascontiguousarray(_u8codes(np.asarray(full["L1_ffn_wq_up"])).T)
    wd = np.ascontiguousarray(_u8codes(np.asarray(full["L1_ffn_wq_down"])).T)
    shared = {
        "wg": wg, "wu": wu, "wd": wd,
        "sg": np.asarray(full["L1_ffn_s_gate"], np.float32).reshape(1, -1),
        "su": np.asarray(full["L1_ffn_s_up"], np.float32).reshape(1, -1),
        "sd": np.asarray(full["L1_ffn_s_down"], np.float32).reshape(1, -1),
        "cdiv": np.full((128, 1), 1.0 / 2048.0, BF16NP),
        "crow": np.ones((1, 128), np.float32),
        "cm128": np.full((128, 1), -128.0, np.float32),
    }
    return shared


def shard_fm(x_flat):
    return [np.ascontiguousarray(x_flat[c * T:(c + 1) * T].T) for c in range(8)]


def unshard_fm(shards):
    return np.concatenate([s.T for s in shards], axis=0)


def run_l1ffn(x_flat, full):
    shared = prep_l1ffn_inputs(full)
    shards = shard_fm(x_flat)
    p = build_l1ffn()
    in_maps = [dict(shared, x_in=s) for s in shards]
    res = run_bass_kernel_spmd(p.nc, in_maps, core_ids=list(range(8)))
    p.close()
    return unshard_fm([res.results[c]["x_out"] for c in range(8)])


def build_fused():
    p = Prog().start()
    nc = p.nc
    x0 = nc.dram_tensor("x_in", [D, T], F32, kind="ExternalInput").ap()
    xo = nc.dram_tensor("x_out", [D, T], F32, kind="ExternalOutput").ap()
    x1 = nc.dram_tensor("x1", [D, T], F32).ap()
    x2 = nc.dram_tensor("x2", [D, T], F32).ap()
    x3 = nc.dram_tensor("x3", [D, T], F32).ap()
    cdiv = nc.dram_tensor("cdiv", [128, 1], BF16, kind="ExternalInput").ap()
    crow = nc.dram_tensor("crow", [1, 128], F32, kind="ExternalInput").ap()
    cid = nc.dram_tensor("cid", [128, 128], BF16, kind="ExternalInput").ap()
    P1 = p.sb("P1", [128, 65536], U8)
    P2 = p.sb("P2", [128, 65536], U8)
    REST = p.sb("REST", [128, 80896], U8)
    C = {
        "ones_col_div": p.sb("g_div", [128, 1], BF16),
        "ones_row_f32": p.sb("g_row", [1, 128], F32),
        "id128": p.sb("g_id", [128, 128], BF16),
    }
    block = p.block

    def s_c(sync):
        p.dma(sync, C["ones_col_div"][:], cdiv)
        p.dma(sync, C["ones_row_f32"][:], crow)
        p.dma(sync, C["id128"][:], cid)
    block.sync(s_c)
    ldc = p.ld

    def f(v):
        v.wait_ge(p.ldh, ldc)
        return v.engine_nop()
    p.do(block.vector, f)

    def barrier():
        def f(g):
            g.wait_ge(p.sth, p.st)
            return g.engine_nop()
        p.do(block.gpsimd, f)

    base = dict(p=p, P1=P1, P2=P2, REST=REST, C=C)
    build_l0attn(fuse=dict(base, pre="a_", x_in=x0, x_out=x1))
    barrier()
    build_l0ffn(fuse=dict(base, pre="f_", x_in=x1, x_out=x2))
    barrier()
    build_l1attn(fuse=dict(base, pre="b_", x_in=x2, x_out=x3))
    barrier()
    build_l1ffn(fuse=dict(base, pre="g_", x_in=x3, x_out=xo))
    return p


def kernel(**inputs):
    x = np.asarray(inputs["L0_attn_hidden_states"], np.float32).reshape(16384, 2048)
    shards = shard_fm(x)
    shared = {}
    shared.update(prep_l0attn_inputs(inputs))
    shared.update(prep_l0ffn_inputs(inputs))
    shared.update(prep_l1ffn_inputs(inputs))
    p = build_fused()
    in_maps = []
    for c in range(8):
        m = dict(shared)
        m.update(prep_l1attn_inputs(inputs, c))
        m["x_in"] = shards[c]
        in_maps.append(m)
    res = run_bass_kernel_spmd(p.nc, in_maps, core_ids=list(range(8)))
    p.close()
    out = unshard_fm([res.results[c]["x_out"] for c in range(8)])
    return np.ascontiguousarray(out.reshape(2, 8192, 2048).astype(np.float32))


def kernel_multilaunch(**inputs):
    x = np.asarray(inputs["L0_attn_hidden_states"], np.float32).reshape(16384, 2048)
    x = run_l0attn(x, inputs)
    x = run_l0ffn(x, inputs)
    x = run_l1attn(x, inputs)
    x = run_l1ffn(x, inputs)
    return np.ascontiguousarray(x.reshape(2, 8192, 2048).astype(np.float32))


# version 05 (final)
# speedup vs baseline: 1.2086x; 1.0268x over previous
import numpy as np
import ml_dtypes
from contextlib import ExitStack

import concourse.bass as bass
import concourse.mybir as mybir
from concourse.bass_utils import run_bass_kernel_spmd

F32 = mybir.dt.float32
BF16 = mybir.dt.bfloat16
U8 = mybir.dt.uint8
U16 = mybir.dt.uint16
AF = mybir.ActivationFunctionType
ALU = mybir.AluOpType
AX = mybir.AxisListType
BF16NP = ml_dtypes.bfloat16

T = 2048
D = 2048
DC = 16
NTT = 4
EPS = 1e-5
TWO126 = float(2.0 ** 126)
RG4 = [[0, 1, 2, 3], [4, 5, 6, 7]]
RG8 = [[0, 1, 2, 3, 4, 5, 6, 7]]
HID1 = 5632
HC1 = HID1 // 128
MLP0 = 16384


class Prog:
    def __init__(self):
        self.nc = bass.Bass("TRN2", target_bir_lowering=False)
        self.ctx = ExitStack()
        self.step = 0
        self.ld = 0
        self.st = 0
        self.cc = 0

    def start(self):
        c = self.ctx.enter_context
        self.fl = c(self.nc.semaphore("fl"))
        self.ldh = c(self.nc.semaphore("ldh"))
        self.sth = c(self.nc.semaphore("sth"))
        self.cch = c(self.nc.semaphore("cch"))
        self.block = c(self.nc.Block())
        self.pA = c(self.nc.psum_tensor("pA", [128, 512], F32))
        self.pB = c(self.nc.psum_tensor("pB", [128, 512], F32))
        self.pC = c(self.nc.psum_tensor("pC", [128, 512], F32))
        self.pD = c(self.nc.psum_tensor("pD", [128, 1024], F32))
        self.pE = c(self.nc.psum_tensor("pE", [128, 512], F32))
        self.pF = c(self.nc.psum_tensor("pF", [128, 512], F32))
        self.pT = c(self.nc.psum_tensor("pT", [128, 1024], BF16))
        return self

    def sb(self, name, shape, dtype):
        return self.ctx.enter_context(self.nc.sbuf_tensor(name, shape, dtype))

    def do(self, section, emit):
        dep = self.step

        def wrapped(eng):
            eng.wait_ge(self.fl, dep)
            last = emit(eng)
            last.then_inc(self.fl, 1)
        section(wrapped)
        self.step += 1
        return self.step

    def load(self, emit, n=1):
        self.block.sync(emit)
        return self.ld

    def dma(self, q, out, in_, accum=False):
        kw = {"accum_op": ALU.add} if accum else {}
        q.dma_start(out=out, in_=in_, **kw).then_inc(self.ldh, 16)
        self.ld += 16
        return self.ld

    def dma_st(self, q, out, in_, accum=False):
        kw = {"accum_op": ALU.add} if accum else {}
        q.dma_start(out=out, in_=in_, **kw).then_inc(self.sth, 16)
        self.st += 16
        return self.st

    def close(self):
        self.ctx.close()


_ESZ = {F32: 4, BF16: 2, U8: 1, U16: 2}


class SAlloc:

    def __init__(self, p, pre, fuse):
        self.p = p
        self.pre = pre
        self.fuse = fuse
        self.off = 0

    def t(self, name, shape, dtype):
        if self.fuse is None:
            return self.p.sb(self.pre + name, shape, dtype)
        nb = shape[1] * _ESZ[dtype]
        nb_pad = (nb + 31) // 32 * 32
        o = self.off
        self.off += nb_pad
        R = self.fuse["REST"]
        assert self.off <= R.shape[1], f"REST overflow at {name}: {self.off}"
        v = R[:, o:o + nb].bitcast(dtype)
        if shape[0] != 128:
            v = v[0:shape[0], :]
        return v



def bview(buf, byte_off, dtype, cols):
    esz = _ESZ[dtype]
    bsz = _ESZ[buf.dtype]
    assert byte_off % bsz == 0 and (cols * esz) % bsz == 0
    return buf[:, byte_off // bsz:(byte_off + cols * esz) // bsz].bitcast(dtype)



def emit_stats(p, src_bf, C, srow, rstd, sq):
    block = p.block
    onesd = C["ones_col_div"]

    for tt in range(NTT):
        def f(t, tt=tt):
            for j in range(DC):
                mm = t.matmul(p.pE[0:1, 0:512], onesd[:, 0:1],
                              src_bf[:, j * T + tt * 512:j * T + (tt + 1) * 512],
                              start=(j == 0), stop=(j == DC - 1))
            return mm
        p.do(block.tensor, f)

        def f(v, tt=tt):
            return v.tensor_copy(srow[0:1, tt * 512:(tt + 1) * 512], p.pE[0:1, 0:512])
        p.do(block.vector, f)

    nsl = min(4, sq.shape[1] // 1024)
    sqv = sq[:, 0:nsl * 1024].bitcast(BF16)
    for tt in range(NTT):
        for jg in range(DC // nsl):
            def f(sc, tt=tt, jg=jg):
                for i in range(nsl):
                    j2 = jg * nsl + i
                    last = sc.activation(sqv[:, i * 512:(i + 1) * 512],
                                         src_bf[:, j2 * T + tt * 512:j2 * T + (tt + 1) * 512],
                                         AF.Square)
                return last
            p.do(block.scalar, f)

            def f(t, tt=tt, jg=jg):
                for i in range(nsl):
                    j2 = jg * nsl + i
                    last = t.matmul(p.pF[0:1, 0:512], onesd[:, 0:1],
                                    sqv[:, i * 512:(i + 1) * 512],
                                    start=(j2 == 0), stop=(j2 == DC - 1))
                return last
            p.do(block.tensor, f)

        def f(v, tt=tt):
            return v.tensor_copy(srow[0:1, T + tt * 512:T + (tt + 1) * 512], p.pF[0:1, 0:512])
        p.do(block.vector, f)

    def f(v):
        v.tensor_tensor(rstd[0:1, 0:T], srow[0:1, 0:T], srow[0:1, 0:T], ALU.mult)
        return v.tensor_tensor(srow[0:1, T:2 * T], srow[0:1, T:2 * T], rstd[0:1, 0:T], ALU.subtract)
    p.do(block.vector, f)

    def f(sc):
        return sc.activation(rstd[0:1, 0:T], srow[0:1, T:2 * T], AF.Sqrt, bias=np.float32(EPS), scale=np.float32(1.0))
    p.do(block.scalar, f)

    def f(v):
        return v.reciprocal(rstd[0:1, 0:T], rstd[0:1, 0:T])
    p.do(block.vector, f)


def emit_bcast_stats(p, C, srow, rstd, mbc, rbc):
    block = p.block
    ones_row = C["ones_row_f32"]
    for tt in range(NTT):
        def f(t, tt=tt):
            t.matmul(p.pD[:, 0:512], ones_row[0:1, :],
                     srow[0:1, tt * 512:(tt + 1) * 512], start=True, stop=True)
            return t.matmul(p.pD[:, 512:1024], ones_row[0:1, :],
                            rstd[0:1, tt * 512:(tt + 1) * 512], start=True, stop=True)
        p.do(block.tensor, f)

        def f(v, tt=tt):
            v.tensor_copy(mbc[:, tt * 512:(tt + 1) * 512], p.pD[:, 0:512])
            return v.tensor_copy(rbc[:, tt * 512:(tt + 1) * 512], p.pD[:, 512:1024])
        p.do(block.vector, f)


def emit_ln(p, x_dram, xbf, C, scratch):
    block = p.block
    xf, sq = scratch["xf"], scratch["sq"]
    srow, rstd = scratch["srow"], scratch["rstd"]
    mbc, rbc, tmp = scratch["mbc"], scratch["rbc"], scratch["tmp"]

    ld0 = p.ld
    copy_step0 = p.step

    def s_load(sync):
        sync.wait_ge(p.fl, copy_step0)
        for j in range(DC):
            if j >= 2:
                sync.wait_ge(p.fl, copy_step0 + (j - 2) + 1)
            p.dma(sync, xf[:, (j % 2) * T:(j % 2 + 1) * T], x_dram[j * 128:(j + 1) * 128, :])
    p.block.sync(s_load)

    for j in range(DC):
        def f(v, j=j):
            v.wait_ge(p.ldh, ld0 + 16 * (j + 1))
            return v.tensor_copy(xbf[:, j * T:(j + 1) * T], xf[:, (j % 2) * T:(j % 2 + 1) * T])
        p.do(block.vector, f)

    emit_stats(p, xbf, C, srow, rstd, sq)
    emit_bcast_stats(p, C, srow, rstd, mbc, rbc)

    for j in range(DC):
        def f(v, j=j):
            for tt in range(NTT):
                sl = slice(j * T + tt * 512, j * T + (tt + 1) * 512)
                st = slice(tt * 512, (tt + 1) * 512)
                v.tensor_tensor(tmp[:, 0:512], xbf[:, sl], mbc[:, st], ALU.subtract)
                last = v.tensor_tensor(xbf[:, sl], tmp[:, 0:512], rbc[:, st], ALU.mult)
            return last
        p.do(block.vector, f)


def emit_postln_residual(p, mix_dram, x_dram, out_dram, C, scratch, mbf):
    block = p.block
    xf, sq = scratch["xf"], scratch["sq"]
    srow, rstd = scratch["srow"], scratch["rstd"]
    mbc, rbc, tmp = scratch["mbc"], scratch["rbc"], scratch["tmp"]
    mf2 = scratch["mf2"]

    ld0 = p.ld
    copy_step0 = p.step

    def s_load(sync):
        sync.wait_ge(p.fl, copy_step0)
        for j in range(DC):
            if j >= 2:
                sync.wait_ge(p.fl, copy_step0 + (j - 2) + 1)
            p.dma(sync, xf[:, (j % 2) * T:(j % 2 + 1) * T], mix_dram[j * 128:(j + 1) * 128, :])
    block.sync(s_load)

    for j in range(DC):
        def f(v, j=j):
            v.wait_ge(p.ldh, ld0 + 16 * (j + 1))
            return v.tensor_copy(mbf[:, j * T:(j + 1) * T], xf[:, (j % 2) * T:(j % 2 + 1) * T])
        p.do(block.vector, f)

    emit_stats(p, mbf, C, srow, rstd, sq)
    emit_bcast_stats(p, C, srow, rstd, mbc, rbc)

    ld1 = p.ld
    norm_step0 = p.step

    for j in range(DC):
        def f(v, j=j):
            v.wait_ge(p.ldh, ld1 + 32 * (j + 1))
            for tt in range(NTT):
                slm = slice((j % 2) * T + tt * 512, (j % 2) * T + (tt + 1) * 512)
                st = slice(tt * 512, (tt + 1) * 512)
                v.tensor_tensor(tmp[:, 0:512], mf2[:, slm], mbc[:, st], ALU.subtract)
                v.tensor_tensor(tmp[:, 0:512], tmp[:, 0:512], rbc[:, st], ALU.mult)
                last = v.tensor_tensor(mf2[:, slm], xf[:, slm], tmp[:, 0:512], ALU.add)
            return last
        p.do(block.vector, f)

    def s_stream(sync):
        st_after = {}
        for j in range(DC):
            if j >= 2:
                sync.wait_ge(p.fl, norm_step0 + (j - 2) + 1)
                p.dma_st(sync, out_dram[(j - 2) * 128:(j - 1) * 128, :],
                         mf2[:, (j % 2) * T:(j % 2 + 1) * T])
                st_after[j - 2] = p.st
            if j >= 4:
                sync.wait_ge(p.sth, st_after[j - 4])
            p.dma(sync, xf[:, (j % 2) * T:(j % 2 + 1) * T], x_dram[j * 128:(j + 1) * 128, :])
            p.dma(sync, mf2[:, (j % 2) * T:(j % 2 + 1) * T], mix_dram[j * 128:(j + 1) * 128, :])
        for j in (DC - 2, DC - 1):
            sync.wait_ge(p.fl, norm_step0 + j + 1)
            p.dma_st(sync, out_dram[j * 128:(j + 1) * 128, :],
                     mf2[:, (j % 2) * T:(j % 2 + 1) * T])
    block.sync(s_stream)


def emit_store_wait(p):
    def f(g):
        g.wait_ge(p.sth, p.st)
        return g.engine_nop()
    p.do(p.block.gpsimd, f)



def build_l1ffn(fuse=None):
    if fuse is None:
        p = Prog().start()
        own = True
    else:
        p, own = fuse["p"], False
    nc = p.nc
    pre = "" if own else fuse["pre"]
    if own:
        x_in = nc.dram_tensor("x_in", [D, T], F32, kind="ExternalInput").ap()
    else:
        x_in = fuse["x_in"]
    wg = nc.dram_tensor("wg", [D, HID1], U8, kind="ExternalInput").ap()
    wu = nc.dram_tensor("wu", [D, HID1], U8, kind="ExternalInput").ap()
    wd = nc.dram_tensor("wd", [HID1, D], U8, kind="ExternalInput").ap()
    sg = nc.dram_tensor("sg", [1, HID1], F32, kind="ExternalInput").ap()
    su = nc.dram_tensor("su", [1, HID1], F32, kind="ExternalInput").ap()
    sd = nc.dram_tensor("sd", [1, D], F32, kind="ExternalInput").ap()
    if own:
        cdiv = nc.dram_tensor("cdiv", [128, 1], BF16, kind="ExternalInput").ap()
        crow = nc.dram_tensor("crow", [1, 128], F32, kind="ExternalInput").ap()
    cm128 = nc.dram_tensor("cm128", [128, 1], F32, kind="ExternalInput").ap()
    if own:
        x_out = nc.dram_tensor("x_out", [D, T], F32, kind="ExternalOutput").ap()
    else:
        x_out = fuse["x_out"]

    i_dram = nc.dram_tensor(pre + "i_dram", [HID1, T], BF16)
    wd_dram = nc.dram_tensor(pre + "wd_dram", [HID1, D], BF16)
    mix_dram = nc.dram_tensor(pre + "mix_dram", [D, T], F32)

    S = SAlloc(p, pre, fuse)
    if fuse is None:
        xbf = p.sb("xbf", [128, DC * T], BF16)
        aW = p.sb("aW", [128, 49152], U8)
    else:
        xbf = bview(fuse["P1"], 0, BF16, DC * T)
        aW = bview(fuse["P2"], 0, U8, 49152)
    aM = S.t("aM", [128, 24576], U8)
    xf = S.t("xf", [128, 2 * T], F32)
    mf2 = bview(aW, 0, F32, 2 * T)
    mbc = bview(aM, 0, F32, T)
    rbc = bview(aM, 8192, F32, T)
    sq = S.t("sq", [128, 2048], U8)
    srow = aW[0:1, 16384:16384 + 8 * T].bitcast(F32)
    rstd = aW[0:1, 32768:32768 + 4 * T].bitcast(F32)
    tmp = S.t("tmp", [128, 512], F32)
    sbS = S.t("sbS", [128, 2 * HID1 + D], BF16)
    scratch = dict(xf=xf, sq=sq, srow=srow, rstd=rstd, mbc=mbc, rbc=rbc,
                   tmp=tmp, mf2=mf2)

    C = {}
    block = p.block
    if fuse is None:
        C["ones_col_div"] = p.sb("c_div", [128, 1], BF16)
        C["ones_row_f32"] = p.sb("c_row", [1, 128], F32)
    else:
        C["ones_col_div"] = fuse["C"]["ones_col_div"]
        C["ones_row_f32"] = fuse["C"]["ones_row_f32"]
    C["m128_col"] = S.t("c_m128", [128, 1], F32)
    C["sg_row"] = aW[0:1, 0:4 * HID1].bitcast(F32)
    C["su_row"] = aW[0:1, 24576:24576 + 4 * HID1].bitcast(F32)
    C["sd_row"] = aM[0:1, 0:4 * D].bitcast(F32)

    def s_consts(sync):
        if fuse is None:
            p.dma(sync, C["ones_col_div"][:], cdiv)
            p.dma(sync, C["ones_row_f32"][:], crow)
        p.dma(sync, C["m128_col"][:], cm128)
        p.dma(sync, C["sg_row"], sg)
        p.dma(sync, C["su_row"], su)
        p.dma(sync, C["sd_row"], sd)
    block.sync(s_consts)
    ld_consts = p.ld

    def f(v):
        v.wait_ge(p.ldh, ld_consts)
        return v.engine_nop()
    p.do(block.vector, f)

    def sbS_view(which):
        if which == "g":
            return sbS[:, 0:HID1]
        if which == "u":
            return sbS[:, HID1:2 * HID1]
        return sbS[:, 2 * HID1:2 * HID1 + D]

    for which, row, n in (("g", "sg_row", HID1), ("u", "su_row", HID1), ("d", "sd_row", D)):
        for i in range(n // 512):
            def f(t, which=which, row=row, i=i):
                return t.matmul(p.pD[:, 0:512], C["ones_row_f32"][0:1, :],
                                C[row][0:1, i * 512:(i + 1) * 512], start=True, stop=True)
            p.do(block.tensor, f)

            def f(v, which=which, i=i):
                return v.tensor_copy(sbS_view(which)[:, i * 512:(i + 1) * 512], p.pD[:, 0:512])
            p.do(block.vector, f)

    emit_ln(p, x_in, xbf, C, scratch)

    cod_g = bview(aW, 0, U8, 16 * 512)
    cod_u = bview(aW, 8192, U8, 16 * 512)
    wg_bf = bview(aW, 16384, BF16, 16 * 512)
    wu_bf = bview(aW, 32768, BF16, 16 * 512)
    cod_d = bview(aM, 0, U8, 4 * 2048)
    wd_bf = bview(aM, 8192, BF16, 4 * 2048)
    gsil = S.t("gsil", [128, 1024], BF16)
    itile = S.t("itile", [128, 1024], BF16)
    it_st = []
    wd_st = []

    NHS = HID1 // 512
    for hs in range(NHS):
        ld0 = p.ld
        dep = p.step

        def s_w(sync, hs=hs, dep=dep):
            sync.wait_ge(p.fl, dep)
            p.dma(sync, cod_g.rearrange("p (c h) -> p c h", h=512),
                  wg[:, hs * 512:(hs + 1) * 512].rearrange("(c p) h -> p c h", p=128))
            p.dma(sync, cod_u.rearrange("p (c h) -> p c h", h=512),
                  wu[:, hs * 512:(hs + 1) * 512].rearrange("(c p) h -> p c h", p=128))
            p.dma(sync, cod_d.rearrange("p (c o) -> p c o", o=2048),
                  wd[hs * 512:(hs + 1) * 512, :].rearrange("(c p) o -> p c o", p=128))
        block.sync(s_w)

        def f(sc, hs=hs, ld0=ld0):
            sc.wait_ge(p.ldh, ld0 + 32)
            sc.activation(wg_bf[:], cod_g[:], AF.Copy, bias=-128.0, scale=1.0)
            return sc.activation(wu_bf[:], cod_u[:], AF.Copy, bias=-128.0, scale=1.0)
        p.do(block.scalar, f)

        def f(v, hs=hs):
            for c in range(DC):
                v.tensor_tensor(wg_bf[:, c * 512:(c + 1) * 512], wg_bf[:, c * 512:(c + 1) * 512],
                                sbS_view("g")[:, hs * 512:(hs + 1) * 512], ALU.mult)
                last = v.tensor_tensor(wu_bf[:, c * 512:(c + 1) * 512], wu_bf[:, c * 512:(c + 1) * 512],
                                       sbS_view("u")[:, hs * 512:(hs + 1) * 512], ALU.mult)
            return last
        p.do(block.vector, f)

        def f(sc, hs=hs, ld0=ld0):
            sc.wait_ge(p.ldh, ld0 + 48)
            if wd_st:
                sc.wait_ge(p.sth, wd_st[-1])
            return sc.activation(wd_bf[:], cod_d[:], AF.Copy, bias=-128.0, scale=1.0)
        p.do(block.scalar, f)

        def f(v, hs=hs):
            for c in range(4):
                last = v.tensor_tensor(wd_bf[:, c * 2048:(c + 1) * 2048],
                                       wd_bf[:, c * 2048:(c + 1) * 2048],
                                       sbS_view("d")[:, 0:2048], ALU.mult)
            return last
        p.do(block.vector, f)

        def s_wd(sync, hs=hs):
            sync.wait_ge(p.fl, p.step)
            p.dma_st(sync, wd_dram[hs * 512:(hs + 1) * 512, :].rearrange("(c p) o -> p c o", p=128),
                     wd_bf.rearrange("p (c o) -> p c o", o=2048))
        block.sync(s_wd)
        wd_st.append(p.st)

        for hc in range(4):
            for tt in range(NTT):
                def f(t, hs=hs, hc=hc, tt=tt):
                    for j in range(DC):
                        t.matmul(p.pA[:, 0:512],
                                 wg_bf[:, j * 512 + hc * 128:j * 512 + hc * 128 + 128],
                                 xbf[:, j * T + tt * 512:j * T + (tt + 1) * 512],
                                 start=(j == 0), stop=(j == DC - 1))
                    for j in range(DC):
                        mm = t.matmul(p.pB[:, 0:512],
                                      wu_bf[:, j * 512 + hc * 128:j * 512 + hc * 128 + 128],
                                      xbf[:, j * T + tt * 512:j * T + (tt + 1) * 512],
                                      start=(j == 0), stop=(j == DC - 1))
                    return mm
                p.do(block.tensor, f)

                def f(sc):
                    return sc.activation(gsil[:, 0:512], p.pA[:, 0:512], AF.Silu)
                p.do(block.scalar, f)

                k = len(it_st)
                slot = slice((k % 2) * 512, (k % 2) * 512 + 512)

                def f(v, k=k, slot=slot):
                    if k >= 2:
                        v.wait_ge(p.sth, it_st[k - 2])
                    return v.tensor_tensor(itile[:, slot], gsil[:, 0:512], p.pB[:, 0:512], ALU.mult)
                p.do(block.vector, f)

                def s_i(sync, hs=hs, hc=hc, tt=tt, slot=slot):
                    sync.wait_ge(p.fl, p.step)
                    p.dma_st(sync, i_dram[hs * 512 + hc * 128:hs * 512 + (hc + 1) * 128,
                                          tt * 512:(tt + 1) * 512], itile[:, slot])
                block.sync(s_i)
                it_st.append(p.st)

    icol = bview(aW, 0, BF16, HC1 * 512)
    wds = bview(aM, 0, BF16, HC1 * 128)
    mtile = S.t("mtile", [128, 1024], F32)
    mt_st = []
    st_wd_done = p.st
    for tt in range(NTT):
        ld0 = p.ld
        dep = p.step

        def s_i2(sync, tt=tt, dep=dep):
            sync.wait_ge(p.fl, dep)
            sync.wait_ge(p.sth, st_wd_done)
            p.dma(sync, icol.rearrange("p (c t) -> p c t", t=512),
                  i_dram[:, tt * 512:(tt + 1) * 512].rearrange("(c p) t -> p c t", p=128))
        block.sync(s_i2)

        for oc in range(DC):
            ld1 = p.ld
            dep = p.step

            def s_wd2(sync, tt=tt, oc=oc, dep=dep):
                sync.wait_ge(p.fl, dep)
                p.dma(sync, wds.rearrange("p (c o) -> p c o", o=128),
                      wd_dram[:, oc * 128:(oc + 1) * 128].rearrange("(c p) o -> p c o", p=128))
            block.sync(s_wd2)

            def f(t, tt=tt, oc=oc, ld0=ld0, ld1=ld1):
                t.wait_ge(p.ldh, ld0 + 16)
                t.wait_ge(p.ldh, ld1 + 16)
                for c in range(HC1):
                    mm = t.matmul(p.pC[:, 0:512], wds[:, c * 128:(c + 1) * 128],
                                  icol[:, c * 512:(c + 1) * 512],
                                  start=(c == 0), stop=(c == HC1 - 1))
                return mm
            p.do(block.tensor, f)

            k = len(mt_st)
            slot = slice((k % 2) * 512, (k % 2) * 512 + 512)

            def f(v, k=k, slot=slot):
                if k >= 2:
                    v.wait_ge(p.sth, mt_st[k - 2])
                return v.tensor_copy(mtile[:, slot], p.pC[:, 0:512])
            p.do(block.vector, f)

            def s_m(sync, tt=tt, oc=oc, slot=slot):
                sync.wait_ge(p.fl, p.step)
                p.dma_st(sync, mix_dram[oc * 128:(oc + 1) * 128, tt * 512:(tt + 1) * 512],
                         mtile[:, slot])
            block.sync(s_m)
            mt_st.append(p.st)

    def f(g):
        g.wait_ge(p.sth, p.st)
        return g.engine_nop()
    p.do(block.gpsimd, f)
    emit_postln_residual(p, mix_dram, x_in, x_out, C, scratch, xbf)
    emit_store_wait(p)
    return p



NC0 = MLP0 // 128
NSL0 = MLP0 // 512


def emit_decode_tile(p, v, wide, cod, bits, outW, B, n):
    v.tensor_copy(wide[:, 0:n], cod)
    v.tensor_scalar(bits[:, 0:n], wide[:, 0:n], 7, 6, ALU.bitwise_and, ALU.logical_shift_left)
    v.tensor_scalar(wide[:, n:2 * n], wide[:, 0:n], 8, 12, ALU.bitwise_and, ALU.logical_shift_left)
    v.tensor_tensor(bits[:, 0:n], bits[:, 0:n], wide[:, n:2 * n], ALU.bitwise_or)
    v.tensor_tensor(outW(0), bits[:, 0:n].bitcast(BF16), B, ALU.mult)
    v.tensor_scalar(bits[:, 0:n], wide[:, 0:n], 0x70, 2, ALU.bitwise_and, ALU.logical_shift_left)
    v.tensor_scalar(wide[:, n:2 * n], wide[:, 0:n], 0x80, 8, ALU.bitwise_and, ALU.logical_shift_left)
    v.tensor_tensor(bits[:, 0:n], bits[:, 0:n], wide[:, n:2 * n], ALU.bitwise_or)
    return v.tensor_tensor(outW(1), bits[:, 0:n].bitcast(BF16), B, ALU.mult)


def build_l0ffn(debug=False, fuse=None):
    if fuse is None:
        p = Prog().start()
        own = True
    else:
        p, own = fuse["p"], False
    nc = p.nc
    pre = "" if own else fuse["pre"]
    if own:
        x_in = nc.dram_tensor("x_in", [D, T], F32, kind="ExternalInput").ap()
    else:
        x_in = fuse["x_in"]
    cg = nc.dram_tensor("cg", [1024, MLP0], U8, kind="ExternalInput").ap()
    cu = nc.dram_tensor("cu", [1024, MLP0], U8, kind="ExternalInput").ap()
    cd = nc.dram_tensor("cd", [MLP0 // 2, D], U8, kind="ExternalInput").ap()
    scg = nc.dram_tensor("scg", [64, MLP0], F32, kind="ExternalInput").ap()
    scu = nc.dram_tensor("scu", [64, MLP0], F32, kind="ExternalInput").ap()
    scd = nc.dram_tensor("scd", [512, D], F32, kind="ExternalInput").ap()
    if own:
        cdiv = nc.dram_tensor("cdiv", [128, 1], BF16, kind="ExternalInput").ap()
        crow = nc.dram_tensor("crow", [1, 128], F32, kind="ExternalInput").ap()
    ceall = nc.dram_tensor("ceall", [64, 1024], BF16, kind="ExternalInput").ap()
    ce8 = nc.dram_tensor("ce8", [8, 128], BF16, kind="ExternalInput").ap()
    cperm = nc.dram_tensor("cperm", [128, 512], BF16, kind="ExternalInput").ap()
    if own:
        x_out = nc.dram_tensor("x_out", [D, T], F32, kind="ExternalOutput").ap()
    else:
        x_out = fuse["x_out"]
    if debug:
        dbg_mix = nc.dram_tensor("dbg_mix", [D, T], F32, kind="ExternalOutput").ap()
        dbg_kv0 = nc.dram_tensor("dbg_kv0", [2048, 132], F32, kind="ExternalOutput").ap()
        dbg_qs0 = nc.dram_tensor("dbg_qs0", [128, 16], F32, kind="ExternalOutput").ap()
        dbg_c = nc.dram_tensor("dbg_c", [128, 1024], F32, kind="ExternalOutput").ap()
        dbg_c2 = nc.dram_tensor("dbg_c2", [128, 132], F32, kind="ExternalOutput").ap()
        dbg_xp = nc.dram_tensor("dbg_xp", [D, T], BF16, kind="ExternalOutput").ap()
        dbg_wg = nc.dram_tensor("dbg_wg", [D, 512], BF16, kind="ExternalOutput").ap()
        dbg_i = nc.dram_tensor("dbg_i", [512, T], BF16, kind="ExternalOutput").ap()
        dbg_wd = nc.dram_tensor("dbg_wd", [512, D], BF16, kind="ExternalOutput").ap()

    i_dram = nc.dram_tensor(pre + "i_dram", [MLP0, T], BF16)
    wd_dram = nc.dram_tensor(pre + "wd_dram", [MLP0, D], BF16)
    mix_dram = nc.dram_tensor(pre + "mix_dram", [D, T], F32)

    S = SAlloc(p, pre, fuse)
    if fuse is None:
        xbf = p.sb("xbf", [128, DC * T], BF16)
        aW = p.sb("aW", [128, 49152], U8)
    else:
        xbf = bview(fuse["P1"], 0, BF16, DC * T)
        aW = bview(fuse["P2"], 0, U8, 49152)
    xpt = S.t("xpt", [128, 2 * T], BF16)
    xP = xbf
    aM = S.t("aM", [128, 24576], U8)
    xf = S.t("xf", [128, 2 * T], F32)
    sq = S.t("sq", [128, 4096], U8)
    tmp = S.t("tmp", [128, 512], F32)
    mf2 = bview(aW, 0, F32, 2 * T)
    mbc = bview(aM, 0, F32, T)
    rbc = bview(aM, 8192, F32, T)
    srow = aW[0:1, 16384:16384 + 8 * T].bitcast(F32)
    rstd = aW[0:1, 32768:32768 + 4 * T].bitcast(F32)
    scratch = dict(xf=xf, sq=sq, srow=srow, rstd=rstd, mbc=mbc, rbc=rbc,
                   tmp=tmp, mf2=mf2)

    C = {}
    if fuse is None:
        C["ones_col_div"] = p.sb("c_div", [128, 1], BF16)
        C["ones_row_f32"] = p.sb("c_row", [1, 128], F32)
    else:
        C["ones_col_div"] = fuse["C"]["ones_col_div"]
        C["ones_row_f32"] = fuse["C"]["ones_row_f32"]
    C["eall"] = S.t("c_eall", [64, 1024], BF16)
    C["e8"] = S.t("c_e8", [8, 128], BF16)
    C["perm"] = S.t("c_perm", [128, 512], BF16)
    block = p.block

    def s_consts(sync):
        if fuse is None:
            p.dma(sync, C["ones_col_div"][:], cdiv)
            p.dma(sync, C["ones_row_f32"][:], crow)
        p.dma(sync, C["eall"][:], ceall)
        p.dma(sync, C["e8"][:], ce8)
        p.dma(sync, C["perm"][:], cperm)
    block.sync(s_consts)
    ld_consts = p.ld

    def f(v):
        v.wait_ge(p.ldh, ld_consts)
        return v.engine_nop()
    p.do(block.vector, f)

    emit_ln(p, x_in, xbf, C, scratch)

    for jc in range(8):
        for s in range(2):
            for tt in range(NTT):
                def f(t, jc=jc, s=s, tt=tt):
                    t.matmul(p.pA[:, 0:512], C["perm"][:, (0 + s * 2) * 128:(1 + s * 2) * 128],
                             xbf[:, (2 * jc) * T + tt * 512:(2 * jc) * T + (tt + 1) * 512],
                             start=True, stop=False)
                    return t.matmul(p.pA[:, 0:512], C["perm"][:, (1 + s * 2) * 128:(2 + s * 2) * 128],
                                    xbf[:, (2 * jc + 1) * T + tt * 512:(2 * jc + 1) * T + (tt + 1) * 512],
                                    start=False, stop=True)
                p.do(block.tensor, f)

                def f(v, s=s, tt=tt):
                    return v.tensor_copy(xpt[:, s * T + tt * 512:s * T + (tt + 1) * 512],
                                         p.pA[:, 0:512])
                p.do(block.vector, f)

        def f(v, jc=jc):
            v.tensor_copy(xbf[:, (2 * jc) * T:(2 * jc + 1) * T], xpt[:, 0:T])
            return v.tensor_copy(xbf[:, (2 * jc + 1) * T:(2 * jc + 2) * T], xpt[:, T:2 * T])
        p.do(block.vector, f)

    codg = bview(aW, 0, U8, 8 * 512)
    codu = bview(aW, 4096, U8, 8 * 512)
    wgd = bview(aW, 8192, BF16, 16 * 512)
    wud = bview(aW, 24576, BF16, 16 * 512)
    wide = bview(aW, 40960, U16, 1024)
    bits = bview(aW, 43008, U16, 1024)
    bsl = bview(aW, 45056, BF16, 1024)
    sgt = aM[0:64, 0:2048].bitcast(F32)
    sut = aM[0:64, 2048:4096].bitcast(F32)
    sgb = aM[0:64, 4096:5120].bitcast(BF16)
    sub = aM[0:64, 5120:6144].bitcast(BF16)
    itile = S.t("itile", [128, 1024], BF16)
    gsil = S.t("gsil", [128, 1024], BF16)
    it_st = []

    for hs in range(NSL0):
        ld0 = p.ld
        dep = p.step

        def s_w(sync, hs=hs, dep=dep):
            sync.wait_ge(p.fl, dep)
            p.dma(sync, codg.rearrange("p (c n) -> p c n", n=512),
                  cg[:, hs * 512:(hs + 1) * 512].rearrange("(c p) n -> p c n", p=128))
            p.dma(sync, codu.rearrange("p (c n) -> p c n", n=512),
                  cu[:, hs * 512:(hs + 1) * 512].rearrange("(c p) n -> p c n", p=128))
            p.dma(sync, sgt, scg[:, hs * 512:(hs + 1) * 512])
            p.dma(sync, sut, scu[:, hs * 512:(hs + 1) * 512])
        block.sync(s_w)

        def f(v, ld0=ld0):
            v.wait_ge(p.ldh, ld0 + 64)
            v.tensor_copy(sgb[:, 0:512], sgt)
            return v.tensor_copy(sub[:, 0:512], sut)
        p.do(block.vector, f)

        for jc in range(8):
            def f(t, jc=jc):
                t.matmul(p.pD[:, 0:512], C["eall"][:, jc * 128:(jc + 1) * 128],
                         sgb[:, 0:512], start=True, stop=True)
                return t.matmul(p.pD[:, 512:1024], C["eall"][:, jc * 128:(jc + 1) * 128],
                                sub[:, 0:512], start=True, stop=True)
            p.do(block.tensor, f)

            def f(v):
                return v.tensor_copy(bsl[:, 0:1024], p.pD[:, 0:1024])
            p.do(block.vector, f)

            def f(v, jc=jc):
                emit_decode_tile(p, v, wide, codg[:, jc * 512:(jc + 1) * 512], bits,
                                 lambda s, jc=jc: wgd[:, (2 * jc + s) * 512:(2 * jc + s + 1) * 512],
                                 bsl[:, 0:512], 512)
                return emit_decode_tile(p, v, wide, codu[:, jc * 512:(jc + 1) * 512], bits,
                                        lambda s, jc=jc: wud[:, (2 * jc + s) * 512:(2 * jc + s + 1) * 512],
                                        bsl[:, 512:1024], 512)
            p.do(block.vector, f)

        if debug and hs == 0:
            def s_dw(sync):
                sync.wait_ge(p.fl, p.step)
                for pc2 in range(16):
                    p.dma_st(sync, dbg_wg[pc2 * 128:(pc2 + 1) * 128, :],
                             wgd[:, pc2 * 512:(pc2 + 1) * 512])
            block.sync(s_dw)

        for w in range(2):
            for s in range(2):
                for tp in range(2):
                    def f(t, hs=hs, w=w, s=s, tp=tp):
                        for half, (pg, pu) in enumerate(((p.pA, p.pB), (p.pC, p.pE))):
                            tt = tp * 2 + half
                            for pc in range(16):
                                t.matmul(pg[:, 0:512],
                                         wgd[:, pc * 512 + w * 256 + s: pc * 512 + (w + 1) * 256: 2],
                                         xP[:, pc * T + tt * 512:pc * T + (tt + 1) * 512],
                                         start=(pc == 0), stop=(pc == 15))
                            for pc in range(16):
                                mm = t.matmul(pu[:, 0:512],
                                              wud[:, pc * 512 + w * 256 + s: pc * 512 + (w + 1) * 256: 2],
                                              xP[:, pc * T + tt * 512:pc * T + (tt + 1) * 512],
                                              start=(pc == 0), stop=(pc == 15))
                        return mm
                    p.do(block.tensor, f)

                    def f(sc):
                        sc.activation(gsil[:, 0:512], p.pA[:, 0:512], AF.Gelu_apprx_tanh)
                        return sc.activation(gsil[:, 512:1024], p.pC[:, 0:512], AF.Gelu_apprx_tanh)
                    p.do(block.scalar, f)

                    k = len(it_st)

                    def f(v, k=k):
                        if k >= 2:
                            v.wait_ge(p.sth, it_st[k - 1])
                        v.tensor_tensor(itile[:, 0:512], gsil[:, 0:512], p.pB[:, 0:512], ALU.mult)
                        return v.tensor_tensor(itile[:, 512:1024], gsil[:, 512:1024], p.pE[:, 0:512], ALU.mult)
                    p.do(block.vector, f)

                    rbase = 128 * (4 * hs + 2 * w + s)

                    def s_i(sync, rbase=rbase, tp=tp):
                        sync.wait_ge(p.fl, p.step)
                        p.dma_st(sync, i_dram[rbase:rbase + 128, tp * 1024:tp * 1024 + 512],
                                 itile[:, 0:512])
                        p.dma_st(sync, i_dram[rbase:rbase + 128, tp * 1024 + 512:(tp + 1) * 1024],
                                 itile[:, 512:1024])
                    block.sync(s_i)
                    it_st.append(p.st)

    codd = bview(aW, 0, U8, 4 * 2048)
    widd = bview(aW, 16384, U16, 2048)
    bitd = bview(aW, 20480, U16, 2048)
    wdd = bview(aW, 24576, BF16, 2048)
    sdt = aM[0:8, 0:8192].bitcast(F32)
    sdb = aM[0:8, 8192:12288].bitcast(BF16)
    bdt = bview(aM, 16384, BF16, 2048)
    wd_st2 = []
    for ds in range(16):
        ld0 = p.ld
        dep = p.step

        def s_w(sync, ds=ds, dep=dep):
            sync.wait_ge(p.fl, dep)
            p.dma(sync, codd.rearrange("p (c o) -> p c o", o=2048),
                  cd[ds * 512:(ds + 1) * 512, :].rearrange("(c p) o -> p c o", p=128))
        block.sync(s_w)

        for jq in range(4):
            jcp = ds * 4 + jq
            def s_s(sync, jcp=jcp):
                sync.wait_ge(p.fl, p.step)
                p.dma(sync, sdt, scd[jcp * 8:(jcp + 1) * 8, :])
            block.sync(s_s)
            ld1 = p.ld

            def f(v, ld1=ld1):
                v.wait_ge(p.ldh, ld1)
                return v.tensor_copy(sdb[:, 0:2048], sdt)
            p.do(block.vector, f)

            def f(t):
                for q in range(4):
                    mm = t.matmul(p.pD[:, 0:512] if q == 0 else [p.pD[:, 512:1024], p.pE[:, 0:512], p.pF[:, 0:512]][q - 1],
                                  C["e8"][:, 0:128], sdb[:, q * 512:(q + 1) * 512],
                                  start=True, stop=True)
                return mm
            p.do(block.tensor, f)

            def f(v):
                v.tensor_copy(bdt[:, 0:512], p.pD[:, 0:512])
                v.tensor_copy(bdt[:, 512:1024], p.pD[:, 512:1024])
                v.tensor_copy(bdt[:, 1024:1536], p.pE[:, 0:512])
                return v.tensor_copy(bdt[:, 1536:2048], p.pF[:, 0:512])
            p.do(block.vector, f)

            for s in range(2):
                k2 = len(wd_st2)

                def f(v, jq=jq, s=s, k2=k2, ld0=ld0):
                    v.wait_ge(p.ldh, ld0 + 16)
                    if k2 >= 1:
                        v.wait_ge(p.sth, wd_st2[k2 - 1])
                    cod = codd[:, jq * 2048:(jq + 1) * 2048]
                    v.tensor_copy(widd[:, 0:2048], cod)
                    if s == 0:
                        v.tensor_scalar(bitd[:, 0:2048], widd[:, 0:2048], 7, 6, ALU.bitwise_and, ALU.logical_shift_left)
                        v.tensor_scalar(widd[:, 0:2048], widd[:, 0:2048], 8, 12, ALU.bitwise_and, ALU.logical_shift_left)
                    else:
                        v.tensor_scalar(bitd[:, 0:2048], widd[:, 0:2048], 0x70, 2, ALU.bitwise_and, ALU.logical_shift_left)
                        v.tensor_scalar(widd[:, 0:2048], widd[:, 0:2048], 0x80, 8, ALU.bitwise_and, ALU.logical_shift_left)
                    v.tensor_tensor(bitd[:, 0:2048], bitd[:, 0:2048], widd[:, 0:2048], ALU.bitwise_or)
                    return v.tensor_tensor(wdd[:, 0:2048], bitd[:, 0:2048].bitcast(BF16), bdt[:, 0:2048], ALU.mult)
                p.do(block.vector, f)

                rbase = 128 * (2 * jcp + s)

                def s_wd(sync, rbase=rbase):
                    sync.wait_ge(p.fl, p.step)
                    p.dma_st(sync, wd_dram[rbase:rbase + 128, :], wdd[:, 0:2048])
                block.sync(s_wd)
                wd_st2.append(p.st)

    if debug:
        def s_dbg(sync):
            sync.wait_ge(p.fl, p.step)
            for j in range(DC):
                p.dma_st(sync, dbg_xp[j * 128:(j + 1) * 128, :], xbf[:, j * T:(j + 1) * T])
        block.sync(s_dbg)

        def s_dbg2(sync):
            p.dma_st(sync, dbg_i[:, :], i_dram[0:512, :])
            p.dma_st(sync, dbg_wd[:, :], wd_dram[0:512, :])
        block.sync(s_dbg2)

    ibuf = xbf
    wcol = bview(aW, 0, BF16, 16 * 128)
    mtile = S.t("mtile", [128, 1024], F32)
    mt_st = []
    acc_prev = {}
    st_before = p.st
    for g in range(8):
        ld0 = p.ld
        dep = p.step

        def s_i2(sync, g=g, dep=dep, st_before=st_before):
            sync.wait_ge(p.fl, dep)
            sync.wait_ge(p.sth, st_before)
            p.dma(sync, ibuf[:].rearrange("p (c t) -> p c t", t=T),
                  i_dram[g * 2048:(g + 1) * 2048, :].rearrange("(c p) t -> p c t", p=128))
        block.sync(s_i2)

        dmm_steps = {}
        for oc in range(DC):
            ld1 = p.ld
            dep = p.step if oc < 2 else dmm_steps[oc - 2]
            wso = (oc % 2) * 4096

            def s_wc(sync, g=g, oc=oc, dep=dep, wso=wso):
                sync.wait_ge(p.fl, dep)
                p.dma(sync, bview(aW, wso, BF16, 16 * 128).rearrange("p (c o) -> p c o", o=128),
                      wd_dram[g * 2048:(g + 1) * 2048, oc * 128:(oc + 1) * 128]
                      .rearrange("(c p) o -> p c o", p=128))
            block.sync(s_wc)

            for tt in range(NTT):
                def f(t, g=g, oc=oc, tt=tt, ld0=ld0, ld1=ld1, wso=wso):
                    t.wait_ge(p.ldh, ld0 + 16)
                    t.wait_ge(p.ldh, ld1 + 16)
                    wv = bview(aW, wso, BF16, 16 * 128)
                    for c in range(16):
                        mm = t.matmul(p.pC[:, 0:512], wv[:, c * 128:(c + 1) * 128],
                                      ibuf[:, c * T + tt * 512:c * T + (tt + 1) * 512],
                                      start=(c == 0), stop=(c == 15))
                    if tt == NTT - 1:
                        dmm_steps[oc] = p.step + 1
                    return mm
                p.do(block.tensor, f)

                k = len(mt_st)
                slot = slice((k % 2) * 512, (k % 2) * 512 + 512)

                def f(v, k=k, slot=slot):
                    if k >= 2:
                        v.wait_ge(p.sth, mt_st[k - 2])
                    return v.tensor_copy(mtile[:, slot], p.pC[:, 0:512])
                p.do(block.vector, f)

                def s_m(gq, g=g, oc=oc, tt=tt, slot=slot):
                    gq.wait_ge(p.fl, p.step)
                    if g > 0:
                        gq.wait_ge(p.sth, acc_prev[(oc, tt)])
                    p.dma_st(gq, mix_dram[oc * 128:(oc + 1) * 128, tt * 512:(tt + 1) * 512],
                             mtile[:, slot], accum=(g > 0))
                block.gpsimd(s_m)
                mt_st.append(p.st)
                acc_prev[(oc, tt)] = p.st

    def f(g):
        g.wait_ge(p.sth, p.st)
        return g.engine_nop()
    p.do(block.gpsimd, f)
    if debug:
        def s_dm2(sync):
            p.dma_st(sync, dbg_mix[:, :], mix_dram[:, :])
        block.sync(s_dm2)
    emit_postln_residual(p, mix_dram, x_in, x_out, C, scratch, xP)
    emit_store_wait(p)
    return p


def prep_l0ffn_inputs(full):
    def codesT(w):
        return np.ascontiguousarray(_u8codes(np.asarray(w)).T)
    perm = np.zeros((128, 512), BF16NP)
    for s in range(2):
        for i in range(64):
            perm[2 * i + s, s * 256 + i] = 1
            perm[2 * i + s, s * 256 + 128 + 64 + i] = 1
    eall = np.zeros((64, 1024), BF16NP)
    for jc in range(8):
        for i in range(128):
            eall[(128 * jc + i) // 16, jc * 128 + i] = BF16NP(TWO126)
    e8 = np.zeros((8, 128), BF16NP)
    for i in range(128):
        e8[i // 16, i] = BF16NP(TWO126)
    shared = {
        "cg": codesT(full["L0_ffn_gate_W_packed"]),
        "cu": codesT(full["L0_ffn_up_W_packed"]),
        "cd": codesT(full["L0_ffn_down_W_packed"]),
        "scg": np.ascontiguousarray(np.asarray(full["L0_ffn_gate_scales"], np.float32).T),
        "scu": np.ascontiguousarray(np.asarray(full["L0_ffn_up_scales"], np.float32).T),
        "scd": np.ascontiguousarray(np.asarray(full["L0_ffn_down_scales"], np.float32).T),
        "cdiv": np.full((128, 1), 1.0 / 2048.0, BF16NP),
        "crow": np.ones((1, 128), np.float32),
        "ceall": eall, "ce8": e8, "cperm": perm,
    }
    return shared


def run_l0ffn(x_flat, full, debug=False):
    shared = prep_l0ffn_inputs(full)
    shards = shard_fm(x_flat)
    p = build_l0ffn(debug)
    in_maps = [dict(shared, x_in=s) for s in shards]
    res = run_bass_kernel_spmd(p.nc, in_maps, core_ids=list(range(8)))
    p.close()
    if debug:
        return res
    return unshard_fm([res.results[c]["x_out"] for c in range(8)])



def build_l0attn(debug=False, fuse=None):
    if fuse is None:
        p = Prog().start()
        own = True
    else:
        p, own = fuse["p"], False
    nc = p.nc
    pre = "" if own else fuse["pre"]
    if own:
        x_in = nc.dram_tensor("x_in", [D, T], F32, kind="ExternalInput").ap()
    else:
        x_in = fuse["x_in"]
    wqkvT = nc.dram_tensor("wqkvT", [D, 3 * D], F32, kind="ExternalInput").ap()
    woutT = nc.dram_tensor("woutT", [D, D], F32, kind="ExternalInput").ap()
    cbias = nc.dram_tensor("cbias", [128, 32], F32, kind="ExternalInput").ap()
    cbrow = nc.dram_tensor("cbrow", [1, D], F32, kind="ExternalInput").ap()
    cbout = nc.dram_tensor("cbout", [128, 16], F32, kind="ExternalInput").ap()
    if own:
        cdiv = nc.dram_tensor("cdiv", [128, 1], BF16, kind="ExternalInput").ap()
        crow = nc.dram_tensor("crow", [1, 128], F32, kind="ExternalInput").ap()
        cid = nc.dram_tensor("cid", [128, 128], BF16, kind="ExternalInput").ap()
    if own:
        x_out = nc.dram_tensor("x_out", [D, T], F32, kind="ExternalOutput").ap()
    else:
        x_out = fuse["x_out"]
    if debug:
        dbg_q = nc.dram_tensor("dbg_q", [D, T], BF16, kind="ExternalOutput").ap()
        dbg_s = nc.dram_tensor("dbg_s", [T, 1024], F32, kind="ExternalOutput").ap()
        dbg_o = nc.dram_tensor("dbg_o", [T, D], F32, kind="ExternalOutput").ap()
        dbg_mx = nc.dram_tensor("dbg_mx", [128, 48], F32, kind="ExternalOutput").ap()
        dbg_mix = nc.dram_tensor("dbg_mix", [D, T], F32, kind="ExternalOutput").ap()
        dbg_kv0 = nc.dram_tensor("dbg_kv0", [2048, 132], F32, kind="ExternalOutput").ap()
        dbg_qs0 = nc.dram_tensor("dbg_qs0", [128, 16], F32, kind="ExternalOutput").ap()
        dbg_c = nc.dram_tensor("dbg_c", [128, 1024], F32, kind="ExternalOutput").ap()
        dbg_c2 = nc.dram_tensor("dbg_c2", [128, 132], F32, kind="ExternalOutput").ap()
        dbg_kg = nc.dram_tensor("dbg_kg", [512, T], BF16, kind="ExternalOutput").ap()

    kdram = nc.dram_tensor(pre + "kdram", [D, T], BF16)
    vdram = nc.dram_tensor(pre + "vdram", [T, D], BF16)
    kg = nc.dram_tensor(pre + "kg", [4 * D, T], BF16)
    vg = nc.dram_tensor(pre + "vg", [4 * T, D], BF16)
    sdram = nc.dram_tensor(pre + "sdram", [T, 4 * T], F32)
    odram = nc.dram_tensor(pre + "odram", [T, D], F32)
    mix_dram = nc.dram_tensor(pre + "mix_dram", [D, T], F32)

    S = SAlloc(p, pre, fuse)
    if fuse is None:
        xbf = p.sb("xbf", [128, DC * T], BF16)
        qfm = p.sb("qfm", [128, DC * T], BF16)
    else:
        xbf = bview(fuse["P1"], 0, BF16, DC * T)
        qfm = bview(fuse["P2"], 0, BF16, DC * T)
    aW = S.t("aW", [128, 24576], U8)
    aM = S.t("aM", [128, 16384], U8)
    xf = S.t("xf", [128, 2 * T], F32)
    sq = S.t("sq", [128, 4096], U8)
    tmp = S.t("tmp", [128, 512], F32)
    mx = S.t("mx", [128, 16], F32)
    lsum = S.t("lsum", [128, 16], F32)
    rs1 = S.t("rs1", [128, 1], F32)
    bias_sb = S.t("bias_sb", [128, 32], F32)
    bout_sb = S.t("bout_sb", [128, 16], F32)

    mbc = bview(aM, 0, F32, T)
    rbc = bview(aM, 8192, F32, T)
    vbias = bview(aM, 0, F32, 2048)
    srow = qfm[0:1, 0:8192].bitcast(F32)
    rstd = qfm[0:1, 8192:12288].bitcast(F32)
    mf2 = bview(qfm, 32768, F32, 2 * T)
    scratch = dict(xf=xf, sq=sq, srow=srow, rstd=rstd, mbc=mbc, rbc=rbc,
                   tmp=tmp, mf2=mf2)

    C = {}
    if fuse is None:
        C["ones_col_div"] = p.sb("c_div", [128, 1], BF16)
        C["ones_row_f32"] = p.sb("c_row", [1, 128], F32)
        C["id128"] = p.sb("c_id", [128, 128], BF16)
    else:
        C["ones_col_div"] = fuse["C"]["ones_col_div"]
        C["ones_row_f32"] = fuse["C"]["ones_row_f32"]
        C["id128"] = fuse["C"]["id128"]
    block = p.block

    def s_consts(sync):
        if fuse is None:
            p.dma(sync, C["ones_col_div"][:], cdiv)
            p.dma(sync, C["ones_row_f32"][:], crow)
            p.dma(sync, C["id128"][:], cid)
        p.dma(sync, bias_sb[:], cbias)
        p.dma(sync, bout_sb[:], cbout)
    block.sync(s_consts)
    ld_consts = p.ld

    def f(v):
        v.wait_ge(p.ldh, ld_consts)
        return v.engine_nop()
    p.do(block.vector, f)

    emit_ln(p, x_in, xbf, C, scratch)

    vbrow = aW[0:1, 0:8192].bitcast(F32)

    def s_vb(sync):
        p.dma(sync, vbrow, cbrow)
    block.sync(s_vb)
    ld_vb = p.ld

    def f(t):
        t.wait_ge(p.ldh, ld_vb)
        for et in range(4):
            mm = t.matmul(p.pD[:, 0:512], C["ones_row_f32"][0:1, :],
                          vbrow[0:1, et * 512:(et + 1) * 512], start=True, stop=True)
        return mm
    for et in range(4):
        def f(t, et=et):
            if et == 0:
                t.wait_ge(p.ldh, ld_vb)
            return t.matmul(p.pD[:, 0:512], C["ones_row_f32"][0:1, :],
                            vbrow[0:1, et * 512:(et + 1) * 512], start=True, stop=True)
        p.do(block.tensor, f)

        def f(v, et=et):
            return v.tensor_copy(vbias[:, et * 512:(et + 1) * 512], p.pD[:, 0:512])
        p.do(block.vector, f)

    wsl = bview(aW, 8192, BF16, 16 * 512)
    ktile = S.t("ktile", [128, 1024], BF16)
    vtile = S.t("vtile", [128, 1024], BF16)
    k_st = []
    v_st = []

    qk_steps = {}
    for ec in range(32):
        dep = p.step if ec < 2 else qk_steps[ec - 2]
        wso = 8192 + (ec % 2) * 4096

        def s_w(gq, ec=ec, dep=dep, wso=wso):
            gq.wait_ge(p.fl, dep)
            p.dma(gq, bview(aW, wso, BF16, 16 * 128).rearrange("p (c o) -> p c o", o=128),
                  wqkvT[:, ec * 128:(ec + 1) * 128].rearrange("(c p) o -> p c o", p=128))
        block.gpsimd(s_w)
        ld0 = p.ld

        for tt in range(NTT):
            def f(t, ec=ec, tt=tt, ld0=ld0, wso=wso):
                t.wait_ge(p.ldh, ld0)
                wv = bview(aW, wso, BF16, 16 * 128)
                for j in range(DC):
                    mm = t.matmul(p.pA[:, 0:512], wv[:, j * 128:(j + 1) * 128],
                                  xbf[:, j * T + tt * 512:j * T + (tt + 1) * 512],
                                  start=(j == 0), stop=(j == DC - 1))
                if tt == NTT - 1:
                    qk_steps[ec] = p.step + 1
                return mm
            p.do(block.tensor, f)

            if ec < 16:
                def f(v, ec=ec, tt=tt):
                    return v.tensor_scalar(qfm[:, ec * T + tt * 512:ec * T + (tt + 1) * 512],
                                           p.pA[:, 0:512], bias_sb[:, ec:ec + 1], None, ALU.add)
                p.do(block.vector, f)
            else:
                kk = len(k_st)
                slot = slice((kk % 2) * 512, (kk % 2) * 512 + 512)

                def f(v, ec=ec, tt=tt, kk=kk, slot=slot):
                    if kk >= 2:
                        v.wait_ge(p.sth, k_st[kk - 2])
                    return v.tensor_scalar(ktile[:, slot], p.pA[:, 0:512],
                                           bias_sb[:, ec:ec + 1], None, ALU.add)
                p.do(block.vector, f)

                def s_k(sync, ec=ec, tt=tt, slot=slot):
                    sync.wait_ge(p.fl, p.step)
                    p.dma_st(sync, kdram[(ec - 16) * 128:(ec - 15) * 128, tt * 512:(tt + 1) * 512],
                             ktile[:, slot])
                block.sync(s_k)
                k_st.append(p.st)

    vsl = bview(aW, 8192, BF16, 16 * 512)
    for et in range(4):
        dep = p.step

        def s_w(gq, et=et, dep=dep):
            gq.wait_ge(p.fl, dep)
            p.dma(gq, vsl[:].rearrange("p (c o) -> p c o", o=512),
                  wqkvT[:, 2 * D + et * 512:2 * D + (et + 1) * 512].rearrange("(c p) o -> p c o", p=128))
        block.gpsimd(s_w)
        ld0 = p.ld

        for tc in range(DC):
            def f(t, et=et, tc=tc, ld0=ld0):
                t.wait_ge(p.ldh, ld0)
                for j in range(DC):
                    mm = t.matmul(p.pA[:, 0:512], xbf[:, j * T + tc * 128:j * T + tc * 128 + 128],
                                  vsl[:, j * 512:(j + 1) * 512],
                                  start=(j == 0), stop=(j == DC - 1))
                return mm
            p.do(block.tensor, f)

            kk = len(v_st)
            slot = slice((kk % 2) * 512, (kk % 2) * 512 + 512)

            def f(v, et=et, kk=kk, slot=slot):
                if kk >= 2:
                    v.wait_ge(p.sth, v_st[kk - 2])
                return v.tensor_tensor(vtile[:, slot], p.pA[:, 0:512],
                                       vbias[:, et * 512:(et + 1) * 512], ALU.add)
            p.do(block.vector, f)

            def s_v(sync, et=et, tc=tc, slot=slot):
                sync.wait_ge(p.fl, p.step)
                p.dma_st(sync, vdram[tc * 128:(tc + 1) * 128, et * 512:(et + 1) * 512],
                         vtile[:, slot])
            block.sync(s_v)
            v_st.append(p.st)

    st_kv = p.st

    def s_cc(gq):
        gq.wait_ge(p.fl, p.step)
        gq.wait_ge(p.sth, st_kv)
        gq.collective_compute("AllGather", ALU.bypass, replica_groups=RG4,
                              ins=[kdram.ap().opt()], outs=[kg.ap().opt()]).then_inc(p.cch)
        gq.collective_compute("AllGather", ALU.bypass, replica_groups=RG4,
                              ins=[vdram.ap().opt()], outs=[vg.ap().opt()]).then_inc(p.cch)
    block.gpsimd(s_cc)
    p.cc += 2
    cc_after = p.cc

    kslab = bview(xbf, 0, BF16, 16 * 512)
    kslab2 = bview(xbf, 16384, BF16, 16 * 512)
    stile = bview(xbf, 32768, F32, 1024)
    s_st = []

    def f(v):
        return v.memset(mx[:], -3.0e38)
    p.do(block.vector, f)

    for kt in range(16):
        dep = p.step
        ksl = [kslab, kslab2][kt % 2]

        def s_ks(sync, kt=kt, dep=dep, ksl=ksl):
            sync.wait_ge(p.fl, max(0, dep - 2))
            if kt == 0:
                sync.wait_ge(p.cch, cc_after)
            r, tl = kt // 4, kt % 4
            p.dma(sync, ksl[:].rearrange("p (c t) -> p c t", t=512),
                  kg[r * D:(r + 1) * D, tl * 512:(tl + 1) * 512].rearrange("(c p) t -> p c t", p=128))
        block.sync(s_ks)
        ld0 = p.ld

        for qc in range(16):
            def f(t, kt=kt, qc=qc, ld0=ld0, ksl=ksl):
                t.wait_ge(p.ldh, ld0)
                for ec in range(DC):
                    mm = t.matmul(p.pA[:, 0:512], qfm[:, ec * T + qc * 128:ec * T + qc * 128 + 128],
                                  ksl[:, ec * 512:(ec + 1) * 512],
                                  start=(ec == 0), stop=(ec == DC - 1))
                return mm
            p.do(block.tensor, f)

            kk = len(s_st)
            slot = slice((kk % 2) * 512, (kk % 2) * 512 + 512)

            def f(v, kk=kk, slot=slot, qc=qc):
                if kk >= 2:
                    v.wait_ge(p.sth, s_st[kk - 2])
                v.tensor_copy(stile[:, slot], p.pA[:, 0:512])
                return v.tensor_reduce(rs1[:], p.pA[:, 0:512], AX.X, ALU.max)
            p.do(block.vector, f)

            def f(v, qc=qc):
                return v.tensor_tensor(mx[:, qc:qc + 1], mx[:, qc:qc + 1], rs1[:], ALU.max)
            p.do(block.vector, f)

            def s_s(sync, kt=kt, qc=qc, slot=slot):
                sync.wait_ge(p.fl, p.step)
                p.dma_st(sync, sdram[qc * 128:(qc + 1) * 128, kt * 512:(kt + 1) * 512],
                         stile[:, slot])
            block.sync(s_s)
            s_st.append(p.st)

    SCALE0 = float(1.0 / np.sqrt(2048.0))
    bmax = S.t("bmax", [128, 16], F32)
    st_ph3 = p.st

    def f(v):
        v.wait_ge(p.sth, st_ph3)
        v.memset(lsum[:], 0.0)
        return v.tensor_scalar(bmax[:], mx[:], float(-SCALE0), None, ALU.mult)
    p.do(block.vector, f)

    vslab = bview(xbf, 0, BF16, 16 * T)
    sf = bview(aW, 0, F32, 1024)
    ptb = bview(aW, 8192, BF16, 16 * 128)
    ptile = bview(aW, 12288, BF16, 512)
    otile = bview(aW, 16384, F32, 1024)
    o_acc_prev = {}
    o_st = []
    sload = []

    for grp in range(4):
        dep = p.step

        def s_vs(sync, grp=grp, dep=dep):
            sync.wait_ge(p.fl, dep)
            p.dma(sync, vslab[:].rearrange("p (c e) -> p c e", e=D),
                  vg[grp * T:(grp + 1) * T, :].rearrange("(c p) e -> p c e", p=128))
        block.sync(s_vs)
        ld_v = p.ld

        for qc in range(16):
            for k4 in range(4):
                kt = grp * 4 + k4
                dep = p.step

                def s_sf(sync, kt=kt, qc=qc, dep=dep):
                    sync.wait_ge(p.fl, max(0, dep - 1))
                    p.dma(sync, sf[:, (kt % 2) * 512:(kt % 2) * 512 + 512],
                          sdram[qc * 128:(qc + 1) * 128, kt * 512:(kt + 1) * 512])
                block.sync(s_sf)
                ld0 = p.ld

                def f(sc, kt=kt, qc=qc, ld0=ld0):
                    sc.wait_ge(p.ldh, ld0)
                    return sc.activation(ptile[:, 0:512], sf[:, (kt % 2) * 512:(kt % 2) * 512 + 512],
                                         AF.Exp, bias=bmax[:, qc:qc + 1], scale=np.float32(SCALE0),
                                         accum_out=rs1[:])
                p.do(block.scalar, f)

                def f(v, qc=qc):
                    return v.tensor_tensor(lsum[:, qc:qc + 1], lsum[:, qc:qc + 1], rs1[:], ALU.add)
                p.do(block.vector, f)

                def f(t):
                    for i in range(4):
                        last = t.transpose(p.pT[:, i * 128:(i + 1) * 128],
                                           ptile[:, i * 128:(i + 1) * 128], C["id128"][:])
                    return last
                p.do(block.tensor, f)

                def f(v, k4=k4):
                    for i in range(4):
                        last = v.tensor_copy(ptb[:, (k4 * 4 + i) * 128:(k4 * 4 + i + 1) * 128],
                                             p.pT[:, i * 128:(i + 1) * 128])
                    return last
                p.do(block.vector, f)

            for et in range(4):
                def f(t, et=et, ld_v=ld_v):
                    t.wait_ge(p.ldh, ld_v)
                    for c in range(16):
                        mm = t.matmul(p.pB[:, 0:512], ptb[:, c * 128:(c + 1) * 128],
                                      vslab[:, c * D + et * 512:c * D + (et + 1) * 512],
                                      start=(c == 0), stop=(c == 15))
                    return mm
                p.do(block.tensor, f)

                kk = len(o_st)
                slot = slice((kk % 2) * 512, (kk % 2) * 512 + 512)

                def f(v, kk=kk, slot=slot):
                    if kk >= 2:
                        v.wait_ge(p.sth, o_st[kk - 2])
                    return v.tensor_copy(otile[:, slot], p.pB[:, 0:512])
                p.do(block.vector, f)

                def s_o(gq, grp=grp, qc=qc, et=et, slot=slot):
                    gq.wait_ge(p.fl, p.step)
                    if grp > 0:
                        gq.wait_ge(p.sth, o_acc_prev[(qc, et)])
                    p.dma_st(gq, odram[qc * 128:(qc + 1) * 128, et * 512:(et + 1) * 512],
                             otile[:, slot], accum=(grp > 0))
                block.gpsimd(s_o)
                o_st.append(p.st)
                o_acc_prev[(qc, et)] = p.st

    if debug:
        def s_d1(sync):
            sync.wait_ge(p.fl, p.step)
            sync.wait_ge(p.sth, p.st)
            for ec2 in range(DC):
                p.dma_st(sync, dbg_q[ec2 * 128:(ec2 + 1) * 128, :], qfm[:, ec2 * T:(ec2 + 1) * T])
            p.dma_st(sync, dbg_s[:, :], sdram[:, 0:1024])
            p.dma_st(sync, dbg_o[:, :], odram[:, :])
            p.dma_st(sync, dbg_mx[:, 0:16], mx[:])
            p.dma_st(sync, dbg_mx[:, 16:32], lsum[:])
            p.dma_st(sync, dbg_mx[:, 32:48], bmax[:])
            p.dma_st(sync, dbg_kg[:, :], kg[0:512, :])
        block.sync(s_d1)

    linv = S.t("linv", [128, 16], F32)

    def f(v):
        return v.reciprocal(linv[:], lsum[:])
    p.do(block.vector, f)

    ctxT = bview(xbf, 0, BF16, DC * T)
    obuf = bview(aW, 0, F32, 2 * T)
    otm = bview(aW, 16384, BF16, T)
    st_o_done = p.st
    for qc in range(16):
        dep = p.step

        def s_ob(sync, qc=qc, dep=dep):
            sync.wait_ge(p.fl, max(0, dep - 1))
            if qc == 0:
                sync.wait_ge(p.sth, st_o_done)
            p.dma(sync, obuf[:, (qc % 2) * T:(qc % 2 + 1) * T], odram[qc * 128:(qc + 1) * 128, :])
        block.sync(s_ob)
        ld0 = p.ld

        def f(v, qc=qc, ld0=ld0):
            v.wait_ge(p.ldh, ld0)
            return v.tensor_scalar(otm[:, 0:T], obuf[:, (qc % 2) * T:(qc % 2 + 1) * T],
                                   linv[:, qc:qc + 1], None, ALU.mult)
        p.do(block.vector, f)

        for eg in range(2):
            def f(t, eg=eg):
                for i in range(8):
                    last = t.transpose(p.pT[:, i * 128:(i + 1) * 128],
                                       otm[:, (eg * 8 + i) * 128:(eg * 8 + i + 1) * 128],
                                       C["id128"][:])
                return last
            p.do(block.tensor, f)

            def f(v, qc=qc, eg=eg):
                for i in range(8):
                    ec = eg * 8 + i
                    last = v.tensor_copy(ctxT[:, ec * T + qc * 128:ec * T + qc * 128 + 128],
                                         p.pT[:, i * 128:(i + 1) * 128])
                return last
            p.do(block.vector, f)

    mtile = S.t("mtile", [128, 1024], F32)
    mt_st = []
    op_steps = {}
    for oc in range(16):
        dep = p.step if oc < 2 else op_steps[oc - 2]
        wso = (oc % 2) * 4096

        def s_w(gq, oc=oc, dep=dep, wso=wso):
            gq.wait_ge(p.fl, dep)
            p.dma(gq, bview(aW, wso, BF16, 16 * 128).rearrange("p (c o) -> p c o", o=128),
                  woutT[:, oc * 128:(oc + 1) * 128].rearrange("(c p) o -> p c o", p=128))
        block.gpsimd(s_w)
        ld0 = p.ld

        for tt in range(NTT):
            def f(t, oc=oc, tt=tt, ld0=ld0, wso=wso):
                t.wait_ge(p.ldh, ld0)
                wv = bview(aW, wso, BF16, 16 * 128)
                for ec in range(DC):
                    mm = t.matmul(p.pC[:, 0:512], wv[:, ec * 128:(ec + 1) * 128],
                                  ctxT[:, ec * T + tt * 512:ec * T + (tt + 1) * 512],
                                  start=(ec == 0), stop=(ec == DC - 1))
                if tt == NTT - 1:
                    op_steps[oc] = p.step + 1
                return mm
            p.do(block.tensor, f)

            kk = len(mt_st)
            slot = slice((kk % 2) * 512, (kk % 2) * 512 + 512)

            def f(v, oc=oc, kk=kk, slot=slot):
                if kk >= 2:
                    v.wait_ge(p.sth, mt_st[kk - 2])
                return v.tensor_scalar(mtile[:, slot], p.pC[:, 0:512],
                                       bout_sb[:, oc:oc + 1], None, ALU.add)
            p.do(block.vector, f)

            def s_m(sync, oc=oc, tt=tt, slot=slot):
                sync.wait_ge(p.fl, p.step)
                p.dma_st(sync, mix_dram[oc * 128:(oc + 1) * 128, tt * 512:(tt + 1) * 512],
                         mtile[:, slot])
            block.sync(s_m)
            mt_st.append(p.st)

    def f(g):
        g.wait_ge(p.sth, p.st)
        return g.engine_nop()
    p.do(block.gpsimd, f)
    if debug:
        def s_d2(sync):
            p.dma_st(sync, dbg_mix[:, :], mix_dram[:, :])
        block.sync(s_d2)
    emit_postln_residual(p, mix_dram, x_in, x_out, C, scratch, xbf)
    emit_store_wait(p)
    return p


def prep_l0attn_inputs(full):
    wqkv = np.asarray(full["L0_attn_w_qkv"], np.float32)
    wout = np.asarray(full["L0_attn_w_out"], np.float32)
    bqkv = np.asarray(full["L0_attn_b_qkv"], np.float32)
    bout = np.asarray(full["L0_attn_b_out"], np.float32)
    shared = {
        "wqkvT": np.ascontiguousarray(wqkv.T),
        "woutT": np.ascontiguousarray(wout.T),
        "cbias": np.ascontiguousarray(bqkv[0:4096].reshape(32, 128).T),
        "cbrow": bqkv[4096:6144].reshape(1, 2048),
        "cbout": np.ascontiguousarray(bout.reshape(16, 128).T),
        "cdiv": np.full((128, 1), 1.0 / 2048.0, BF16NP),
        "crow": np.ones((1, 128), np.float32),
        "cid": np.eye(128, dtype=BF16NP),
    }
    return shared


def run_l0attn(x_flat, full, debug=False):
    shared = prep_l0attn_inputs(full)
    shards = shard_fm(x_flat)
    p = build_l0attn(debug)
    in_maps = [dict(shared, x_in=s) for s in shards]
    res = run_bass_kernel_spmd(p.nc, in_maps, core_ids=list(range(8)))
    p.close()
    if debug:
        return res
    return unshard_fm([res.results[c]["x_out"] for c in range(8)])



def build_l1attn(debug=False, fuse=None):
    if fuse is None:
        p = Prog().start()
        own = True
    else:
        p, own = fuse["p"], False
    nc = p.nc
    pre = "" if own else fuse["pre"]
    if own:
        x_in = nc.dram_tensor("x_in", [D, T], F32, kind="ExternalInput").ap()
    else:
        x_in = fuse["x_in"]
    wqT = nc.dram_tensor("wqT", [D, D], F32, kind="ExternalInput").ap()
    wkT = nc.dram_tensor("wkT", [D, D], F32, kind="ExternalInput").ap()
    wvT = nc.dram_tensor("wvT", [D, D], F32, kind="ExternalInput").ap()
    woT = nc.dram_tensor("woT", [D, D], F32, kind="ExternalInput").ap()
    memaug = nc.dram_tensor("memaug", [64 * 128, 132], F32, kind="ExternalInput").ap()
    mnormT = nc.dram_tensor("mnormT", [128, 64], F32, kind="ExternalInput").ap()
    gatein = nc.dram_tensor("gatein", [1, 16], F32, kind="ExternalInput").ap()
    maskpre = nc.dram_tensor("maskpre", [128, 4], F32, kind="ExternalInput").ap()
    ctril = nc.dram_tensor("ctril", [128, 128], BF16, kind="ExternalInput").ap()
    if own:
        cdiv = nc.dram_tensor("cdiv", [128, 1], BF16, kind="ExternalInput").ap()
        crow = nc.dram_tensor("crow", [1, 128], F32, kind="ExternalInput").ap()
        cid = nc.dram_tensor("cid", [128, 128], BF16, kind="ExternalInput").ap()
    conebf = nc.dram_tensor("conebf", [128, 1], BF16, kind="ExternalInput").ap()
    crelrow = nc.dram_tensor("crelrow", [128, 1], F32, kind="ExternalInput").ap()
    if own:
        x_out = nc.dram_tensor("x_out", [D, T], F32, kind="ExternalOutput").ap()
    else:
        x_out = fuse["x_out"]
    if debug:
        dbg_sk = nc.dram_tensor("dbg_sk", [T, D], BF16, kind="ExternalOutput").ap()
        dbg_v = nc.dram_tensor("dbg_v", [T, D], BF16, kind="ExternalOutput").ap()
        dbg_kvg = nc.dram_tensor("dbg_kvg", [4 * 2048, 132], F32, kind="ExternalOutput").ap()
        dbg_qs = nc.dram_tensor("dbg_qs", [128, 16], F32, kind="ExternalOutput").ap()
        dbg_w = nc.dram_tensor("dbg_w", [128, 64 + 16 + 132], F32, kind="ExternalOutput").ap()
        dbg_mix = nc.dram_tensor("dbg_mix", [D, T], F32, kind="ExternalOutput").ap()
        dbg_kv0 = nc.dram_tensor("dbg_kv0", [2048, 132], F32, kind="ExternalOutput").ap()
        dbg_qs0 = nc.dram_tensor("dbg_qs0", [128, 16], F32, kind="ExternalOutput").ap()
        dbg_c = nc.dram_tensor("dbg_c", [128, 1024], F32, kind="ExternalOutput").ap()
        dbg_c2 = nc.dram_tensor("dbg_c2", [128, 132], F32, kind="ExternalOutput").ap()

    sk_dram = nc.dram_tensor(pre + "sk_dram", [T, D], BF16)
    v_dram = nc.dram_tensor(pre + "v_dram", [T, D], BF16)
    kv_dram = nc.dram_tensor(pre + "kv_dram", [16 * 128, 132], F32)
    kvg_dram = nc.dram_tensor(pre + "kvg_dram", [4 * 16 * 128, 132], F32)
    qs_dram = nc.dram_tensor(pre + "qs_dram", [128, 16], F32)
    qsg_dram = nc.dram_tensor(pre + "qsg_dram", [128, 16], F32)
    w_dram = nc.dram_tensor(pre + "w_dram", [16, 4], F32)
    mix_dram = nc.dram_tensor(pre + "mix_dram", [D, T], F32)

    S = SAlloc(p, pre, fuse)
    if fuse is None:
        xbf = p.sb("xbf", [128, DC * T], BF16)
        aG = p.sb("aG", [128, 65536], U8)
    else:
        xbf = bview(fuse["P1"], 0, BF16, DC * T)
        aG = bview(fuse["P2"], 0, U8, 65536)
    aM = S.t("aM", [128, 16384], U8)
    xf = S.t("xf", [128, 2 * T], F32)
    sq = S.t("sq", [128, 4096], U8)
    tmp = S.t("tmp", [128, 512], F32)
    memb = S.t("memb", [128, 64 * 132], BF16)
    spv = S.t("spv", [128, 16 * 132], F32)
    stbf = S.t("stbf", [128, 132], BF16)
    mtb = S.t("mtb", [128, 132], BF16)
    sqt = S.t("sqt", [128, 128], BF16)
    wsl = S.t("wsl", [128, 2 * 16 * 128], BF16)
    qsum = S.t("qsum", [128, 16], F32)
    xsb = S.t("xsb", [128, 16], BF16)
    wcol = S.t("wcol", [128, 64], F32)
    gcol = S.t("gcol", [128, 16], F32)
    relc = S.t("relc", [16, 8], F32)
    rs1 = S.t("rs1", [128, 1], F32)
    accmem = S.t("accmem", [128, 128], F32)
    rs2 = S.t("rs2", [128, 1], F32)
    rs4a = S.t("rs4a", [128, 4], F32)
    rs4b = S.t("rs4b", [128, 4], F32)
    oloc = S.t("oloc", [128, 128], F32)
    t128 = S.t("t128", [128, 128], F32)
    sktm = S.t("sktm", [128, 128], BF16)
    stm = S.t("stm", [128, 128], BF16)
    otm = S.t("otm", [128, 128], BF16)

    kq_buf = bview(aG, 0, BF16, 16 * 512)
    skt_buf = bview(aG, 16384, BF16, 4 * 2048)
    vg_buf = bview(aG, 32768, BF16, 4 * 2048)
    ofm_buf = bview(aG, 49152, BF16, 16 * 512)
    mf2 = bview(aG, 0, F32, 2 * T)
    srow = aG[0:1, 16384:32768].bitcast(F32)
    rstd = aG[0:1, 32768:40960].bitcast(F32)
    mbc = bview(aM, 0, F32, T)
    rbc = bview(aM, 8192, F32, T)
    scratch = dict(xf=xf, sq=sq, srow=srow, rstd=rstd, mbc=mbc, rbc=rbc,
                   tmp=tmp, mf2=mf2)

    C = {}
    if fuse is None:
        C["ones_col_div"] = p.sb("c_div", [128, 1], BF16)
        C["ones_row_f32"] = p.sb("c_row", [1, 128], F32)
        C["id128"] = p.sb("c_id", [128, 128], BF16)
    else:
        C["ones_col_div"] = fuse["C"]["ones_col_div"]
        C["ones_row_f32"] = fuse["C"]["ones_row_f32"]
        C["id128"] = fuse["C"]["id128"]
    C["onebf"] = S.t("c_onebf", [128, 1], BF16)
    C["relrow"] = S.t("c_relrow", [128, 1], F32)
    C["tril"] = S.t("c_tril", [128, 128], BF16)
    C["maskpre"] = S.t("c_maskpre", [128, 4], F32)
    mnorm_sb = S.t("mnorm_sb", [128, 64], F32)
    grow = S.t("grow", [1, 16], F32)
    block = p.block

    def s_consts(sync):
        if fuse is None:
            p.dma(sync, C["ones_col_div"][:], cdiv)
            p.dma(sync, C["ones_row_f32"][:], crow)
            p.dma(sync, C["id128"][:], cid)
        p.dma(sync, C["onebf"][:], conebf)
        p.dma(sync, C["relrow"][:], crelrow)
        p.dma(sync, C["tril"][:], ctril)
        p.dma(sync, C["maskpre"][:], maskpre)
        p.dma(sync, mnorm_sb[:], mnormT)
        p.dma(sync, grow[:], gatein)
    block.sync(s_consts)

    def s_memb(gq):
        p.dma(gq, memb[:].rearrange("p (c e) -> p c e", e=132),
              memaug.rearrange("(c p) e -> p c e", p=128))
    block.gpsimd(s_memb)
    ld_consts = p.ld

    def f(v):
        v.wait_ge(p.ldh, ld_consts)
        return v.engine_nop()
    p.do(block.vector, f)

    emit_ln(p, x_in, xbf, C, scratch)

    def f(v):
        for j in range(DC):
            last = v.tensor_reduce(qsum[:, j:j + 1], xbf[:, j * T:(j + 1) * T], AX.X, ALU.add)
        return last
    p.do(block.vector, f)

    def f(v):
        return v.tensor_copy(xsb[:], qsum[:])
    p.do(block.vector, f)

    mm_steps = {}
    for ec in range(DC):
        dep = p.step if ec < 2 else mm_steps[ec - 2]
        so = (ec % 2) * 2048

        def s_w(gq, ec=ec, dep=dep, so=so):
            gq.wait_ge(p.fl, dep)
            p.dma(gq, wsl[:, so:so + 2048].rearrange("p (c o) -> p c o", o=128),
                  wqT[:, ec * 128:(ec + 1) * 128].rearrange("(c p) o -> p c o", p=128))
        block.gpsimd(s_w)
        ld0 = p.ld

        def f(t, ec=ec, ld0=ld0, so=so):
            t.wait_ge(p.ldh, ld0)
            for j in range(DC):
                mm = t.matmul(p.pA[:, 0:1], wsl[:, so + j * 128:so + (j + 1) * 128], xsb[:, j:j + 1],
                              start=(j == 0), stop=(j == DC - 1))
            return mm
        mm_steps[ec] = p.do(block.tensor, f)

        def f(v, ec=ec):
            return v.tensor_copy(qsum[:, ec:ec + 1], p.pA[:, 0:1])
        p.do(block.vector, f)

    def s_q(sync):
        sync.wait_ge(p.fl, p.step)
        p.dma_st(sync, qs_dram.ap(), qsum[:])
    block.sync(s_q)
    st_q = p.st

    def s_cc1(gq):
        gq.wait_ge(p.sth, st_q)
        gq.collective_compute("AllReduce", ALU.add, replica_groups=RG4,
                              ins=[qs_dram.ap().opt()], outs=[qsg_dram.ap().opt()]).then_inc(p.cch)
    block.gpsimd(s_cc1)
    p.cc += 1

    def s_qg(sync):
        sync.wait_ge(p.cch, p.cc)
        p.dma(sync, qsum[:], qsg_dram.ap())
    block.sync(s_qg)
    ld_qg = p.ld

    for m in range(4):
        def f(v, m=m, ld_qg=ld_qg):
            v.wait_ge(p.ldh, ld_qg)
            return v.tensor_tensor(t128[:, 0:16], qsum[:], mnorm_sb[:, m * 16:(m + 1) * 16], ALU.mult)
        p.do(block.vector, f)

        def f(t, m=m):
            t.matmul(p.pA[0:1, 0:16], C["relrow"][:, 0:1], t128[:, 0:16], start=True, stop=True)
            return t.transpose(p.pB[0:16, 0:1], p.pA[0:1, 0:16], C["ones_row_f32"][0:1, 0:1])
        break
    relrow_sb = S.t("relrow_sb", [1, 16], F32)
    for m in range(4):
        def f(v, m=m, ld_qg=ld_qg):
            v.wait_ge(p.ldh, ld_qg)
            return v.tensor_tensor(t128[:, 0:16], qsum[:], mnorm_sb[:, m * 16:(m + 1) * 16], ALU.mult)
        p.do(block.vector, f)

        def f(t, m=m):
            return t.matmul(p.pA[0:1, 0:16], C["relrow"][:, 0:1], t128[:, 0:16], start=True, stop=True)
        p.do(block.tensor, f)

        def f(v, m=m):
            return v.tensor_copy(relrow_sb[0:1, :], p.pA[0:1, 0:16])
        p.do(block.vector, f)

        def f(t, m=m):
            return t.transpose(p.pB[0:16, 0:1], relrow_sb[0:1, 0:16], C["ones_row_f32"][0:1, 0:1])
        p.do(block.tensor, f)

        def f(v, m=m):
            return v.tensor_copy(relc[0:16, m:m + 1], p.pB[0:16, 0:1])
        p.do(block.vector, f)

    def f(v):
        v.tensor_reduce(relc[0:16, 4:5], relc[0:16, 0:4], AX.X, ALU.max, negate=True)
        return v.engine_nop()
    p.do(block.vector, f)

    def f(sc):
        return sc.activation(relc[0:16, 0:4], relc[0:16, 0:4], AF.Exp, bias=relc[0:16, 4:5],
                             scale=np.float32(1.0))
    p.do(block.scalar, f)

    def f(v):
        return v.tensor_reduce(relc[0:16, 5:6], relc[0:16, 0:4], AX.X, ALU.add)
    p.do(block.vector, f)

    def f(v):
        return v.reciprocal(relc[0:16, 6:7], relc[0:16, 5:6])
    p.do(block.vector, f)

    def f(v):
        return v.tensor_scalar(relc[0:16, 0:4], relc[0:16, 0:4], relc[0:16, 6:7], None, ALU.mult)
    p.do(block.vector, f)

    def s_wd(sync):
        sync.wait_ge(p.fl, p.step)
        p.dma_st(sync, w_dram.ap(), relc[0:16, 0:4])
    block.sync(s_wd)
    st_w = p.st

    def s_wl(sync):
        sync.wait_ge(p.sth, st_w)
        p.dma(sync, t128[0:1, 0:64].bitcast(F32), w_dram.ap().rearrange("(o h) m -> o (h m)", o=1))
    block.sync(s_wl)
    ld_w = p.ld

    def f(t, ld_w=ld_w):
        t.wait_ge(p.ldh, ld_w)
        return t.matmul(p.pA[:, 0:64], C["ones_row_f32"][0:1, :], t128[0:1, 0:64],
                        start=True, stop=True)
    p.do(block.tensor, f)

    def f(v):
        return v.tensor_copy(wcol[:], p.pA[:, 0:64])
    p.do(block.vector, f)

    def f(sc):
        return sc.activation(grow[0:1, :], grow[0:1, :], AF.Sigmoid)
    p.do(block.scalar, f)

    def f(t):
        return t.matmul(p.pA[:, 0:16], C["ones_row_f32"][0:1, :], grow[0:1, 0:16],
                        start=True, stop=True)
    p.do(block.tensor, f)

    def f(v):
        return v.tensor_copy(gcol[:], p.pA[:, 0:16])
    p.do(block.vector, f)

    def f(v):
        return v.memset(t128[:], 0.0)
    p.do(block.vector, f)

    totkv = spv

    def f(v):
        return v.memset(totkv[:], 0.0)
    p.do(block.vector, f)

    sct = S.t("sct", [128, 512], F32)
    sp_st = []

    def emit_sigma(v, buf, ec):
        sl = slice(ec * 512, (ec + 1) * 512)
        v.tensor_scalar(sct[:, 0:512], buf[:, sl], 0.0, None, ALU.min)
        return sl

    for g in range(4):
        kmm_steps = {}
        for ec in range(DC):
            dep = p.step if ec < 2 else kmm_steps[ec - 2]
            so = (ec % 2) * 2048

            def s_w(gq, g=g, ec=ec, dep=dep, so=so):
                gq.wait_ge(p.fl, dep)
                p.dma(gq, wsl[:, so:so + 2048].rearrange("p (c o) -> p c o", o=128),
                      wkT[:, ec * 128:(ec + 1) * 128].rearrange("(c p) o -> p c o", p=128))
            block.gpsimd(s_w)
            ld0 = p.ld

            def f(t, g=g, ec=ec, ld0=ld0, so=so):
                t.wait_ge(p.ldh, ld0)
                for j in range(DC):
                    mm = t.matmul(p.pA[:, 0:512], wsl[:, so + j * 128:so + (j + 1) * 128],
                                  xbf[:, j * T + g * 512:j * T + (g + 1) * 512],
                                  start=(j == 0), stop=(j == DC - 1))
                return mm
            kmm_steps[ec] = p.do(block.tensor, f)

            def f(v, ec=ec):
                return v.tensor_scalar(sct[:, 0:512], p.pA[:, 0:512], 0.0, None, ALU.min)
            p.do(block.vector, f)

            def f(sc, ec=ec):
                return sc.activation(sct[:, 0:512], sct[:, 0:512], AF.Exp)
            p.do(block.scalar, f)

            def f(v, ec=ec):
                v.tensor_scalar(kq_buf[:, ec * 512:(ec + 1) * 512], p.pA[:, 0:512], 0.0, None, ALU.max)
                return v.tensor_tensor(kq_buf[:, ec * 512:(ec + 1) * 512],
                                       kq_buf[:, ec * 512:(ec + 1) * 512], sct[:, 0:512], ALU.add)
            p.do(block.vector, f)

        for et in range(4):
            dep = p.step

            def s_w(gq, g=g, et=et, dep=dep):
                gq.wait_ge(p.fl, dep)
                p.dma(gq, bview(aG, 49152, BF16, 16 * 512)[:].rearrange("p (c o) -> p c o", o=512),
                      wvT[:, et * 512:(et + 1) * 512].rearrange("(c p) o -> p c o", p=128))
            block.gpsimd(s_w)
            ld0 = p.ld
            vsl = bview(aG, 49152, BF16, 16 * 512)

            for tc in range(4):
                def f(t, g=g, et=et, tc=tc, ld0=ld0, vsl=vsl):
                    t.wait_ge(p.ldh, ld0)
                    for j in range(DC):
                        mm = t.matmul(p.pB[:, 0:512],
                                      xbf[:, j * T + g * 512 + tc * 128:j * T + g * 512 + tc * 128 + 128],
                                      vsl[:, j * 512:(j + 1) * 512],
                                      start=(j == 0), stop=(j == DC - 1))
                    return mm
                p.do(block.tensor, f)

                def f(v, tc=tc, et=et):
                    if tc == 0 and et == 0 and sp_st:
                        v.wait_ge(p.sth, sp_st[-1])
                    return v.tensor_copy(vg_buf[:, tc * 2048 + et * 512:tc * 2048 + (et + 1) * 512],
                                         p.pB[:, 0:512])
                p.do(block.vector, f)

        for tc in range(4):
            for hg in range(2):
                def f(t, tc=tc, hg=hg):
                    for i in range(8):
                        h2 = hg * 8 + i
                        last = t.transpose(p.pT[:, i * 128:(i + 1) * 128],
                                           kq_buf[:, h2 * 512 + tc * 128:h2 * 512 + tc * 128 + 128],
                                           C["id128"][:])
                    return last
                p.do(block.tensor, f)

                def f(v, tc=tc, hg=hg):
                    if tc == 0 and hg == 0 and sp_st:
                        v.wait_ge(p.sth, sp_st[-1])
                    for i in range(8):
                        h2 = hg * 8 + i
                        last = v.tensor_copy(skt_buf[:, tc * 2048 + h2 * 128:tc * 2048 + (h2 + 1) * 128],
                                             p.pT[:, i * 128:(i + 1) * 128])
                    return last
                p.do(block.vector, f)

        def s_sp(sync, g=g):
            sync.wait_ge(p.fl, p.step)
            for tc in range(4):
                p.dma_st(sync, sk_dram[(g * 4 + tc) * 128:(g * 4 + tc + 1) * 128, :],
                         skt_buf[:, tc * 2048:(tc + 1) * 2048])
                p.dma_st(sync, v_dram[(g * 4 + tc) * 128:(g * 4 + tc + 1) * 128, :],
                         vg_buf[:, tc * 2048:(tc + 1) * 2048])
        block.sync(s_sp)
        sp_st.append(p.st)

        for h in range(DC):
            def f(t, g=g, h=h):
                for tc in range(4):
                    t.matmul(p.pD[:, 0:128], skt_buf[:, tc * 2048 + h * 128:tc * 2048 + (h + 1) * 128],
                             vg_buf[:, tc * 2048 + h * 128:tc * 2048 + (h + 1) * 128],
                             start=(tc == 0), stop=False)
                    mm = t.matmul(p.pD[:, 128:129], skt_buf[:, tc * 2048 + h * 128:tc * 2048 + (h + 1) * 128],
                                  C["onebf"][:, 0:1], start=False, stop=(tc == 3))
                return mm
            p.do(block.tensor, f)

            def f(v, h=h):
                return v.tensor_tensor(totkv[:, h * 132:h * 132 + 129], totkv[:, h * 132:h * 132 + 129],
                                       p.pD[:, 0:129], ALU.add)
            p.do(block.vector, f)

    def s_kv(sync):
        sync.wait_ge(p.fl, p.step)
        p.dma_st(sync, kv_dram.ap().rearrange("(h p) c -> p h c", p=128),
                 totkv[:].rearrange("p (h c) -> p h c", c=132))
    block.sync(s_kv)
    st_kv = p.st

    def s_cc2(gq):
        gq.wait_ge(p.sth, st_kv)
        gq.collective_compute("AllGather", ALU.bypass, replica_groups=RG4,
                              ins=[kv_dram.ap().opt()], outs=[kvg_dram.ap().opt()]).then_inc(p.cch)
    block.gpsimd(s_cc2)
    p.cc += 1

    def f(v):
        v.wait_ge(p.sth, st_kv)
        v.wait_ge(p.cch, p.cc)
        return v.memset(spv[:], 0.0)
    p.do(block.vector, f)

    rbuf = bview(aG, 49152, F32, 16 * 132)
    for r in range(4):
        dep = p.step

        def s_r(sync, r=r, dep=dep):
            sync.wait_ge(p.fl, dep)
            if r == 0:
                sync.wait_ge(p.cch, p.cc)
            p.dma(sync, rbuf[:].rearrange("p (h c) -> p h c", c=132),
                  kvg_dram[r * 2048:(r + 1) * 2048, :].rearrange("(h p) c -> p h c", p=128))
        block.sync(s_r)
        ld0 = p.ld

        def f(v, r=r, ld0=ld0):
            v.wait_ge(p.ldh, ld0)
            return v.tensor_scalar(rbuf[:], rbuf[:], C["maskpre"][:, r:r + 1], None, ALU.mult)
        p.do(block.vector, f)

        def f(v, r=r):
            return v.tensor_tensor(spv[:], spv[:], rbuf[:], ALU.add)
        p.do(block.vector, f)

    if debug:
        def s_dbg(sync):
            sync.wait_ge(p.fl, p.step)
            sync.wait_ge(p.sth, p.st)
            p.dma_st(sync, dbg_sk[:, :], sk_dram.ap())
            p.dma_st(sync, dbg_v[:, :], v_dram.ap())
            p.dma_st(sync, dbg_kvg[:, :], kvg_dram.ap())
            p.dma_st(sync, dbg_qs[:, :], qsum[:])
            p.dma_st(sync, dbg_w[:, 0:64], wcol[:])
            p.dma_st(sync, dbg_w[:, 64:80], gcol[:])
            p.dma_st(sync, dbg_w[:, 80:212], spv[:, 0:132])
            p.dma_st(sync, dbg_kv0[:, :], kv_dram.ap())
            p.dma_st(sync, dbg_qs0[:, :], qs_dram.ap())
        block.sync(s_dbg)

    EPS1 = 1e-6
    for g in range(4):
        qmm_steps = {}
        for ec in range(DC):
            dep = p.step if ec < 2 else qmm_steps[ec - 2]
            so = (ec % 2) * 2048

            def s_w(gq, g=g, ec=ec, dep=dep, so=so):
                gq.wait_ge(p.fl, dep)
                p.dma(gq, wsl[:, so:so + 2048].rearrange("p (c o) -> p c o", o=128),
                      wqT[:, ec * 128:(ec + 1) * 128].rearrange("(c p) o -> p c o", p=128))
            block.gpsimd(s_w)
            ld0 = p.ld

            def f(t, g=g, ec=ec, ld0=ld0, so=so):
                t.wait_ge(p.ldh, ld0)
                for j in range(DC):
                    mm = t.matmul(p.pA[:, 0:512], wsl[:, so + j * 128:so + (j + 1) * 128],
                                  xbf[:, j * T + g * 512:j * T + (g + 1) * 512],
                                  start=(j == 0), stop=(j == DC - 1))
                return mm
            qmm_steps[ec] = p.do(block.tensor, f)

            def f(v, ec=ec):
                return v.tensor_scalar(sct[:, 0:512], p.pA[:, 0:512], 0.0, None, ALU.min)
            p.do(block.vector, f)

            def f(sc, ec=ec):
                return sc.activation(sct[:, 0:512], sct[:, 0:512], AF.Exp)
            p.do(block.scalar, f)

            def f(v, ec=ec):
                v.tensor_scalar(kq_buf[:, ec * 512:(ec + 1) * 512], p.pA[:, 0:512], 0.0, None, ALU.max)
                return v.tensor_tensor(kq_buf[:, ec * 512:(ec + 1) * 512],
                                       kq_buf[:, ec * 512:(ec + 1) * 512], sct[:, 0:512], ALU.add)
            p.do(block.vector, f)

        dep = p.step

        def s_rl(sync, g=g, dep=dep):
            sync.wait_ge(p.fl, dep)
            if sp_st:
                sync.wait_ge(p.sth, sp_st[-1])
            p.dma(sync, skt_buf[:].rearrange("p (c e) -> p c e", e=2048),
                  sk_dram[g * 512:(g + 1) * 512, :].rearrange("(c p) e -> p c e", p=128))
            p.dma(sync, vg_buf[:].rearrange("p (c e) -> p c e", e=2048),
                  v_dram[g * 512:(g + 1) * 512, :].rearrange("(c p) e -> p c e", p=128))
        block.sync(s_rl)
        ld_rl = p.ld

        for tc in range(4):
            for h in range(DC):
                skt_sl = skt_buf[:, tc * 2048 + h * 128:tc * 2048 + (h + 1) * 128]
                vg_sl = vg_buf[:, tc * 2048 + h * 128:tc * 2048 + (h + 1) * 128]
                sq_sl = kq_buf[:, h * 512 + tc * 128:h * 512 + tc * 128 + 128]

                def f(v, h=h):
                    return v.tensor_copy(stbf[:, 0:132], spv[:, h * 132:(h + 1) * 132])
                p.do(block.vector, f)

                def f(t, tc=tc, h=h, ld_rl=ld_rl, skt_sl=skt_sl):
                    t.wait_ge(p.ldh, ld_rl)
                    return t.transpose(p.pT[:, 0:128], skt_sl, C["id128"][:])
                p.do(block.tensor, f)

                def f(v):
                    return v.tensor_copy(sktm[:], p.pT[:, 0:128])
                p.do(block.vector, f)

                def f(t, sq_sl=sq_sl):
                    return t.matmul(p.pE[:, 0:128], sktm[:], sq_sl, start=True, stop=True)
                p.do(block.tensor, f)

                def f(v):
                    return v.scalar_tensor_tensor(stm[:], p.pE[:, 0:128], 1.0, C["tril"][:],
                                                  ALU.mult, ALU.mult)
                p.do(block.vector, f)

                def f(t, sq_sl=sq_sl, vg_sl=vg_sl):
                    t.matmul(p.pD[:, 0:129], sq_sl, stbf[:, 0:129], start=True, stop=False)
                    t.matmul(p.pD[:, 0:128], stm[:], vg_sl, start=False, stop=False)
                    return t.matmul(p.pD[:, 128:129], stm[:], C["onebf"][:, 0:1],
                                    start=False, stop=True)
                p.do(block.tensor, f)

                if debug and g == 0 and tc == 0 and h == 0:
                    dbgc = bview(aM, 0, F32, 1024)

                    def f(v):
                        v.tensor_copy(dbgc[:, 0:129], p.pD[:, 0:129])
                        return v.tensor_copy(dbgc[:, 256:256 + 128], stm[:])
                    p.do(block.vector, f)

                mem_ps = [(p.pB, 0), (p.pB, 256), (p.pE, 0), (p.pE, 256)]

                def f(t, h=h, sq_sl=sq_sl):
                    for m in range(4):
                        ps, off = mem_ps[m]
                        mm = t.matmul(ps[:, off:off + 132], sq_sl,
                                      memb[:, (m * 16 + h) * 132:(m * 16 + h + 1) * 132],
                                      start=True, stop=True)
                    return mm
                p.do(block.tensor, f)

                def f(v, h=h):
                    for m in range(4):
                        ps, off = mem_ps[m]
                        last = v.tensor_scalar(rs4a[:, m:m + 1], ps[:, off + 128:off + 129],
                                               float(EPS1), None, ALU.max)
                    return last
                p.do(block.vector, f)

                def f(v):
                    return v.reciprocal(rs4b[:], rs4a[:])
                p.do(block.vector, f)

                def f(v, h=h):
                    return v.tensor_tensor(rs4b[:], rs4b[:], wcol[:, h * 4:h * 4 + 4], ALU.mult)
                p.do(block.vector, f)

                def f(v, h=h):
                    for m in range(4):
                        ps, off = mem_ps[m]
                        last = v.tensor_scalar(sct[:, m * 128:(m + 1) * 128], ps[:, off:off + 128],
                                               rs4b[:, m:m + 1], None, ALU.mult)
                    return last
                p.do(block.vector, f)

                def f(v):
                    v.tensor_tensor(oloc[:], sct[:, 0:128], sct[:, 128:256], ALU.add)
                    v.tensor_tensor(t128[:], sct[:, 256:384], sct[:, 384:512], ALU.add)
                    return v.tensor_tensor(accmem[:], oloc[:], t128[:], ALU.add)
                p.do(block.vector, f)

                def f(v, h=h):
                    return v.tensor_scalar(rs1[:], p.pD[:, 128:129], float(EPS1), None, ALU.max)
                p.do(block.vector, f)

                def f(v):
                    return v.reciprocal(rs2[:], rs1[:])
                p.do(block.vector, f)

                def f(v, h=h):
                    v.tensor_scalar(oloc[:], p.pD[:, 0:128], rs2[:], None, ALU.mult)
                    v.tensor_tensor(t128[:], accmem[:], oloc[:], ALU.subtract)
                    return v.scalar_tensor_tensor(otm[:], t128[:], gcol[:, h:h + 1], oloc[:],
                                                  ALU.mult, ALU.add)
                p.do(block.vector, f)

                if debug and g == 0 and tc == 0 and h == 0:
                    def f(v):
                        v.tensor_copy(dbgc[:, 384:384 + 128], accmem[:])
                        v.tensor_copy(dbgc[:, 512:512 + 128], oloc[:])
                        v.tensor_copy(dbgc[:, 640:640 + 128], otm[:])
                        return v.tensor_copy(dbgc[:, 768:768 + 132], stbf[:, 0:132])
                    p.do(block.vector, f)

                def f(t, skt_sl=skt_sl, vg_sl=vg_sl):
                    t.transpose(p.pT[:, 128:256], otm[:], C["id128"][:])
                    t.matmul(p.pF[:, 0:128], skt_sl, vg_sl, start=True, stop=True)
                    return t.matmul(p.pF[:, 128:129], skt_sl, C["onebf"][:, 0:1],
                                    start=True, stop=True)
                p.do(block.tensor, f)

                def f(v, tc=tc, h=h):
                    v.tensor_copy(ofm_buf[:, h * 512 + tc * 128:h * 512 + tc * 128 + 128],
                                  p.pT[:, 128:256])
                    return v.tensor_tensor(spv[:, h * 132:h * 132 + 129],
                                           spv[:, h * 132:h * 132 + 129], p.pF[:, 0:129], ALU.add)
                p.do(block.vector, f)

        mt_st = []
        omm_steps = {}
        for oc in range(DC):
            dep = p.step if oc < 2 else omm_steps[oc - 2]
            so = (oc % 2) * 2048

            def s_w(gq, oc=oc, dep=dep, so=so):
                gq.wait_ge(p.fl, dep)
                p.dma(gq, wsl[:, so:so + 2048].rearrange("p (c o) -> p c o", o=128),
                      woT[:, oc * 128:(oc + 1) * 128].rearrange("(c p) o -> p c o", p=128))
            block.gpsimd(s_w)
            ld0 = p.ld

            def f(t, oc=oc, ld0=ld0, so=so):
                t.wait_ge(p.ldh, ld0)
                for ec in range(DC):
                    mm = t.matmul(p.pC[:, 0:512], wsl[:, so + ec * 128:so + (ec + 1) * 128],
                                  ofm_buf[:, ec * 512:(ec + 1) * 512],
                                  start=(ec == 0), stop=(ec == DC - 1))
                return mm
            omm_steps[oc] = p.do(block.tensor, f)

            def f(v):
                return v.tensor_copy(tmp[:, 0:512], p.pC[:, 0:512])
            p.do(block.vector, f)

            def s_m(sync, g=g, oc=oc):
                sync.wait_ge(p.fl, p.step)
                p.dma_st(sync, mix_dram[oc * 128:(oc + 1) * 128, g * 512:(g + 1) * 512],
                         tmp[:, 0:512])
            block.sync(s_m)
            def f(v):
                v.wait_ge(p.sth, p.st)
                return v.engine_nop()
            p.do(block.vector, f)

    if debug:
        def s_dm(sync):
            sync.wait_ge(p.fl, p.step)
            p.dma_st(sync, dbg_mix[:, :], mix_dram.ap())
            p.dma_st(sync, dbg_c[:, :], bview(aM, 0, F32, 1024))
            p.dma_st(sync, dbg_c2[:, :], bview(aM, 8192, F32, 132))
        block.sync(s_dm)

    def f(g2):
        g2.wait_ge(p.sth, p.st)
        return g2.engine_nop()
    p.do(block.gpsimd, f)
    emit_postln_residual(p, mix_dram, x_in, x_out, C, scratch, xbf)
    emit_store_wait(p)
    return p


def prep_l1attn_inputs(full, core):
    mem = np.asarray(full["L1_attn_memories"], np.float32)
    mn = np.asarray(full["L1_attn_memory_norms"], np.float32)
    memaug = np.concatenate([mem, mn[..., None], np.zeros((4, 16, 128, 3), np.float32)], axis=3)
    memaug = memaug.reshape(64 * 128, 132)
    mnormT = np.ascontiguousarray(mn.transpose(2, 0, 1).reshape(128, 64))
    tril = np.triu(np.ones((128, 128), np.float32)).astype(BF16NP)
    maskpre = np.zeros((128, 4), np.float32)
    pos = core % 4
    maskpre[:, 0:pos] = 1.0
    shared = {
        "wqT": np.ascontiguousarray(np.asarray(full["L1_attn_w_q"], np.float32).T),
        "wkT": np.ascontiguousarray(np.asarray(full["L1_attn_w_k"], np.float32).T),
        "wvT": np.ascontiguousarray(np.asarray(full["L1_attn_w_v"], np.float32).T),
        "woT": np.ascontiguousarray(np.asarray(full["L1_attn_w_o"], np.float32).T),
        "memaug": memaug, "mnormT": mnormT,
        "gatein": np.asarray(full["L1_attn_gate"], np.float32).reshape(1, 16),
        "maskpre": maskpre,
        "ctril": tril,
        "cdiv": np.full((128, 1), 1.0 / 2048.0, BF16NP),
        "crow": np.ones((1, 128), np.float32),
        "cid": np.eye(128, dtype=BF16NP),
        "conebf": np.ones((128, 1), BF16NP),
        "crelrow": np.full((128, 1), (128.0 ** -0.5) / 8192.0, np.float32),
    }
    return shared


def run_l1attn(x_flat, full, debug=False):
    shards = shard_fm(x_flat)
    p = build_l1attn(debug)
    in_maps = [dict(prep_l1attn_inputs(full, c), x_in=s) for c, s in enumerate(shards)]
    res = run_bass_kernel_spmd(p.nc, in_maps, core_ids=list(range(8)))
    p.close()
    if debug:
        return res
    return unshard_fm([res.results[c]["x_out"] for c in range(8)])



def _u8codes(w_int32):
    return np.ascontiguousarray(w_int32.view(np.uint8).reshape(w_int32.shape[0], -1)[:, 0::4])


def prep_l1ffn_inputs(full):
    wg = np.ascontiguousarray(_u8codes(np.asarray(full["L1_ffn_wq_gate"])).T)
    wu = np.ascontiguousarray(_u8codes(np.asarray(full["L1_ffn_wq_up"])).T)
    wd = np.ascontiguousarray(_u8codes(np.asarray(full["L1_ffn_wq_down"])).T)
    shared = {
        "wg": wg, "wu": wu, "wd": wd,
        "sg": np.asarray(full["L1_ffn_s_gate"], np.float32).reshape(1, -1),
        "su": np.asarray(full["L1_ffn_s_up"], np.float32).reshape(1, -1),
        "sd": np.asarray(full["L1_ffn_s_down"], np.float32).reshape(1, -1),
        "cdiv": np.full((128, 1), 1.0 / 2048.0, BF16NP),
        "crow": np.ones((1, 128), np.float32),
        "cm128": np.full((128, 1), -128.0, np.float32),
    }
    return shared


def shard_fm(x_flat):
    return [np.ascontiguousarray(x_flat[c * T:(c + 1) * T].T) for c in range(8)]


def unshard_fm(shards):
    return np.concatenate([s.T for s in shards], axis=0)


def run_l1ffn(x_flat, full):
    shared = prep_l1ffn_inputs(full)
    shards = shard_fm(x_flat)
    p = build_l1ffn()
    in_maps = [dict(shared, x_in=s) for s in shards]
    res = run_bass_kernel_spmd(p.nc, in_maps, core_ids=list(range(8)))
    p.close()
    return unshard_fm([res.results[c]["x_out"] for c in range(8)])


def build_fused():
    p = Prog().start()
    nc = p.nc
    x0 = nc.dram_tensor("x_in", [D, T], F32, kind="ExternalInput").ap()
    xo = nc.dram_tensor("x_out", [D, T], F32, kind="ExternalOutput").ap()
    x1 = nc.dram_tensor("x1", [D, T], F32).ap()
    x2 = nc.dram_tensor("x2", [D, T], F32).ap()
    x3 = nc.dram_tensor("x3", [D, T], F32).ap()
    cdiv = nc.dram_tensor("cdiv", [128, 1], BF16, kind="ExternalInput").ap()
    crow = nc.dram_tensor("crow", [1, 128], F32, kind="ExternalInput").ap()
    cid = nc.dram_tensor("cid", [128, 128], BF16, kind="ExternalInput").ap()
    P1 = p.sb("P1", [128, 65536], U8)
    P2 = p.sb("P2", [128, 65536], U8)
    REST = p.sb("REST", [128, 80896], U8)
    C = {
        "ones_col_div": p.sb("g_div", [128, 1], BF16),
        "ones_row_f32": p.sb("g_row", [1, 128], F32),
        "id128": p.sb("g_id", [128, 128], BF16),
    }
    block = p.block

    def s_c(sync):
        p.dma(sync, C["ones_col_div"][:], cdiv)
        p.dma(sync, C["ones_row_f32"][:], crow)
        p.dma(sync, C["id128"][:], cid)
    block.sync(s_c)
    ldc = p.ld

    def f(v):
        v.wait_ge(p.ldh, ldc)
        return v.engine_nop()
    p.do(block.vector, f)

    def barrier():
        def f(g):
            g.wait_ge(p.sth, p.st)
            return g.engine_nop()
        p.do(block.gpsimd, f)

    base = dict(p=p, P1=P1, P2=P2, REST=REST, C=C)
    build_l0attn(fuse=dict(base, pre="a_", x_in=x0, x_out=x1))
    barrier()
    build_l0ffn(fuse=dict(base, pre="f_", x_in=x1, x_out=x2))
    barrier()
    build_l1attn(fuse=dict(base, pre="b_", x_in=x2, x_out=x3))
    barrier()
    build_l1ffn(fuse=dict(base, pre="g_", x_in=x3, x_out=xo))
    return p


def kernel(**inputs):
    x = np.asarray(inputs["L0_attn_hidden_states"], np.float32).reshape(16384, 2048)
    shards = shard_fm(x)
    shared = {}
    shared.update(prep_l0attn_inputs(inputs))
    shared.update(prep_l0ffn_inputs(inputs))
    shared.update(prep_l1ffn_inputs(inputs))
    p = build_fused()
    in_maps = []
    for c in range(8):
        m = dict(shared)
        m.update(prep_l1attn_inputs(inputs, c))
        m["x_in"] = shards[c]
        in_maps.append(m)
    res = run_bass_kernel_spmd(p.nc, in_maps, core_ids=list(range(8)))
    p.close()
    out = unshard_fm([res.results[c]["x_out"] for c in range(8)])
    return np.ascontiguousarray(out.reshape(2, 8192, 2048).astype(np.float32))


def kernel_multilaunch(**inputs):
    x = np.asarray(inputs["L0_attn_hidden_states"], np.float32).reshape(16384, 2048)
    x = run_l0attn(x, inputs)
    x = run_l0ffn(x, inputs)
    x = run_l1attn(x, inputs)
    x = run_l1ffn(x, inputs)
    return np.ascontiguousarray(x.reshape(2, 8192, 2048).astype(np.float32))
